# version 33
# baseline (speedup 1.0000x reference)
import math
import numpy as np

import concourse.bass as bass
import concourse.mybir as mybir
from concourse.bass_utils import run_bass_kernel_spmd

F32 = mybir.dt.float32
F16 = mybir.dt.float16
BF16 = mybir.dt.bfloat16
I32 = mybir.dt.int32
AF = mybir.ActivationFunctionType
ALU = mybir.AluOpType
AXX = mybir.AxisListType.X

D = 1024
LAT = 256
T = 2048
NSEQ = 2
F = 4096
NCH = 16
NSP = 4
PI = math.pi
TWO_PI = 2.0 * math.pi
EPS_LN = 1e-5
EPS_RMS = 1e-6
BASE_G = 1000000.0
BASE_L = 10000.0


def _act_set_ids():
    fallback = {"natural_log_exp_and_others": 6, "trig_and_small": 9,
                "gelu_and_others": 10, "silu_and_others": 18}
    try:
        from concourse.hw_specs import get_activation_tables
        names = list(get_activation_tables("TRN2"))
        return {k: names.index(k) for k in fallback}
    except Exception:
        return fallback


_ACT_SETS = _act_set_ids()
ACT_COMBO_SET = _ACT_SETS["natural_log_exp_and_others"]
ACT_TRIG_SET = _ACT_SETS["trig_and_small"]
ACT_GELU_SET = _ACT_SETS["gelu_and_others"]
ACT_SILU_SET = _ACT_SETS["silu_and_others"]

DEBUG_TAPS = False
PHASES = 99
SUBPH = 9


class Prog:

    def __init__(self, nc, blk):
        self.nc, self.blk = nc, blk
        self.eng = {}
        self.reg = {}
        self.overlap = {}
        self.opvc = {}

    WAVE = 12

    def engine(self, name, sec, step):
        sem = self.nc.alloc_semaphore(name=f"sem_{name}")
        self.eng[name] = dict(sem=sem, cnt=0, step=step, sec=sec, pend=[],
                              barriers=[0], bar_cnt=0, vc={})

    def _learn(self, know, pe_, pc_):
        if pc_ > know.get(pe_, -1):
            know[pe_] = pc_
        src = self.opvc.get((pe_, pc_))
        if src:
            for k2, v2 in src.items():
                if v2 > know.get(k2, -1):
                    know[k2] = v2

    def close_wave(self, en):
        e = self.eng[en]
        if e["cnt"] == e["barriers"][-1]:
            return
        val = e["cnt"]

        def body(obj):
            obj.wait_ge(e["sem"], val)

        e["sec"](body)
        e["barriers"].append(val)
        e["bar_cnt"] = 0
        self._learn(e["vc"], en, val)

    def _expand(self, keys):
        out = []
        for k in keys:
            if isinstance(k, str) and k in self.overlap:
                out.extend(self.overlap[k])
            else:
                out.append(k)
        return out

    def _need(self, waits, en, pe, pc):
        if pe == en == "pe":
            return
        if pc > waits.get(pe, -1):
            waits[pe] = pc

    def emit(self, en, f, reads=(), writes=(), inc=True):
        e = self.eng[en]
        reads = self._expand(reads)
        writes = self._expand(writes)
        waits = {}
        for r in reads:
            rec = self.reg.get(r)
            if rec:
                for pe_, pc_ in rec[0].items():
                    self._need(waits, en, pe_, pc_)
        for wk in writes:
            rec = self.reg.get(wk)
            if rec:
                for pe_, pc_ in rec[0].items():
                    self._need(waits, en, pe_, pc_)
                for pe_, pc_ in rec[1].items():
                    self._need(waits, en, pe_, pc_)
        know = e["vc"]
        real = []
        for pe_, pc_ in sorted(waits.items(), key=str):
            if self.eng[pe_]["step"] == 16 and pe_ != en:
                bars = self.eng[pe_]["barriers"]
                if pc_ > bars[-1]:
                    self.close_wave(pe_)
                    bars = self.eng[pe_]["barriers"]
                import bisect as _bi
                pc_ = bars[_bi.bisect_left(bars, pc_)]
            if pc_ > know.get(pe_, -1):
                real.append((pe_, pc_))
                self._learn(know, pe_, pc_)

        def body(obj):
            for pe_, pc_ in real:
                obj.wait_ge(self.eng[pe_]["sem"], pc_)
            ins = f(obj)
            if inc:
                if isinstance(ins, (list, tuple)):
                    ins = ins[-1]
                ins.then_inc(e["sem"], e["step"])

        e["sec"](body)
        if inc and e["step"] == 16:
            e["bar_cnt"] += 1
        if inc:
            e["cnt"] += e["step"]
            c = e["cnt"]
            self.opvc[(en, c)] = dict(know, **{en: c})
            if e["step"] == 16 and e["bar_cnt"] >= self.WAVE:
                self.close_wave(en)
            for wk in e["pend"]:
                self.reg[wk] = [{en: c}, {}]
            e["pend"] = []
            for wk in writes:
                self.reg[wk] = [{en: c}, {}]
            for r in reads:
                rec = self.reg.setdefault(r, [{}, {}])
                rec[1][en] = c
        else:
            nxt = e["cnt"] + e["step"]
            for wk in writes:
                self.reg[wk] = [{en: nxt}, {}]
                e["pend"].append(wk)
            for r in reads:
                rec = self.reg.setdefault(r, [{}, {}])
                rec[1][en] = nxt

    def wait_all(self, en):
        e = self.eng[en]
        for pe_, pe in self.eng.items():
            if pe["step"] == 16:
                self.close_wave(pe_)
        know = e["vc"]
        real = []
        for pe_, pe in self.eng.items():
            if pe_ == en:
                continue
            if pe["cnt"] > know.get(pe_, -1) and pe["cnt"] > 0:
                real.append((pe_, pe["cnt"]))
                self._learn(know, pe_, pe["cnt"])

        def body(obj):
            for pe_, pc_ in real:
                obj.wait_ge(self.eng[pe_]["sem"], pc_)

        e["sec"](body)


class Arena:
    def __init__(self, nc, name, ncols):
        self.t = nc.alloc_sbuf_tensor(name, [128, ncols], BF16)
        self.ncols = ncols
        self.regions = {}

    def view(self, name, off, ncols, dtype, shape):
        assert name not in self.regions
        assert off + ncols <= self.ncols, (name, off, ncols)
        self.regions[name] = (off, ncols)
        ap = self.t[:, off:off + ncols]
        if dtype != BF16:
            ap = ap.bitcast(dtype)
        free = shape[1:]
        n = 1
        for xx in free:
            n *= xx
        assert n == ap.shape[1], (name, n, ap.shape)
        if len(free) > 1:
            subs = " ".join(f"f{i}" for i in range(len(free)))
            kw = {f"f{i}": free[i] for i in range(len(free))}
            ap = ap.rearrange(f"p ({subs}) -> p {subs}", **kw)
        if shape[0] != 128:
            ap = ap[0:shape[0]]
        return ap

    def overlaps(self):
        names = list(self.regions)
        out = {}
        for a in names:
            oa, na = self.regions[a]
            out[a] = tuple(
                b for b in names
                if not (self.regions[b][0] + self.regions[b][1] <= oa
                        or oa + na <= self.regions[b][0]))
        return out


def build_program():
    nc = bass.Bass("TRN2", target_bir_lowering=False, debug=False)

    x = nc.dram_tensor("x", [NSEQ, T, D], F32, kind="ExternalInput").ap()
    n1w = nc.dram_tensor("norm1_w", [D], F32, kind="ExternalInput").ap()
    n2w = nc.dram_tensor("norm2_w", [D], F32, kind="ExternalInput").ap()
    g_wq = nc.dram_tensor("g_wq", [D, D], F32, kind="ExternalInput").ap()
    g_kvd = nc.dram_tensor("g_kv_down", [D, LAT], F32, kind="ExternalInput").ap()
    g_kup = nc.dram_tensor("g_k_up", [LAT, LAT], F32, kind="ExternalInput").ap()
    g_vup = nc.dram_tensor("g_v_up", [LAT, LAT], F32, kind="ExternalInput").ap()
    g_wo = nc.dram_tensor("g_wo", [D, D], F32, kind="ExternalInput").ap()
    l_wq = nc.dram_tensor("l_wq", [D, D], F32, kind="ExternalInput").ap()
    l_wk = nc.dram_tensor("l_wk", [D, LAT], F32, kind="ExternalInput").ap()
    l_wv = nc.dram_tensor("l_wv", [D, LAT], F32, kind="ExternalInput").ap()
    l_wo = nc.dram_tensor("l_wo", [D, D], F32, kind="ExternalInput").ap()
    W_fc = nc.dram_tensor("W_fc", [F, D], F32, kind="ExternalInput").ap()
    b_fc = nc.dram_tensor("b_fc", [F], F32, kind="ExternalInput").ap()
    A_fc = nc.dram_tensor("A_fc", [16, D], F32, kind="ExternalInput").ap()
    B_fc = nc.dram_tensor("B_fc", [F, 16], F32, kind="ExternalInput").ap()
    W_pj = nc.dram_tensor("W_proj", [D, F], F32, kind="ExternalInput").ap()
    b_pj = nc.dram_tensor("b_proj", [D], F32, kind="ExternalInput").ap()
    A_pj = nc.dram_tensor("A_proj", [16, F], F32, kind="ExternalInput").ap()
    B_pj = nc.dram_tensor("B_proj", [D, 16], F32, kind="ExternalInput").ap()
    out = nc.dram_tensor("out", [NSEQ, T, D], F32, kind="ExternalOutput").ap()

    sk = "ExternalOutput" if DEBUG_TAPS else "Internal"
    xnd = nc.dram_tensor("xnd", [NSEQ * T, D], BF16, kind=sk).ap()
    s1d = nc.dram_tensor("s1d", [NSEQ * T, D], BF16, kind=sk).ap()
    sdd = nc.dram_tensor("sdd", [NSEQ * T, D], BF16, kind=sk).ap()
    h2d = nc.dram_tensor("h2d", [F, NSEQ * T], BF16, kind=sk).ap()
    h2dv = h2d.rearrange("(fk p) tok -> p fk tok", fk=32, p=128)
    wqfcd = nc.dram_tensor("wqfcd", [F, D], BF16, kind=sk).ap()
    wqpjd = nc.dram_tensor("wqpjd", [D, F], BF16, kind=sk).ap()
    recd = nc.dram_tensor("recd", [2, 2048], BF16, kind="Internal").ap()
    dbg1 = nc.dram_tensor("dbg1", [128, F], BF16, kind=sk).ap()
    dbg2 = nc.dram_tensor("dbg2", [17, F], BF16, kind=sk).ap()

    AR = Arena(nc, "arena", 98304)
    V = AR.view
    wq1 = [V(f"wq1_{k}", 1024 * k, 1024, BF16, [128, 1024]) for k in range(8)]
    kvd = [V(f"kvd_{k}", 8192 + 256 * k, 256, BF16, [128, 256]) for k in range(8)]
    kup = [V(f"kup_{k}", 10240 + 256 * k, 256, BF16, [128, 256]) for k in range(2)]
    vup = [V(f"vup_{k}", 10752 + 256 * k, 256, BF16, [128, 256]) for k in range(2)]
    wo1 = [V(f"wo1_{k}", 11264 + 1024 * k, 1024, BF16, [128, 1024]) for k in range(8)]
    wq2 = [V(f"wq2_{k}", 19456 + 1024 * k, 1024, BF16, [128, 1024]) for k in range(8)]
    wk2 = [V(f"wk2_{k}", 27648 + 256 * k, 256, BF16, [128, 256]) for k in range(8)]
    wv2 = [V(f"wv2_{k}", 29696 + 256 * k, 256, BF16, [128, 256]) for k in range(8)]
    wo2 = [V(f"wo2_{k}", 31744 + 1024 * k, 1024, BF16, [128, 1024]) for k in range(8)]
    qwi = [V(f"qwi_{i}", 2048 * i, 2048, F32, [128, 1024]) for i in range(2)]
    qw_y = V("qw_y", 4096, 2048, F32, [128, 1024])
    qw_i = V("qw_i", 6144, 2048, I32, [128, 1024])
    qw_g = V("qw_g", 8192, 2048, F32, [128, 1024])
    qw_o = V("qw_o", 10240, 1024, BF16, [128, 1024])
    wfcT = [V(f"wfcT_{k}", 4096 * k, 4096, BF16, [128, 4096]) for k in range(8)]
    wpjT = [V(f"wpjT_{k}", 1024 * k, 1024, BF16, [128, 1024]) for k in range(32)]

    C0 = 39936
    q1f = V("q1f", C0, 4096, BF16, [128, 8, 512])
    k1f = V("k1f", C0 + 4096, 4096, BF16, [128, 2, 2048])
    vb = V("vb", C0 + 8192, 4160, BF16, [128, 4160])
    attn_m = V("attn_m", C0 + 12352, 4096, BF16, [128, 8, 512])
    latf = V("latf", C0 + 16448, 1024, BF16, [128, 2, 512])
    xnf = V("xnf", C0 + 17472, 4096, BF16, [128, 8, 512])
    ptb = [V(f"ptb_{i}", C0 + 21568 + 512 * i, 512, BF16, [128, 512])
           for i in range(2)]
    qtm = V("qtm", C0 + 22592, 1024, F16, [128, 1024])
    qn_ = V("qn", C0 + 23616, 1024, F16, [128, 1024])
    qrn = V("qrn", 0, 0, F16, [128, 0]) if False else None
    rtmp = V("rtmp", C0 + 24640, 512, F16, [128, 512])
    ktm = V("ktm", C0 + 25152, 256, F16, [128, 256])
    kn_ = V("kn", C0 + 25408, 256, F16, [128, 256])
    krn = None
    recb = V("recb", C0 + 25664, 2048, BF16, [64, 2048])
    rec_sb = V("rec_sb", C0 + 25664, 2048, BF16, [1, 2048])
    sqj = V("sqj", C0 + 27712, 1024, BF16, [128, 1024])
    ld_a = V("ld_a", C0 + 28736, 1024, BF16, [128, 1024])
    ld_b = V("ld_b", C0 + 29760, 1024, BF16, [128, 1024])
    s1b = [V(f"s1b_{i}", C0 + 30784 + 1024 * i, 1024, BF16, [128, 1024])
           for i in range(2)]
    u2all = V("u2all", C0 + 22592, 4096, BF16, [17, 8, 512])
    xnf2 = V("xnf2", C0 + 8192, 4096, BF16, [128, 8, 512])
    h2r = [V(f"h2r_{i}", C0 + 4096 * i, 4096, BF16, [128, 32, 128])
           for i in range(2)]
    h2w = [V(f"h2w_{i}", C0 + 21568 + 512 * i, 512, BF16, [128, 512])
           for i in range(2)]
    h2wp = [V("h2wp_0", C0 + 21568, 1024, BF16, [128, 1024]),
            V("h2wp_1", C0 + 27712, 1024, BF16, [128, 1024])]
    CEND = C0 + 32832

    D0 = CEND
    mA = [V(f"mA_{r}", D0 + 512 * r, 512, BF16, [128, 512]) for r in range(4)]
    mB = [V(f"mB_{r}", D0 + 2048 + 512 * r, 512, BF16, [128, 512])
          for r in range(6)]
    tab = {}
    for i, nm in enumerate(("cos_g", "sin_g", "cos_l", "sin_l")):
        tab[nm] = V(nm, D0 + 5120 + 512 * i, 512, F16, [128, 16, 32])
    id_bf = V("id_bf", D0 + 7168, 128, BF16, [128, 128])
    id_f16 = V("id_f16", D0 + 7296, 128, F16, [128, 128])
    id_f32 = V("id_f32", D0 + 7424, 256, F32, [128, 128])
    ones_bf = V("ones_bf", D0 + 7680, 64, BF16, [1, 64])
    ones_f32 = V("ones_f32", D0 + 7744, 256, F32, [1, 128])
    n12 = V("n12", D0 + 8000, 32, F32, [128, 16])
    r1b = V("r1b", D0 + 8032, 64, F32, [128, 32])
    r2b = V("r2b", D0 + 8096, 64, F32, [128, 32])
    lncol = V("lncol", D0 + 8160, 64, F32, [128, 32])
    scl = V("scl", D0 + 8224, 16, F32, [128, 8])
    bfcT = V("bfcT", D0 + 8240, 4096, BF16, [17, 4096])
    q1wi = [V(f"q1wi_{i}", D0 + 8240 + 2048 * i, 2048, F32, [128, 1024])
            for i in range(2)]
    bpjT = V("bpjT", D0 + 12336, 1024, BF16, [17, 1024])
    afcT = V("afcT", D0 + 13360, 128, BF16, [128, 128])
    apjT = V("apjT", D0 + 13488, 512, BF16, [128, 512])
    up1 = V("up1", D0 + 14000, 512, BF16, [17, 512])
    cent = V("cent", D0 + 14512, 2048, F32, [128, 1024])
    gbuf = V("gbuf", D0 + 14512, 2048, F32, [128, 1024])
    apstg = V("apstg", D0 + 14512, 1024, BF16, [16, 1024])
    xb = [V(f"xb_{i}", D0 + 16560 + 2048 * i, 2048, F32, [128, 1024])
          for i in range(2)]
    lnrow = V("lnrow", D0 + 16560, 4096, F32, [1, 2048])
    wtmp = V("wtmp", D0 + 20656, 2048, F32, [128, 1024])
    tb = [V(f"tb_{i}", D0 + 16560 + 1024 * i, 1024, F32, [128, 512])
          for i in range(4)]
    tbi = V("tbi", D0 + 20656, 1024, I32, [128, 512])
    invb = V("invb", D0 + 22704, 64, F32, [128, 32])
    epsc = V("epsc", D0 + 22768, 16, F32, [128, 8])
    D0X = D0 + 22784
    ptb += [V(f"ptb_{i + 2}", D0X + 512 * i, 512, BF16, [128, 512])
            for i in range(2)]
    ptbp = [V("ptbA", C0 + 21568, 1024, BF16, [128, 2, 512]),
            V("ptbB", D0X, 1024, BF16, [128, 2, 512])]
    ptbp2 = [V("ptbA2", C0 + 22592, 1024, BF16, [128, 2, 512]),
             V("ptbB2", C0 + 23616, 1024, BF16, [128, 2, 512])]
    lnc_q = V("lnc_q", D0X + 1024, 40, F32, [128, 20])
    lnc_k = V("lnc_k", D0X + 1064, 16, F32, [128, 8])
    qrn = V("qrn", D0X + 1080, 1024, F16, [128, 1024])
    krn = V("krn", D0X + 2104, 256, F16, [128, 256])
    TOT = D0X + 2360
    assert TOT <= 98304, TOT
    qrn2 = V("qrn2", C0 + 12352, 1024, F16, [128, 1024])
    krn2 = V("krn2", C0 + 13376, 256, F16, [128, 256])

    ps = nc.alloc_psum_tensor("ps", [128, 4096], F32)
    ps16 = ps[:].bitcast(F16)
    psbf = ps[:].bitcast(BF16)

    def bank(i, rows=128):
        return ps[0:rows, 512 * i:512 * (i + 1)]

    with nc.Block() as blk:
        p = Prog(nc, blk)
        p.overlap = AR.overlaps()
        for name, sec, step in [("pe", blk.tensor, 1), ("act", blk.scalar, 1),
                                ("dve", blk.vector, 1), ("pool", blk.gpsimd, 1),
                                ("sd", blk.sync, 16), ("gd", blk.gpsimd, 16),
                                ("ad", blk.scalar, 16)]:
            p.engine(name, sec, step)
        E = p.emit

        def PB(i):
            return ("pb", i)

        def act_set_load(set_id):
            def body(s):
                s.add_instruction(mybir.InstLoadActFuncSet(
                    name=nc.get_next_instruction_name(),
                    act_func_set_id=set_id, ins=[], outs=[]))
            p.eng["act"]["sec"](body)

        for idv, dtv in ((id_bf, "id_bf"), (id_f16, "id_f16"), (id_f32, "id_f32")):
            E("dve", lambda v, idv=idv: v.memset(idv, 1.0), writes=[dtv])
            E("pool", lambda g, idv=idv: g.affine_select(
                out=idv, in_=idv, pattern=[[1, 128]], compare_op=ALU.is_equal,
                fill=0.0, base=0, channel_multiplier=-1),
                reads=[dtv], writes=[dtv])
        E("dve", lambda v: v.memset(ones_bf, 1.0), writes=["ones_bf"])
        E("dve", lambda v: v.memset(epsc[:, 0:1], EPS_LN), writes=["epsc"])
        E("dve", lambda v: v.memset(epsc[:, 1:2], EPS_RMS), reads=["epsc"], writes=["epsc"])
        E("dve", lambda v: v.memset(epsc[:, 2:3], 0.5), reads=["epsc"], writes=["epsc"])
        E("dve", lambda v: v.memset(ones_f32, 1.0), writes=["ones_f32"])
        E("dve", lambda v: v.memset(vb[:, 64::65], 1.0), writes=["vb"])
        for r in range(4):
            E("dve", lambda v, r=r: v.memset(mA[r], 1.0), writes=[f"mA_{r}"])
            E("pool", lambda g, r=r: g.affine_select(
                out=mA[r], in_=mA[r], pattern=[[1, 512]], compare_op=ALU.is_ge,
                fill=0.0, base=-128 * r, channel_multiplier=-1),
                reads=[f"mA_{r}"], writes=[f"mA_{r}"])
        for ri in range(6):
            r = ri - 2
            E("dve", lambda v, ri=ri: v.memset(mB[ri], 1.0), writes=[f"mB_{ri}"])
            E("pool", lambda g, ri=ri, r=r: g.affine_select(
                out=mB[ri], in_=mB[ri], pattern=[[1, 512]], compare_op=ALU.is_ge,
                fill=0.0, base=-128 * r, channel_multiplier=-1),
                reads=[f"mB_{ri}"], writes=[f"mB_{ri}"])
            E("pool", lambda g, ri=ri, r=r: g.affine_select(
                out=mB[ri], in_=mB[ri], pattern=[[-1, 512]], compare_op=ALU.is_ge,
                fill=0.0, base=128 * r + 255, channel_multiplier=1),
                reads=[f"mB_{ri}"], writes=[f"mB_{ri}"])
        E("sd", lambda e: e.dma_start(
            out=n12[:, 0:8], in_=n1w.rearrange("(c p) -> p c", p=128),
            allow_slow_non_contiguous=True), writes=["n12"])
        E("sd", lambda e: e.dma_start(
            out=n12[:, 8:16], in_=n2w.rearrange("(c p) -> p c", p=128),
            allow_slow_non_contiguous=True), reads=["n12"], writes=["n12"])

        for base, cname, sname in ((BASE_G, "cos_g", "sin_g"),
                                   (BASE_L, "cos_l", "sin_l")):
            E("pool", lambda g: g.iota(tb[3][0:1, 0:32], pattern=[[1, 32]], base=0,
                                       channel_multiplier=0,
                                       allow_small_or_imprecise_dtypes=True),
              writes=["tb_3"])
            act_set_load(ACT_COMBO_SET)
            E("act", lambda s, base=base: s.activation(
                tb[3][0:1, 32:64], tb[3][0:1, 0:32], AF.Exp,
                scale=-math.log(base) / 32.0), reads=["tb_3"], writes=["tb_3"])
            E("pe", lambda t: t.matmul(ps[0:128, 0:32], ones_f32, tb[3][0:1, 32:64],
                                       start=True, stop=True),
              reads=["ones_f32", "tb_3"], writes=[PB(0)])
            E("dve", lambda v: v.tensor_copy(invb[:, 0:32], ps[0:128, 0:32]),
              reads=[PB(0)], writes=["invb"])
            E("pool", lambda g: g.iota(tb[0][:], pattern=[[128, 16], [0, 32]],
                                       base=0, channel_multiplier=1,
                                       allow_small_or_imprecise_dtypes=True),
              writes=["tb_0"])
            E("dve", lambda v: v.tensor_tensor(
                tb[1][:].rearrange("p (a b) -> p a b", a=16),
                tb[0][:].rearrange("p (a b) -> p a b", a=16),
                invb[:, 0:32].unsqueeze(1).broadcast_to([128, 16, 32]),
                ALU.mult), reads=["tb_0", "invb"], writes=["tb_1"])
            act_set_load(ACT_TRIG_SET)
            for trig, shift in ((sname, 0.0), (cname, PI / 2.0)):
                E("dve", lambda v, shift=shift: v.tensor_scalar(
                    tb[0][:], tb[1][:], shift, 1.0 / TWO_PI, ALU.add, ALU.mult),
                  reads=["tb_1"], writes=["tb_0"])
                E("dve", lambda v: v.tensor_copy(tbi[:], tb[0][:]),
                  reads=["tb_0"], writes=["tbi"])
                E("dve", lambda v, shift=shift: v.tensor_scalar(
                    tb[2][:], tb[1][:], shift, None, ALU.add),
                  reads=["tb_1"], writes=["tb_2"])
                E("dve", lambda v: v.scalar_tensor_tensor(
                    tb[2][:], tbi[:], -TWO_PI, tb[2][:], ALU.mult, ALU.add),
                  reads=["tbi", "tb_2"], writes=["tb_2"])
                for cmp_op, thr, fix in ((ALU.is_gt, PI, -TWO_PI),
                                         (ALU.is_lt, -PI, TWO_PI)):
                    E("dve",
                      lambda v, cmp_op=cmp_op, thr=thr: v.tensor_single_scalar(
                          tb[0][:], tb[2][:], thr, cmp_op),
                      reads=["tb_2"], writes=["tb_0"])
                    E("dve", lambda v, fix=fix: v.scalar_tensor_tensor(
                        tb[2][:], tb[0][:], fix, tb[2][:], ALU.mult, ALU.add),
                      reads=["tb_0", "tb_2"], writes=["tb_2"])
                E("dve", lambda v: v.tensor_scalar(
                    tb[2][:], tb[2][:], PI, -PI, ALU.min, ALU.max),
                  reads=["tb_2"], writes=["tb_2"])
                E("act", lambda s, trig=trig: s.activation(
                    tab[trig].rearrange("p a b -> p (a b)"), tb[2][:], AF.Sin),
                  reads=["tb_2"], writes=[trig])

        act_set_load(ACT_COMBO_SET)

        def perm_in(ap2d):
            return ap2d.rearrange("p (h i two) -> p h two i", two=2, i=32)

        def perm_q_pairs(src2d, dst2d):
            src6 = src2d.rearrange("p (a b c d e) -> p a b c d e",
                                   a=2, b=2, c=4, d=32, e=2)
            dst6 = dst2d.rearrange("p (a c b e d) -> p a c b e d",
                                   a=2, c=4, b=2, e=2, d=32)
            pairs = []
            for a in range(2):
                for b in range(2):
                    inp = src6[:, a, b].transpose([0, 1, 3, 2])
                    outp = dst6[:, a, :, b]
                    pairs.append((inp, outp))
            return pairs

        def prep_tile(dst, src, k, n_out, fold_col, permute, tag):
            E("ad", lambda e, k=k: e.dma_start(
                out=wtmp[0:128, 0:n_out], in_=src[128 * k:128 * (k + 1), :]),
              reads=["wtmp"], writes=["wtmp"])
            inv = wtmp[0:128, 0:n_out]
            col = (n12[:, fold_col + k:fold_col + k + 1]
                   if fold_col is not None else None)
            if permute == "q":
                for inp, outp in perm_q_pairs(inv, dst[k][:]):
                    E("dve", lambda v, inp=inp, outp=outp, col=col:
                      v.tensor_scalar(outp, inp, col, None, ALU.mult),
                      reads=["wtmp", "n12"], writes=[f"{tag}_{k}"])
                return
            if permute:
                inv = perm_in(inv)
            if col is not None:
                E("dve", lambda v, inv=inv, col=col: v.tensor_scalar(
                    dst[k][:], inv, col, None, ALU.mult),
                  reads=["wtmp", "n12"], writes=[f"{tag}_{k}"])
            else:
                E("dve", lambda v, inv=inv: v.tensor_copy(dst[k][:], inv),
                  reads=["wtmp"], writes=[f"{tag}_{k}"])

        w1_q = []
        for k in range(8):
            w1_q.append(lambda k=k: prep_tile(wq1, g_wq, k, 1024, 0, "q", "wq1"))
        for k in range(8):
            w1_q.append(lambda k=k: prep_tile(kvd, g_kvd, k, 256, 0, False, "kvd"))
        for k in range(2):
            def kup_tile(k=k):
                E("ad", lambda e: e.dma_start(
                    out=wtmp[0:128, 0:256], in_=g_kup[128 * k:128 * (k + 1), :]),
                  reads=["wtmp"], writes=["wtmp"])
                E("dve", lambda v: v.tensor_copy(
                    kup[k][:], perm_in(wtmp[0:128, 0:256])),
                  reads=["wtmp"], writes=[f"kup_{k}"])
            w1_q.append(kup_tile)

        w2_q = []
        for k in range(8):
            w2_q.append(lambda k=k: prep_tile(wq2, l_wq, k, 1024, 8, "q", "wq2"))
        for k in range(8):
            w2_q.append(lambda k=k: prep_tile(wk2, l_wk, k, 256, 8, True, "wk2"))
        for k in range(8):
            w2_q.append(lambda k=k: prep_tile(wv2, l_wv, k, 256, 8, False, "wv2"))

        lora_q = []

        def lp_afc():
            E("ad", lambda e: e.dma_start(out=wtmp[0:16, 0:1024], in_=A_fc),
              reads=["wtmp"], writes=["wtmp"])
            E("dve", lambda v: v.tensor_copy(sqj[0:16, :], wtmp[0:16, 0:1024]),
              reads=["wtmp"], writes=["sqj"])
            for k in range(8):
                E("pe", lambda t, k=k: t.transpose(
                    psbf[0:128, 7168 + 16 * k:7168 + 16 * (k + 1)],
                    sqj[0:16, 128 * k:128 * (k + 1)], id_bf[0:16, 0:16]),
                  reads=["sqj", "id_bf"], inc=(k == 7),
                  writes=[PB(7)])
            E("dve", lambda v: v.tensor_copy(afcT[:], psbf[0:128, 7168:7296]),
              reads=[PB(7)], writes=["afcT"])
        lora_q.append(lp_afc)

        for q_ in range(4):
            def lp_apj(q_=q_):
                E("ad", lambda e: e.dma_start(
                    out=wtmp[0:16, 0:1024], in_=A_pj[:, 1024 * q_:1024 * (q_ + 1)]),
                  reads=["wtmp"], writes=["wtmp"])
                E("dve", lambda v: v.tensor_copy(apstg[0:16, :],
                                                 wtmp[0:16, 0:1024]),
                  reads=["wtmp"], writes=["apstg"])
                for j in range(8):
                    fk = 8 * q_ + j
                    E("pe", lambda t, j=j, fk=fk: t.transpose(
                        psbf[0:128, 7168 + 16 * j:7168 + 16 * (j + 1)],
                        apstg[0:16, 128 * j:128 * (j + 1)], id_bf[0:16, 0:16]),
                      reads=["apstg", "id_bf"], inc=(j == 7),
                      writes=[PB(7)])
                E("dve", lambda v, q_=q_: v.tensor_copy(
                    apjT[:, 128 * q_:128 * (q_ + 1)], psbf[0:128, 7168:7296]),
                  reads=[PB(7)], writes=["apjT"])
            lora_q.append(lp_apj)

        for k in range(32):
            def lp_bfc(k=k):
                E("ad", lambda e: e.dma_start(
                    out=wtmp[0:128, 0:16], in_=B_fc[128 * k:128 * (k + 1), :]),
                  reads=["wtmp"], writes=["wtmp"])
                E("dve", lambda v: v.tensor_copy(sqj[0:128, 0:16],
                                                 wtmp[0:128, 0:16]),
                  reads=["wtmp"], writes=["sqj"])
                E("pe", lambda t: t.transpose(
                    psbf[0:16, 7168:7296], sqj[0:128, 0:16], id_bf),
                  reads=["sqj", "id_bf"], writes=[PB(7)])
                E("dve", lambda v: v.tensor_copy(
                    bfcT[0:16, 128 * k:128 * (k + 1)], psbf[0:16, 7168:7296]),
                  reads=[PB(7)], writes=["bfcT"])
            lora_q.append(lp_bfc)
        for k in range(8):
            def lp_bpj(k=k):
                E("ad", lambda e: e.dma_start(
                    out=wtmp[0:128, 0:16], in_=B_pj[128 * k:128 * (k + 1), :]),
                  reads=["wtmp"], writes=["wtmp"])
                E("dve", lambda v: v.tensor_copy(sqj[0:128, 0:16],
                                                 wtmp[0:128, 0:16]),
                  reads=["wtmp"], writes=["sqj"])
                E("pe", lambda t: t.transpose(
                    psbf[0:16, 7168:7296], sqj[0:128, 0:16], id_bf),
                  reads=["sqj", "id_bf"], writes=[PB(7)])
                E("dve", lambda v: v.tensor_copy(
                    bpjT[0:16, 128 * k:128 * (k + 1)], psbf[0:16, 7168:7296]),
                  reads=[PB(7)], writes=["bpjT"])
            lora_q.append(lp_bpj)

        def lp_bias():
            E("gd", lambda e: e.dma_start(out=bfcT[16:17, :], in_=b_fc[None, :]),
              writes=["bfcT"])
            E("gd", lambda e: e.dma_start(out=bpjT[16:17, :], in_=b_pj[None, :]),
              writes=["bpjT"])
        lora_q.append(lp_bias)

        q1_q = []
        W_pj_t = W_pj.rearrange("d (a b) -> (d a) b", a=4)

        def qp1_tile(W, i, maxc, tag, first):
            b = i % 2
            E(("gd", "sd", "ad")[i % 3],
              lambda e, i=i, W=W, b=b: e.dma_start(
                  out=q1wi[b][:], in_=W[128 * i:128 * (i + 1), :]),
              reads=[f"q1wi_{b}"], writes=[f"q1wi_{b}"])
            E("dve", lambda v, b=b: v.tensor_reduce(
                lncol[:, 20:21], q1wi[b][:], AXX, ALU.max,
                apply_absolute_value=True),
              reads=[f"q1wi_{b}"], writes=["lncol"])
            if first:
                E("dve", lambda v, maxc=maxc: v.tensor_copy(maxc, lncol[:, 20:21]),
                  reads=["lncol"], writes=[tag])
            else:
                E("dve", lambda v, maxc=maxc: v.tensor_tensor(
                    maxc, maxc, lncol[:, 20:21], ALU.max),
                  reads=["lncol", tag], writes=[tag])

        for i in range(32):
            q1_q.append(lambda i=i: qp1_tile(W_fc, i, lncol[:, 16:17], "mx_fc",
                                             i == 0))
        for i in range(32):
            q1_q.append(lambda i=i: qp1_tile(W_pj_t, i, lncol[:, 17:18], "mx_pj",
                                             i == 0))

        def qscale(maxc, tag, wi):
            E("pe", lambda t, maxc=maxc: t.transpose(ps[0:1, 3584:3712], maxc,
                                                     id_f32),
              reads=[tag, "id_f32"], writes=[PB(7)])
            E("dve", lambda v: v.tensor_reduce(
                lncol[0:1, 24:25], ps[0:1, 3584:3712], AXX, ALU.max),
              reads=[PB(7)], writes=["lncol"])
            E("dve", lambda v: v.reciprocal(lncol[0:1, 25:26], lncol[0:1, 24:25]),
              reads=["lncol"], writes=["lncol"])
            E("dve", lambda v: v.tensor_scalar(
                lncol[0:1, 26:27], lncol[0:1, 24:25], 1.0 / 127.0, None, ALU.mult),
              reads=["lncol"], writes=["lncol"])
            E("dve", lambda v: v.tensor_scalar(
                lncol[0:1, 27:28], lncol[0:1, 25:26], 127.0, None, ALU.mult),
              reads=["lncol"], writes=["lncol"])
            E("pe", lambda t: t.matmul(ps[0:128, 3584:3586], ones_f32,
                                       lncol[0:1, 26:28], start=True, stop=True),
              reads=["lncol", "ones_f32"], writes=[PB(7)])
            E("dve", lambda v, wi=wi: v.tensor_copy(
                scl[:, 2 * wi:2 * wi + 2], ps[0:128, 3584:3586]),
              reads=[PB(7)], writes=["scl"])

        q1_q.append(lambda: qscale(lncol[:, 16:17], "mx_fc", 0))
        q1_q.append(lambda: qscale(lncol[:, 17:18], "mx_pj", 1))

        q2_q = []
        wqpjd_t = wqpjd.rearrange("d (a b) -> (d a) b", a=4)

        def qp2_tile(W, Wdst, i, wi, tag):
            b = i % 2
            icol = scl[:, 2 * wi + 1:2 * wi + 2]
            scol = scl[:, 2 * wi:2 * wi + 1]
            E(("gd", "sd", "ad")[i % 3],
              lambda e, i=i, W=W, b=b: e.dma_start(
                  out=qwi[b][:], in_=W[128 * i:128 * (i + 1), :]),
              reads=[f"qwi_{b}"], writes=[f"qwi_{b}"])
            E("act", lambda s_, b=b, icol=icol: s_.activation(
                qw_y[:], qwi[b][:], AF.Identity, scale=icol,
                bias=epsc[:, 2:3]),
              reads=[f"qwi_{b}", "scl", "epsc"], writes=["qw_y"])
            E("dve", lambda v: v.tensor_copy(qw_i[:], qw_y[:]),
              reads=["qw_y"], writes=["qw_i"])
            E("dve", lambda v: v.scalar_tensor_tensor(
                qw_g[:], qw_i[:], 1.0, qw_y[:], ALU.bypass, ALU.is_gt),
              reads=["qw_i", "qw_y"], writes=["qw_g"])
            E("dve", lambda v: v.scalar_tensor_tensor(
                qw_g[:], qw_g[:], -1.0, qw_i[:], ALU.mult, ALU.add),
              reads=["qw_g", "qw_i"], writes=["qw_g"])
            E("dve", lambda v, scol=scol: v.tensor_scalar(
                qw_o[:], qw_g[:], scol, None, ALU.mult),
              reads=["qw_g", "scl"], writes=["qw_o"])
            E(("ad", "gd", "sd")[i % 3],
              lambda e, i=i, Wdst=Wdst: e.dma_start(
                  out=Wdst[128 * i:128 * (i + 1), :], in_=qw_o[:]),
              reads=["qw_o"], writes=[(tag, i)])

        for i in range(32):
            q2_q.append(lambda i=i: qp2_tile(W_fc, wqfcd, i, 0, "wqfc"))
        for i in range(32):
            q2_q.append(lambda i=i: qp2_tile(W_pj_t, wqpjd_t, i, 1, "wqpj"))

        def drain(q, n):
            for _ in range(min(n, len(q))):
                q.pop(0)()

        def rsqrt_cols(src_col, dst_col, n_mean, eps_col, wkey):
            E("act", lambda s: s.activation(lncol[:, 12:13], src_col, AF.Ln,
                                            scale=1.0 / n_mean, bias=eps_col),
              reads=["lncol", "epsc"], writes=["lncol"])
            E("act", lambda s: s.activation(dst_col, lncol[:, 12:13], AF.Exp,
                                            scale=-0.5),
              reads=["lncol"], writes=[wkey])

        def ln_chunk(s, c):
            i = c % 2
            E("sd", lambda e: e.dma_start(
                out=xb[i][:], in_=x[s, 128 * c:128 * (c + 1), :]),
              reads=[f"xb_{i}"], writes=[f"xb_{i}"])
            E("dve", lambda v: v.tensor_reduce(lncol[:, 0:1], xb[i][:], AXX,
                                               ALU.add),
              reads=[f"xb_{i}"], writes=["lncol"])
            E("dve", lambda v: v.tensor_scalar(lncol[:, 1:2], lncol[:, 0:1],
                                               1.0 / D, None, ALU.mult),
              reads=["lncol"], writes=["lncol"])
            E("dve", lambda v: v.tensor_scalar(cent[:], xb[i][:], lncol[:, 1:2],
                                               None, ALU.subtract),
              reads=[f"xb_{i}", "lncol"], writes=["cent"])
            E("dve", lambda v: v.scalar_tensor_tensor(
                sqj[:], cent[:], 1.0, cent[:], ALU.bypass, ALU.mult,
                accum_out=lncol[:, 2:3]),
              reads=["cent"], writes=["sqj", "lncol"])
            rsqrt_cols(lncol[:, 2:3], lncol[:, 4:5], D, epsc[:, 0:1], "lncol")
            xnb = s1b[i]
            E("dve", lambda v, xnb=xnb: v.tensor_scalar(
                xnb[:], cent[:], lncol[:, 4:5], None, ALU.mult),
              reads=["cent", "lncol"], writes=[f"s1b_{i}"])
            E("dve", lambda v, xnb=xnb: v.scalar_tensor_tensor(
                sqj[:], xnb[:], 1.0, xnb[:], ALU.bypass, ALU.mult,
                accum_out=lncol[:, 5:6]),
              reads=[f"s1b_{i}"], writes=["sqj", "lncol"])
            rsqrt_cols(lncol[:, 5:6], r1b[:, 16 * s + c:16 * s + c + 1], D,
                       epsc[:, 1:2], "r1b")
            E("ad", lambda e, xnb=xnb, s=s, c=c: e.dma_start(
                out=xnd[2048 * s + 128 * c:2048 * s + 128 * (c + 1), :],
                in_=xnb[:]),
              reads=[f"s1b_{i}"], writes=[("xnd", s, c)])

        def rope_qknorm(src, n_cols, cname, sname, rk, dst, wk, cc):
            nh = n_cols // 64
            lc, lck = (lnc_q, "lnc_q") if n_cols == 1024 else (lnc_k, "lnc_k")
            E("dve", lambda v: v.tensor_tensor(sqj[:, 0:n_cols], src, src,
                                               ALU.mult),
              reads=[rk], writes=["sqj"])
            E("dve", lambda v: v.tensor_reduce(
                lc[:, 0:nh],
                sqj[:, 0:n_cols].rearrange("p (h i) -> p h i", i=64),
                AXX, ALU.add),
              reads=["sqj"], writes=[lck])
            E("act", lambda s: s.activation(lc[:, 0:nh], lc[:, 0:nh], AF.Ln,
                                            scale=1.0 / 64.0, bias=epsc[:, 1:2]),
              reads=[lck, "epsc"], writes=[lck])
            E("act", lambda s: s.activation(lc[:, 0:nh], lc[:, 0:nh],
                                            AF.Exp, scale=-0.5),
              reads=[lck], writes=[lck])
            nbuf = qn_ if n_cols == 1024 else kn_
            nk = "qn" if n_cols == 1024 else "kn"
            E("dve", lambda v, nbuf=nbuf: v.tensor_tensor(
                nbuf[:, 0:n_cols].rearrange("p (h i) -> p h i", i=64),
                src.rearrange("p (h i) -> p h i", i=64),
                lc[:, 0:nh].unsqueeze(2).broadcast_to([128, nh, 64]),
                ALU.mult),
              reads=[rk, lck], writes=[nk])
            nb = nbuf[:, 0:n_cols].rearrange("p (h two i) -> p h two i",
                                             two=2, i=32)
            db = dst.rearrange("p (h two i) -> p h two i", two=2, i=32)
            e_, o_ = nb[:, :, 0, :], nb[:, :, 1, :]
            de, do = db[:, :, 0, :], db[:, :, 1, :]
            cosb = tab[cname][:, cc, :].unsqueeze(1).broadcast_to([128, nh, 32])
            sinb = tab[sname][:, cc, :].unsqueeze(1).broadcast_to([128, nh, 32])
            half = rtmp[:, 0:n_cols // 2].rearrange("p (h i) -> p h i", i=32)
            E("dve", lambda v: v.tensor_tensor(half, o_, sinb, ALU.mult),
              reads=[nk, sname], writes=["rtmp"])
            E("dve", lambda v: v.tensor_tensor(de, e_, cosb, ALU.mult),
              reads=[nk, cname], writes=[wk])
            E("dve", lambda v: v.tensor_tensor(de, de, half, ALU.subtract),
              reads=["rtmp", wk], writes=[wk])
            E("dve", lambda v: v.tensor_tensor(half, e_, sinb, ALU.mult),
              reads=[nk, sname], writes=["rtmp"])
            E("dve", lambda v: v.tensor_tensor(do, o_, cosb, ALU.mult),
              reads=[nk, cname], writes=[wk])
            E("dve", lambda v: v.tensor_tensor(do, do, half, ALU.add),
              reads=["rtmp", wk], writes=[wk])

        def attention(s, aidx, hook_q=None):
            src_d = xnd if aidx == 0 else s1d
            srctag = "xnd" if aidx == 0 else "s1d"
            rbuf, rtag = (r1b, "r1b") if aidx == 0 else (r2b, "r2b")
            cname = "cos_g" if aidx == 0 else "cos_l"
            sname = "sin_g" if aidx == 0 else "sin_l"
            qw, qtag = (wq1, "wq1") if aidx == 0 else (wq2, "wq2")
            wo_w, wo_tag = (wo1, "wo1") if aidx == 0 else (wo2, "wo2")
            for sp in range(NSP):
                row0 = 2048 * s + 512 * sp
                span_keys = [(srctag, s, 4 * sp + j) for j in range(4)]
                for dk in range(8):
                    E("sd",
                      lambda e, dk=dk: e.dma_start(
                          out=xnf[:, dk, :],
                          in_=src_d[row0:row0 + 512, 128 * dk:128 * (dk + 1)],
                          transpose=True),
                      reads=span_keys + ["xnf"], writes=["xnf"])
                if aidx == 0 and SUBPH >= 2:
                    for lc in range(2):
                        for k in range(8):
                            E("pe", lambda t, k=k, lc=lc: t.matmul(
                                bank(lc), kvd[k][:, 128 * lc:128 * (lc + 1)],
                                xnf[:, k, :],
                                start=(k == 0), stop=(k == 7)),
                              reads=["xnf", f"kvd_{k}"], inc=(k == 7),
                              writes=[PB(lc)])
                        E("act", lambda s_, lc=lc: s_.activation(
                            latf[:, lc, :], bank(lc), AF.Copy),
                          reads=[PB(lc)], writes=["latf"])
                def prod_mm(tci):
                    c = 4 * sp + tci
                    rcol = rbuf[:, 16 * s + c:16 * s + c + 1]
                    for half in range(2):
                        for k in range(8):
                            E("pe", lambda t, k=k, half=half, tci=tci: t.matmul(
                                bank(2 + half),
                                xnf[:, k, 128 * tci:128 * (tci + 1)],
                                qw[k][:, 512 * half:512 * (half + 1)],
                                start=(k == 0), stop=(k == 7)),
                              reads=["xnf", f"{qtag}_{k}"], inc=(k == 7),
                              writes=[PB(2 + half)])
                    E("act", lambda s_, rcol=rcol: s_.activation(
                        qtm[:], ps[0:128, 1024:2048], AF.Copy, scale=rcol),
                      reads=[PB(2), PB(3), rtag], writes=["qtm"])
                    if aidx == 0:
                        for lc in range(2):
                            E("pe", lambda t, lc=lc, tci=tci: t.matmul(
                                bank(2)[:, 0:256],
                                latf[:, lc, 128 * tci:128 * (tci + 1)],
                                kup[lc][:], start=(lc == 0), stop=(lc == 1)),
                              reads=["latf", f"kup_{lc}"], inc=(lc == 1),
                              writes=[PB(2)])
                    else:
                        for k in range(8):
                            E("pe", lambda t, k=k, tci=tci: t.matmul(
                                bank(2)[:, 0:256],
                                xnf[:, k, 128 * tci:128 * (tci + 1)],
                                wk2[k][:], start=(k == 0), stop=(k == 7)),
                              reads=["xnf", f"wk2_{k}"], inc=(k == 7),
                              writes=[PB(2)])
                    E("act", lambda s_, rcol=rcol: s_.activation(
                        ktm[:], bank(2)[:, 0:256], AF.Copy, scale=rcol),
                      reads=[PB(2), rtag], writes=["ktm"])
                    if aidx == 0:
                        for lc in range(2):
                            E("pe", lambda t, lc=lc, tci=tci: t.matmul(
                                bank(2)[:, 0:256],
                                latf[:, lc, 128 * tci:128 * (tci + 1)],
                                vup[lc][:], start=(lc == 0), stop=(lc == 1)),
                              reads=["latf", f"vup_{lc}"], inc=(lc == 1),
                              writes=[PB(2)])
                    else:
                        for k in range(8):
                            E("pe", lambda t, k=k, tci=tci: t.matmul(
                                bank(2)[:, 0:256],
                                xnf[:, k, 128 * tci:128 * (tci + 1)],
                                wv2[k][:], start=(k == 0), stop=(k == 7)),
                              reads=["xnf", f"wv2_{k}"], inc=(k == 7),
                              writes=[PB(2)])
                    E("act", lambda s_, rcol=rcol, c=c: s_.activation(
                        vb.rearrange("p (g y) -> p g y", g=4)[:, :,
                                                              65 * c:65 * c + 64],
                        bank(2)[:, 0:256].rearrange("p (g y) -> p g y", g=4),
                        AF.Copy, scale=rcol),
                      reads=[PB(2), rtag], writes=["vb"])

                def prod_rope(tci):
                    c = 4 * sp + tci
                    qb_, qk_ = (qrn, "qrn") if tci % 2 == 0 else (qrn2, "qrn2")
                    kb_, kk_ = (krn, "krn") if tci % 2 == 0 else (krn2, "krn2")
                    rope_qknorm(qtm[:], 1024, cname, sname, "qtm", qb_[:],
                                qk_, c)
                    rope_qknorm(ktm[:], 256, cname, sname, "ktm", kb_[:],
                                kk_, c)

                def prod_tr(tci):
                    c = 4 * sp + tci
                    qb_, qk_ = (qrn, "qrn") if tci % 2 == 0 else (qrn2, "qrn2")
                    kb_, kk_ = (krn, "krn") if tci % 2 == 0 else (krn2, "krn2")
                    for mc in range(8):
                        E("pe", lambda t, mc=mc, qb_=qb_: t.transpose(
                            ps16[0:128, 6144 + 128 * mc:6144 + 128 * (mc + 1)],
                            qb_[:, 128 * mc:128 * (mc + 1)], id_f16),
                          reads=[qk_, "id_f16"], inc=(mc == 7),
                          writes=[PB(6), PB(7)])
                    E("dve", lambda v, tci=tci: v.tensor_copy(
                        q1f[:, :, 128 * tci:128 * (tci + 1)],
                        ps16[0:128, 6144:7168].rearrange("p (a b) -> p a b",
                                                         a=8)),
                      reads=[PB(6), PB(7)], writes=["q1f"])
                    for mc in range(2):
                        E("pe", lambda t, mc=mc, kb_=kb_: t.transpose(
                            ps16[0:128, 6144 + 128 * mc:6144 + 128 * (mc + 1)],
                            kb_[:, 128 * mc:128 * (mc + 1)], id_f16),
                          reads=[kk_, "id_f16"], inc=(mc == 1),
                          writes=[PB(6)])
                    E("dve", lambda v, c=c: v.tensor_copy(
                        k1f[:, :, 128 * c:128 * (c + 1)],
                        ps16[0:128, 6144:6400].rearrange("p (a b) -> p a b",
                                                         a=2)),
                      reads=[PB(6)], writes=["k1f"])

                for tci in range(4):
                    prod_mm(tci)
                    if tci > 1:
                        prod_tr(tci - 2)
                    prod_rope(tci)
                prod_tr(2)
                prod_tr(3)
                for g in range(4):
                    kb = 64 * (g % 2)
                    mck = g // 2
                    if aidx == 0:
                        krange = list(range(0, 4 * sp + 4))
                    else:
                        krange = list(range(max(4 * sp - 2, 0), 4 * sp + 4))
                    nk_ = len(krange)
                    heads = []
                    for hh in range(4):
                        h = 4 * g + hh
                        qb = 64 * ((h // 4) % 2)
                        qmc = (h % 4) + 4 * (h // 8)
                        assert qb == kb
                        heads.append((hh, qmc))

                    def mask_for(kc):
                        if aidx == 0:
                            rr = kc - 4 * sp
                            if rr < 0:
                                return None, None
                            return mA[rr], f"mA_{rr}"
                        rr = kc - 4 * sp + 2
                        return mB[rr], f"mB_{rr}"

                    def pbuf(ki):
                        return (ptbp, ("ptbA", "ptbB")) if ki % 2 == 0 \
                            else (ptbp2, ("ptbA2", "ptbB2"))

                    def emit_qk(kc, pr, kb=kb, mck=mck, heads=heads):
                        for hh, qmc in heads[2 * pr:2 * pr + 2]:
                            xtra = [PB(2), PB(3)] if (pr, hh % 2) == (0, 0) \
                                else []
                            E("pe", lambda t, kc=kc, hh=hh, qmc=qmc: t.matmul(
                                bank(hh),
                                k1f[kb:kb + 64, mck, 128 * kc:128 * (kc + 1)],
                                q1f[kb:kb + 64, qmc, :],
                                start=True, stop=True),
                              reads=["k1f", "q1f"] + xtra, writes=[PB(hh)])

                    def emit_evac(ki, kc, pr, mask_for=mask_for):
                        mask, mkey = mask_for(kc)
                        buf, keys = pbuf(ki)
                        E("act", lambda s_, pr=pr, buf=buf: s_.activation(
                            buf[pr][:].rearrange("p a b -> p (a b)"),
                            ps[0:128, 1024 * pr:1024 * (pr + 1)],
                            AF.Exp, scale=0.125),
                          reads=[PB(2 * pr), PB(2 * pr + 1)],
                          writes=[keys[pr]])
                        if mask is not None:
                            E("dve", lambda v, pr=pr, mask=mask, buf=buf:
                              v.tensor_tensor(
                                  buf[pr][:], buf[pr][:],
                                  mask[:].unsqueeze(1).broadcast_to(
                                      [128, 2, 512]), ALU.mult),
                              reads=[keys[pr], mkey], writes=[keys[pr]])

                    def emit_pv(ki, kc, g=g, heads=heads, nk_=nk_):
                        buf, keys = pbuf(ki)
                        for hh, qmc in heads:
                            xtra = [keys[1]] if hh == 0 else []
                            E("pe", lambda t, kc=kc, hh=hh, ki=ki, buf=buf:
                              t.matmul(
                                  ps[0:65, 2048 + 512 * hh:2048 + 512 * (hh + 1)],
                                  vb[:, 1040 * g + 65 * kc:
                                     1040 * g + 65 * (kc + 1)],
                                  buf[hh // 2][:, hh % 2, :],
                                  start=(ki == 0), stop=(ki == nk_ - 1)),
                              reads=[keys[hh // 2], "vb"] + xtra,
                              inc=(hh == 3), writes=[PB(4 + hh)])

                    for ki, kc in enumerate(krange):
                        emit_qk(kc, 0)
                        emit_evac(ki, kc, 0)
                        emit_qk(kc, 1)
                        emit_evac(ki, kc, 1)
                        if ki > 0:
                            emit_pv(ki - 1, krange[ki - 1])
                    emit_pv(nk_ - 1, krange[nk_ - 1])

                    gp = g % 2
                    E("act", lambda s_: s_.activation(
                        lnrow[:], ps[64:65, 2048:4096], AF.Ln),
                      reads=[PB(4), PB(5), PB(6), PB(7)],
                      writes=["lnrow"])
                    E("act", lambda s_: s_.activation(
                        rec_sb[0:1, 0:2048], lnrow[:], AF.Exp,
                        scale=-1.0),
                      reads=["lnrow", PB(4), PB(5), PB(6), PB(7)],
                      writes=["recb"])
                    E("ad", lambda e, gp=gp: e.dma_start(
                        out=recd[gp:gp + 1, :], in_=rec_sb[0:1, 0:2048]),
                      reads=["recb"], writes=[("recd", gp)])
                    E("gd", lambda e, gp=gp: e.dma_start(
                        out=recb[0:64, 0:2048],
                        in_=recd[gp][None, :].broadcast_to([64, 2048])),
                      reads=[("recd", gp), "recb"], writes=["recb"])
                    for par in range(2):
                        E("dve", lambda v, par=par, g=g: v.tensor_tensor(
                            attn_m[64 * par:64 * (par + 1),
                                   2 * g:2 * g + 2, :],
                            ps[0:64, 2048:4096].rearrange(
                                "p (a c b) -> p a c b", a=2, c=2)[:, :, par, :],
                            recb[0:64, 0:2048].rearrange(
                                "p (a c b) -> p a c b", a=2, c=2)[:, :, par, :],
                            ALU.mult),
                          reads=[PB(4 + par), PB(6 + par), "recb"],
                          writes=["attn_m"])
                for tci in range(4 if SUBPH >= 5 else 0):
                    c = 4 * sp + tci
                    i = tci % 2
                    ldb = ld_a if i == 0 else ld_b
                    ldk = "ld_a" if i == 0 else "ld_b"
                    E("sd", lambda e, c=c, ldb=ldb, s=s: e.dma_start(
                        out=ldb[:],
                        in_=src_d[2048 * s + 128 * c:2048 * s + 128 * (c + 1), :]),
                      reads=[ldk, (srctag, s, c)], writes=[ldk])
                    for half in range(2):
                        for mc in range(8):
                            E("pe", lambda t, mc=mc, half=half, tci=tci: t.matmul(
                                bank(2 + half),
                                attn_m[:, mc, 128 * tci:128 * (tci + 1)],
                                wo_w[mc][:, 512 * half:512 * (half + 1)],
                                start=(mc == 0), stop=(mc == 7)),
                              reads=["attn_m", f"{wo_tag}_{mc}"], inc=(mc == 7),
                              writes=[PB(2 + half)])
                    sb_o = s1b[i]
                    for half in range(2):
                        E("dve", lambda v, half=half, sb_o=sb_o, ldb=ldb:
                          v.scalar_tensor_tensor(
                              sb_o[:, 512 * half:512 * (half + 1)],
                              bank(2 + half), 1.0,
                              ldb[:, 512 * half:512 * (half + 1)],
                              ALU.bypass, ALU.add),
                          reads=[PB(2 + half), ldk], writes=[f"s1b_{i}"])
                    if aidx == 0:
                        E("dve", lambda v, sb_o=sb_o: v.scalar_tensor_tensor(
                            sqj[:], sb_o[:], 1.0, sb_o[:],
                            ALU.bypass, ALU.mult, accum_out=lncol[:, 6:7]),
                          reads=[f"s1b_{i}"], writes=["sqj", "lncol"])
                        rsqrt_cols(lncol[:, 6:7],
                                   r2b[:, 16 * s + c:16 * s + c + 1], D,
                                   epsc[:, 1:2], "r2b")
                        dst, dk_ = s1d, ("s1d", s, c)
                    else:
                        dst, dk_ = sdd, ("sdd", s, c)
                    E("gd", lambda e, sb_o=sb_o, s=s, c=c, dst=dst: e.dma_start(
                        out=dst[2048 * s + 128 * c:2048 * s + 128 * (c + 1), :],
                        in_=sb_o[:]),
                      reads=[f"s1b_{i}"], writes=[dk_])
                    if hook_q is not None:
                        rem = (NSP - sp) * 4 - tci
                        drain(hook_q, (len(hook_q) + rem - 1) // rem)

        def ffn_pass1():
            act_set_load(ACT_GELU_SET)
            for k in range(8):
                E("sd",
                  lambda e, k=k: e.dma_start(
                      out=wfcT[k][:], in_=wqfcd[:, 128 * k:128 * (k + 1)],
                      transpose=True),
                  reads=[("wqfc", i) for i in range(32)] + [f"wfcT_{k}"],
                  writes=[f"wfcT_{k}"])
            def xnf_load(gsp):
                row0 = 512 * gsp
                xb_, xk_ = (xnf, "xnf") if gsp % 2 == 0 else (xnf2, "xnf2")
                span_keys = [("xnd", gsp // 4, 4 * (gsp % 4) + j)
                             for j in range(4)]
                for dk in range(8):
                    E("sd",
                      lambda e, dk=dk, row0=row0, xb_=xb_: e.dma_start(
                          out=xb_[:, dk, :],
                          in_=xnd[row0:row0 + 512, 128 * dk:128 * (dk + 1)],
                          transpose=True),
                      reads=span_keys + [xk_], writes=[xk_])

            xnf_load(0)
            for gsp in range(8):
                if gsp + 1 < 8:
                    xnf_load(gsp + 1)
                row0 = 512 * gsp
                xb_, xk_ = (xnf, "xnf") if gsp % 2 == 0 else (xnf2, "xnf2")
                for k in range(8):
                    E("pe", lambda t, k=k, xb_=xb_: t.matmul(
                        ps[0:16, 3584:4096], afcT[:, 16 * k:16 * (k + 1)],
                        xb_[:, k, :], start=(k == 0), stop=(k == 7)),
                      reads=[xk_, "afcT"], inc=(k == 7),
                      writes=[PB(7)])
                E("dve", lambda v: v.memset(up1[:], 1.0), writes=["up1"])
                E("act", lambda s_: s_.activation(
                    up1[0:16, :], ps[0:16, 3584:4096], AF.Copy, scale=2.0),
                  reads=[PB(7), "up1"], writes=["up1"])
                for fk in range(32):
                    bk = fk % 4
                    for k in range(8):
                        E("pe", lambda t, k=k, fk=fk, bk=bk, xb_=xb_: t.matmul(
                            bank(bk), wfcT[k][:, 128 * fk:128 * (fk + 1)],
                            xb_[:, k, :], start=(k == 0), stop=False),
                          reads=[xk_, f"wfcT_{k}"], inc=False,
                          writes=[("pb", bk)])
                    E("pe", lambda t, fk=fk, bk=bk: t.matmul(
                        bank(bk), bfcT[0:17, 128 * fk:128 * (fk + 1)], up1[:],
                        start=False, stop=True),
                      reads=["bfcT", "up1"], writes=[PB(bk)])
                    if fk % 2 == 0:
                        continue
                    wi = (fk // 2) % 2
                    wkey = f"h2wp_{wi}"
                    bk0 = bk - 1
                    E("act", lambda s_, bk0=bk0, wi=wi: s_.activation(
                        h2wp[wi][:], ps[0:128, 512 * bk0:512 * bk0 + 1024],
                        AF.Gelu),
                      reads=[PB(bk0), PB(bk0 + 1)], writes=[wkey])
                    E("gd", lambda e, fk=fk, wi=wi, row0=row0, gsp=gsp:
                      e.dma_start(
                          out=h2d[128 * (fk - 1):128 * (fk + 1),
                                  row0:row0 + 512].rearrange(
                              "(a p) t -> p a t", a=2),
                          in_=h2wp[wi][:].rearrange("p (a t) -> p a t", a=2)),
                      reads=[wkey],
                      writes=[("h2d", gsp, fk - 1), ("h2d", gsp, fk)])
                    for j in range(2):
                        fj = fk - 1 + j
                        E("pe", lambda t, fj=fj, wi=wi, j=j: t.matmul(
                            ps[0:16, 3072:3584],
                            apjT[:, 16 * fj:16 * (fj + 1)],
                            h2wp[wi][:, 512 * j:512 * (j + 1)],
                            start=(fj == 0), stop=(fj == 31)),
                          reads=[wkey, "apjT"], inc=(fj == 31),
                          writes=[PB(6)])
                E("dve", lambda v, gsp=gsp: v.memset(u2all[:, gsp, :], 1.0),
                  writes=["u2all"])
                E("act", lambda s_, gsp=gsp: s_.activation(
                    u2all[0:16, gsp, :], ps[0:16, 3072:3584], AF.Copy,
                    scale=2.0),
                  reads=[PB(6), "u2all"], writes=["u2all"])

        def ffn_pass2():
            act_set_load(ACT_SILU_SET)
            for fk in range(32):
                E("sd",
                  lambda e, fk=fk: e.dma_start(
                      out=wpjT[fk][:], in_=wqpjd[:, 128 * fk:128 * (fk + 1)],
                      transpose=True),
                  reads=[("wqpj", i) for i in range(32)] + [f"wpjT_{fk}"],
                  writes=[f"wpjT_{fk}"])
            for cg in range(32):
                s, c = cg // 16, cg % 16
                i = cg % 2
                hr, hk = h2r[i], f"h2r_{i}"
                E("sd",
                  lambda e, cg=cg, hr=hr: e.dma_start(
                      out=hr[:],
                      in_=h2dv[:, :, 128 * cg:128 * (cg + 1)]),
                  reads=[("h2d", cg // 4, fk) for fk in range(32)] + [hk],
                  writes=[hk])
                pb0 = 2 * i
                for half in range(2):
                    for fk in range(32):
                        E("pe", lambda t, fk=fk, half=half, hr=hr, pb0=pb0:
                          t.matmul(
                              bank(pb0 + half), hr[:, fk, :],
                              wpjT[fk][:, 512 * half:512 * (half + 1)],
                              start=(fk == 0), stop=False),
                          reads=[hk, f"wpjT_{fk}"], inc=False,
                          writes=[("pb", pb0 + half)])
                    E("pe", lambda t, half=half, cg=cg, pb0=pb0: t.matmul(
                        bank(pb0 + half),
                        u2all[:, cg // 4, 128 * (cg % 4):128 * (cg % 4) + 128],
                        bpjT[0:17, 512 * half:512 * (half + 1)],
                        start=False, stop=True),
                      reads=["u2all", "bpjT"], writes=[PB(pb0 + half)])
                    E("act", lambda s_, half=half, pb0=pb0: s_.activation(
                        gbuf[:, 512 * half:512 * (half + 1)],
                        bank(pb0 + half), AF.Silu),
                      reads=[PB(pb0 + half)], writes=["gbuf"])
                ldb = ld_a if i == 0 else ld_b
                ldk = "ld_a" if i == 0 else "ld_b"
                E("ad", lambda e, s=s, c=c, ldb=ldb: e.dma_start(
                    out=ldb[:],
                    in_=sdd[2048 * s + 128 * c:2048 * s + 128 * (c + 1), :]),
                  reads=[("sdd", s, c), ldk], writes=[ldk])
                E("sd", lambda e, s=s, c=c, i=i: e.dma_start(
                    out=xb[i][:], in_=x[s, 128 * c:128 * (c + 1), :]),
                  reads=[f"xb_{i}"], writes=[f"xb_{i}"])
                E("dve", lambda v, ldb=ldb: v.tensor_tensor(
                    gbuf[:], gbuf[:], ldb[:], ALU.mult),
                  reads=["gbuf", ldk], writes=["gbuf"])
                E("dve", lambda v, i=i: v.tensor_tensor(
                    gbuf[:], gbuf[:], xb[i][:], ALU.add),
                  reads=["gbuf", f"xb_{i}"], writes=["gbuf"])
                E("gd", lambda e, s=s, c=c: e.dma_start(
                    out=out[s, 128 * c:128 * (c + 1), :], in_=gbuf[:]),
                  reads=["gbuf"], writes=[("out", s, c)])

        for k in range(8):
            E("gd", lambda e, k=k: e.dma_start(
                out=wo1[k][:], in_=g_wo[128 * k:128 * (k + 1), :]),
              writes=[f"wo1_{k}"])
        for k in range(2):
            E("gd", lambda e, k=k: e.dma_start(
                out=vup[k][:], in_=g_vup[128 * k:128 * (k + 1), :]),
              writes=[f"vup_{k}"])
        if PHASES >= 1:
            for c in range(NCH):
                ln_chunk(0, c)
                drain(w1_q, 2)
            drain(w1_q, 99)
            for k in range(8):
                E("gd", lambda e, k=k: e.dma_start(
                    out=wo2[k][:], in_=l_wo[128 * k:128 * (k + 1), :]),
                  writes=[f"wo2_{k}"])
        if PHASES >= 2:
            attention(0, 0, hook_q=w2_q + q1_q)
        if PHASES >= 3:
            for c in range(NCH):
                ln_chunk(1, c)
            attention(1, 0, hook_q=lora_q)
        if PHASES >= 4:
            q2_first = q2_q[:len(q2_q) // 2]
            q2_second = q2_q[len(q2_q) // 2:]
            attention(0, 1, hook_q=q2_first)
            attention(1, 1, hook_q=q2_second)
            drain(q2_first, 999)
            drain(q2_second, 999)
        if PHASES >= 5:
            ffn_pass1()
        if DEBUG_TAPS and PHASES >= 5:
            E("sd", lambda e: e.dma_start(out=dbg1, in_=wfcT[0][:]),
              reads=["wfcT_0"], writes=[("dbg1",)])
            E("sd", lambda e: e.dma_start(out=dbg2, in_=bfcT[0:17, :]),
              reads=["bfcT"], writes=[("dbg2",)])
        if PHASES >= 6:
            ffn_pass2()
        p.wait_all("sd")

    return nc


_INPUT_MAP = [
    ("norm1_w", "L0_attn_norm1_w"), ("norm2_w", "L0_attn_norm2_w"),
    ("g_wq", "L0_attn_g_wq"), ("g_kv_down", "L0_attn_g_kv_down"),
    ("g_k_up", "L0_attn_g_k_up"), ("g_v_up", "L0_attn_g_v_up"),
    ("g_wo", "L0_attn_g_wo"), ("l_wq", "L0_attn_l_wq"),
    ("l_wk", "L0_attn_l_wk"), ("l_wv", "L0_attn_l_wv"),
    ("l_wo", "L0_attn_l_wo"), ("W_fc", "L0_ffn_W_fc"),
    ("b_fc", "L0_ffn_b_fc"), ("A_fc", "L0_ffn_A_fc"),
    ("B_fc", "L0_ffn_B_fc"), ("W_proj", "L0_ffn_W_proj"),
    ("b_proj", "L0_ffn_b_proj"), ("A_proj", "L0_ffn_A_proj"),
    ("B_proj", "L0_ffn_B_proj"),
]


def kernel(**inputs):
    nc = build_program()
    xfull = np.ascontiguousarray(np.asarray(inputs["L0_attn_hidden"], np.float32))
    w = {k: np.ascontiguousarray(np.asarray(inputs[v], np.float32))
         for k, v in _INPUT_MAP}
    in_maps = []
    for core in range(8):
        m = {"x": xfull[2 * core:2 * core + 2]}
        m.update(w)
        in_maps.append(m)
    res = run_bass_kernel_spmd(nc, in_maps, core_ids=list(range(8)))
    return np.concatenate([np.asarray(r["out"], np.float32)
                           for r in res.results], axis=0)


if __name__ == "__main__":
    nc = build_program()
    print("build ok")


# version 34
# speedup vs baseline: 1.0007x; 1.0007x over previous
import math
import numpy as np

import concourse.bass as bass
import concourse.mybir as mybir
from concourse.bass_utils import run_bass_kernel_spmd

F32 = mybir.dt.float32
F16 = mybir.dt.float16
BF16 = mybir.dt.bfloat16
I32 = mybir.dt.int32
AF = mybir.ActivationFunctionType
ALU = mybir.AluOpType
AXX = mybir.AxisListType.X

D = 1024
LAT = 256
T = 2048
NSEQ = 2
F = 4096
NCH = 16
NSP = 4
PI = math.pi
TWO_PI = 2.0 * math.pi
EPS_LN = 1e-5
EPS_RMS = 1e-6
BASE_G = 1000000.0
BASE_L = 10000.0


def _act_set_ids():
    fallback = {"natural_log_exp_and_others": 6, "trig_and_small": 9,
                "gelu_and_others": 10, "silu_and_others": 18}
    try:
        from concourse.hw_specs import get_activation_tables
        names = list(get_activation_tables("TRN2"))
        return {k: names.index(k) for k in fallback}
    except Exception:
        return fallback


_ACT_SETS = _act_set_ids()
ACT_COMBO_SET = _ACT_SETS["natural_log_exp_and_others"]
ACT_TRIG_SET = _ACT_SETS["trig_and_small"]
ACT_GELU_SET = _ACT_SETS["gelu_and_others"]
ACT_SILU_SET = _ACT_SETS["silu_and_others"]

DEBUG_TAPS = False
PHASES = 99
SUBPH = 9


class Prog:

    def __init__(self, nc, blk):
        self.nc, self.blk = nc, blk
        self.eng = {}
        self.reg = {}
        self.overlap = {}
        self.opvc = {}

    WAVE = 12

    def engine(self, name, sec, step):
        sem = self.nc.alloc_semaphore(name=f"sem_{name}")
        self.eng[name] = dict(sem=sem, cnt=0, step=step, sec=sec, pend=[],
                              barriers=[0], bar_cnt=0, vc={})

    def _learn(self, know, pe_, pc_):
        if pc_ > know.get(pe_, -1):
            know[pe_] = pc_
        src = self.opvc.get((pe_, pc_))
        if src:
            for k2, v2 in src.items():
                if v2 > know.get(k2, -1):
                    know[k2] = v2

    def close_wave(self, en):
        e = self.eng[en]
        if e["cnt"] == e["barriers"][-1]:
            return
        val = e["cnt"]

        def body(obj):
            obj.wait_ge(e["sem"], val)

        e["sec"](body)
        e["barriers"].append(val)
        e["bar_cnt"] = 0
        self._learn(e["vc"], en, val)

    def _expand(self, keys):
        out = []
        for k in keys:
            if isinstance(k, str) and k in self.overlap:
                out.extend(self.overlap[k])
            else:
                out.append(k)
        return out

    def _need(self, waits, en, pe, pc):
        if pe == en == "pe":
            return
        if pc > waits.get(pe, -1):
            waits[pe] = pc

    def emit(self, en, f, reads=(), writes=(), inc=True):
        e = self.eng[en]
        reads = self._expand(reads)
        writes = self._expand(writes)
        waits = {}
        for r in reads:
            rec = self.reg.get(r)
            if rec:
                for pe_, pc_ in rec[0].items():
                    self._need(waits, en, pe_, pc_)
        for wk in writes:
            rec = self.reg.get(wk)
            if rec:
                for pe_, pc_ in rec[0].items():
                    self._need(waits, en, pe_, pc_)
                for pe_, pc_ in rec[1].items():
                    self._need(waits, en, pe_, pc_)
        know = e["vc"]
        real = []
        for pe_, pc_ in sorted(waits.items(), key=str):
            if self.eng[pe_]["step"] == 16 and pe_ != en:
                bars = self.eng[pe_]["barriers"]
                if pc_ > bars[-1]:
                    self.close_wave(pe_)
                    bars = self.eng[pe_]["barriers"]
                import bisect as _bi
                pc_ = bars[_bi.bisect_left(bars, pc_)]
            if pc_ > know.get(pe_, -1):
                real.append((pe_, pc_))
                self._learn(know, pe_, pc_)

        def body(obj):
            for pe_, pc_ in real:
                obj.wait_ge(self.eng[pe_]["sem"], pc_)
            ins = f(obj)
            if inc:
                if isinstance(ins, (list, tuple)):
                    ins = ins[-1]
                ins.then_inc(e["sem"], e["step"])

        e["sec"](body)
        if inc and e["step"] == 16:
            e["bar_cnt"] += 1
        if inc:
            e["cnt"] += e["step"]
            c = e["cnt"]
            self.opvc[(en, c)] = dict(know, **{en: c})
            if e["step"] == 16 and e["bar_cnt"] >= self.WAVE:
                self.close_wave(en)
            for wk in e["pend"]:
                self.reg[wk] = [{en: c}, {}]
            e["pend"] = []
            for wk in writes:
                self.reg[wk] = [{en: c}, {}]
            for r in reads:
                rec = self.reg.setdefault(r, [{}, {}])
                rec[1][en] = c
        else:
            nxt = e["cnt"] + e["step"]
            for wk in writes:
                self.reg[wk] = [{en: nxt}, {}]
                e["pend"].append(wk)
            for r in reads:
                rec = self.reg.setdefault(r, [{}, {}])
                rec[1][en] = nxt

    def wait_all(self, en):
        e = self.eng[en]
        for pe_, pe in self.eng.items():
            if pe["step"] == 16:
                self.close_wave(pe_)
        know = e["vc"]
        real = []
        for pe_, pe in self.eng.items():
            if pe_ == en:
                continue
            if pe["cnt"] > know.get(pe_, -1) and pe["cnt"] > 0:
                real.append((pe_, pe["cnt"]))
                self._learn(know, pe_, pe["cnt"])

        def body(obj):
            for pe_, pc_ in real:
                obj.wait_ge(self.eng[pe_]["sem"], pc_)

        e["sec"](body)


class Arena:
    def __init__(self, nc, name, ncols):
        self.t = nc.alloc_sbuf_tensor(name, [128, ncols], BF16)
        self.ncols = ncols
        self.regions = {}

    def view(self, name, off, ncols, dtype, shape):
        assert name not in self.regions
        assert off + ncols <= self.ncols, (name, off, ncols)
        self.regions[name] = (off, ncols)
        ap = self.t[:, off:off + ncols]
        if dtype != BF16:
            ap = ap.bitcast(dtype)
        free = shape[1:]
        n = 1
        for xx in free:
            n *= xx
        assert n == ap.shape[1], (name, n, ap.shape)
        if len(free) > 1:
            subs = " ".join(f"f{i}" for i in range(len(free)))
            kw = {f"f{i}": free[i] for i in range(len(free))}
            ap = ap.rearrange(f"p ({subs}) -> p {subs}", **kw)
        if shape[0] != 128:
            ap = ap[0:shape[0]]
        return ap

    def overlaps(self):
        names = list(self.regions)
        out = {}
        for a in names:
            oa, na = self.regions[a]
            out[a] = tuple(
                b for b in names
                if not (self.regions[b][0] + self.regions[b][1] <= oa
                        or oa + na <= self.regions[b][0]))
        return out


def build_program():
    nc = bass.Bass("TRN2", target_bir_lowering=False, debug=False)

    x = nc.dram_tensor("x", [NSEQ, T, D], F32, kind="ExternalInput").ap()
    n1w = nc.dram_tensor("norm1_w", [D], F32, kind="ExternalInput").ap()
    n2w = nc.dram_tensor("norm2_w", [D], F32, kind="ExternalInput").ap()
    g_wq = nc.dram_tensor("g_wq", [D, D], F32, kind="ExternalInput").ap()
    g_kvd = nc.dram_tensor("g_kv_down", [D, LAT], F32, kind="ExternalInput").ap()
    g_kup = nc.dram_tensor("g_k_up", [LAT, LAT], F32, kind="ExternalInput").ap()
    g_vup = nc.dram_tensor("g_v_up", [LAT, LAT], F32, kind="ExternalInput").ap()
    g_wo = nc.dram_tensor("g_wo", [D, D], F32, kind="ExternalInput").ap()
    l_wq = nc.dram_tensor("l_wq", [D, D], F32, kind="ExternalInput").ap()
    l_wk = nc.dram_tensor("l_wk", [D, LAT], F32, kind="ExternalInput").ap()
    l_wv = nc.dram_tensor("l_wv", [D, LAT], F32, kind="ExternalInput").ap()
    l_wo = nc.dram_tensor("l_wo", [D, D], F32, kind="ExternalInput").ap()
    W_fc = nc.dram_tensor("W_fc", [F, D], F32, kind="ExternalInput").ap()
    b_fc = nc.dram_tensor("b_fc", [F], F32, kind="ExternalInput").ap()
    A_fc = nc.dram_tensor("A_fc", [16, D], F32, kind="ExternalInput").ap()
    B_fc = nc.dram_tensor("B_fc", [F, 16], F32, kind="ExternalInput").ap()
    W_pj = nc.dram_tensor("W_proj", [D, F], F32, kind="ExternalInput").ap()
    b_pj = nc.dram_tensor("b_proj", [D], F32, kind="ExternalInput").ap()
    A_pj = nc.dram_tensor("A_proj", [16, F], F32, kind="ExternalInput").ap()
    B_pj = nc.dram_tensor("B_proj", [D, 16], F32, kind="ExternalInput").ap()
    out = nc.dram_tensor("out", [NSEQ, T, D], F32, kind="ExternalOutput").ap()

    sk = "ExternalOutput" if DEBUG_TAPS else "Internal"
    xnd = nc.dram_tensor("xnd", [NSEQ * T, D], BF16, kind=sk).ap()
    s1d = nc.dram_tensor("s1d", [NSEQ * T, D], BF16, kind=sk).ap()
    sdd = nc.dram_tensor("sdd", [NSEQ * T, D], BF16, kind=sk).ap()
    h2d = nc.dram_tensor("h2d", [F, NSEQ * T], BF16, kind=sk).ap()
    h2dv = h2d.rearrange("(fk p) tok -> p fk tok", fk=32, p=128)
    wqfcd = nc.dram_tensor("wqfcd", [F, D], BF16, kind=sk).ap()
    wqpjd = nc.dram_tensor("wqpjd", [D, F], BF16, kind=sk).ap()
    recd = nc.dram_tensor("recd", [2, 2048], BF16, kind="Internal").ap()
    dbg1 = nc.dram_tensor("dbg1", [128, F], BF16, kind=sk).ap()
    dbg2 = nc.dram_tensor("dbg2", [17, F], BF16, kind=sk).ap()

    AR = Arena(nc, "arena", 98304)
    V = AR.view
    wq1 = [V(f"wq1_{k}", 1024 * k, 1024, BF16, [128, 1024]) for k in range(8)]
    kvd = [V(f"kvd_{k}", 8192 + 256 * k, 256, BF16, [128, 256]) for k in range(8)]
    kup = [V(f"kup_{k}", 10240 + 256 * k, 256, BF16, [128, 256]) for k in range(2)]
    vup = [V(f"vup_{k}", 10752 + 256 * k, 256, BF16, [128, 256]) for k in range(2)]
    wo1 = [V(f"wo1_{k}", 11264 + 1024 * k, 1024, BF16, [128, 1024]) for k in range(8)]
    wq2 = [V(f"wq2_{k}", 19456 + 1024 * k, 1024, BF16, [128, 1024]) for k in range(8)]
    wk2 = [V(f"wk2_{k}", 27648 + 256 * k, 256, BF16, [128, 256]) for k in range(8)]
    wv2 = [V(f"wv2_{k}", 29696 + 256 * k, 256, BF16, [128, 256]) for k in range(8)]
    wo2 = [V(f"wo2_{k}", 31744 + 1024 * k, 1024, BF16, [128, 1024]) for k in range(8)]
    qwi = [V(f"qwi_{i}", 2048 * i, 2048, F32, [128, 1024]) for i in range(2)]
    qw_y = V("qw_y", 4096, 2048, F32, [128, 1024])
    qw_i = V("qw_i", 6144, 2048, I32, [128, 1024])
    qw_g = V("qw_g", 8192, 2048, F32, [128, 1024])
    qw_o = V("qw_o", 10240, 1024, BF16, [128, 1024])
    wfcT = [V(f"wfcT_{k}", 4096 * k, 4096, BF16, [128, 4096]) for k in range(8)]
    wpjT = [V(f"wpjT_{k}", 1024 * k, 1024, BF16, [128, 1024]) for k in range(32)]

    C0 = 39936
    q1f = V("q1f", C0, 4096, BF16, [128, 8, 512])
    k1f = V("k1f", C0 + 4096, 4096, BF16, [128, 2, 2048])
    vb = V("vb", C0 + 8192, 4160, BF16, [128, 4160])
    attn_m = V("attn_m", C0 + 12352, 4096, BF16, [128, 8, 512])
    latf = V("latf", C0 + 16448, 1024, BF16, [128, 2, 512])
    xnf = V("xnf", C0 + 17472, 4096, BF16, [128, 8, 512])
    ptb = [V(f"ptb_{i}", C0 + 21568 + 512 * i, 512, BF16, [128, 512])
           for i in range(2)]
    qtm = V("qtm", C0 + 22592, 1024, F16, [128, 1024])
    qn_ = V("qn", C0 + 23616, 1024, F16, [128, 1024])
    qrn = V("qrn", 0, 0, F16, [128, 0]) if False else None
    rtmp = V("rtmp", C0 + 24640, 512, F16, [128, 512])
    ktm = V("ktm", C0 + 25152, 256, F16, [128, 256])
    kn_ = V("kn", C0 + 25408, 256, F16, [128, 256])
    krn = None
    recb = V("recb", C0 + 25664, 2048, BF16, [64, 2048])
    rec_sb = V("rec_sb", C0 + 25664, 2048, BF16, [1, 2048])
    sqj = V("sqj", C0 + 27712, 1024, BF16, [128, 1024])
    ld_a = V("ld_a", C0 + 28736, 1024, BF16, [128, 1024])
    ld_b = V("ld_b", C0 + 29760, 1024, BF16, [128, 1024])
    s1b = [V(f"s1b_{i}", C0 + 30784 + 1024 * i, 1024, BF16, [128, 1024])
           for i in range(2)]
    u2all = V("u2all", C0 + 22592, 4096, BF16, [17, 8, 512])
    xnf2 = V("xnf2", C0 + 8192, 4096, BF16, [128, 8, 512])
    h2r = [V(f"h2r_{i}", C0 + 4096 * i, 4096, BF16, [128, 32, 128])
           for i in range(2)]
    h2w = [V(f"h2w_{i}", C0 + 21568 + 512 * i, 512, BF16, [128, 512])
           for i in range(2)]
    h2wp = [V("h2wp_0", C0 + 21568, 1024, BF16, [128, 1024]),
            V("h2wp_1", C0 + 27712, 1024, BF16, [128, 1024])]
    CEND = C0 + 32832

    D0 = CEND
    mA = [V(f"mA_{r}", D0 + 512 * r, 512, BF16, [128, 512]) for r in range(4)]
    mB = [V(f"mB_{r}", D0 + 2048 + 512 * r, 512, BF16, [128, 512])
          for r in range(6)]
    tab = {}
    for i, nm in enumerate(("cos_g", "sin_g", "cos_l", "sin_l")):
        tab[nm] = V(nm, D0 + 5120 + 512 * i, 512, F16, [128, 16, 32])
    id_bf = V("id_bf", D0 + 7168, 128, BF16, [128, 128])
    id_f16 = V("id_f16", D0 + 7296, 128, F16, [128, 128])
    id_f32 = V("id_f32", D0 + 7424, 256, F32, [128, 128])
    ones_bf = V("ones_bf", D0 + 7680, 64, BF16, [1, 64])
    ones_f32 = V("ones_f32", D0 + 7744, 256, F32, [1, 128])
    n12 = V("n12", D0 + 8000, 32, F32, [128, 16])
    r1b = V("r1b", D0 + 8032, 64, F32, [128, 32])
    r2b = V("r2b", D0 + 8096, 64, F32, [128, 32])
    lncol = V("lncol", D0 + 8160, 64, F32, [128, 32])
    scl = V("scl", D0 + 8224, 16, F32, [128, 8])
    bfcT = V("bfcT", D0 + 8240, 4096, BF16, [17, 4096])
    q1wi = [V(f"q1wi_{i}", D0 + 8240 + 2048 * i, 2048, F32, [128, 1024])
            for i in range(2)]
    bpjT = V("bpjT", D0 + 12336, 1024, BF16, [17, 1024])
    afcT = V("afcT", D0 + 13360, 128, BF16, [128, 128])
    apjT = V("apjT", D0 + 13488, 512, BF16, [128, 512])
    up1 = V("up1", D0 + 14000, 512, BF16, [17, 512])
    cent = V("cent", D0 + 14512, 2048, F32, [128, 1024])
    gbuf = V("gbuf", D0 + 14512, 2048, F32, [128, 1024])
    apstg = V("apstg", D0 + 14512, 1024, BF16, [16, 1024])
    xb = [V(f"xb_{i}", D0 + 16560 + 2048 * i, 2048, F32, [128, 1024])
          for i in range(2)]
    lnrow = V("lnrow", D0 + 16560, 4096, F32, [1, 2048])
    wtmp = V("wtmp", D0 + 20656, 2048, F32, [128, 1024])
    tb = [V(f"tb_{i}", D0 + 16560 + 1024 * i, 1024, F32, [128, 512])
          for i in range(4)]
    tbi = V("tbi", D0 + 20656, 1024, I32, [128, 512])
    invb = V("invb", D0 + 22704, 64, F32, [128, 32])
    epsc = V("epsc", D0 + 22768, 16, F32, [128, 8])
    D0X = D0 + 22784
    ptb += [V(f"ptb_{i + 2}", D0X + 512 * i, 512, BF16, [128, 512])
            for i in range(2)]
    ptbp = [V("ptbA", C0 + 21568, 1024, BF16, [128, 2, 512]),
            V("ptbB", D0X, 1024, BF16, [128, 2, 512])]
    ptbp2 = [V("ptbA2", C0 + 22592, 1024, BF16, [128, 2, 512]),
             V("ptbB2", C0 + 23616, 1024, BF16, [128, 2, 512])]
    lnc_q = V("lnc_q", D0X + 1024, 40, F32, [128, 20])
    lnc_k = V("lnc_k", D0X + 1064, 16, F32, [128, 8])
    qrn = V("qrn", D0X + 1080, 1024, F16, [128, 1024])
    krn = V("krn", D0X + 2104, 256, F16, [128, 256])
    TOT = D0X + 2360
    assert TOT <= 98304, TOT
    qrn2 = V("qrn2", C0 + 12352, 1024, F16, [128, 1024])
    krn2 = V("krn2", C0 + 13376, 256, F16, [128, 256])
    rtmpk = V("rtmpk", TOT, 128, F16, [128, 128])
    assert TOT + 128 <= 98304

    ps = nc.alloc_psum_tensor("ps", [128, 4096], F32)
    ps16 = ps[:].bitcast(F16)
    psbf = ps[:].bitcast(BF16)

    def bank(i, rows=128):
        return ps[0:rows, 512 * i:512 * (i + 1)]

    with nc.Block() as blk:
        p = Prog(nc, blk)
        p.overlap = AR.overlaps()
        for name, sec, step in [("pe", blk.tensor, 1), ("act", blk.scalar, 1),
                                ("dve", blk.vector, 1), ("pool", blk.gpsimd, 1),
                                ("sd", blk.sync, 16), ("gd", blk.gpsimd, 16),
                                ("ad", blk.scalar, 16)]:
            p.engine(name, sec, step)
        E = p.emit

        def PB(i):
            return ("pb", i)

        def act_set_load(set_id):
            def body(s):
                s.add_instruction(mybir.InstLoadActFuncSet(
                    name=nc.get_next_instruction_name(),
                    act_func_set_id=set_id, ins=[], outs=[]))
            p.eng["act"]["sec"](body)

        for idv, dtv in ((id_bf, "id_bf"), (id_f16, "id_f16"), (id_f32, "id_f32")):
            E("dve", lambda v, idv=idv: v.memset(idv, 1.0), writes=[dtv])
            E("pool", lambda g, idv=idv: g.affine_select(
                out=idv, in_=idv, pattern=[[1, 128]], compare_op=ALU.is_equal,
                fill=0.0, base=0, channel_multiplier=-1),
                reads=[dtv], writes=[dtv])
        E("dve", lambda v: v.memset(ones_bf, 1.0), writes=["ones_bf"])
        E("dve", lambda v: v.memset(epsc[:, 0:1], EPS_LN), writes=["epsc"])
        E("dve", lambda v: v.memset(epsc[:, 1:2], EPS_RMS), reads=["epsc"], writes=["epsc"])
        E("dve", lambda v: v.memset(epsc[:, 2:3], 0.5), reads=["epsc"], writes=["epsc"])
        E("dve", lambda v: v.memset(ones_f32, 1.0), writes=["ones_f32"])
        E("dve", lambda v: v.memset(vb[:, 64::65], 1.0), writes=["vb"])
        for r in range(4):
            E("dve", lambda v, r=r: v.memset(mA[r], 1.0), writes=[f"mA_{r}"])
            E("pool", lambda g, r=r: g.affine_select(
                out=mA[r], in_=mA[r], pattern=[[1, 512]], compare_op=ALU.is_ge,
                fill=0.0, base=-128 * r, channel_multiplier=-1),
                reads=[f"mA_{r}"], writes=[f"mA_{r}"])
        for ri in range(6):
            r = ri - 2
            E("dve", lambda v, ri=ri: v.memset(mB[ri], 1.0), writes=[f"mB_{ri}"])
            E("pool", lambda g, ri=ri, r=r: g.affine_select(
                out=mB[ri], in_=mB[ri], pattern=[[1, 512]], compare_op=ALU.is_ge,
                fill=0.0, base=-128 * r, channel_multiplier=-1),
                reads=[f"mB_{ri}"], writes=[f"mB_{ri}"])
            E("pool", lambda g, ri=ri, r=r: g.affine_select(
                out=mB[ri], in_=mB[ri], pattern=[[-1, 512]], compare_op=ALU.is_ge,
                fill=0.0, base=128 * r + 255, channel_multiplier=1),
                reads=[f"mB_{ri}"], writes=[f"mB_{ri}"])
        E("sd", lambda e: e.dma_start(
            out=n12[:, 0:8], in_=n1w.rearrange("(c p) -> p c", p=128),
            allow_slow_non_contiguous=True), writes=["n12"])
        E("sd", lambda e: e.dma_start(
            out=n12[:, 8:16], in_=n2w.rearrange("(c p) -> p c", p=128),
            allow_slow_non_contiguous=True), reads=["n12"], writes=["n12"])

        for base, cname, sname in ((BASE_G, "cos_g", "sin_g"),
                                   (BASE_L, "cos_l", "sin_l")):
            E("pool", lambda g: g.iota(tb[3][0:1, 0:32], pattern=[[1, 32]], base=0,
                                       channel_multiplier=0,
                                       allow_small_or_imprecise_dtypes=True),
              writes=["tb_3"])
            act_set_load(ACT_COMBO_SET)
            E("act", lambda s, base=base: s.activation(
                tb[3][0:1, 32:64], tb[3][0:1, 0:32], AF.Exp,
                scale=-math.log(base) / 32.0), reads=["tb_3"], writes=["tb_3"])
            E("pe", lambda t: t.matmul(ps[0:128, 0:32], ones_f32, tb[3][0:1, 32:64],
                                       start=True, stop=True),
              reads=["ones_f32", "tb_3"], writes=[PB(0)])
            E("dve", lambda v: v.tensor_copy(invb[:, 0:32], ps[0:128, 0:32]),
              reads=[PB(0)], writes=["invb"])
            E("pool", lambda g: g.iota(tb[0][:], pattern=[[128, 16], [0, 32]],
                                       base=0, channel_multiplier=1,
                                       allow_small_or_imprecise_dtypes=True),
              writes=["tb_0"])
            E("dve", lambda v: v.tensor_tensor(
                tb[1][:].rearrange("p (a b) -> p a b", a=16),
                tb[0][:].rearrange("p (a b) -> p a b", a=16),
                invb[:, 0:32].unsqueeze(1).broadcast_to([128, 16, 32]),
                ALU.mult), reads=["tb_0", "invb"], writes=["tb_1"])
            act_set_load(ACT_TRIG_SET)
            for trig, shift in ((sname, 0.0), (cname, PI / 2.0)):
                E("dve", lambda v, shift=shift: v.tensor_scalar(
                    tb[0][:], tb[1][:], shift, 1.0 / TWO_PI, ALU.add, ALU.mult),
                  reads=["tb_1"], writes=["tb_0"])
                E("dve", lambda v: v.tensor_copy(tbi[:], tb[0][:]),
                  reads=["tb_0"], writes=["tbi"])
                E("dve", lambda v, shift=shift: v.tensor_scalar(
                    tb[2][:], tb[1][:], shift, None, ALU.add),
                  reads=["tb_1"], writes=["tb_2"])
                E("dve", lambda v: v.scalar_tensor_tensor(
                    tb[2][:], tbi[:], -TWO_PI, tb[2][:], ALU.mult, ALU.add),
                  reads=["tbi", "tb_2"], writes=["tb_2"])
                for cmp_op, thr, fix in ((ALU.is_gt, PI, -TWO_PI),
                                         (ALU.is_lt, -PI, TWO_PI)):
                    E("dve",
                      lambda v, cmp_op=cmp_op, thr=thr: v.tensor_single_scalar(
                          tb[0][:], tb[2][:], thr, cmp_op),
                      reads=["tb_2"], writes=["tb_0"])
                    E("dve", lambda v, fix=fix: v.scalar_tensor_tensor(
                        tb[2][:], tb[0][:], fix, tb[2][:], ALU.mult, ALU.add),
                      reads=["tb_0", "tb_2"], writes=["tb_2"])
                E("dve", lambda v: v.tensor_scalar(
                    tb[2][:], tb[2][:], PI, -PI, ALU.min, ALU.max),
                  reads=["tb_2"], writes=["tb_2"])
                E("act", lambda s, trig=trig: s.activation(
                    tab[trig].rearrange("p a b -> p (a b)"), tb[2][:], AF.Sin),
                  reads=["tb_2"], writes=[trig])

        act_set_load(ACT_COMBO_SET)

        def perm_in(ap2d):
            return ap2d.rearrange("p (h i two) -> p h two i", two=2, i=32)

        def perm_q_pairs(src2d, dst2d):
            src6 = src2d.rearrange("p (a b c d e) -> p a b c d e",
                                   a=2, b=2, c=4, d=32, e=2)
            dst6 = dst2d.rearrange("p (a c b e d) -> p a c b e d",
                                   a=2, c=4, b=2, e=2, d=32)
            pairs = []
            for a in range(2):
                for b in range(2):
                    inp = src6[:, a, b].transpose([0, 1, 3, 2])
                    outp = dst6[:, a, :, b]
                    pairs.append((inp, outp))
            return pairs

        def prep_tile(dst, src, k, n_out, fold_col, permute, tag):
            E("ad", lambda e, k=k: e.dma_start(
                out=wtmp[0:128, 0:n_out], in_=src[128 * k:128 * (k + 1), :]),
              reads=["wtmp"], writes=["wtmp"])
            inv = wtmp[0:128, 0:n_out]
            col = (n12[:, fold_col + k:fold_col + k + 1]
                   if fold_col is not None else None)
            if permute == "q":
                for inp, outp in perm_q_pairs(inv, dst[k][:]):
                    E("dve", lambda v, inp=inp, outp=outp, col=col:
                      v.tensor_scalar(outp, inp, col, None, ALU.mult),
                      reads=["wtmp", "n12"], writes=[f"{tag}_{k}"])
                return
            if permute:
                inv = perm_in(inv)
            if col is not None:
                E("dve", lambda v, inv=inv, col=col: v.tensor_scalar(
                    dst[k][:], inv, col, None, ALU.mult),
                  reads=["wtmp", "n12"], writes=[f"{tag}_{k}"])
            else:
                E("dve", lambda v, inv=inv: v.tensor_copy(dst[k][:], inv),
                  reads=["wtmp"], writes=[f"{tag}_{k}"])

        w1_q = []
        for k in range(8):
            w1_q.append(lambda k=k: prep_tile(wq1, g_wq, k, 1024, 0, "q", "wq1"))
        for k in range(8):
            w1_q.append(lambda k=k: prep_tile(kvd, g_kvd, k, 256, 0, False, "kvd"))
        for k in range(2):
            def kup_tile(k=k):
                E("ad", lambda e: e.dma_start(
                    out=wtmp[0:128, 0:256], in_=g_kup[128 * k:128 * (k + 1), :]),
                  reads=["wtmp"], writes=["wtmp"])
                E("dve", lambda v: v.tensor_copy(
                    kup[k][:], perm_in(wtmp[0:128, 0:256])),
                  reads=["wtmp"], writes=[f"kup_{k}"])
            w1_q.append(kup_tile)

        w2_q = []
        for k in range(8):
            w2_q.append(lambda k=k: prep_tile(wq2, l_wq, k, 1024, 8, "q", "wq2"))
        for k in range(8):
            w2_q.append(lambda k=k: prep_tile(wk2, l_wk, k, 256, 8, True, "wk2"))
        for k in range(8):
            w2_q.append(lambda k=k: prep_tile(wv2, l_wv, k, 256, 8, False, "wv2"))

        lora_q = []

        def lp_afc():
            E("ad", lambda e: e.dma_start(out=wtmp[0:16, 0:1024], in_=A_fc),
              reads=["wtmp"], writes=["wtmp"])
            E("dve", lambda v: v.tensor_copy(sqj[0:16, :], wtmp[0:16, 0:1024]),
              reads=["wtmp"], writes=["sqj"])
            for k in range(8):
                E("pe", lambda t, k=k: t.transpose(
                    psbf[0:128, 7168 + 16 * k:7168 + 16 * (k + 1)],
                    sqj[0:16, 128 * k:128 * (k + 1)], id_bf[0:16, 0:16]),
                  reads=["sqj", "id_bf"], inc=(k == 7),
                  writes=[PB(7)])
            E("dve", lambda v: v.tensor_copy(afcT[:], psbf[0:128, 7168:7296]),
              reads=[PB(7)], writes=["afcT"])
        lora_q.append(lp_afc)

        for q_ in range(4):
            def lp_apj(q_=q_):
                E("ad", lambda e: e.dma_start(
                    out=wtmp[0:16, 0:1024], in_=A_pj[:, 1024 * q_:1024 * (q_ + 1)]),
                  reads=["wtmp"], writes=["wtmp"])
                E("dve", lambda v: v.tensor_copy(apstg[0:16, :],
                                                 wtmp[0:16, 0:1024]),
                  reads=["wtmp"], writes=["apstg"])
                for j in range(8):
                    fk = 8 * q_ + j
                    E("pe", lambda t, j=j, fk=fk: t.transpose(
                        psbf[0:128, 7168 + 16 * j:7168 + 16 * (j + 1)],
                        apstg[0:16, 128 * j:128 * (j + 1)], id_bf[0:16, 0:16]),
                      reads=["apstg", "id_bf"], inc=(j == 7),
                      writes=[PB(7)])
                E("dve", lambda v, q_=q_: v.tensor_copy(
                    apjT[:, 128 * q_:128 * (q_ + 1)], psbf[0:128, 7168:7296]),
                  reads=[PB(7)], writes=["apjT"])
            lora_q.append(lp_apj)

        for k in range(32):
            def lp_bfc(k=k):
                E("ad", lambda e: e.dma_start(
                    out=wtmp[0:128, 0:16], in_=B_fc[128 * k:128 * (k + 1), :]),
                  reads=["wtmp"], writes=["wtmp"])
                E("dve", lambda v: v.tensor_copy(sqj[0:128, 0:16],
                                                 wtmp[0:128, 0:16]),
                  reads=["wtmp"], writes=["sqj"])
                E("pe", lambda t: t.transpose(
                    psbf[0:16, 7168:7296], sqj[0:128, 0:16], id_bf),
                  reads=["sqj", "id_bf"], writes=[PB(7)])
                E("dve", lambda v: v.tensor_copy(
                    bfcT[0:16, 128 * k:128 * (k + 1)], psbf[0:16, 7168:7296]),
                  reads=[PB(7)], writes=["bfcT"])
            lora_q.append(lp_bfc)
        for k in range(8):
            def lp_bpj(k=k):
                E("ad", lambda e: e.dma_start(
                    out=wtmp[0:128, 0:16], in_=B_pj[128 * k:128 * (k + 1), :]),
                  reads=["wtmp"], writes=["wtmp"])
                E("dve", lambda v: v.tensor_copy(sqj[0:128, 0:16],
                                                 wtmp[0:128, 0:16]),
                  reads=["wtmp"], writes=["sqj"])
                E("pe", lambda t: t.transpose(
                    psbf[0:16, 7168:7296], sqj[0:128, 0:16], id_bf),
                  reads=["sqj", "id_bf"], writes=[PB(7)])
                E("dve", lambda v: v.tensor_copy(
                    bpjT[0:16, 128 * k:128 * (k + 1)], psbf[0:16, 7168:7296]),
                  reads=[PB(7)], writes=["bpjT"])
            lora_q.append(lp_bpj)

        def lp_bias():
            E("gd", lambda e: e.dma_start(out=bfcT[16:17, :], in_=b_fc[None, :]),
              writes=["bfcT"])
            E("gd", lambda e: e.dma_start(out=bpjT[16:17, :], in_=b_pj[None, :]),
              writes=["bpjT"])
        lora_q.append(lp_bias)

        q1_q = []
        W_pj_t = W_pj.rearrange("d (a b) -> (d a) b", a=4)

        def qp1_tile(W, i, maxc, tag, first):
            b = i % 2
            E(("gd", "sd", "ad")[i % 3],
              lambda e, i=i, W=W, b=b: e.dma_start(
                  out=q1wi[b][:], in_=W[128 * i:128 * (i + 1), :]),
              reads=[f"q1wi_{b}"], writes=[f"q1wi_{b}"])
            E("dve", lambda v, b=b: v.tensor_reduce(
                lncol[:, 20:21], q1wi[b][:], AXX, ALU.max,
                apply_absolute_value=True),
              reads=[f"q1wi_{b}"], writes=["lncol"])
            if first:
                E("dve", lambda v, maxc=maxc: v.tensor_copy(maxc, lncol[:, 20:21]),
                  reads=["lncol"], writes=[tag])
            else:
                E("dve", lambda v, maxc=maxc: v.tensor_tensor(
                    maxc, maxc, lncol[:, 20:21], ALU.max),
                  reads=["lncol", tag], writes=[tag])

        for i in range(32):
            q1_q.append(lambda i=i: qp1_tile(W_fc, i, lncol[:, 16:17], "mx_fc",
                                             i == 0))
        for i in range(32):
            q1_q.append(lambda i=i: qp1_tile(W_pj_t, i, lncol[:, 17:18], "mx_pj",
                                             i == 0))

        def qscale(maxc, tag, wi):
            E("pe", lambda t, maxc=maxc: t.transpose(ps[0:1, 3584:3712], maxc,
                                                     id_f32),
              reads=[tag, "id_f32"], writes=[PB(7)])
            E("dve", lambda v: v.tensor_reduce(
                lncol[0:1, 24:25], ps[0:1, 3584:3712], AXX, ALU.max),
              reads=[PB(7)], writes=["lncol"])
            E("dve", lambda v: v.reciprocal(lncol[0:1, 25:26], lncol[0:1, 24:25]),
              reads=["lncol"], writes=["lncol"])
            E("dve", lambda v: v.tensor_scalar(
                lncol[0:1, 26:27], lncol[0:1, 24:25], 1.0 / 127.0, None, ALU.mult),
              reads=["lncol"], writes=["lncol"])
            E("dve", lambda v: v.tensor_scalar(
                lncol[0:1, 27:28], lncol[0:1, 25:26], 127.0, None, ALU.mult),
              reads=["lncol"], writes=["lncol"])
            E("pe", lambda t: t.matmul(ps[0:128, 3584:3586], ones_f32,
                                       lncol[0:1, 26:28], start=True, stop=True),
              reads=["lncol", "ones_f32"], writes=[PB(7)])
            E("dve", lambda v, wi=wi: v.tensor_copy(
                scl[:, 2 * wi:2 * wi + 2], ps[0:128, 3584:3586]),
              reads=[PB(7)], writes=["scl"])

        q1_q.append(lambda: qscale(lncol[:, 16:17], "mx_fc", 0))
        q1_q.append(lambda: qscale(lncol[:, 17:18], "mx_pj", 1))

        q2_q = []
        wqpjd_t = wqpjd.rearrange("d (a b) -> (d a) b", a=4)

        def qp2_tile(W, Wdst, i, wi, tag):
            b = i % 2
            icol = scl[:, 2 * wi + 1:2 * wi + 2]
            scol = scl[:, 2 * wi:2 * wi + 1]
            E(("gd", "sd", "ad")[i % 3],
              lambda e, i=i, W=W, b=b: e.dma_start(
                  out=qwi[b][:], in_=W[128 * i:128 * (i + 1), :]),
              reads=[f"qwi_{b}"], writes=[f"qwi_{b}"])
            E("act", lambda s_, b=b, icol=icol: s_.activation(
                qw_y[:], qwi[b][:], AF.Identity, scale=icol,
                bias=epsc[:, 2:3]),
              reads=[f"qwi_{b}", "scl", "epsc"], writes=["qw_y"])
            E("dve", lambda v: v.tensor_copy(qw_i[:], qw_y[:]),
              reads=["qw_y"], writes=["qw_i"])
            E("dve", lambda v: v.scalar_tensor_tensor(
                qw_g[:], qw_i[:], 1.0, qw_y[:], ALU.bypass, ALU.is_gt),
              reads=["qw_i", "qw_y"], writes=["qw_g"])
            E("dve", lambda v: v.scalar_tensor_tensor(
                qw_g[:], qw_g[:], -1.0, qw_i[:], ALU.mult, ALU.add),
              reads=["qw_g", "qw_i"], writes=["qw_g"])
            E("dve", lambda v, scol=scol: v.tensor_scalar(
                qw_o[:], qw_g[:], scol, None, ALU.mult),
              reads=["qw_g", "scl"], writes=["qw_o"])
            E(("ad", "gd", "sd")[i % 3],
              lambda e, i=i, Wdst=Wdst: e.dma_start(
                  out=Wdst[128 * i:128 * (i + 1), :], in_=qw_o[:]),
              reads=["qw_o"], writes=[(tag, i)])

        for i in range(32):
            q2_q.append(lambda i=i: qp2_tile(W_fc, wqfcd, i, 0, "wqfc"))
        for i in range(32):
            q2_q.append(lambda i=i: qp2_tile(W_pj_t, wqpjd_t, i, 1, "wqpj"))

        def drain(q, n):
            for _ in range(min(n, len(q))):
                q.pop(0)()

        def rsqrt_cols(src_col, dst_col, n_mean, eps_col, wkey):
            E("act", lambda s: s.activation(lncol[:, 12:13], src_col, AF.Ln,
                                            scale=1.0 / n_mean, bias=eps_col),
              reads=["lncol", "epsc"], writes=["lncol"])
            E("act", lambda s: s.activation(dst_col, lncol[:, 12:13], AF.Exp,
                                            scale=-0.5),
              reads=["lncol"], writes=[wkey])

        def ln_chunk(s, c):
            i = c % 2
            E("sd", lambda e: e.dma_start(
                out=xb[i][:], in_=x[s, 128 * c:128 * (c + 1), :]),
              reads=[f"xb_{i}"], writes=[f"xb_{i}"])
            E("dve", lambda v: v.tensor_reduce(lncol[:, 0:1], xb[i][:], AXX,
                                               ALU.add),
              reads=[f"xb_{i}"], writes=["lncol"])
            E("dve", lambda v: v.tensor_scalar(lncol[:, 1:2], lncol[:, 0:1],
                                               1.0 / D, None, ALU.mult),
              reads=["lncol"], writes=["lncol"])
            E("dve", lambda v: v.tensor_scalar(cent[:], xb[i][:], lncol[:, 1:2],
                                               None, ALU.subtract),
              reads=[f"xb_{i}", "lncol"], writes=["cent"])
            E("dve", lambda v: v.scalar_tensor_tensor(
                sqj[:], cent[:], 1.0, cent[:], ALU.bypass, ALU.mult,
                accum_out=lncol[:, 2:3]),
              reads=["cent"], writes=["sqj", "lncol"])
            rsqrt_cols(lncol[:, 2:3], lncol[:, 4:5], D, epsc[:, 0:1], "lncol")
            xnb = s1b[i]
            E("dve", lambda v, xnb=xnb: v.tensor_scalar(
                xnb[:], cent[:], lncol[:, 4:5], None, ALU.mult),
              reads=["cent", "lncol"], writes=[f"s1b_{i}"])
            E("dve", lambda v, xnb=xnb: v.scalar_tensor_tensor(
                sqj[:], xnb[:], 1.0, xnb[:], ALU.bypass, ALU.mult,
                accum_out=lncol[:, 5:6]),
              reads=[f"s1b_{i}"], writes=["sqj", "lncol"])
            rsqrt_cols(lncol[:, 5:6], r1b[:, 16 * s + c:16 * s + c + 1], D,
                       epsc[:, 1:2], "r1b")
            E("ad", lambda e, xnb=xnb, s=s, c=c: e.dma_start(
                out=xnd[2048 * s + 128 * c:2048 * s + 128 * (c + 1), :],
                in_=xnb[:]),
              reads=[f"s1b_{i}"], writes=[("xnd", s, c)])

        def rope_qknorm(src, n_cols, cname, sname, rk, dst, wk, cc,
                        eng="dve", rbuf=None, rbk="rtmp"):
            nh = n_cols // 64
            lc, lck = (lnc_q, "lnc_q") if n_cols == 1024 else (lnc_k, "lnc_k")
            E("dve", lambda v: v.tensor_tensor(sqj[:, 0:n_cols], src, src,
                                               ALU.mult),
              reads=[rk], writes=["sqj"])
            E("dve", lambda v: v.tensor_reduce(
                lc[:, 0:nh],
                sqj[:, 0:n_cols].rearrange("p (h i) -> p h i", i=64),
                AXX, ALU.add),
              reads=["sqj"], writes=[lck])
            E("act", lambda s: s.activation(lc[:, 0:nh], lc[:, 0:nh], AF.Ln,
                                            scale=1.0 / 64.0, bias=epsc[:, 1:2]),
              reads=[lck, "epsc"], writes=[lck])
            E("act", lambda s: s.activation(lc[:, 0:nh], lc[:, 0:nh],
                                            AF.Exp, scale=-0.5),
              reads=[lck], writes=[lck])
            nbuf = qn_ if n_cols == 1024 else kn_
            nk = "qn" if n_cols == 1024 else "kn"
            E("dve", lambda v, nbuf=nbuf: v.tensor_tensor(
                nbuf[:, 0:n_cols].rearrange("p (h i) -> p h i", i=64),
                src.rearrange("p (h i) -> p h i", i=64),
                lc[:, 0:nh].unsqueeze(2).broadcast_to([128, nh, 64]),
                ALU.mult),
              reads=[rk, lck], writes=[nk])
            nb = nbuf[:, 0:n_cols].rearrange("p (h two i) -> p h two i",
                                             two=2, i=32)
            db = dst.rearrange("p (h two i) -> p h two i", two=2, i=32)
            e_, o_ = nb[:, :, 0, :], nb[:, :, 1, :]
            de, do = db[:, :, 0, :], db[:, :, 1, :]
            cosb = tab[cname][:, cc, :].unsqueeze(1).broadcast_to([128, nh, 32])
            sinb = tab[sname][:, cc, :].unsqueeze(1).broadcast_to([128, nh, 32])
            if rbuf is None:
                rbuf = rtmp
            half = rbuf[:, 0:n_cols // 2].rearrange("p (h i) -> p h i", i=32)
            E(eng, lambda v: v.tensor_tensor(half, o_, sinb, ALU.mult),
              reads=[nk, sname], writes=[rbk])
            E(eng, lambda v: v.tensor_tensor(de, e_, cosb, ALU.mult),
              reads=[nk, cname], writes=[wk])
            E(eng, lambda v: v.tensor_tensor(de, de, half, ALU.subtract),
              reads=[rbk, wk], writes=[wk])
            E(eng, lambda v: v.tensor_tensor(half, e_, sinb, ALU.mult),
              reads=[nk, sname], writes=[rbk])
            E(eng, lambda v: v.tensor_tensor(do, o_, cosb, ALU.mult),
              reads=[nk, cname], writes=[wk])
            E(eng, lambda v: v.tensor_tensor(do, do, half, ALU.add),
              reads=[rbk, wk], writes=[wk])

        def attention(s, aidx, hook_q=None):
            src_d = xnd if aidx == 0 else s1d
            srctag = "xnd" if aidx == 0 else "s1d"
            rbuf, rtag = (r1b, "r1b") if aidx == 0 else (r2b, "r2b")
            cname = "cos_g" if aidx == 0 else "cos_l"
            sname = "sin_g" if aidx == 0 else "sin_l"
            qw, qtag = (wq1, "wq1") if aidx == 0 else (wq2, "wq2")
            wo_w, wo_tag = (wo1, "wo1") if aidx == 0 else (wo2, "wo2")
            for sp in range(NSP):
                row0 = 2048 * s + 512 * sp
                span_keys = [(srctag, s, 4 * sp + j) for j in range(4)]
                for dk in range(8):
                    E("sd",
                      lambda e, dk=dk: e.dma_start(
                          out=xnf[:, dk, :],
                          in_=src_d[row0:row0 + 512, 128 * dk:128 * (dk + 1)],
                          transpose=True),
                      reads=span_keys + ["xnf"], writes=["xnf"])
                if aidx == 0 and SUBPH >= 2:
                    for lc in range(2):
                        for k in range(8):
                            E("pe", lambda t, k=k, lc=lc: t.matmul(
                                bank(lc), kvd[k][:, 128 * lc:128 * (lc + 1)],
                                xnf[:, k, :],
                                start=(k == 0), stop=(k == 7)),
                              reads=["xnf", f"kvd_{k}"], inc=(k == 7),
                              writes=[PB(lc)])
                        E("act", lambda s_, lc=lc: s_.activation(
                            latf[:, lc, :], bank(lc), AF.Copy),
                          reads=[PB(lc)], writes=["latf"])
                def prod_mm(tci):
                    c = 4 * sp + tci
                    rcol = rbuf[:, 16 * s + c:16 * s + c + 1]
                    for half in range(2):
                        for k in range(8):
                            E("pe", lambda t, k=k, half=half, tci=tci: t.matmul(
                                bank(2 + half),
                                xnf[:, k, 128 * tci:128 * (tci + 1)],
                                qw[k][:, 512 * half:512 * (half + 1)],
                                start=(k == 0), stop=(k == 7)),
                              reads=["xnf", f"{qtag}_{k}"], inc=(k == 7),
                              writes=[PB(2 + half)])
                    E("act", lambda s_, rcol=rcol: s_.activation(
                        qtm[:], ps[0:128, 1024:2048], AF.Copy, scale=rcol),
                      reads=[PB(2), PB(3), rtag], writes=["qtm"])
                    if aidx == 0:
                        for lc in range(2):
                            E("pe", lambda t, lc=lc, tci=tci: t.matmul(
                                bank(2)[:, 0:256],
                                latf[:, lc, 128 * tci:128 * (tci + 1)],
                                kup[lc][:], start=(lc == 0), stop=(lc == 1)),
                              reads=["latf", f"kup_{lc}"], inc=(lc == 1),
                              writes=[PB(2)])
                    else:
                        for k in range(8):
                            E("pe", lambda t, k=k, tci=tci: t.matmul(
                                bank(2)[:, 0:256],
                                xnf[:, k, 128 * tci:128 * (tci + 1)],
                                wk2[k][:], start=(k == 0), stop=(k == 7)),
                              reads=["xnf", f"wk2_{k}"], inc=(k == 7),
                              writes=[PB(2)])
                    E("act", lambda s_, rcol=rcol: s_.activation(
                        ktm[:], bank(2)[:, 0:256], AF.Copy, scale=rcol),
                      reads=[PB(2), rtag], writes=["ktm"])
                    if aidx == 0:
                        for lc in range(2):
                            E("pe", lambda t, lc=lc, tci=tci: t.matmul(
                                bank(2)[:, 0:256],
                                latf[:, lc, 128 * tci:128 * (tci + 1)],
                                vup[lc][:], start=(lc == 0), stop=(lc == 1)),
                              reads=["latf", f"vup_{lc}"], inc=(lc == 1),
                              writes=[PB(2)])
                    else:
                        for k in range(8):
                            E("pe", lambda t, k=k, tci=tci: t.matmul(
                                bank(2)[:, 0:256],
                                xnf[:, k, 128 * tci:128 * (tci + 1)],
                                wv2[k][:], start=(k == 0), stop=(k == 7)),
                              reads=["xnf", f"wv2_{k}"], inc=(k == 7),
                              writes=[PB(2)])
                    E("act", lambda s_, rcol=rcol, c=c: s_.activation(
                        vb.rearrange("p (g y) -> p g y", g=4)[:, :,
                                                              65 * c:65 * c + 64],
                        bank(2)[:, 0:256].rearrange("p (g y) -> p g y", g=4),
                        AF.Copy, scale=rcol),
                      reads=[PB(2), rtag], writes=["vb"])

                def prod_rope(tci):
                    c = 4 * sp + tci
                    qb_, qk_ = (qrn, "qrn") if tci % 2 == 0 else (qrn2, "qrn2")
                    kb_, kk_ = (krn, "krn") if tci % 2 == 0 else (krn2, "krn2")
                    rope_qknorm(qtm[:], 1024, cname, sname, "qtm", qb_[:],
                                qk_, c)
                    rope_qknorm(ktm[:], 256, cname, sname, "ktm", kb_[:],
                                kk_, c, eng="pool", rbuf=rtmpk, rbk="rtmpk")

                def prod_tr(tci):
                    c = 4 * sp + tci
                    qb_, qk_ = (qrn, "qrn") if tci % 2 == 0 else (qrn2, "qrn2")
                    kb_, kk_ = (krn, "krn") if tci % 2 == 0 else (krn2, "krn2")
                    for mc in range(8):
                        E("pe", lambda t, mc=mc, qb_=qb_: t.transpose(
                            ps16[0:128, 6144 + 128 * mc:6144 + 128 * (mc + 1)],
                            qb_[:, 128 * mc:128 * (mc + 1)], id_f16),
                          reads=[qk_, "id_f16"], inc=(mc == 7),
                          writes=[PB(6), PB(7)])
                    E("dve", lambda v, tci=tci: v.tensor_copy(
                        q1f[:, :, 128 * tci:128 * (tci + 1)],
                        ps16[0:128, 6144:7168].rearrange("p (a b) -> p a b",
                                                         a=8)),
                      reads=[PB(6), PB(7)], writes=["q1f"])
                    for mc in range(2):
                        E("pe", lambda t, mc=mc, kb_=kb_: t.transpose(
                            ps16[0:128, 6144 + 128 * mc:6144 + 128 * (mc + 1)],
                            kb_[:, 128 * mc:128 * (mc + 1)], id_f16),
                          reads=[kk_, "id_f16"], inc=(mc == 1),
                          writes=[PB(6)])
                    E("dve", lambda v, c=c: v.tensor_copy(
                        k1f[:, :, 128 * c:128 * (c + 1)],
                        ps16[0:128, 6144:6400].rearrange("p (a b) -> p a b",
                                                         a=2)),
                      reads=[PB(6)], writes=["k1f"])

                for tci in range(4):
                    prod_mm(tci)
                    if tci > 1:
                        prod_tr(tci - 2)
                    prod_rope(tci)
                prod_tr(2)
                prod_tr(3)
                for g in range(4):
                    kb = 64 * (g % 2)
                    mck = g // 2
                    if aidx == 0:
                        krange = list(range(0, 4 * sp + 4))
                    else:
                        krange = list(range(max(4 * sp - 2, 0), 4 * sp + 4))
                    nk_ = len(krange)
                    heads = []
                    for hh in range(4):
                        h = 4 * g + hh
                        qb = 64 * ((h // 4) % 2)
                        qmc = (h % 4) + 4 * (h // 8)
                        assert qb == kb
                        heads.append((hh, qmc))

                    def mask_for(kc):
                        if aidx == 0:
                            rr = kc - 4 * sp
                            if rr < 0:
                                return None, None
                            return mA[rr], f"mA_{rr}"
                        rr = kc - 4 * sp + 2
                        return mB[rr], f"mB_{rr}"

                    def pbuf(ki):
                        return (ptbp, ("ptbA", "ptbB")) if ki % 2 == 0 \
                            else (ptbp2, ("ptbA2", "ptbB2"))

                    def emit_qk(kc, pr, kb=kb, mck=mck, heads=heads):
                        for hh, qmc in heads[2 * pr:2 * pr + 2]:
                            xtra = [PB(2), PB(3)] if (pr, hh % 2) == (0, 0) \
                                else []
                            E("pe", lambda t, kc=kc, hh=hh, qmc=qmc: t.matmul(
                                bank(hh),
                                k1f[kb:kb + 64, mck, 128 * kc:128 * (kc + 1)],
                                q1f[kb:kb + 64, qmc, :],
                                start=True, stop=True),
                              reads=["k1f", "q1f"] + xtra, writes=[PB(hh)])

                    def emit_evac(ki, kc, pr, mask_for=mask_for):
                        mask, mkey = mask_for(kc)
                        buf, keys = pbuf(ki)
                        E("act", lambda s_, pr=pr, buf=buf: s_.activation(
                            buf[pr][:].rearrange("p a b -> p (a b)"),
                            ps[0:128, 1024 * pr:1024 * (pr + 1)],
                            AF.Exp, scale=0.125),
                          reads=[PB(2 * pr), PB(2 * pr + 1)],
                          writes=[keys[pr]])
                        if mask is not None:
                            E("dve", lambda v, pr=pr, mask=mask, buf=buf:
                              v.tensor_tensor(
                                  buf[pr][:], buf[pr][:],
                                  mask[:].unsqueeze(1).broadcast_to(
                                      [128, 2, 512]), ALU.mult),
                              reads=[keys[pr], mkey], writes=[keys[pr]])

                    def emit_pv(ki, kc, g=g, heads=heads, nk_=nk_):
                        buf, keys = pbuf(ki)
                        for hh, qmc in heads:
                            xtra = [keys[1]] if hh == 0 else []
                            E("pe", lambda t, kc=kc, hh=hh, ki=ki, buf=buf:
                              t.matmul(
                                  ps[0:65, 2048 + 512 * hh:2048 + 512 * (hh + 1)],
                                  vb[:, 1040 * g + 65 * kc:
                                     1040 * g + 65 * (kc + 1)],
                                  buf[hh // 2][:, hh % 2, :],
                                  start=(ki == 0), stop=(ki == nk_ - 1)),
                              reads=[keys[hh // 2], "vb"] + xtra,
                              inc=(hh == 3), writes=[PB(4 + hh)])

                    for ki, kc in enumerate(krange):
                        emit_qk(kc, 0)
                        emit_evac(ki, kc, 0)
                        emit_qk(kc, 1)
                        emit_evac(ki, kc, 1)
                        if ki > 0:
                            emit_pv(ki - 1, krange[ki - 1])
                    emit_pv(nk_ - 1, krange[nk_ - 1])

                    gp = g % 2
                    E("act", lambda s_: s_.activation(
                        lnrow[:], ps[64:65, 2048:4096], AF.Ln),
                      reads=[PB(4), PB(5), PB(6), PB(7)],
                      writes=["lnrow"])
                    E("act", lambda s_: s_.activation(
                        rec_sb[0:1, 0:2048], lnrow[:], AF.Exp,
                        scale=-1.0),
                      reads=["lnrow", PB(4), PB(5), PB(6), PB(7)],
                      writes=["recb"])
                    E("ad", lambda e, gp=gp: e.dma_start(
                        out=recd[gp:gp + 1, :], in_=rec_sb[0:1, 0:2048]),
                      reads=["recb"], writes=[("recd", gp)])
                    E("gd", lambda e, gp=gp: e.dma_start(
                        out=recb[0:64, 0:2048],
                        in_=recd[gp][None, :].broadcast_to([64, 2048])),
                      reads=[("recd", gp), "recb"], writes=["recb"])
                    for par in range(2):
                        E("dve", lambda v, par=par, g=g: v.tensor_tensor(
                            attn_m[64 * par:64 * (par + 1),
                                   2 * g:2 * g + 2, :],
                            ps[0:64, 2048:4096].rearrange(
                                "p (a c b) -> p a c b", a=2, c=2)[:, :, par, :],
                            recb[0:64, 0:2048].rearrange(
                                "p (a c b) -> p a c b", a=2, c=2)[:, :, par, :],
                            ALU.mult),
                          reads=[PB(4 + par), PB(6 + par), "recb"],
                          writes=["attn_m"])
                for tci in range(4 if SUBPH >= 5 else 0):
                    c = 4 * sp + tci
                    i = tci % 2
                    ldb = ld_a if i == 0 else ld_b
                    ldk = "ld_a" if i == 0 else "ld_b"
                    E("sd", lambda e, c=c, ldb=ldb, s=s: e.dma_start(
                        out=ldb[:],
                        in_=src_d[2048 * s + 128 * c:2048 * s + 128 * (c + 1), :]),
                      reads=[ldk, (srctag, s, c)], writes=[ldk])
                    for half in range(2):
                        for mc in range(8):
                            E("pe", lambda t, mc=mc, half=half, tci=tci: t.matmul(
                                bank(2 + half),
                                attn_m[:, mc, 128 * tci:128 * (tci + 1)],
                                wo_w[mc][:, 512 * half:512 * (half + 1)],
                                start=(mc == 0), stop=(mc == 7)),
                              reads=["attn_m", f"{wo_tag}_{mc}"], inc=(mc == 7),
                              writes=[PB(2 + half)])
                    sb_o = s1b[i]
                    for half in range(2):
                        E("dve", lambda v, half=half, sb_o=sb_o, ldb=ldb:
                          v.scalar_tensor_tensor(
                              sb_o[:, 512 * half:512 * (half + 1)],
                              bank(2 + half), 1.0,
                              ldb[:, 512 * half:512 * (half + 1)],
                              ALU.bypass, ALU.add),
                          reads=[PB(2 + half), ldk], writes=[f"s1b_{i}"])
                    if aidx == 0:
                        E("dve", lambda v, sb_o=sb_o: v.scalar_tensor_tensor(
                            sqj[:], sb_o[:], 1.0, sb_o[:],
                            ALU.bypass, ALU.mult, accum_out=lncol[:, 6:7]),
                          reads=[f"s1b_{i}"], writes=["sqj", "lncol"])
                        rsqrt_cols(lncol[:, 6:7],
                                   r2b[:, 16 * s + c:16 * s + c + 1], D,
                                   epsc[:, 1:2], "r2b")
                        dst, dk_ = s1d, ("s1d", s, c)
                    else:
                        dst, dk_ = sdd, ("sdd", s, c)
                    E("gd", lambda e, sb_o=sb_o, s=s, c=c, dst=dst: e.dma_start(
                        out=dst[2048 * s + 128 * c:2048 * s + 128 * (c + 1), :],
                        in_=sb_o[:]),
                      reads=[f"s1b_{i}"], writes=[dk_])
                    if hook_q is not None:
                        rem = (NSP - sp) * 4 - tci
                        drain(hook_q, (len(hook_q) + rem - 1) // rem)

        def ffn_pass1():
            act_set_load(ACT_GELU_SET)
            for k in range(8):
                E("sd",
                  lambda e, k=k: e.dma_start(
                      out=wfcT[k][:], in_=wqfcd[:, 128 * k:128 * (k + 1)],
                      transpose=True),
                  reads=[("wqfc", i) for i in range(32)] + [f"wfcT_{k}"],
                  writes=[f"wfcT_{k}"])
            def xnf_load(gsp):
                row0 = 512 * gsp
                xb_, xk_ = (xnf, "xnf") if gsp % 2 == 0 else (xnf2, "xnf2")
                span_keys = [("xnd", gsp // 4, 4 * (gsp % 4) + j)
                             for j in range(4)]
                for dk in range(8):
                    E("sd",
                      lambda e, dk=dk, row0=row0, xb_=xb_: e.dma_start(
                          out=xb_[:, dk, :],
                          in_=xnd[row0:row0 + 512, 128 * dk:128 * (dk + 1)],
                          transpose=True),
                      reads=span_keys + [xk_], writes=[xk_])

            xnf_load(0)
            for gsp in range(8):
                if gsp + 1 < 8:
                    xnf_load(gsp + 1)
                row0 = 512 * gsp
                xb_, xk_ = (xnf, "xnf") if gsp % 2 == 0 else (xnf2, "xnf2")
                for k in range(8):
                    E("pe", lambda t, k=k, xb_=xb_: t.matmul(
                        ps[0:16, 3584:4096], afcT[:, 16 * k:16 * (k + 1)],
                        xb_[:, k, :], start=(k == 0), stop=(k == 7)),
                      reads=[xk_, "afcT"], inc=(k == 7),
                      writes=[PB(7)])
                E("dve", lambda v: v.memset(up1[:], 1.0), writes=["up1"])
                E("act", lambda s_: s_.activation(
                    up1[0:16, :], ps[0:16, 3584:4096], AF.Copy, scale=2.0),
                  reads=[PB(7), "up1"], writes=["up1"])
                for fk in range(32):
                    bk = fk % 4
                    for k in range(8):
                        E("pe", lambda t, k=k, fk=fk, bk=bk, xb_=xb_: t.matmul(
                            bank(bk), wfcT[k][:, 128 * fk:128 * (fk + 1)],
                            xb_[:, k, :], start=(k == 0), stop=False),
                          reads=[xk_, f"wfcT_{k}"], inc=False,
                          writes=[("pb", bk)])
                    E("pe", lambda t, fk=fk, bk=bk: t.matmul(
                        bank(bk), bfcT[0:17, 128 * fk:128 * (fk + 1)], up1[:],
                        start=False, stop=True),
                      reads=["bfcT", "up1"], writes=[PB(bk)])
                    if fk % 2 == 0:
                        continue
                    wi = (fk // 2) % 2
                    wkey = f"h2wp_{wi}"
                    bk0 = bk - 1
                    E("act", lambda s_, bk0=bk0, wi=wi: s_.activation(
                        h2wp[wi][:], ps[0:128, 512 * bk0:512 * bk0 + 1024],
                        AF.Gelu),
                      reads=[PB(bk0), PB(bk0 + 1)], writes=[wkey])
                    E("gd", lambda e, fk=fk, wi=wi, row0=row0, gsp=gsp:
                      e.dma_start(
                          out=h2d[128 * (fk - 1):128 * (fk + 1),
                                  row0:row0 + 512].rearrange(
                              "(a p) t -> p a t", a=2),
                          in_=h2wp[wi][:].rearrange("p (a t) -> p a t", a=2)),
                      reads=[wkey],
                      writes=[("h2d", gsp, fk - 1), ("h2d", gsp, fk)])
                    for j in range(2):
                        fj = fk - 1 + j
                        E("pe", lambda t, fj=fj, wi=wi, j=j: t.matmul(
                            ps[0:16, 3072:3584],
                            apjT[:, 16 * fj:16 * (fj + 1)],
                            h2wp[wi][:, 512 * j:512 * (j + 1)],
                            start=(fj == 0), stop=(fj == 31)),
                          reads=[wkey, "apjT"], inc=(fj == 31),
                          writes=[PB(6)])
                E("dve", lambda v, gsp=gsp: v.memset(u2all[:, gsp, :], 1.0),
                  writes=["u2all"])
                E("act", lambda s_, gsp=gsp: s_.activation(
                    u2all[0:16, gsp, :], ps[0:16, 3072:3584], AF.Copy,
                    scale=2.0),
                  reads=[PB(6), "u2all"], writes=["u2all"])

        def ffn_pass2():
            act_set_load(ACT_SILU_SET)
            for fk in range(32):
                E("sd",
                  lambda e, fk=fk: e.dma_start(
                      out=wpjT[fk][:], in_=wqpjd[:, 128 * fk:128 * (fk + 1)],
                      transpose=True),
                  reads=[("wqpj", i) for i in range(32)] + [f"wpjT_{fk}"],
                  writes=[f"wpjT_{fk}"])
            for cg in range(32):
                s, c = cg // 16, cg % 16
                i = cg % 2
                hr, hk = h2r[i], f"h2r_{i}"
                E("sd",
                  lambda e, cg=cg, hr=hr: e.dma_start(
                      out=hr[:],
                      in_=h2dv[:, :, 128 * cg:128 * (cg + 1)]),
                  reads=[("h2d", cg // 4, fk) for fk in range(32)] + [hk],
                  writes=[hk])
                pb0 = 2 * i
                for half in range(2):
                    for fk in range(32):
                        E("pe", lambda t, fk=fk, half=half, hr=hr, pb0=pb0:
                          t.matmul(
                              bank(pb0 + half), hr[:, fk, :],
                              wpjT[fk][:, 512 * half:512 * (half + 1)],
                              start=(fk == 0), stop=False),
                          reads=[hk, f"wpjT_{fk}"], inc=False,
                          writes=[("pb", pb0 + half)])
                    E("pe", lambda t, half=half, cg=cg, pb0=pb0: t.matmul(
                        bank(pb0 + half),
                        u2all[:, cg // 4, 128 * (cg % 4):128 * (cg % 4) + 128],
                        bpjT[0:17, 512 * half:512 * (half + 1)],
                        start=False, stop=True),
                      reads=["u2all", "bpjT"], writes=[PB(pb0 + half)])
                    E("act", lambda s_, half=half, pb0=pb0: s_.activation(
                        gbuf[:, 512 * half:512 * (half + 1)],
                        bank(pb0 + half), AF.Silu),
                      reads=[PB(pb0 + half)], writes=["gbuf"])
                ldb = ld_a if i == 0 else ld_b
                ldk = "ld_a" if i == 0 else "ld_b"
                E("ad", lambda e, s=s, c=c, ldb=ldb: e.dma_start(
                    out=ldb[:],
                    in_=sdd[2048 * s + 128 * c:2048 * s + 128 * (c + 1), :]),
                  reads=[("sdd", s, c), ldk], writes=[ldk])
                E("sd", lambda e, s=s, c=c, i=i: e.dma_start(
                    out=xb[i][:], in_=x[s, 128 * c:128 * (c + 1), :]),
                  reads=[f"xb_{i}"], writes=[f"xb_{i}"])
                E("dve", lambda v, ldb=ldb: v.tensor_tensor(
                    gbuf[:], gbuf[:], ldb[:], ALU.mult),
                  reads=["gbuf", ldk], writes=["gbuf"])
                E("dve", lambda v, i=i: v.tensor_tensor(
                    gbuf[:], gbuf[:], xb[i][:], ALU.add),
                  reads=["gbuf", f"xb_{i}"], writes=["gbuf"])
                E("gd", lambda e, s=s, c=c: e.dma_start(
                    out=out[s, 128 * c:128 * (c + 1), :], in_=gbuf[:]),
                  reads=["gbuf"], writes=[("out", s, c)])

        for k in range(8):
            E("gd", lambda e, k=k: e.dma_start(
                out=wo1[k][:], in_=g_wo[128 * k:128 * (k + 1), :]),
              writes=[f"wo1_{k}"])
        for k in range(2):
            E("gd", lambda e, k=k: e.dma_start(
                out=vup[k][:], in_=g_vup[128 * k:128 * (k + 1), :]),
              writes=[f"vup_{k}"])
        if PHASES >= 1:
            for c in range(NCH):
                ln_chunk(0, c)
                drain(w1_q, 2)
            drain(w1_q, 99)
            for k in range(8):
                E("gd", lambda e, k=k: e.dma_start(
                    out=wo2[k][:], in_=l_wo[128 * k:128 * (k + 1), :]),
                  writes=[f"wo2_{k}"])
        if PHASES >= 2:
            attention(0, 0, hook_q=w2_q + q1_q)
        if PHASES >= 3:
            for c in range(NCH):
                ln_chunk(1, c)
            attention(1, 0, hook_q=lora_q)
        if PHASES >= 4:
            q2_first = q2_q[:len(q2_q) // 2]
            q2_second = q2_q[len(q2_q) // 2:]
            attention(0, 1, hook_q=q2_first)
            attention(1, 1, hook_q=q2_second)
            drain(q2_first, 999)
            drain(q2_second, 999)
        if PHASES >= 5:
            ffn_pass1()
        if DEBUG_TAPS and PHASES >= 5:
            E("sd", lambda e: e.dma_start(out=dbg1, in_=wfcT[0][:]),
              reads=["wfcT_0"], writes=[("dbg1",)])
            E("sd", lambda e: e.dma_start(out=dbg2, in_=bfcT[0:17, :]),
              reads=["bfcT"], writes=[("dbg2",)])
        if PHASES >= 6:
            ffn_pass2()
        p.wait_all("sd")

    return nc


_INPUT_MAP = [
    ("norm1_w", "L0_attn_norm1_w"), ("norm2_w", "L0_attn_norm2_w"),
    ("g_wq", "L0_attn_g_wq"), ("g_kv_down", "L0_attn_g_kv_down"),
    ("g_k_up", "L0_attn_g_k_up"), ("g_v_up", "L0_attn_g_v_up"),
    ("g_wo", "L0_attn_g_wo"), ("l_wq", "L0_attn_l_wq"),
    ("l_wk", "L0_attn_l_wk"), ("l_wv", "L0_attn_l_wv"),
    ("l_wo", "L0_attn_l_wo"), ("W_fc", "L0_ffn_W_fc"),
    ("b_fc", "L0_ffn_b_fc"), ("A_fc", "L0_ffn_A_fc"),
    ("B_fc", "L0_ffn_B_fc"), ("W_proj", "L0_ffn_W_proj"),
    ("b_proj", "L0_ffn_b_proj"), ("A_proj", "L0_ffn_A_proj"),
    ("B_proj", "L0_ffn_B_proj"),
]


def kernel(**inputs):
    nc = build_program()
    xfull = np.ascontiguousarray(np.asarray(inputs["L0_attn_hidden"], np.float32))
    w = {k: np.ascontiguousarray(np.asarray(inputs[v], np.float32))
         for k, v in _INPUT_MAP}
    in_maps = []
    for core in range(8):
        m = {"x": xfull[2 * core:2 * core + 2]}
        m.update(w)
        in_maps.append(m)
    res = run_bass_kernel_spmd(nc, in_maps, core_ids=list(range(8)))
    return np.concatenate([np.asarray(r["out"], np.float32)
                           for r in res.results], axis=0)


if __name__ == "__main__":
    nc = build_program()
    print("build ok")


# version 35
# speedup vs baseline: 1.0419x; 1.0412x over previous
import math
import numpy as np

import concourse.bass as bass
import concourse.mybir as mybir
from concourse.bass_utils import run_bass_kernel_spmd

F32 = mybir.dt.float32
F16 = mybir.dt.float16
BF16 = mybir.dt.bfloat16
I32 = mybir.dt.int32
AF = mybir.ActivationFunctionType
ALU = mybir.AluOpType
AXX = mybir.AxisListType.X

D = 1024
LAT = 256
T = 2048
NSEQ = 2
F = 4096
NCH = 16
NSP = 4
PI = math.pi
TWO_PI = 2.0 * math.pi
EPS_LN = 1e-5
EPS_RMS = 1e-6
BASE_G = 1000000.0
BASE_L = 10000.0


def _act_set_ids():
    fallback = {"natural_log_exp_and_others": 6, "trig_and_small": 9,
                "gelu_and_others": 10, "silu_and_others": 18}
    try:
        from concourse.hw_specs import get_activation_tables
        names = list(get_activation_tables("TRN2"))
        return {k: names.index(k) for k in fallback}
    except Exception:
        return fallback


_ACT_SETS = _act_set_ids()
ACT_COMBO_SET = _ACT_SETS["natural_log_exp_and_others"]
ACT_TRIG_SET = _ACT_SETS["trig_and_small"]
ACT_GELU_SET = _ACT_SETS["gelu_and_others"]
ACT_SILU_SET = _ACT_SETS["silu_and_others"]

DEBUG_TAPS = False
PHASES = 99
SUBPH = 9


class Prog:

    def __init__(self, nc, blk):
        self.nc, self.blk = nc, blk
        self.eng = {}
        self.reg = {}
        self.overlap = {}
        self.opvc = {}

    WAVE = 12

    def engine(self, name, sec, step):
        sem = self.nc.alloc_semaphore(name=f"sem_{name}")
        self.eng[name] = dict(sem=sem, cnt=0, step=step, sec=sec, pend=[],
                              barriers=[0], bar_cnt=0, vc={})

    def _learn(self, know, pe_, pc_):
        if pc_ > know.get(pe_, -1):
            know[pe_] = pc_
        src = self.opvc.get((pe_, pc_))
        if src:
            for k2, v2 in src.items():
                if v2 > know.get(k2, -1):
                    know[k2] = v2

    def close_wave(self, en):
        e = self.eng[en]
        if e["cnt"] == e["barriers"][-1]:
            return
        val = e["cnt"]

        def body(obj):
            obj.wait_ge(e["sem"], val)

        e["sec"](body)
        e["barriers"].append(val)
        e["bar_cnt"] = 0
        self._learn(e["vc"], en, val)

    def _expand(self, keys):
        out = []
        for k in keys:
            if isinstance(k, str) and k in self.overlap:
                out.extend(self.overlap[k])
            else:
                out.append(k)
        return out

    def _need(self, waits, en, pe, pc):
        if pe == en == "pe":
            return
        if pc > waits.get(pe, -1):
            waits[pe] = pc

    def emit(self, en, f, reads=(), writes=(), inc=True):
        e = self.eng[en]
        reads = self._expand(reads)
        writes = self._expand(writes)
        waits = {}
        for r in reads:
            rec = self.reg.get(r)
            if rec:
                for pe_, pc_ in rec[0].items():
                    self._need(waits, en, pe_, pc_)
        for wk in writes:
            rec = self.reg.get(wk)
            if rec:
                for pe_, pc_ in rec[0].items():
                    self._need(waits, en, pe_, pc_)
                for pe_, pc_ in rec[1].items():
                    self._need(waits, en, pe_, pc_)
        know = e["vc"]
        real = []
        for pe_, pc_ in sorted(waits.items(), key=str):
            if self.eng[pe_]["step"] == 16 and pe_ != en:
                bars = self.eng[pe_]["barriers"]
                if pc_ > bars[-1]:
                    self.close_wave(pe_)
                    bars = self.eng[pe_]["barriers"]
                import bisect as _bi
                pc_ = bars[_bi.bisect_left(bars, pc_)]
            if pc_ > know.get(pe_, -1):
                real.append((pe_, pc_))
                self._learn(know, pe_, pc_)

        def body(obj):
            for pe_, pc_ in real:
                obj.wait_ge(self.eng[pe_]["sem"], pc_)
            ins = f(obj)
            if inc:
                if isinstance(ins, (list, tuple)):
                    ins = ins[-1]
                ins.then_inc(e["sem"], e["step"])

        e["sec"](body)
        if inc and e["step"] == 16:
            e["bar_cnt"] += 1
        if inc:
            e["cnt"] += e["step"]
            c = e["cnt"]
            self.opvc[(en, c)] = dict(know, **{en: c})
            if e["step"] == 16 and e["bar_cnt"] >= self.WAVE:
                self.close_wave(en)
            for wk in e["pend"]:
                self.reg[wk] = [{en: c}, {}]
            e["pend"] = []
            for wk in writes:
                self.reg[wk] = [{en: c}, {}]
            for r in reads:
                rec = self.reg.setdefault(r, [{}, {}])
                rec[1][en] = c
        else:
            nxt = e["cnt"] + e["step"]
            for wk in writes:
                self.reg[wk] = [{en: nxt}, {}]
                e["pend"].append(wk)
            for r in reads:
                rec = self.reg.setdefault(r, [{}, {}])
                rec[1][en] = nxt

    def wait_all(self, en):
        e = self.eng[en]
        for pe_, pe in self.eng.items():
            if pe["step"] == 16:
                self.close_wave(pe_)
        know = e["vc"]
        real = []
        for pe_, pe in self.eng.items():
            if pe_ == en:
                continue
            if pe["cnt"] > know.get(pe_, -1) and pe["cnt"] > 0:
                real.append((pe_, pe["cnt"]))
                self._learn(know, pe_, pe["cnt"])

        def body(obj):
            for pe_, pc_ in real:
                obj.wait_ge(self.eng[pe_]["sem"], pc_)

        e["sec"](body)


class Arena:
    def __init__(self, nc, name, ncols):
        self.t = nc.alloc_sbuf_tensor(name, [128, ncols], BF16)
        self.ncols = ncols
        self.regions = {}

    def view(self, name, off, ncols, dtype, shape):
        assert name not in self.regions
        assert off + ncols <= self.ncols, (name, off, ncols)
        self.regions[name] = (off, ncols)
        ap = self.t[:, off:off + ncols]
        if dtype != BF16:
            ap = ap.bitcast(dtype)
        free = shape[1:]
        n = 1
        for xx in free:
            n *= xx
        assert n == ap.shape[1], (name, n, ap.shape)
        if len(free) > 1:
            subs = " ".join(f"f{i}" for i in range(len(free)))
            kw = {f"f{i}": free[i] for i in range(len(free))}
            ap = ap.rearrange(f"p ({subs}) -> p {subs}", **kw)
        if shape[0] != 128:
            ap = ap[0:shape[0]]
        return ap

    def overlaps(self):
        names = list(self.regions)
        out = {}
        for a in names:
            oa, na = self.regions[a]
            out[a] = tuple(
                b for b in names
                if not (self.regions[b][0] + self.regions[b][1] <= oa
                        or oa + na <= self.regions[b][0]))
        return out


def build_program():
    nc = bass.Bass("TRN2", target_bir_lowering=False, debug=False)

    x = nc.dram_tensor("x", [NSEQ, T, D], F32, kind="ExternalInput").ap()
    n1w = nc.dram_tensor("norm1_w", [D], F32, kind="ExternalInput").ap()
    n2w = nc.dram_tensor("norm2_w", [D], F32, kind="ExternalInput").ap()
    g_wq = nc.dram_tensor("g_wq", [D, D], F32, kind="ExternalInput").ap()
    g_kvd = nc.dram_tensor("g_kv_down", [D, LAT], F32, kind="ExternalInput").ap()
    g_kup = nc.dram_tensor("g_k_up", [LAT, LAT], F32, kind="ExternalInput").ap()
    g_vup = nc.dram_tensor("g_v_up", [LAT, LAT], F32, kind="ExternalInput").ap()
    g_wo = nc.dram_tensor("g_wo", [D, D], F32, kind="ExternalInput").ap()
    l_wq = nc.dram_tensor("l_wq", [D, D], F32, kind="ExternalInput").ap()
    l_wk = nc.dram_tensor("l_wk", [D, LAT], F32, kind="ExternalInput").ap()
    l_wv = nc.dram_tensor("l_wv", [D, LAT], F32, kind="ExternalInput").ap()
    l_wo = nc.dram_tensor("l_wo", [D, D], F32, kind="ExternalInput").ap()
    W_fc = nc.dram_tensor("W_fc", [F, D], F32, kind="ExternalInput").ap()
    b_fc = nc.dram_tensor("b_fc", [F], F32, kind="ExternalInput").ap()
    A_fc = nc.dram_tensor("A_fc", [16, D], F32, kind="ExternalInput").ap()
    B_fc = nc.dram_tensor("B_fc", [F, 16], F32, kind="ExternalInput").ap()
    W_pj = nc.dram_tensor("W_proj", [D, F], F32, kind="ExternalInput").ap()
    b_pj = nc.dram_tensor("b_proj", [D], F32, kind="ExternalInput").ap()
    A_pj = nc.dram_tensor("A_proj", [16, F], F32, kind="ExternalInput").ap()
    B_pj = nc.dram_tensor("B_proj", [D, 16], F32, kind="ExternalInput").ap()
    out = nc.dram_tensor("out", [NSEQ, T, D], F32, kind="ExternalOutput").ap()

    sk = "ExternalOutput" if DEBUG_TAPS else "Internal"
    xnd = nc.dram_tensor("xnd", [NSEQ * T, D], BF16, kind=sk).ap()
    s1d = nc.dram_tensor("s1d", [NSEQ * T, D], BF16, kind=sk).ap()
    sdd = nc.dram_tensor("sdd", [NSEQ * T, D], BF16, kind=sk).ap()
    h2d = nc.dram_tensor("h2d", [F, NSEQ * T], BF16, kind=sk).ap()
    h2dv = h2d.rearrange("(fk p) tok -> p fk tok", fk=32, p=128)
    wqfcd = nc.dram_tensor("wqfcd", [F, D], BF16, kind=sk).ap()
    wqpjd = nc.dram_tensor("wqpjd", [D, F], BF16, kind=sk).ap()
    recd = nc.dram_tensor("recd", [2, 2048], BF16, kind="Internal").ap()
    dbg1 = nc.dram_tensor("dbg1", [128, F], BF16, kind=sk).ap()
    dbg2 = nc.dram_tensor("dbg2", [17, F], BF16, kind=sk).ap()

    AR = Arena(nc, "arena", 98304)
    V = AR.view
    wq1 = [V(f"wq1_{k}", 1024 * k, 1024, BF16, [128, 1024]) for k in range(8)]
    kvd = [V(f"kvd_{k}", 8192 + 256 * k, 256, BF16, [128, 256]) for k in range(8)]
    kup = [V(f"kup_{k}", 10240 + 256 * k, 256, BF16, [128, 256]) for k in range(2)]
    vup = [V(f"vup_{k}", 10752 + 256 * k, 256, BF16, [128, 256]) for k in range(2)]
    wo1 = [V(f"wo1_{k}", 11264 + 1024 * k, 1024, BF16, [128, 1024]) for k in range(8)]
    wq2 = [V(f"wq2_{k}", 19456 + 1024 * k, 1024, BF16, [128, 1024]) for k in range(8)]
    wk2 = [V(f"wk2_{k}", 27648 + 256 * k, 256, BF16, [128, 256]) for k in range(8)]
    wv2 = [V(f"wv2_{k}", 29696 + 256 * k, 256, BF16, [128, 256]) for k in range(8)]
    wo2 = [V(f"wo2_{k}", 31744 + 1024 * k, 1024, BF16, [128, 1024]) for k in range(8)]
    qwi = [V(f"qwi_{i}", 2048 * i, 2048, F32, [128, 1024]) for i in range(2)]
    qw_y = V("qw_y", 4096, 2048, F32, [128, 1024])
    qw_i = V("qw_i", 6144, 2048, I32, [128, 1024])
    qw_g = V("qw_g", 8192, 2048, F32, [128, 1024])
    qw_o = V("qw_o", 10240, 1024, BF16, [128, 1024])
    wfcT = [V(f"wfcT_{k}", 4096 * k, 4096, BF16, [128, 4096]) for k in range(8)]
    wpjT = [V(f"wpjT_{k}", 1024 * k, 1024, BF16, [128, 1024]) for k in range(32)]

    C0 = 39936
    q1f = V("q1f", C0, 4096, BF16, [128, 8, 512])
    k1f = V("k1f", C0 + 4096, 4096, BF16, [128, 2, 2048])
    vb = V("vb", C0 + 8192, 4160, BF16, [128, 4160])
    attn_m = V("attn_m", C0 + 12352, 4096, BF16, [128, 8, 512])
    latf = V("latf", C0 + 16448, 1024, BF16, [128, 2, 512])
    xnf = V("xnf", C0 + 17472, 4096, BF16, [128, 8, 512])
    ptb = [V(f"ptb_{i}", C0 + 21568 + 512 * i, 512, BF16, [128, 512])
           for i in range(2)]
    qtm = V("qtm", C0 + 22592, 1024, F16, [128, 1024])
    qn_ = V("qn", C0 + 23616, 1024, F16, [128, 1024])
    qrn = V("qrn", 0, 0, F16, [128, 0]) if False else None
    rtmp = V("rtmp", C0 + 24640, 512, F16, [128, 512])
    ktm = V("ktm", C0 + 25152, 256, F16, [128, 256])
    kn_ = V("kn", C0 + 25408, 256, F16, [128, 256])
    krn = None
    recb = V("recb", C0 + 25664, 2048, BF16, [64, 2048])
    rec_sb = V("rec_sb", C0 + 25664, 2048, BF16, [1, 2048])
    sqj = V("sqj", C0 + 27712, 1024, BF16, [128, 1024])
    ld_a = V("ld_a", C0 + 28736, 1024, BF16, [128, 1024])
    ld_b = V("ld_b", C0 + 29760, 1024, BF16, [128, 1024])
    s1b = [V(f"s1b_{i}", C0 + 30784 + 1024 * i, 1024, BF16, [128, 1024])
           for i in range(2)]
    u2all = V("u2all", C0 + 22592, 4096, BF16, [17, 8, 512])
    xnf2 = V("xnf2", C0 + 8192, 4096, BF16, [128, 8, 512])
    h2r = [V(f"h2r_{i}", C0 + 4096 * i, 4096, BF16, [128, 32, 128])
           for i in range(2)]
    h2w = [V(f"h2w_{i}", C0 + 21568 + 512 * i, 512, BF16, [128, 512])
           for i in range(2)]
    h2wp = [V("h2wp_0", C0 + 21568, 1024, BF16, [128, 1024]),
            V("h2wp_1", C0 + 27712, 1024, BF16, [128, 1024])]
    CEND = C0 + 32832

    D0 = CEND
    mA = [V(f"mA_{r}", D0 + 512 * r, 512, BF16, [128, 512]) for r in range(4)]
    mB = [V(f"mB_{r}", D0 + 2048 + 512 * r, 512, BF16, [128, 512])
          for r in range(6)]
    tab = {}
    for i, nm in enumerate(("cos_g", "sin_g", "cos_l", "sin_l")):
        tab[nm] = V(nm, D0 + 5120 + 512 * i, 512, F16, [128, 16, 32])
    id_bf = V("id_bf", D0 + 7168, 128, BF16, [128, 128])
    id_f16 = V("id_f16", D0 + 7296, 128, F16, [128, 128])
    id_f32 = V("id_f32", D0 + 7424, 256, F32, [128, 128])
    ones_bf = V("ones_bf", D0 + 7680, 64, BF16, [1, 64])
    ones_f32 = V("ones_f32", D0 + 7744, 256, F32, [1, 128])
    n12 = V("n12", D0 + 8000, 32, F32, [128, 16])
    r1b = V("r1b", D0 + 8032, 64, F32, [128, 32])
    r2b = V("r2b", D0 + 8096, 64, F32, [128, 32])
    lncol = V("lncol", D0 + 8160, 64, F32, [128, 32])
    scl = V("scl", D0 + 8224, 16, F32, [128, 8])
    bfcT = V("bfcT", D0 + 8240, 4096, BF16, [17, 4096])
    q1wi = [V(f"q1wi_{i}", D0 + 8240 + 2048 * i, 2048, F32, [128, 1024])
            for i in range(2)]
    bpjT = V("bpjT", D0 + 12336, 1024, BF16, [17, 1024])
    afcT = V("afcT", D0 + 13360, 128, BF16, [128, 128])
    apjT = V("apjT", D0 + 13488, 512, BF16, [128, 512])
    up1 = V("up1", D0 + 14000, 512, BF16, [17, 512])
    cent = V("cent", D0 + 14512, 2048, F32, [128, 1024])
    qtm2 = V("qtm2", D0 + 14512, 1024, F16, [128, 1024])
    ktm2 = V("ktm2", D0 + 15536, 256, F16, [128, 256])
    gbuf = V("gbuf", D0 + 14512, 2048, F32, [128, 1024])
    apstg = V("apstg", D0 + 14512, 1024, BF16, [16, 1024])
    xb = [V(f"xb_{i}", D0 + 16560 + 2048 * i, 2048, F32, [128, 1024])
          for i in range(2)]
    lnrow = V("lnrow", D0 + 16560, 4096, F32, [1, 2048])
    wtmp = V("wtmp", D0 + 20656, 2048, F32, [128, 1024])
    tb = [V(f"tb_{i}", D0 + 16560 + 1024 * i, 1024, F32, [128, 512])
          for i in range(4)]
    tbi = V("tbi", D0 + 20656, 1024, I32, [128, 512])
    invb = V("invb", D0 + 22704, 64, F32, [128, 32])
    epsc = V("epsc", D0 + 22768, 16, F32, [128, 8])
    D0X = D0 + 22784
    ptb += [V(f"ptb_{i + 2}", D0X + 512 * i, 512, BF16, [128, 512])
            for i in range(2)]
    ptbp = [V("ptbA", C0 + 21568, 1024, BF16, [128, 2, 512]),
            V("ptbB", D0X, 1024, BF16, [128, 2, 512])]
    ptbp2 = [V("ptbA2", C0 + 22592, 1024, BF16, [128, 2, 512]),
             V("ptbB2", C0 + 23616, 1024, BF16, [128, 2, 512])]
    lnc_q = V("lnc_q", D0X + 1024, 40, F32, [128, 20])
    lnc_k = V("lnc_k", D0X + 1064, 16, F32, [128, 8])
    qrn = V("qrn", D0X + 1080, 1024, F16, [128, 1024])
    krn = V("krn", D0X + 2104, 256, F16, [128, 256])
    TOT = D0X + 2360
    assert TOT <= 98304, TOT
    qrn2 = V("qrn2", C0 + 12352, 1024, F16, [128, 1024])
    krn2 = V("krn2", C0 + 13376, 256, F16, [128, 256])
    rtmpk = V("rtmpk", TOT, 128, F16, [128, 128])
    assert TOT + 128 <= 98304

    ps = nc.alloc_psum_tensor("ps", [128, 4096], F32)
    ps16 = ps[:].bitcast(F16)
    psbf = ps[:].bitcast(BF16)

    def bank(i, rows=128):
        return ps[0:rows, 512 * i:512 * (i + 1)]

    with nc.Block() as blk:
        p = Prog(nc, blk)
        p.overlap = AR.overlaps()
        for name, sec, step in [("pe", blk.tensor, 1), ("act", blk.scalar, 1),
                                ("dve", blk.vector, 1), ("pool", blk.gpsimd, 1),
                                ("sd", blk.sync, 16), ("gd", blk.gpsimd, 16),
                                ("ad", blk.scalar, 16)]:
            p.engine(name, sec, step)
        E = p.emit

        def PB(i):
            return ("pb", i)

        def act_set_load(set_id):
            def body(s):
                s.add_instruction(mybir.InstLoadActFuncSet(
                    name=nc.get_next_instruction_name(),
                    act_func_set_id=set_id, ins=[], outs=[]))
            p.eng["act"]["sec"](body)

        for idv, dtv in ((id_bf, "id_bf"), (id_f16, "id_f16"), (id_f32, "id_f32")):
            E("dve", lambda v, idv=idv: v.memset(idv, 1.0), writes=[dtv])
            E("pool", lambda g, idv=idv: g.affine_select(
                out=idv, in_=idv, pattern=[[1, 128]], compare_op=ALU.is_equal,
                fill=0.0, base=0, channel_multiplier=-1),
                reads=[dtv], writes=[dtv])
        E("dve", lambda v: v.memset(ones_bf, 1.0), writes=["ones_bf"])
        E("dve", lambda v: v.memset(epsc[:, 0:1], EPS_LN), writes=["epsc"])
        E("dve", lambda v: v.memset(epsc[:, 1:2], EPS_RMS), reads=["epsc"], writes=["epsc"])
        E("dve", lambda v: v.memset(epsc[:, 2:3], 0.5), reads=["epsc"], writes=["epsc"])
        E("dve", lambda v: v.memset(ones_f32, 1.0), writes=["ones_f32"])
        E("dve", lambda v: v.memset(vb[:, 64::65], 1.0), writes=["vb"])
        for r in range(4):
            E("dve", lambda v, r=r: v.memset(mA[r], 1.0), writes=[f"mA_{r}"])
            E("pool", lambda g, r=r: g.affine_select(
                out=mA[r], in_=mA[r], pattern=[[1, 512]], compare_op=ALU.is_ge,
                fill=0.0, base=-128 * r, channel_multiplier=-1),
                reads=[f"mA_{r}"], writes=[f"mA_{r}"])
        for ri in range(6):
            r = ri - 2
            E("dve", lambda v, ri=ri: v.memset(mB[ri], 1.0), writes=[f"mB_{ri}"])
            E("pool", lambda g, ri=ri, r=r: g.affine_select(
                out=mB[ri], in_=mB[ri], pattern=[[1, 512]], compare_op=ALU.is_ge,
                fill=0.0, base=-128 * r, channel_multiplier=-1),
                reads=[f"mB_{ri}"], writes=[f"mB_{ri}"])
            E("pool", lambda g, ri=ri, r=r: g.affine_select(
                out=mB[ri], in_=mB[ri], pattern=[[-1, 512]], compare_op=ALU.is_ge,
                fill=0.0, base=128 * r + 255, channel_multiplier=1),
                reads=[f"mB_{ri}"], writes=[f"mB_{ri}"])
        E("sd", lambda e: e.dma_start(
            out=n12[:, 0:8], in_=n1w.rearrange("(c p) -> p c", p=128),
            allow_slow_non_contiguous=True), writes=["n12"])
        E("sd", lambda e: e.dma_start(
            out=n12[:, 8:16], in_=n2w.rearrange("(c p) -> p c", p=128),
            allow_slow_non_contiguous=True), reads=["n12"], writes=["n12"])

        for base, cname, sname in ((BASE_G, "cos_g", "sin_g"),
                                   (BASE_L, "cos_l", "sin_l")):
            E("pool", lambda g: g.iota(tb[3][0:1, 0:32], pattern=[[1, 32]], base=0,
                                       channel_multiplier=0,
                                       allow_small_or_imprecise_dtypes=True),
              writes=["tb_3"])
            act_set_load(ACT_COMBO_SET)
            E("act", lambda s, base=base: s.activation(
                tb[3][0:1, 32:64], tb[3][0:1, 0:32], AF.Exp,
                scale=-math.log(base) / 32.0), reads=["tb_3"], writes=["tb_3"])
            E("pe", lambda t: t.matmul(ps[0:128, 0:32], ones_f32, tb[3][0:1, 32:64],
                                       start=True, stop=True),
              reads=["ones_f32", "tb_3"], writes=[PB(0)])
            E("dve", lambda v: v.tensor_copy(invb[:, 0:32], ps[0:128, 0:32]),
              reads=[PB(0)], writes=["invb"])
            E("pool", lambda g: g.iota(tb[0][:], pattern=[[128, 16], [0, 32]],
                                       base=0, channel_multiplier=1,
                                       allow_small_or_imprecise_dtypes=True),
              writes=["tb_0"])
            E("dve", lambda v: v.tensor_tensor(
                tb[1][:].rearrange("p (a b) -> p a b", a=16),
                tb[0][:].rearrange("p (a b) -> p a b", a=16),
                invb[:, 0:32].unsqueeze(1).broadcast_to([128, 16, 32]),
                ALU.mult), reads=["tb_0", "invb"], writes=["tb_1"])
            act_set_load(ACT_TRIG_SET)
            for trig, shift in ((sname, 0.0), (cname, PI / 2.0)):
                E("dve", lambda v, shift=shift: v.tensor_scalar(
                    tb[0][:], tb[1][:], shift, 1.0 / TWO_PI, ALU.add, ALU.mult),
                  reads=["tb_1"], writes=["tb_0"])
                E("dve", lambda v: v.tensor_copy(tbi[:], tb[0][:]),
                  reads=["tb_0"], writes=["tbi"])
                E("dve", lambda v, shift=shift: v.tensor_scalar(
                    tb[2][:], tb[1][:], shift, None, ALU.add),
                  reads=["tb_1"], writes=["tb_2"])
                E("dve", lambda v: v.scalar_tensor_tensor(
                    tb[2][:], tbi[:], -TWO_PI, tb[2][:], ALU.mult, ALU.add),
                  reads=["tbi", "tb_2"], writes=["tb_2"])
                for cmp_op, thr, fix in ((ALU.is_gt, PI, -TWO_PI),
                                         (ALU.is_lt, -PI, TWO_PI)):
                    E("dve",
                      lambda v, cmp_op=cmp_op, thr=thr: v.tensor_single_scalar(
                          tb[0][:], tb[2][:], thr, cmp_op),
                      reads=["tb_2"], writes=["tb_0"])
                    E("dve", lambda v, fix=fix: v.scalar_tensor_tensor(
                        tb[2][:], tb[0][:], fix, tb[2][:], ALU.mult, ALU.add),
                      reads=["tb_0", "tb_2"], writes=["tb_2"])
                E("dve", lambda v: v.tensor_scalar(
                    tb[2][:], tb[2][:], PI, -PI, ALU.min, ALU.max),
                  reads=["tb_2"], writes=["tb_2"])
                E("act", lambda s, trig=trig: s.activation(
                    tab[trig].rearrange("p a b -> p (a b)"), tb[2][:], AF.Sin),
                  reads=["tb_2"], writes=[trig])

        act_set_load(ACT_COMBO_SET)

        def perm_in(ap2d):
            return ap2d.rearrange("p (h i two) -> p h two i", two=2, i=32)

        def perm_q_pairs(src2d, dst2d):
            src6 = src2d.rearrange("p (a b c d e) -> p a b c d e",
                                   a=2, b=2, c=4, d=32, e=2)
            dst6 = dst2d.rearrange("p (a c b e d) -> p a c b e d",
                                   a=2, c=4, b=2, e=2, d=32)
            pairs = []
            for a in range(2):
                for b in range(2):
                    inp = src6[:, a, b].transpose([0, 1, 3, 2])
                    outp = dst6[:, a, :, b]
                    pairs.append((inp, outp))
            return pairs

        def prep_tile(dst, src, k, n_out, fold_col, permute, tag):
            E("ad", lambda e, k=k: e.dma_start(
                out=wtmp[0:128, 0:n_out], in_=src[128 * k:128 * (k + 1), :]),
              reads=["wtmp"], writes=["wtmp"])
            inv = wtmp[0:128, 0:n_out]
            col = (n12[:, fold_col + k:fold_col + k + 1]
                   if fold_col is not None else None)
            if permute == "q":
                for inp, outp in perm_q_pairs(inv, dst[k][:]):
                    E("dve", lambda v, inp=inp, outp=outp, col=col:
                      v.tensor_scalar(outp, inp, col, None, ALU.mult),
                      reads=["wtmp", "n12"], writes=[f"{tag}_{k}"])
                return
            if permute:
                inv = perm_in(inv)
            if col is not None:
                E("dve", lambda v, inv=inv, col=col: v.tensor_scalar(
                    dst[k][:], inv, col, None, ALU.mult),
                  reads=["wtmp", "n12"], writes=[f"{tag}_{k}"])
            else:
                E("dve", lambda v, inv=inv: v.tensor_copy(dst[k][:], inv),
                  reads=["wtmp"], writes=[f"{tag}_{k}"])

        w1_q = []
        for k in range(8):
            w1_q.append(lambda k=k: prep_tile(wq1, g_wq, k, 1024, 0, "q", "wq1"))
        for k in range(8):
            w1_q.append(lambda k=k: prep_tile(kvd, g_kvd, k, 256, 0, False, "kvd"))
        for k in range(2):
            def kup_tile(k=k):
                E("ad", lambda e: e.dma_start(
                    out=wtmp[0:128, 0:256], in_=g_kup[128 * k:128 * (k + 1), :]),
                  reads=["wtmp"], writes=["wtmp"])
                E("dve", lambda v: v.tensor_copy(
                    kup[k][:], perm_in(wtmp[0:128, 0:256])),
                  reads=["wtmp"], writes=[f"kup_{k}"])
            w1_q.append(kup_tile)

        w2_q = []
        for k in range(8):
            w2_q.append(lambda k=k: prep_tile(wq2, l_wq, k, 1024, 8, "q", "wq2"))
        for k in range(8):
            w2_q.append(lambda k=k: prep_tile(wk2, l_wk, k, 256, 8, True, "wk2"))
        for k in range(8):
            w2_q.append(lambda k=k: prep_tile(wv2, l_wv, k, 256, 8, False, "wv2"))

        lora_q = []

        def lp_afc():
            E("ad", lambda e: e.dma_start(out=wtmp[0:16, 0:1024], in_=A_fc),
              reads=["wtmp"], writes=["wtmp"])
            E("dve", lambda v: v.tensor_copy(sqj[0:16, :], wtmp[0:16, 0:1024]),
              reads=["wtmp"], writes=["sqj"])
            for k in range(8):
                E("pe", lambda t, k=k: t.transpose(
                    psbf[0:128, 7168 + 16 * k:7168 + 16 * (k + 1)],
                    sqj[0:16, 128 * k:128 * (k + 1)], id_bf[0:16, 0:16]),
                  reads=["sqj", "id_bf"], inc=(k == 7),
                  writes=[PB(7)])
            E("dve", lambda v: v.tensor_copy(afcT[:], psbf[0:128, 7168:7296]),
              reads=[PB(7)], writes=["afcT"])
        lora_q.append(lp_afc)

        for q_ in range(4):
            def lp_apj(q_=q_):
                E("ad", lambda e: e.dma_start(
                    out=wtmp[0:16, 0:1024], in_=A_pj[:, 1024 * q_:1024 * (q_ + 1)]),
                  reads=["wtmp"], writes=["wtmp"])
                E("dve", lambda v: v.tensor_copy(apstg[0:16, :],
                                                 wtmp[0:16, 0:1024]),
                  reads=["wtmp"], writes=["apstg"])
                for j in range(8):
                    fk = 8 * q_ + j
                    E("pe", lambda t, j=j, fk=fk: t.transpose(
                        psbf[0:128, 7168 + 16 * j:7168 + 16 * (j + 1)],
                        apstg[0:16, 128 * j:128 * (j + 1)], id_bf[0:16, 0:16]),
                      reads=["apstg", "id_bf"], inc=(j == 7),
                      writes=[PB(7)])
                E("dve", lambda v, q_=q_: v.tensor_copy(
                    apjT[:, 128 * q_:128 * (q_ + 1)], psbf[0:128, 7168:7296]),
                  reads=[PB(7)], writes=["apjT"])
            lora_q.append(lp_apj)

        for k in range(32):
            def lp_bfc(k=k):
                E("ad", lambda e: e.dma_start(
                    out=wtmp[0:128, 0:16], in_=B_fc[128 * k:128 * (k + 1), :]),
                  reads=["wtmp"], writes=["wtmp"])
                E("dve", lambda v: v.tensor_copy(sqj[0:128, 0:16],
                                                 wtmp[0:128, 0:16]),
                  reads=["wtmp"], writes=["sqj"])
                E("pe", lambda t: t.transpose(
                    psbf[0:16, 7168:7296], sqj[0:128, 0:16], id_bf),
                  reads=["sqj", "id_bf"], writes=[PB(7)])
                E("dve", lambda v: v.tensor_copy(
                    bfcT[0:16, 128 * k:128 * (k + 1)], psbf[0:16, 7168:7296]),
                  reads=[PB(7)], writes=["bfcT"])
            lora_q.append(lp_bfc)
        for k in range(8):
            def lp_bpj(k=k):
                E("ad", lambda e: e.dma_start(
                    out=wtmp[0:128, 0:16], in_=B_pj[128 * k:128 * (k + 1), :]),
                  reads=["wtmp"], writes=["wtmp"])
                E("dve", lambda v: v.tensor_copy(sqj[0:128, 0:16],
                                                 wtmp[0:128, 0:16]),
                  reads=["wtmp"], writes=["sqj"])
                E("pe", lambda t: t.transpose(
                    psbf[0:16, 7168:7296], sqj[0:128, 0:16], id_bf),
                  reads=["sqj", "id_bf"], writes=[PB(7)])
                E("dve", lambda v: v.tensor_copy(
                    bpjT[0:16, 128 * k:128 * (k + 1)], psbf[0:16, 7168:7296]),
                  reads=[PB(7)], writes=["bpjT"])
            lora_q.append(lp_bpj)

        def lp_bias():
            E("gd", lambda e: e.dma_start(out=bfcT[16:17, :], in_=b_fc[None, :]),
              writes=["bfcT"])
            E("gd", lambda e: e.dma_start(out=bpjT[16:17, :], in_=b_pj[None, :]),
              writes=["bpjT"])
        lora_q.append(lp_bias)

        q1_q = []
        W_pj_t = W_pj.rearrange("d (a b) -> (d a) b", a=4)

        def qp1_tile(W, i, maxc, tag, first):
            b = i % 2
            E(("gd", "sd", "ad")[i % 3],
              lambda e, i=i, W=W, b=b: e.dma_start(
                  out=q1wi[b][:], in_=W[128 * i:128 * (i + 1), :]),
              reads=[f"q1wi_{b}"], writes=[f"q1wi_{b}"])
            E("dve", lambda v, b=b: v.tensor_reduce(
                lncol[:, 20:21], q1wi[b][:], AXX, ALU.max,
                apply_absolute_value=True),
              reads=[f"q1wi_{b}"], writes=["lncol"])
            if first:
                E("dve", lambda v, maxc=maxc: v.tensor_copy(maxc, lncol[:, 20:21]),
                  reads=["lncol"], writes=[tag])
            else:
                E("dve", lambda v, maxc=maxc: v.tensor_tensor(
                    maxc, maxc, lncol[:, 20:21], ALU.max),
                  reads=["lncol", tag], writes=[tag])

        for i in range(32):
            q1_q.append(lambda i=i: qp1_tile(W_fc, i, lncol[:, 16:17], "mx_fc",
                                             i == 0))
        for i in range(32):
            q1_q.append(lambda i=i: qp1_tile(W_pj_t, i, lncol[:, 17:18], "mx_pj",
                                             i == 0))

        def qscale(maxc, tag, wi):
            E("pe", lambda t, maxc=maxc: t.transpose(ps[0:1, 3584:3712], maxc,
                                                     id_f32),
              reads=[tag, "id_f32"], writes=[PB(7)])
            E("dve", lambda v: v.tensor_reduce(
                lncol[0:1, 24:25], ps[0:1, 3584:3712], AXX, ALU.max),
              reads=[PB(7)], writes=["lncol"])
            E("dve", lambda v: v.reciprocal(lncol[0:1, 25:26], lncol[0:1, 24:25]),
              reads=["lncol"], writes=["lncol"])
            E("dve", lambda v: v.tensor_scalar(
                lncol[0:1, 26:27], lncol[0:1, 24:25], 1.0 / 127.0, None, ALU.mult),
              reads=["lncol"], writes=["lncol"])
            E("dve", lambda v: v.tensor_scalar(
                lncol[0:1, 27:28], lncol[0:1, 25:26], 127.0, None, ALU.mult),
              reads=["lncol"], writes=["lncol"])
            E("pe", lambda t: t.matmul(ps[0:128, 3584:3586], ones_f32,
                                       lncol[0:1, 26:28], start=True, stop=True),
              reads=["lncol", "ones_f32"], writes=[PB(7)])
            E("dve", lambda v, wi=wi: v.tensor_copy(
                scl[:, 2 * wi:2 * wi + 2], ps[0:128, 3584:3586]),
              reads=[PB(7)], writes=["scl"])

        q1_q.append(lambda: qscale(lncol[:, 16:17], "mx_fc", 0))
        q1_q.append(lambda: qscale(lncol[:, 17:18], "mx_pj", 1))

        q2_q = []
        wqpjd_t = wqpjd.rearrange("d (a b) -> (d a) b", a=4)

        def qp2_tile(W, Wdst, i, wi, tag):
            b = i % 2
            icol = scl[:, 2 * wi + 1:2 * wi + 2]
            scol = scl[:, 2 * wi:2 * wi + 1]
            E(("gd", "sd", "ad")[i % 3],
              lambda e, i=i, W=W, b=b: e.dma_start(
                  out=qwi[b][:], in_=W[128 * i:128 * (i + 1), :]),
              reads=[f"qwi_{b}"], writes=[f"qwi_{b}"])
            E("act", lambda s_, b=b, icol=icol: s_.activation(
                qw_y[:], qwi[b][:], AF.Identity, scale=icol,
                bias=epsc[:, 2:3]),
              reads=[f"qwi_{b}", "scl", "epsc"], writes=["qw_y"])
            E("dve", lambda v: v.tensor_copy(qw_i[:], qw_y[:]),
              reads=["qw_y"], writes=["qw_i"])
            E("dve", lambda v: v.scalar_tensor_tensor(
                qw_g[:], qw_i[:], 1.0, qw_y[:], ALU.bypass, ALU.is_gt),
              reads=["qw_i", "qw_y"], writes=["qw_g"])
            E("dve", lambda v: v.scalar_tensor_tensor(
                qw_g[:], qw_g[:], -1.0, qw_i[:], ALU.mult, ALU.add),
              reads=["qw_g", "qw_i"], writes=["qw_g"])
            E("dve", lambda v, scol=scol: v.tensor_scalar(
                qw_o[:], qw_g[:], scol, None, ALU.mult),
              reads=["qw_g", "scl"], writes=["qw_o"])
            E(("ad", "gd", "sd")[i % 3],
              lambda e, i=i, Wdst=Wdst: e.dma_start(
                  out=Wdst[128 * i:128 * (i + 1), :], in_=qw_o[:]),
              reads=["qw_o"], writes=[(tag, i)])

        for i in range(32):
            q2_q.append(lambda i=i: qp2_tile(W_fc, wqfcd, i, 0, "wqfc"))
        for i in range(32):
            q2_q.append(lambda i=i: qp2_tile(W_pj_t, wqpjd_t, i, 1, "wqpj"))

        def drain(q, n):
            for _ in range(min(n, len(q))):
                q.pop(0)()

        def rsqrt_cols(src_col, dst_col, n_mean, eps_col, wkey):
            E("act", lambda s: s.activation(lncol[:, 12:13], src_col, AF.Ln,
                                            scale=1.0 / n_mean, bias=eps_col),
              reads=["lncol", "epsc"], writes=["lncol"])
            E("act", lambda s: s.activation(dst_col, lncol[:, 12:13], AF.Exp,
                                            scale=-0.5),
              reads=["lncol"], writes=[wkey])

        def ln_chunk(s, c):
            i = c % 2
            E("sd", lambda e: e.dma_start(
                out=xb[i][:], in_=x[s, 128 * c:128 * (c + 1), :]),
              reads=[f"xb_{i}"], writes=[f"xb_{i}"])
            E("dve", lambda v: v.tensor_reduce(lncol[:, 0:1], xb[i][:], AXX,
                                               ALU.add),
              reads=[f"xb_{i}"], writes=["lncol"])
            E("dve", lambda v: v.tensor_scalar(lncol[:, 1:2], lncol[:, 0:1],
                                               1.0 / D, None, ALU.mult),
              reads=["lncol"], writes=["lncol"])
            E("dve", lambda v: v.tensor_scalar(cent[:], xb[i][:], lncol[:, 1:2],
                                               None, ALU.subtract),
              reads=[f"xb_{i}", "lncol"], writes=["cent"])
            E("dve", lambda v: v.scalar_tensor_tensor(
                sqj[:], cent[:], 1.0, cent[:], ALU.bypass, ALU.mult,
                accum_out=lncol[:, 2:3]),
              reads=["cent"], writes=["sqj", "lncol"])
            rsqrt_cols(lncol[:, 2:3], lncol[:, 4:5], D, epsc[:, 0:1], "lncol")
            xnb = s1b[i]
            E("dve", lambda v, xnb=xnb: v.tensor_scalar(
                xnb[:], cent[:], lncol[:, 4:5], None, ALU.mult),
              reads=["cent", "lncol"], writes=[f"s1b_{i}"])
            E("dve", lambda v, xnb=xnb: v.scalar_tensor_tensor(
                sqj[:], xnb[:], 1.0, xnb[:], ALU.bypass, ALU.mult,
                accum_out=lncol[:, 5:6]),
              reads=[f"s1b_{i}"], writes=["sqj", "lncol"])
            rsqrt_cols(lncol[:, 5:6], r1b[:, 16 * s + c:16 * s + c + 1], D,
                       epsc[:, 1:2], "r1b")
            E("ad", lambda e, xnb=xnb, s=s, c=c: e.dma_start(
                out=xnd[2048 * s + 128 * c:2048 * s + 128 * (c + 1), :],
                in_=xnb[:]),
              reads=[f"s1b_{i}"], writes=[("xnd", s, c)])

        def rope_qknorm(src, n_cols, cname, sname, rk, dst, wk, cc,
                        eng="dve", rbuf=None, rbk="rtmp"):
            nh = n_cols // 64
            lc, lck = (lnc_q, "lnc_q") if n_cols == 1024 else (lnc_k, "lnc_k")
            E("dve", lambda v: v.tensor_tensor(sqj[:, 0:n_cols], src, src,
                                               ALU.mult),
              reads=[rk], writes=["sqj"])
            E("dve", lambda v: v.tensor_reduce(
                lc[:, 0:nh],
                sqj[:, 0:n_cols].rearrange("p (h i) -> p h i", i=64),
                AXX, ALU.add),
              reads=["sqj"], writes=[lck])
            E("act", lambda s: s.activation(lc[:, 0:nh], lc[:, 0:nh], AF.Ln,
                                            scale=1.0 / 64.0, bias=epsc[:, 1:2]),
              reads=[lck, "epsc"], writes=[lck])
            E("act", lambda s: s.activation(lc[:, 0:nh], lc[:, 0:nh],
                                            AF.Exp, scale=-0.5),
              reads=[lck], writes=[lck])
            nbuf = qn_ if n_cols == 1024 else kn_
            nk = "qn" if n_cols == 1024 else "kn"
            E("dve", lambda v, nbuf=nbuf: v.tensor_tensor(
                nbuf[:, 0:n_cols].rearrange("p (h i) -> p h i", i=64),
                src.rearrange("p (h i) -> p h i", i=64),
                lc[:, 0:nh].unsqueeze(2).broadcast_to([128, nh, 64]),
                ALU.mult),
              reads=[rk, lck], writes=[nk])
            nb = nbuf[:, 0:n_cols].rearrange("p (h two i) -> p h two i",
                                             two=2, i=32)
            db = dst.rearrange("p (h two i) -> p h two i", two=2, i=32)
            e_, o_ = nb[:, :, 0, :], nb[:, :, 1, :]
            de, do = db[:, :, 0, :], db[:, :, 1, :]
            cosb = tab[cname][:, cc, :].unsqueeze(1).broadcast_to([128, nh, 32])
            sinb = tab[sname][:, cc, :].unsqueeze(1).broadcast_to([128, nh, 32])
            if rbuf is None:
                rbuf = rtmp
            half = rbuf[:, 0:n_cols // 2].rearrange("p (h i) -> p h i", i=32)
            E(eng, lambda v: v.tensor_tensor(half, o_, sinb, ALU.mult),
              reads=[nk, sname], writes=[rbk])
            E(eng, lambda v: v.tensor_tensor(de, e_, cosb, ALU.mult),
              reads=[nk, cname], writes=[wk])
            E(eng, lambda v: v.tensor_tensor(de, de, half, ALU.subtract),
              reads=[rbk, wk], writes=[wk])
            E(eng, lambda v: v.tensor_tensor(half, e_, sinb, ALU.mult),
              reads=[nk, sname], writes=[rbk])
            E(eng, lambda v: v.tensor_tensor(do, o_, cosb, ALU.mult),
              reads=[nk, cname], writes=[wk])
            E(eng, lambda v: v.tensor_tensor(do, do, half, ALU.add),
              reads=[rbk, wk], writes=[wk])

        def attention(s, aidx, hook_q=None):
            src_d = xnd if aidx == 0 else s1d
            srctag = "xnd" if aidx == 0 else "s1d"
            rbuf, rtag = (r1b, "r1b") if aidx == 0 else (r2b, "r2b")
            cname = "cos_g" if aidx == 0 else "cos_l"
            sname = "sin_g" if aidx == 0 else "sin_l"
            qw, qtag = (wq1, "wq1") if aidx == 0 else (wq2, "wq2")
            wo_w, wo_tag = (wo1, "wo1") if aidx == 0 else (wo2, "wo2")
            for sp in range(NSP):
                row0 = 2048 * s + 512 * sp
                span_keys = [(srctag, s, 4 * sp + j) for j in range(4)]
                for dk in range(8):
                    E("sd",
                      lambda e, dk=dk: e.dma_start(
                          out=xnf[:, dk, :],
                          in_=src_d[row0:row0 + 512, 128 * dk:128 * (dk + 1)],
                          transpose=True),
                      reads=span_keys + ["xnf"], writes=["xnf"])
                if aidx == 0 and SUBPH >= 2:
                    for lc in range(2):
                        for k in range(8):
                            E("pe", lambda t, k=k, lc=lc: t.matmul(
                                bank(lc), kvd[k][:, 128 * lc:128 * (lc + 1)],
                                xnf[:, k, :],
                                start=(k == 0), stop=(k == 7)),
                              reads=["xnf", f"kvd_{k}"], inc=(k == 7),
                              writes=[PB(lc)])
                        E("act", lambda s_, lc=lc: s_.activation(
                            latf[:, lc, :], bank(lc), AF.Copy),
                          reads=[PB(lc)], writes=["latf"])
                def prod_mm(tci):
                    c = 4 * sp + tci
                    qtb, qtk = (qtm, "qtm") if tci % 2 == 0 else (qtm2, "qtm2")
                    ktb, ktk = (ktm, "ktm") if tci % 2 == 0 else (ktm2, "ktm2")
                    rcol = rbuf[:, 16 * s + c:16 * s + c + 1]
                    for half in range(2):
                        for k in range(8):
                            E("pe", lambda t, k=k, half=half, tci=tci: t.matmul(
                                bank(2 + half),
                                xnf[:, k, 128 * tci:128 * (tci + 1)],
                                qw[k][:, 512 * half:512 * (half + 1)],
                                start=(k == 0), stop=(k == 7)),
                              reads=["xnf", f"{qtag}_{k}"], inc=(k == 7),
                              writes=[PB(2 + half)])
                    E("act", lambda s_, rcol=rcol, qtb=qtb: s_.activation(
                        qtb[:], ps[0:128, 1024:2048], AF.Copy, scale=rcol),
                      reads=[PB(2), PB(3), rtag], writes=[qtk])
                    if aidx == 0:
                        for lc in range(2):
                            E("pe", lambda t, lc=lc, tci=tci: t.matmul(
                                bank(0)[:, 0:256],
                                latf[:, lc, 128 * tci:128 * (tci + 1)],
                                kup[lc][:], start=(lc == 0), stop=(lc == 1)),
                              reads=["latf", f"kup_{lc}"], inc=(lc == 1),
                              writes=[PB(0)])
                    else:
                        for k in range(8):
                            E("pe", lambda t, k=k, tci=tci: t.matmul(
                                bank(0)[:, 0:256],
                                xnf[:, k, 128 * tci:128 * (tci + 1)],
                                wk2[k][:], start=(k == 0), stop=(k == 7)),
                              reads=["xnf", f"wk2_{k}"], inc=(k == 7),
                              writes=[PB(0)])
                    E("act", lambda s_, rcol=rcol, ktb=ktb: s_.activation(
                        ktb[:], bank(0)[:, 0:256], AF.Copy, scale=rcol),
                      reads=[PB(0), rtag], writes=[ktk])
                    if aidx == 0:
                        for lc in range(2):
                            E("pe", lambda t, lc=lc, tci=tci: t.matmul(
                                bank(1)[:, 0:256],
                                latf[:, lc, 128 * tci:128 * (tci + 1)],
                                vup[lc][:], start=(lc == 0), stop=(lc == 1)),
                              reads=["latf", f"vup_{lc}"], inc=(lc == 1),
                              writes=[PB(1)])
                    else:
                        for k in range(8):
                            E("pe", lambda t, k=k, tci=tci: t.matmul(
                                bank(1)[:, 0:256],
                                xnf[:, k, 128 * tci:128 * (tci + 1)],
                                wv2[k][:], start=(k == 0), stop=(k == 7)),
                              reads=["xnf", f"wv2_{k}"], inc=(k == 7),
                              writes=[PB(1)])
                    E("act", lambda s_, rcol=rcol, c=c: s_.activation(
                        vb.rearrange("p (g y) -> p g y", g=4)[:, :,
                                                              65 * c:65 * c + 64],
                        bank(1)[:, 0:256].rearrange("p (g y) -> p g y", g=4),
                        AF.Copy, scale=rcol),
                      reads=[PB(1), rtag], writes=["vb"])

                def prod_rope(tci):
                    c = 4 * sp + tci
                    qb_, qk_ = (qrn, "qrn") if tci % 2 == 0 else (qrn2, "qrn2")
                    kb_, kk_ = (krn, "krn") if tci % 2 == 0 else (krn2, "krn2")
                    qtb, qtk = (qtm, "qtm") if tci % 2 == 0 else (qtm2, "qtm2")
                    ktb, ktk = (ktm, "ktm") if tci % 2 == 0 else (ktm2, "ktm2")
                    rope_qknorm(qtb[:], 1024, cname, sname, qtk, qb_[:],
                                qk_, c)
                    rope_qknorm(ktb[:], 256, cname, sname, ktk, kb_[:],
                                kk_, c, eng="pool", rbuf=rtmpk, rbk="rtmpk")

                def prod_tr(tci):
                    c = 4 * sp + tci
                    qb_, qk_ = (qrn, "qrn") if tci % 2 == 0 else (qrn2, "qrn2")
                    kb_, kk_ = (krn, "krn") if tci % 2 == 0 else (krn2, "krn2")
                    for mc in range(8):
                        E("pe", lambda t, mc=mc, qb_=qb_: t.transpose(
                            ps16[0:128, 6144 + 128 * mc:6144 + 128 * (mc + 1)],
                            qb_[:, 128 * mc:128 * (mc + 1)], id_f16),
                          reads=[qk_, "id_f16"], inc=(mc == 7),
                          writes=[PB(6), PB(7)])
                    E("dve", lambda v, tci=tci: v.tensor_copy(
                        q1f[:, :, 128 * tci:128 * (tci + 1)],
                        ps16[0:128, 6144:7168].rearrange("p (a b) -> p a b",
                                                         a=8)),
                      reads=[PB(6), PB(7)], writes=["q1f"])
                    for mc in range(2):
                        E("pe", lambda t, mc=mc, kb_=kb_: t.transpose(
                            ps16[0:128, 6144 + 128 * mc:6144 + 128 * (mc + 1)],
                            kb_[:, 128 * mc:128 * (mc + 1)], id_f16),
                          reads=[kk_, "id_f16"], inc=(mc == 1),
                          writes=[PB(6)])
                    E("dve", lambda v, c=c: v.tensor_copy(
                        k1f[:, :, 128 * c:128 * (c + 1)],
                        ps16[0:128, 6144:6400].rearrange("p (a b) -> p a b",
                                                         a=2)),
                      reads=[PB(6)], writes=["k1f"])

                for tci in range(4):
                    prod_mm(tci)
                    if tci > 1:
                        prod_tr(tci - 2)
                    prod_rope(tci)
                prod_tr(2)
                prod_tr(3)
                for g in range(4):
                    kb = 64 * (g % 2)
                    mck = g // 2
                    if aidx == 0:
                        krange = list(range(0, 4 * sp + 4))
                    else:
                        krange = list(range(max(4 * sp - 2, 0), 4 * sp + 4))
                    nk_ = len(krange)
                    heads = []
                    for hh in range(4):
                        h = 4 * g + hh
                        qb = 64 * ((h // 4) % 2)
                        qmc = (h % 4) + 4 * (h // 8)
                        assert qb == kb
                        heads.append((hh, qmc))

                    def mask_for(kc):
                        if aidx == 0:
                            rr = kc - 4 * sp
                            if rr < 0:
                                return None, None
                            return mA[rr], f"mA_{rr}"
                        rr = kc - 4 * sp + 2
                        return mB[rr], f"mB_{rr}"

                    def pbuf(ki):
                        return (ptbp, ("ptbA", "ptbB")) if ki % 2 == 0 \
                            else (ptbp2, ("ptbA2", "ptbB2"))

                    def emit_qk(kc, pr, kb=kb, mck=mck, heads=heads):
                        for hh, qmc in heads[2 * pr:2 * pr + 2]:
                            xtra = [PB(2), PB(3)] if (pr, hh % 2) == (0, 0) \
                                else []
                            E("pe", lambda t, kc=kc, hh=hh, qmc=qmc: t.matmul(
                                bank(hh),
                                k1f[kb:kb + 64, mck, 128 * kc:128 * (kc + 1)],
                                q1f[kb:kb + 64, qmc, :],
                                start=True, stop=True),
                              reads=["k1f", "q1f"] + xtra, writes=[PB(hh)])

                    def emit_evac(ki, kc, pr, mask_for=mask_for):
                        mask, mkey = mask_for(kc)
                        buf, keys = pbuf(ki)
                        E("act", lambda s_, pr=pr, buf=buf: s_.activation(
                            buf[pr][:].rearrange("p a b -> p (a b)"),
                            ps[0:128, 1024 * pr:1024 * (pr + 1)],
                            AF.Exp, scale=0.125),
                          reads=[PB(2 * pr), PB(2 * pr + 1)],
                          writes=[keys[pr]])
                        if mask is not None:
                            E("dve", lambda v, pr=pr, mask=mask, buf=buf:
                              v.tensor_tensor(
                                  buf[pr][:], buf[pr][:],
                                  mask[:].unsqueeze(1).broadcast_to(
                                      [128, 2, 512]), ALU.mult),
                              reads=[keys[pr], mkey], writes=[keys[pr]])

                    def emit_pv(ki, kc, g=g, heads=heads, nk_=nk_):
                        buf, keys = pbuf(ki)
                        for hh, qmc in heads:
                            xtra = [keys[1]] if hh == 0 else []
                            E("pe", lambda t, kc=kc, hh=hh, ki=ki, buf=buf:
                              t.matmul(
                                  ps[0:65, 2048 + 512 * hh:2048 + 512 * (hh + 1)],
                                  vb[:, 1040 * g + 65 * kc:
                                     1040 * g + 65 * (kc + 1)],
                                  buf[hh // 2][:, hh % 2, :],
                                  start=(ki == 0), stop=(ki == nk_ - 1)),
                              reads=[keys[hh // 2], "vb"] + xtra,
                              inc=(hh == 3), writes=[PB(4 + hh)])

                    for ki, kc in enumerate(krange):
                        emit_qk(kc, 0)
                        emit_evac(ki, kc, 0)
                        emit_qk(kc, 1)
                        emit_evac(ki, kc, 1)
                        if ki > 0:
                            emit_pv(ki - 1, krange[ki - 1])
                    emit_pv(nk_ - 1, krange[nk_ - 1])

                    gp = g % 2
                    E("act", lambda s_: s_.activation(
                        lnrow[:], ps[64:65, 2048:4096], AF.Ln),
                      reads=[PB(4), PB(5), PB(6), PB(7)],
                      writes=["lnrow"])
                    E("act", lambda s_: s_.activation(
                        rec_sb[0:1, 0:2048], lnrow[:], AF.Exp,
                        scale=-1.0),
                      reads=["lnrow", PB(4), PB(5), PB(6), PB(7)],
                      writes=["recb"])
                    E("ad", lambda e, gp=gp: e.dma_start(
                        out=recd[gp:gp + 1, :], in_=rec_sb[0:1, 0:2048]),
                      reads=["recb"], writes=[("recd", gp)])
                    E("gd", lambda e, gp=gp: e.dma_start(
                        out=recb[0:64, 0:2048],
                        in_=recd[gp][None, :].broadcast_to([64, 2048])),
                      reads=[("recd", gp), "recb"], writes=["recb"])
                    for par in range(2):
                        E("dve", lambda v, par=par, g=g: v.tensor_tensor(
                            attn_m[64 * par:64 * (par + 1),
                                   2 * g:2 * g + 2, :],
                            ps[0:64, 2048:4096].rearrange(
                                "p (a c b) -> p a c b", a=2, c=2)[:, :, par, :],
                            recb[0:64, 0:2048].rearrange(
                                "p (a c b) -> p a c b", a=2, c=2)[:, :, par, :],
                            ALU.mult),
                          reads=[PB(4 + par), PB(6 + par), "recb"],
                          writes=["attn_m"])
                for tci in range(4 if SUBPH >= 5 else 0):
                    c = 4 * sp + tci
                    i = tci % 2
                    ldb = ld_a if i == 0 else ld_b
                    ldk = "ld_a" if i == 0 else "ld_b"
                    E("sd", lambda e, c=c, ldb=ldb, s=s: e.dma_start(
                        out=ldb[:],
                        in_=src_d[2048 * s + 128 * c:2048 * s + 128 * (c + 1), :]),
                      reads=[ldk, (srctag, s, c)], writes=[ldk])
                    for half in range(2):
                        for mc in range(8):
                            E("pe", lambda t, mc=mc, half=half, tci=tci: t.matmul(
                                bank(2 + half),
                                attn_m[:, mc, 128 * tci:128 * (tci + 1)],
                                wo_w[mc][:, 512 * half:512 * (half + 1)],
                                start=(mc == 0), stop=(mc == 7)),
                              reads=["attn_m", f"{wo_tag}_{mc}"], inc=(mc == 7),
                              writes=[PB(2 + half)])
                    sb_o = s1b[i]
                    for half in range(2):
                        E("dve", lambda v, half=half, sb_o=sb_o, ldb=ldb:
                          v.scalar_tensor_tensor(
                              sb_o[:, 512 * half:512 * (half + 1)],
                              bank(2 + half), 1.0,
                              ldb[:, 512 * half:512 * (half + 1)],
                              ALU.bypass, ALU.add),
                          reads=[PB(2 + half), ldk], writes=[f"s1b_{i}"])
                    if aidx == 0:
                        E("dve", lambda v, sb_o=sb_o: v.scalar_tensor_tensor(
                            sqj[:], sb_o[:], 1.0, sb_o[:],
                            ALU.bypass, ALU.mult, accum_out=lncol[:, 6:7]),
                          reads=[f"s1b_{i}"], writes=["sqj", "lncol"])
                        rsqrt_cols(lncol[:, 6:7],
                                   r2b[:, 16 * s + c:16 * s + c + 1], D,
                                   epsc[:, 1:2], "r2b")
                        dst, dk_ = s1d, ("s1d", s, c)
                    else:
                        dst, dk_ = sdd, ("sdd", s, c)
                    E("gd", lambda e, sb_o=sb_o, s=s, c=c, dst=dst: e.dma_start(
                        out=dst[2048 * s + 128 * c:2048 * s + 128 * (c + 1), :],
                        in_=sb_o[:]),
                      reads=[f"s1b_{i}"], writes=[dk_])
                    if hook_q is not None:
                        rem = (NSP - sp) * 4 - tci
                        drain(hook_q, (len(hook_q) + rem - 1) // rem)

        def ffn_pass1():
            act_set_load(ACT_GELU_SET)
            for k in range(8):
                E("sd",
                  lambda e, k=k: e.dma_start(
                      out=wfcT[k][:], in_=wqfcd[:, 128 * k:128 * (k + 1)],
                      transpose=True),
                  reads=[("wqfc", i) for i in range(32)] + [f"wfcT_{k}"],
                  writes=[f"wfcT_{k}"])
            def xnf_load(gsp):
                row0 = 512 * gsp
                xb_, xk_ = (xnf, "xnf") if gsp % 2 == 0 else (xnf2, "xnf2")
                span_keys = [("xnd", gsp // 4, 4 * (gsp % 4) + j)
                             for j in range(4)]
                for dk in range(8):
                    E("sd",
                      lambda e, dk=dk, row0=row0, xb_=xb_: e.dma_start(
                          out=xb_[:, dk, :],
                          in_=xnd[row0:row0 + 512, 128 * dk:128 * (dk + 1)],
                          transpose=True),
                      reads=span_keys + [xk_], writes=[xk_])

            xnf_load(0)
            for gsp in range(8):
                if gsp + 1 < 8:
                    xnf_load(gsp + 1)
                row0 = 512 * gsp
                xb_, xk_ = (xnf, "xnf") if gsp % 2 == 0 else (xnf2, "xnf2")
                for k in range(8):
                    E("pe", lambda t, k=k, xb_=xb_: t.matmul(
                        ps[0:16, 3584:4096], afcT[:, 16 * k:16 * (k + 1)],
                        xb_[:, k, :], start=(k == 0), stop=(k == 7)),
                      reads=[xk_, "afcT"], inc=(k == 7),
                      writes=[PB(7)])
                E("dve", lambda v: v.memset(up1[:], 1.0), writes=["up1"])
                E("act", lambda s_: s_.activation(
                    up1[0:16, :], ps[0:16, 3584:4096], AF.Copy, scale=2.0),
                  reads=[PB(7), "up1"], writes=["up1"])
                for fk in range(32):
                    bk = fk % 4
                    for k in range(8):
                        E("pe", lambda t, k=k, fk=fk, bk=bk, xb_=xb_: t.matmul(
                            bank(bk), wfcT[k][:, 128 * fk:128 * (fk + 1)],
                            xb_[:, k, :], start=(k == 0), stop=False),
                          reads=[xk_, f"wfcT_{k}"], inc=False,
                          writes=[("pb", bk)])
                    E("pe", lambda t, fk=fk, bk=bk: t.matmul(
                        bank(bk), bfcT[0:17, 128 * fk:128 * (fk + 1)], up1[:],
                        start=False, stop=True),
                      reads=["bfcT", "up1"], writes=[PB(bk)])
                    if fk % 2 == 0:
                        continue
                    wi = (fk // 2) % 2
                    wkey = f"h2wp_{wi}"
                    bk0 = bk - 1
                    E("act", lambda s_, bk0=bk0, wi=wi: s_.activation(
                        h2wp[wi][:], ps[0:128, 512 * bk0:512 * bk0 + 1024],
                        AF.Gelu),
                      reads=[PB(bk0), PB(bk0 + 1)], writes=[wkey])
                    E("gd", lambda e, fk=fk, wi=wi, row0=row0, gsp=gsp:
                      e.dma_start(
                          out=h2d[128 * (fk - 1):128 * (fk + 1),
                                  row0:row0 + 512].rearrange(
                              "(a p) t -> p a t", a=2),
                          in_=h2wp[wi][:].rearrange("p (a t) -> p a t", a=2)),
                      reads=[wkey],
                      writes=[("h2d", gsp, fk - 1), ("h2d", gsp, fk)])
                    for j in range(2):
                        fj = fk - 1 + j
                        E("pe", lambda t, fj=fj, wi=wi, j=j: t.matmul(
                            ps[0:16, 3072:3584],
                            apjT[:, 16 * fj:16 * (fj + 1)],
                            h2wp[wi][:, 512 * j:512 * (j + 1)],
                            start=(fj == 0), stop=(fj == 31)),
                          reads=[wkey, "apjT"], inc=(fj == 31),
                          writes=[PB(6)])
                E("dve", lambda v, gsp=gsp: v.memset(u2all[:, gsp, :], 1.0),
                  writes=["u2all"])
                E("act", lambda s_, gsp=gsp: s_.activation(
                    u2all[0:16, gsp, :], ps[0:16, 3072:3584], AF.Copy,
                    scale=2.0),
                  reads=[PB(6), "u2all"], writes=["u2all"])

        def ffn_pass2():
            act_set_load(ACT_SILU_SET)
            for fk in range(32):
                E("sd",
                  lambda e, fk=fk: e.dma_start(
                      out=wpjT[fk][:], in_=wqpjd[:, 128 * fk:128 * (fk + 1)],
                      transpose=True),
                  reads=[("wqpj", i) for i in range(32)] + [f"wpjT_{fk}"],
                  writes=[f"wpjT_{fk}"])
            for cg in range(32):
                s, c = cg // 16, cg % 16
                i = cg % 2
                hr, hk = h2r[i], f"h2r_{i}"
                E("sd",
                  lambda e, cg=cg, hr=hr: e.dma_start(
                      out=hr[:],
                      in_=h2dv[:, :, 128 * cg:128 * (cg + 1)]),
                  reads=[("h2d", cg // 4, fk) for fk in range(32)] + [hk],
                  writes=[hk])
                pb0 = 2 * i
                for half in range(2):
                    for fk in range(32):
                        E("pe", lambda t, fk=fk, half=half, hr=hr, pb0=pb0:
                          t.matmul(
                              bank(pb0 + half), hr[:, fk, :],
                              wpjT[fk][:, 512 * half:512 * (half + 1)],
                              start=(fk == 0), stop=False),
                          reads=[hk, f"wpjT_{fk}"], inc=False,
                          writes=[("pb", pb0 + half)])
                    E("pe", lambda t, half=half, cg=cg, pb0=pb0: t.matmul(
                        bank(pb0 + half),
                        u2all[:, cg // 4, 128 * (cg % 4):128 * (cg % 4) + 128],
                        bpjT[0:17, 512 * half:512 * (half + 1)],
                        start=False, stop=True),
                      reads=["u2all", "bpjT"], writes=[PB(pb0 + half)])
                    E("act", lambda s_, half=half, pb0=pb0: s_.activation(
                        gbuf[:, 512 * half:512 * (half + 1)],
                        bank(pb0 + half), AF.Silu),
                      reads=[PB(pb0 + half)], writes=["gbuf"])
                ldb = ld_a if i == 0 else ld_b
                ldk = "ld_a" if i == 0 else "ld_b"
                E("ad", lambda e, s=s, c=c, ldb=ldb: e.dma_start(
                    out=ldb[:],
                    in_=sdd[2048 * s + 128 * c:2048 * s + 128 * (c + 1), :]),
                  reads=[("sdd", s, c), ldk], writes=[ldk])
                E("sd", lambda e, s=s, c=c, i=i: e.dma_start(
                    out=xb[i][:], in_=x[s, 128 * c:128 * (c + 1), :]),
                  reads=[f"xb_{i}"], writes=[f"xb_{i}"])
                E("dve", lambda v, ldb=ldb: v.tensor_tensor(
                    gbuf[:], gbuf[:], ldb[:], ALU.mult),
                  reads=["gbuf", ldk], writes=["gbuf"])
                E("dve", lambda v, i=i: v.tensor_tensor(
                    gbuf[:], gbuf[:], xb[i][:], ALU.add),
                  reads=["gbuf", f"xb_{i}"], writes=["gbuf"])
                E("gd", lambda e, s=s, c=c: e.dma_start(
                    out=out[s, 128 * c:128 * (c + 1), :], in_=gbuf[:]),
                  reads=["gbuf"], writes=[("out", s, c)])

        for k in range(8):
            E("gd", lambda e, k=k: e.dma_start(
                out=wo1[k][:], in_=g_wo[128 * k:128 * (k + 1), :]),
              writes=[f"wo1_{k}"])
        for k in range(2):
            E("gd", lambda e, k=k: e.dma_start(
                out=vup[k][:], in_=g_vup[128 * k:128 * (k + 1), :]),
              writes=[f"vup_{k}"])
        if PHASES >= 1:
            for c in range(NCH):
                ln_chunk(0, c)
                drain(w1_q, 2)
            drain(w1_q, 99)
            for k in range(8):
                E("gd", lambda e, k=k: e.dma_start(
                    out=wo2[k][:], in_=l_wo[128 * k:128 * (k + 1), :]),
                  writes=[f"wo2_{k}"])
        if PHASES >= 2:
            attention(0, 0, hook_q=w2_q + q1_q)
        if PHASES >= 3:
            for c in range(NCH):
                ln_chunk(1, c)
            attention(1, 0, hook_q=lora_q)
        if PHASES >= 4:
            q2_first = q2_q[:len(q2_q) // 2]
            q2_second = q2_q[len(q2_q) // 2:]
            attention(0, 1, hook_q=q2_first)
            attention(1, 1, hook_q=q2_second)
            drain(q2_first, 999)
            drain(q2_second, 999)
        if PHASES >= 5:
            ffn_pass1()
        if DEBUG_TAPS and PHASES >= 5:
            E("sd", lambda e: e.dma_start(out=dbg1, in_=wfcT[0][:]),
              reads=["wfcT_0"], writes=[("dbg1",)])
            E("sd", lambda e: e.dma_start(out=dbg2, in_=bfcT[0:17, :]),
              reads=["bfcT"], writes=[("dbg2",)])
        if PHASES >= 6:
            ffn_pass2()
        p.wait_all("sd")

    return nc


_INPUT_MAP = [
    ("norm1_w", "L0_attn_norm1_w"), ("norm2_w", "L0_attn_norm2_w"),
    ("g_wq", "L0_attn_g_wq"), ("g_kv_down", "L0_attn_g_kv_down"),
    ("g_k_up", "L0_attn_g_k_up"), ("g_v_up", "L0_attn_g_v_up"),
    ("g_wo", "L0_attn_g_wo"), ("l_wq", "L0_attn_l_wq"),
    ("l_wk", "L0_attn_l_wk"), ("l_wv", "L0_attn_l_wv"),
    ("l_wo", "L0_attn_l_wo"), ("W_fc", "L0_ffn_W_fc"),
    ("b_fc", "L0_ffn_b_fc"), ("A_fc", "L0_ffn_A_fc"),
    ("B_fc", "L0_ffn_B_fc"), ("W_proj", "L0_ffn_W_proj"),
    ("b_proj", "L0_ffn_b_proj"), ("A_proj", "L0_ffn_A_proj"),
    ("B_proj", "L0_ffn_B_proj"),
]


def kernel(**inputs):
    nc = build_program()
    xfull = np.ascontiguousarray(np.asarray(inputs["L0_attn_hidden"], np.float32))
    w = {k: np.ascontiguousarray(np.asarray(inputs[v], np.float32))
         for k, v in _INPUT_MAP}
    in_maps = []
    for core in range(8):
        m = {"x": xfull[2 * core:2 * core + 2]}
        m.update(w)
        in_maps.append(m)
    res = run_bass_kernel_spmd(nc, in_maps, core_ids=list(range(8)))
    return np.concatenate([np.asarray(r["out"], np.float32)
                           for r in res.results], axis=0)


if __name__ == "__main__":
    nc = build_program()
    print("build ok")


# version 36
# speedup vs baseline: 1.0507x; 1.0085x over previous
import math
import numpy as np

import concourse.bass as bass
import concourse.mybir as mybir
from concourse.bass_utils import run_bass_kernel_spmd

F32 = mybir.dt.float32
F16 = mybir.dt.float16
BF16 = mybir.dt.bfloat16
I32 = mybir.dt.int32
AF = mybir.ActivationFunctionType
ALU = mybir.AluOpType
AXX = mybir.AxisListType.X

D = 1024
LAT = 256
T = 2048
NSEQ = 2
F = 4096
NCH = 16
NSP = 4
PI = math.pi
TWO_PI = 2.0 * math.pi
EPS_LN = 1e-5
EPS_RMS = 1e-6
BASE_G = 1000000.0
BASE_L = 10000.0


def _act_set_ids():
    fallback = {"natural_log_exp_and_others": 6, "trig_and_small": 9,
                "gelu_and_others": 10, "silu_and_others": 18}
    try:
        from concourse.hw_specs import get_activation_tables
        names = list(get_activation_tables("TRN2"))
        return {k: names.index(k) for k in fallback}
    except Exception:
        return fallback


_ACT_SETS = _act_set_ids()
ACT_COMBO_SET = _ACT_SETS["natural_log_exp_and_others"]
ACT_TRIG_SET = _ACT_SETS["trig_and_small"]
ACT_GELU_SET = _ACT_SETS["gelu_and_others"]
ACT_SILU_SET = _ACT_SETS["silu_and_others"]

DEBUG_TAPS = False
PHASES = 99
SUBPH = 9


class Prog:

    def __init__(self, nc, blk):
        self.nc, self.blk = nc, blk
        self.eng = {}
        self.reg = {}
        self.overlap = {}
        self.opvc = {}

    WAVE = 12

    def engine(self, name, sec, step):
        sem = self.nc.alloc_semaphore(name=f"sem_{name}")
        self.eng[name] = dict(sem=sem, cnt=0, step=step, sec=sec, pend=[],
                              barriers=[0], bar_cnt=0, vc={})

    def _learn(self, know, pe_, pc_):
        if pc_ > know.get(pe_, -1):
            know[pe_] = pc_
        src = self.opvc.get((pe_, pc_))
        if src:
            for k2, v2 in src.items():
                if v2 > know.get(k2, -1):
                    know[k2] = v2

    def close_wave(self, en):
        e = self.eng[en]
        if e["cnt"] == e["barriers"][-1]:
            return
        val = e["cnt"]

        def body(obj):
            obj.wait_ge(e["sem"], val)

        e["sec"](body)
        e["barriers"].append(val)
        e["bar_cnt"] = 0
        self._learn(e["vc"], en, val)

    def _expand(self, keys):
        out = []
        for k in keys:
            if isinstance(k, str) and k in self.overlap:
                out.extend(self.overlap[k])
            else:
                out.append(k)
        return out

    def _need(self, waits, en, pe, pc):
        if pe == en == "pe":
            return
        if pc > waits.get(pe, -1):
            waits[pe] = pc

    def emit(self, en, f, reads=(), writes=(), inc=True):
        e = self.eng[en]
        reads = self._expand(reads)
        writes = self._expand(writes)
        waits = {}
        for r in reads:
            rec = self.reg.get(r)
            if rec:
                for pe_, pc_ in rec[0].items():
                    self._need(waits, en, pe_, pc_)
        for wk in writes:
            rec = self.reg.get(wk)
            if rec:
                for pe_, pc_ in rec[0].items():
                    self._need(waits, en, pe_, pc_)
                for pe_, pc_ in rec[1].items():
                    self._need(waits, en, pe_, pc_)
        know = e["vc"]
        real = []
        for pe_, pc_ in sorted(waits.items(), key=str):
            if self.eng[pe_]["step"] == 16 and pe_ != en:
                bars = self.eng[pe_]["barriers"]
                if pc_ > bars[-1]:
                    self.close_wave(pe_)
                    bars = self.eng[pe_]["barriers"]
                import bisect as _bi
                pc_ = bars[_bi.bisect_left(bars, pc_)]
            if pc_ > know.get(pe_, -1):
                real.append((pe_, pc_))
                self._learn(know, pe_, pc_)

        def body(obj):
            for pe_, pc_ in real:
                obj.wait_ge(self.eng[pe_]["sem"], pc_)
            ins = f(obj)
            if inc:
                if isinstance(ins, (list, tuple)):
                    ins = ins[-1]
                ins.then_inc(e["sem"], e["step"])

        e["sec"](body)
        if inc and e["step"] == 16:
            e["bar_cnt"] += 1
        if inc:
            e["cnt"] += e["step"]
            c = e["cnt"]
            self.opvc[(en, c)] = dict(know, **{en: c})
            if e["step"] == 16 and e["bar_cnt"] >= self.WAVE:
                self.close_wave(en)
            for wk in e["pend"]:
                self.reg[wk] = [{en: c}, {}]
            e["pend"] = []
            for wk in writes:
                self.reg[wk] = [{en: c}, {}]
            for r in reads:
                rec = self.reg.setdefault(r, [{}, {}])
                rec[1][en] = c
        else:
            nxt = e["cnt"] + e["step"]
            for wk in writes:
                self.reg[wk] = [{en: nxt}, {}]
                e["pend"].append(wk)
            for r in reads:
                rec = self.reg.setdefault(r, [{}, {}])
                rec[1][en] = nxt

    def wait_all(self, en):
        e = self.eng[en]
        for pe_, pe in self.eng.items():
            if pe["step"] == 16:
                self.close_wave(pe_)
        know = e["vc"]
        real = []
        for pe_, pe in self.eng.items():
            if pe_ == en:
                continue
            if pe["cnt"] > know.get(pe_, -1) and pe["cnt"] > 0:
                real.append((pe_, pe["cnt"]))
                self._learn(know, pe_, pe["cnt"])

        def body(obj):
            for pe_, pc_ in real:
                obj.wait_ge(self.eng[pe_]["sem"], pc_)

        e["sec"](body)


class Arena:
    def __init__(self, nc, name, ncols):
        self.t = nc.alloc_sbuf_tensor(name, [128, ncols], BF16)
        self.ncols = ncols
        self.regions = {}

    def view(self, name, off, ncols, dtype, shape):
        assert name not in self.regions
        assert off + ncols <= self.ncols, (name, off, ncols)
        self.regions[name] = (off, ncols)
        ap = self.t[:, off:off + ncols]
        if dtype != BF16:
            ap = ap.bitcast(dtype)
        free = shape[1:]
        n = 1
        for xx in free:
            n *= xx
        assert n == ap.shape[1], (name, n, ap.shape)
        if len(free) > 1:
            subs = " ".join(f"f{i}" for i in range(len(free)))
            kw = {f"f{i}": free[i] for i in range(len(free))}
            ap = ap.rearrange(f"p ({subs}) -> p {subs}", **kw)
        if shape[0] != 128:
            ap = ap[0:shape[0]]
        return ap

    def overlaps(self):
        names = list(self.regions)
        out = {}
        for a in names:
            oa, na = self.regions[a]
            out[a] = tuple(
                b for b in names
                if not (self.regions[b][0] + self.regions[b][1] <= oa
                        or oa + na <= self.regions[b][0]))
        return out


def build_program():
    nc = bass.Bass("TRN2", target_bir_lowering=False, debug=False)

    x = nc.dram_tensor("x", [NSEQ, T, D], F32, kind="ExternalInput").ap()
    n1w = nc.dram_tensor("norm1_w", [D], F32, kind="ExternalInput").ap()
    n2w = nc.dram_tensor("norm2_w", [D], F32, kind="ExternalInput").ap()
    g_wq = nc.dram_tensor("g_wq", [D, D], F32, kind="ExternalInput").ap()
    g_kvd = nc.dram_tensor("g_kv_down", [D, LAT], F32, kind="ExternalInput").ap()
    g_kup = nc.dram_tensor("g_k_up", [LAT, LAT], F32, kind="ExternalInput").ap()
    g_vup = nc.dram_tensor("g_v_up", [LAT, LAT], F32, kind="ExternalInput").ap()
    g_wo = nc.dram_tensor("g_wo", [D, D], F32, kind="ExternalInput").ap()
    l_wq = nc.dram_tensor("l_wq", [D, D], F32, kind="ExternalInput").ap()
    l_wk = nc.dram_tensor("l_wk", [D, LAT], F32, kind="ExternalInput").ap()
    l_wv = nc.dram_tensor("l_wv", [D, LAT], F32, kind="ExternalInput").ap()
    l_wo = nc.dram_tensor("l_wo", [D, D], F32, kind="ExternalInput").ap()
    W_fc = nc.dram_tensor("W_fc", [F, D], F32, kind="ExternalInput").ap()
    b_fc = nc.dram_tensor("b_fc", [F], F32, kind="ExternalInput").ap()
    A_fc = nc.dram_tensor("A_fc", [16, D], F32, kind="ExternalInput").ap()
    B_fc = nc.dram_tensor("B_fc", [F, 16], F32, kind="ExternalInput").ap()
    W_pj = nc.dram_tensor("W_proj", [D, F], F32, kind="ExternalInput").ap()
    b_pj = nc.dram_tensor("b_proj", [D], F32, kind="ExternalInput").ap()
    A_pj = nc.dram_tensor("A_proj", [16, F], F32, kind="ExternalInput").ap()
    B_pj = nc.dram_tensor("B_proj", [D, 16], F32, kind="ExternalInput").ap()
    out = nc.dram_tensor("out", [NSEQ, T, D], F32, kind="ExternalOutput").ap()

    sk = "ExternalOutput" if DEBUG_TAPS else "Internal"
    xnd = nc.dram_tensor("xnd", [NSEQ * T, D], BF16, kind=sk).ap()
    s1d = nc.dram_tensor("s1d", [NSEQ * T, D], BF16, kind=sk).ap()
    sdd = nc.dram_tensor("sdd", [NSEQ * T, D], BF16, kind=sk).ap()
    h2d = nc.dram_tensor("h2d", [F, NSEQ * T], BF16, kind=sk).ap()
    h2dv = h2d.rearrange("(fk p) tok -> p fk tok", fk=32, p=128)
    wqfcd = nc.dram_tensor("wqfcd", [F, D], BF16, kind=sk).ap()
    wqpjd = nc.dram_tensor("wqpjd", [D, F], BF16, kind=sk).ap()
    recd = nc.dram_tensor("recd", [2, 2048], BF16, kind="Internal").ap()
    dbg1 = nc.dram_tensor("dbg1", [128, F], BF16, kind=sk).ap()
    dbg2 = nc.dram_tensor("dbg2", [17, F], BF16, kind=sk).ap()

    AR = Arena(nc, "arena", 98304)
    V = AR.view
    wq1 = [V(f"wq1_{k}", 1024 * k, 1024, BF16, [128, 1024]) for k in range(8)]
    kvd = [V(f"kvd_{k}", 8192 + 256 * k, 256, BF16, [128, 256]) for k in range(8)]
    kup = [V(f"kup_{k}", 10240 + 256 * k, 256, BF16, [128, 256]) for k in range(2)]
    vup = [V(f"vup_{k}", 10752 + 256 * k, 256, BF16, [128, 256]) for k in range(2)]
    wo1 = [V(f"wo1_{k}", 11264 + 1024 * k, 1024, BF16, [128, 1024]) for k in range(8)]
    wq2 = [V(f"wq2_{k}", 19456 + 1024 * k, 1024, BF16, [128, 1024]) for k in range(8)]
    wk2 = [V(f"wk2_{k}", 27648 + 256 * k, 256, BF16, [128, 256]) for k in range(8)]
    wv2 = [V(f"wv2_{k}", 29696 + 256 * k, 256, BF16, [128, 256]) for k in range(8)]
    wo2 = [V(f"wo2_{k}", 31744 + 1024 * k, 1024, BF16, [128, 1024]) for k in range(8)]
    qwi = [V(f"qwi_{i}", 2048 * i, 2048, F32, [128, 1024]) for i in range(2)]
    qw_y = V("qw_y", 4096, 2048, F32, [128, 1024])
    qw_i = V("qw_i", 6144, 2048, I32, [128, 1024])
    qw_g = V("qw_g", 8192, 2048, F32, [128, 1024])
    qw_o = V("qw_o", 10240, 1024, BF16, [128, 1024])
    wfcT = [V(f"wfcT_{k}", 4096 * k, 4096, BF16, [128, 4096]) for k in range(8)]
    wpjT = [V(f"wpjT_{k}", 1024 * k, 1024, BF16, [128, 1024]) for k in range(32)]

    C0 = 39936
    q1f = V("q1f", C0, 4096, BF16, [128, 8, 512])
    k1f = V("k1f", C0 + 4096, 4096, BF16, [128, 2, 2048])
    vb = V("vb", C0 + 8192, 4160, BF16, [128, 4160])
    attn_m = V("attn_m", C0 + 12352, 4096, BF16, [128, 8, 512])
    latf = V("latf", C0 + 16448, 1024, BF16, [128, 2, 512])
    xnf = V("xnf", C0 + 17472, 4096, BF16, [128, 8, 512])
    ptb = [V(f"ptb_{i}", C0 + 21568 + 512 * i, 512, BF16, [128, 512])
           for i in range(2)]
    qtm = V("qtm", C0 + 22592, 1024, F16, [128, 1024])
    qn_ = V("qn", C0 + 23616, 1024, F16, [128, 1024])
    qrn = V("qrn", 0, 0, F16, [128, 0]) if False else None
    rtmp = V("rtmp", C0 + 24640, 512, F16, [128, 512])
    ktm = V("ktm", C0 + 25152, 256, F16, [128, 256])
    kn_ = V("kn", C0 + 25408, 256, F16, [128, 256])
    krn = None
    recb = V("recb", C0 + 25664, 2048, BF16, [64, 2048])
    rec_sb = V("rec_sb", C0 + 25664, 2048, BF16, [1, 2048])
    sqj = V("sqj", C0 + 27712, 1024, BF16, [128, 1024])
    ld_a = V("ld_a", C0 + 28736, 1024, BF16, [128, 1024])
    ld_b = V("ld_b", C0 + 29760, 1024, BF16, [128, 1024])
    s1b = [V(f"s1b_{i}", C0 + 30784 + 1024 * i, 1024, BF16, [128, 1024])
           for i in range(2)]
    u2all = V("u2all", C0 + 22592, 4096, BF16, [17, 8, 512])
    xnf2 = V("xnf2", C0 + 8192, 4096, BF16, [128, 8, 512])
    h2r = [V(f"h2r_{i}", C0 + 4096 * i, 4096, BF16, [128, 32, 128])
           for i in range(2)]
    h2w = [V(f"h2w_{i}", C0 + 21568 + 512 * i, 512, BF16, [128, 512])
           for i in range(2)]
    h2wp = [V("h2wp_0", C0 + 21568, 1024, BF16, [128, 1024]),
            V("h2wp_1", C0 + 27712, 1024, BF16, [128, 1024])]
    CEND = C0 + 32832

    D0 = CEND
    mA = [V(f"mA_{r}", D0 + 512 * r, 512, BF16, [128, 512]) for r in range(4)]
    mB = [V(f"mB_{r}", D0 + 2048 + 512 * r, 512, BF16, [128, 512])
          for r in range(6)]
    tab = {}
    for i, nm in enumerate(("cos_g", "sin_g", "cos_l", "sin_l")):
        tab[nm] = V(nm, D0 + 5120 + 512 * i, 512, F16, [128, 16, 32])
    id_bf = V("id_bf", D0 + 7168, 128, BF16, [128, 128])
    id_f16 = V("id_f16", D0 + 7296, 128, F16, [128, 128])
    id_f32 = V("id_f32", D0 + 7424, 256, F32, [128, 128])
    ones_bf = V("ones_bf", D0 + 7680, 64, BF16, [1, 64])
    ones_f32 = V("ones_f32", D0 + 7744, 256, F32, [1, 128])
    n12 = V("n12", D0 + 8000, 32, F32, [128, 16])
    r1b = V("r1b", D0 + 8032, 64, F32, [128, 32])
    r2b = V("r2b", D0 + 8096, 64, F32, [128, 32])
    lncol = V("lncol", D0 + 8160, 64, F32, [128, 32])
    scl = V("scl", D0 + 8224, 16, F32, [128, 8])
    bfcT = V("bfcT", D0 + 8240, 4096, BF16, [17, 4096])
    q1wi = [V(f"q1wi_{i}", D0 + 8240 + 2048 * i, 2048, F32, [128, 1024])
            for i in range(2)]
    bpjT = V("bpjT", D0 + 12336, 1024, BF16, [17, 1024])
    afcT = V("afcT", D0 + 13360, 128, BF16, [128, 128])
    apjT = V("apjT", D0 + 13488, 512, BF16, [128, 512])
    up1 = V("up1", D0 + 14000, 512, BF16, [17, 512])
    cent = V("cent", D0 + 14512, 2048, F32, [128, 1024])
    qtm2 = V("qtm2", D0 + 14512, 1024, F16, [128, 1024])
    ktm2 = V("ktm2", D0 + 15536, 256, F16, [128, 256])
    gbuf = V("gbuf", D0 + 14512, 2048, F32, [128, 1024])
    apstg = V("apstg", D0 + 14512, 1024, BF16, [16, 1024])
    xb = [V(f"xb_{i}", D0 + 16560 + 2048 * i, 2048, F32, [128, 1024])
          for i in range(2)]
    lnrow = V("lnrow", D0 + 16560, 4096, F32, [1, 2048])
    wtmp = V("wtmp", D0 + 20656, 2048, F32, [128, 1024])
    tb = [V(f"tb_{i}", D0 + 16560 + 1024 * i, 1024, F32, [128, 512])
          for i in range(4)]
    tbi = V("tbi", D0 + 20656, 1024, I32, [128, 512])
    invb = V("invb", D0 + 22704, 64, F32, [128, 32])
    epsc = V("epsc", D0 + 22768, 16, F32, [128, 8])
    D0X = D0 + 22784
    ptb += [V(f"ptb_{i + 2}", D0X + 512 * i, 512, BF16, [128, 512])
            for i in range(2)]
    ptbp = [V("ptbA", C0 + 21568, 1024, BF16, [128, 2, 512]),
            V("ptbB", D0X, 1024, BF16, [128, 2, 512])]
    ptbp2 = [V("ptbA2", C0 + 22592, 1024, BF16, [128, 2, 512]),
             V("ptbB2", C0 + 23616, 1024, BF16, [128, 2, 512])]
    lnc_q = V("lnc_q", D0X + 1024, 40, F32, [128, 20])
    lnc_k = V("lnc_k", D0X + 1064, 16, F32, [128, 8])
    qrn = V("qrn", D0X + 1080, 1024, F16, [128, 1024])
    krn = V("krn", D0X + 2104, 256, F16, [128, 256])
    TOT = D0X + 2360
    assert TOT <= 98304, TOT
    qrn2 = V("qrn2", C0 + 12352, 1024, F16, [128, 1024])
    krn2 = V("krn2", C0 + 13376, 256, F16, [128, 256])
    rtmpk = V("rtmpk", TOT, 128, F16, [128, 128])
    assert TOT + 128 <= 98304

    ps = nc.alloc_psum_tensor("ps", [128, 4096], F32)
    ps16 = ps[:].bitcast(F16)
    psbf = ps[:].bitcast(BF16)

    def bank(i, rows=128):
        return ps[0:rows, 512 * i:512 * (i + 1)]

    with nc.Block() as blk:
        p = Prog(nc, blk)
        p.overlap = AR.overlaps()
        for name, sec, step in [("pe", blk.tensor, 1), ("act", blk.scalar, 1),
                                ("dve", blk.vector, 1), ("pool", blk.gpsimd, 1),
                                ("sd", blk.sync, 16), ("gd", blk.gpsimd, 16),
                                ("ad", blk.scalar, 16)]:
            p.engine(name, sec, step)
        E = p.emit

        def PB(i):
            return ("pb", i)

        def act_set_load(set_id):
            def body(s):
                s.add_instruction(mybir.InstLoadActFuncSet(
                    name=nc.get_next_instruction_name(),
                    act_func_set_id=set_id, ins=[], outs=[]))
            p.eng["act"]["sec"](body)

        for idv, dtv in ((id_bf, "id_bf"), (id_f16, "id_f16"), (id_f32, "id_f32")):
            E("dve", lambda v, idv=idv: v.memset(idv, 1.0), writes=[dtv])
            E("pool", lambda g, idv=idv: g.affine_select(
                out=idv, in_=idv, pattern=[[1, 128]], compare_op=ALU.is_equal,
                fill=0.0, base=0, channel_multiplier=-1),
                reads=[dtv], writes=[dtv])
        E("dve", lambda v: v.memset(ones_bf, 1.0), writes=["ones_bf"])
        E("dve", lambda v: v.memset(epsc[:, 0:1], EPS_LN), writes=["epsc"])
        E("dve", lambda v: v.memset(epsc[:, 1:2], EPS_RMS), reads=["epsc"], writes=["epsc"])
        E("dve", lambda v: v.memset(epsc[:, 2:3], 0.5), reads=["epsc"], writes=["epsc"])
        E("dve", lambda v: v.memset(ones_f32, 1.0), writes=["ones_f32"])
        E("dve", lambda v: v.memset(vb[:, 64::65], 1.0), writes=["vb"])
        for r in range(4):
            E("dve", lambda v, r=r: v.memset(mA[r], 1.0), writes=[f"mA_{r}"])
            E("pool", lambda g, r=r: g.affine_select(
                out=mA[r], in_=mA[r], pattern=[[1, 512]], compare_op=ALU.is_ge,
                fill=0.0, base=-128 * r, channel_multiplier=-1),
                reads=[f"mA_{r}"], writes=[f"mA_{r}"])
        for ri in range(6):
            r = ri - 2
            E("dve", lambda v, ri=ri: v.memset(mB[ri], 1.0), writes=[f"mB_{ri}"])
            E("pool", lambda g, ri=ri, r=r: g.affine_select(
                out=mB[ri], in_=mB[ri], pattern=[[1, 512]], compare_op=ALU.is_ge,
                fill=0.0, base=-128 * r, channel_multiplier=-1),
                reads=[f"mB_{ri}"], writes=[f"mB_{ri}"])
            E("pool", lambda g, ri=ri, r=r: g.affine_select(
                out=mB[ri], in_=mB[ri], pattern=[[-1, 512]], compare_op=ALU.is_ge,
                fill=0.0, base=128 * r + 255, channel_multiplier=1),
                reads=[f"mB_{ri}"], writes=[f"mB_{ri}"])
        E("sd", lambda e: e.dma_start(
            out=n12[:, 0:8], in_=n1w.rearrange("(c p) -> p c", p=128),
            allow_slow_non_contiguous=True), writes=["n12"])
        E("sd", lambda e: e.dma_start(
            out=n12[:, 8:16], in_=n2w.rearrange("(c p) -> p c", p=128),
            allow_slow_non_contiguous=True), reads=["n12"], writes=["n12"])

        for base, cname, sname in ((BASE_G, "cos_g", "sin_g"),
                                   (BASE_L, "cos_l", "sin_l")):
            E("pool", lambda g: g.iota(tb[3][0:1, 0:32], pattern=[[1, 32]], base=0,
                                       channel_multiplier=0,
                                       allow_small_or_imprecise_dtypes=True),
              writes=["tb_3"])
            act_set_load(ACT_COMBO_SET)
            E("act", lambda s, base=base: s.activation(
                tb[3][0:1, 32:64], tb[3][0:1, 0:32], AF.Exp,
                scale=-math.log(base) / 32.0), reads=["tb_3"], writes=["tb_3"])
            E("pe", lambda t: t.matmul(ps[0:128, 0:32], ones_f32, tb[3][0:1, 32:64],
                                       start=True, stop=True),
              reads=["ones_f32", "tb_3"], writes=[PB(0)])
            E("dve", lambda v: v.tensor_copy(invb[:, 0:32], ps[0:128, 0:32]),
              reads=[PB(0)], writes=["invb"])
            E("pool", lambda g: g.iota(tb[0][:], pattern=[[128, 16], [0, 32]],
                                       base=0, channel_multiplier=1,
                                       allow_small_or_imprecise_dtypes=True),
              writes=["tb_0"])
            E("dve", lambda v: v.tensor_tensor(
                tb[1][:].rearrange("p (a b) -> p a b", a=16),
                tb[0][:].rearrange("p (a b) -> p a b", a=16),
                invb[:, 0:32].unsqueeze(1).broadcast_to([128, 16, 32]),
                ALU.mult), reads=["tb_0", "invb"], writes=["tb_1"])
            act_set_load(ACT_TRIG_SET)
            for trig, shift in ((sname, 0.0), (cname, PI / 2.0)):
                E("dve", lambda v, shift=shift: v.tensor_scalar(
                    tb[0][:], tb[1][:], shift, 1.0 / TWO_PI, ALU.add, ALU.mult),
                  reads=["tb_1"], writes=["tb_0"])
                E("dve", lambda v: v.tensor_copy(tbi[:], tb[0][:]),
                  reads=["tb_0"], writes=["tbi"])
                E("dve", lambda v, shift=shift: v.tensor_scalar(
                    tb[2][:], tb[1][:], shift, None, ALU.add),
                  reads=["tb_1"], writes=["tb_2"])
                E("dve", lambda v: v.scalar_tensor_tensor(
                    tb[2][:], tbi[:], -TWO_PI, tb[2][:], ALU.mult, ALU.add),
                  reads=["tbi", "tb_2"], writes=["tb_2"])
                for cmp_op, thr, fix in ((ALU.is_gt, PI, -TWO_PI),
                                         (ALU.is_lt, -PI, TWO_PI)):
                    E("dve",
                      lambda v, cmp_op=cmp_op, thr=thr: v.tensor_single_scalar(
                          tb[0][:], tb[2][:], thr, cmp_op),
                      reads=["tb_2"], writes=["tb_0"])
                    E("dve", lambda v, fix=fix: v.scalar_tensor_tensor(
                        tb[2][:], tb[0][:], fix, tb[2][:], ALU.mult, ALU.add),
                      reads=["tb_0", "tb_2"], writes=["tb_2"])
                E("dve", lambda v: v.tensor_scalar(
                    tb[2][:], tb[2][:], PI, -PI, ALU.min, ALU.max),
                  reads=["tb_2"], writes=["tb_2"])
                E("act", lambda s, trig=trig: s.activation(
                    tab[trig].rearrange("p a b -> p (a b)"), tb[2][:], AF.Sin),
                  reads=["tb_2"], writes=[trig])

        act_set_load(ACT_COMBO_SET)

        def perm_in(ap2d):
            return ap2d.rearrange("p (h i two) -> p h two i", two=2, i=32)

        def perm_q_pairs(src2d, dst2d):
            src6 = src2d.rearrange("p (a b c d e) -> p a b c d e",
                                   a=2, b=2, c=4, d=32, e=2)
            dst6 = dst2d.rearrange("p (a c b e d) -> p a c b e d",
                                   a=2, c=4, b=2, e=2, d=32)
            pairs = []
            for a in range(2):
                for b in range(2):
                    inp = src6[:, a, b].transpose([0, 1, 3, 2])
                    outp = dst6[:, a, :, b]
                    pairs.append((inp, outp))
            return pairs

        def prep_tile(dst, src, k, n_out, fold_col, permute, tag):
            E("ad", lambda e, k=k: e.dma_start(
                out=wtmp[0:128, 0:n_out], in_=src[128 * k:128 * (k + 1), :]),
              reads=["wtmp"], writes=["wtmp"])
            inv = wtmp[0:128, 0:n_out]
            col = (n12[:, fold_col + k:fold_col + k + 1]
                   if fold_col is not None else None)
            if permute == "q":
                for inp, outp in perm_q_pairs(inv, dst[k][:]):
                    E("dve", lambda v, inp=inp, outp=outp, col=col:
                      v.tensor_scalar(outp, inp, col, None, ALU.mult),
                      reads=["wtmp", "n12"], writes=[f"{tag}_{k}"])
                return
            if permute:
                inv = perm_in(inv)
            if col is not None:
                E("dve", lambda v, inv=inv, col=col: v.tensor_scalar(
                    dst[k][:], inv, col, None, ALU.mult),
                  reads=["wtmp", "n12"], writes=[f"{tag}_{k}"])
            else:
                E("dve", lambda v, inv=inv: v.tensor_copy(dst[k][:], inv),
                  reads=["wtmp"], writes=[f"{tag}_{k}"])

        w1_q = []
        for k in range(8):
            w1_q.append(lambda k=k: prep_tile(wq1, g_wq, k, 1024, 0, "q", "wq1"))
        for k in range(8):
            w1_q.append(lambda k=k: prep_tile(kvd, g_kvd, k, 256, 0, False, "kvd"))
        for k in range(2):
            def kup_tile(k=k):
                E("ad", lambda e: e.dma_start(
                    out=wtmp[0:128, 0:256], in_=g_kup[128 * k:128 * (k + 1), :]),
                  reads=["wtmp"], writes=["wtmp"])
                E("dve", lambda v: v.tensor_copy(
                    kup[k][:], perm_in(wtmp[0:128, 0:256])),
                  reads=["wtmp"], writes=[f"kup_{k}"])
            w1_q.append(kup_tile)

        w2_q = []
        for k in range(8):
            w2_q.append(lambda k=k: prep_tile(wq2, l_wq, k, 1024, 8, "q", "wq2"))
        for k in range(8):
            w2_q.append(lambda k=k: prep_tile(wk2, l_wk, k, 256, 8, True, "wk2"))
        for k in range(8):
            w2_q.append(lambda k=k: prep_tile(wv2, l_wv, k, 256, 8, False, "wv2"))

        lora_q = []

        def lp_afc():
            E("ad", lambda e: e.dma_start(out=wtmp[0:16, 0:1024], in_=A_fc),
              reads=["wtmp"], writes=["wtmp"])
            E("dve", lambda v: v.tensor_copy(sqj[0:16, :], wtmp[0:16, 0:1024]),
              reads=["wtmp"], writes=["sqj"])
            for k in range(8):
                E("pe", lambda t, k=k: t.transpose(
                    psbf[0:128, 7168 + 16 * k:7168 + 16 * (k + 1)],
                    sqj[0:16, 128 * k:128 * (k + 1)], id_bf[0:16, 0:16]),
                  reads=["sqj", "id_bf"], inc=(k == 7),
                  writes=[PB(7)])
            E("dve", lambda v: v.tensor_copy(afcT[:], psbf[0:128, 7168:7296]),
              reads=[PB(7)], writes=["afcT"])
        lora_q.append(lp_afc)

        for q_ in range(4):
            def lp_apj(q_=q_):
                E("ad", lambda e: e.dma_start(
                    out=wtmp[0:16, 0:1024], in_=A_pj[:, 1024 * q_:1024 * (q_ + 1)]),
                  reads=["wtmp"], writes=["wtmp"])
                E("dve", lambda v: v.tensor_copy(apstg[0:16, :],
                                                 wtmp[0:16, 0:1024]),
                  reads=["wtmp"], writes=["apstg"])
                for j in range(8):
                    fk = 8 * q_ + j
                    E("pe", lambda t, j=j, fk=fk: t.transpose(
                        psbf[0:128, 7168 + 16 * j:7168 + 16 * (j + 1)],
                        apstg[0:16, 128 * j:128 * (j + 1)], id_bf[0:16, 0:16]),
                      reads=["apstg", "id_bf"], inc=(j == 7),
                      writes=[PB(7)])
                E("dve", lambda v, q_=q_: v.tensor_copy(
                    apjT[:, 128 * q_:128 * (q_ + 1)], psbf[0:128, 7168:7296]),
                  reads=[PB(7)], writes=["apjT"])
            lora_q.append(lp_apj)

        for k in range(32):
            def lp_bfc(k=k):
                E("ad", lambda e: e.dma_start(
                    out=wtmp[0:128, 0:16], in_=B_fc[128 * k:128 * (k + 1), :]),
                  reads=["wtmp"], writes=["wtmp"])
                E("dve", lambda v: v.tensor_copy(sqj[0:128, 0:16],
                                                 wtmp[0:128, 0:16]),
                  reads=["wtmp"], writes=["sqj"])
                E("pe", lambda t: t.transpose(
                    psbf[0:16, 7168:7296], sqj[0:128, 0:16], id_bf),
                  reads=["sqj", "id_bf"], writes=[PB(7)])
                E("dve", lambda v: v.tensor_copy(
                    bfcT[0:16, 128 * k:128 * (k + 1)], psbf[0:16, 7168:7296]),
                  reads=[PB(7)], writes=["bfcT"])
            lora_q.append(lp_bfc)
        for k in range(8):
            def lp_bpj(k=k):
                E("ad", lambda e: e.dma_start(
                    out=wtmp[0:128, 0:16], in_=B_pj[128 * k:128 * (k + 1), :]),
                  reads=["wtmp"], writes=["wtmp"])
                E("dve", lambda v: v.tensor_copy(sqj[0:128, 0:16],
                                                 wtmp[0:128, 0:16]),
                  reads=["wtmp"], writes=["sqj"])
                E("pe", lambda t: t.transpose(
                    psbf[0:16, 7168:7296], sqj[0:128, 0:16], id_bf),
                  reads=["sqj", "id_bf"], writes=[PB(7)])
                E("dve", lambda v: v.tensor_copy(
                    bpjT[0:16, 128 * k:128 * (k + 1)], psbf[0:16, 7168:7296]),
                  reads=[PB(7)], writes=["bpjT"])
            lora_q.append(lp_bpj)

        def lp_bias():
            E("gd", lambda e: e.dma_start(out=bfcT[16:17, :], in_=b_fc[None, :]),
              writes=["bfcT"])
            E("gd", lambda e: e.dma_start(out=bpjT[16:17, :], in_=b_pj[None, :]),
              writes=["bpjT"])
        lora_q.append(lp_bias)

        q1_q = []
        W_pj_t = W_pj.rearrange("d (a b) -> (d a) b", a=4)

        def qp1_tile(W, i, maxc, tag, first):
            b = i % 2
            E(("gd", "ad")[i % 2],
              lambda e, i=i, W=W, b=b: e.dma_start(
                  out=q1wi[b][:], in_=W[128 * i:128 * (i + 1), :]),
              reads=[f"q1wi_{b}"], writes=[f"q1wi_{b}"])
            E("dve", lambda v, b=b: v.tensor_reduce(
                lncol[:, 20:21], q1wi[b][:], AXX, ALU.max,
                apply_absolute_value=True),
              reads=[f"q1wi_{b}"], writes=["lncol"])
            if first:
                E("dve", lambda v, maxc=maxc: v.tensor_copy(maxc, lncol[:, 20:21]),
                  reads=["lncol"], writes=[tag])
            else:
                E("dve", lambda v, maxc=maxc: v.tensor_tensor(
                    maxc, maxc, lncol[:, 20:21], ALU.max),
                  reads=["lncol", tag], writes=[tag])

        for i in range(32):
            q1_q.append(lambda i=i: qp1_tile(W_fc, i, lncol[:, 16:17], "mx_fc",
                                             i == 0))
        for i in range(32):
            q1_q.append(lambda i=i: qp1_tile(W_pj_t, i, lncol[:, 17:18], "mx_pj",
                                             i == 0))

        def qscale(maxc, tag, wi):
            E("pe", lambda t, maxc=maxc: t.transpose(ps[0:1, 3584:3712], maxc,
                                                     id_f32),
              reads=[tag, "id_f32"], writes=[PB(7)])
            E("dve", lambda v: v.tensor_reduce(
                lncol[0:1, 24:25], ps[0:1, 3584:3712], AXX, ALU.max),
              reads=[PB(7)], writes=["lncol"])
            E("dve", lambda v: v.reciprocal(lncol[0:1, 25:26], lncol[0:1, 24:25]),
              reads=["lncol"], writes=["lncol"])
            E("dve", lambda v: v.tensor_scalar(
                lncol[0:1, 26:27], lncol[0:1, 24:25], 1.0 / 127.0, None, ALU.mult),
              reads=["lncol"], writes=["lncol"])
            E("dve", lambda v: v.tensor_scalar(
                lncol[0:1, 27:28], lncol[0:1, 25:26], 127.0, None, ALU.mult),
              reads=["lncol"], writes=["lncol"])
            E("pe", lambda t: t.matmul(ps[0:128, 3584:3586], ones_f32,
                                       lncol[0:1, 26:28], start=True, stop=True),
              reads=["lncol", "ones_f32"], writes=[PB(7)])
            E("dve", lambda v, wi=wi: v.tensor_copy(
                scl[:, 2 * wi:2 * wi + 2], ps[0:128, 3584:3586]),
              reads=[PB(7)], writes=["scl"])

        q1_q.append(lambda: qscale(lncol[:, 16:17], "mx_fc", 0))
        q1_q.append(lambda: qscale(lncol[:, 17:18], "mx_pj", 1))

        q2_q = []
        wqpjd_t = wqpjd.rearrange("d (a b) -> (d a) b", a=4)

        def qp2_tile(W, Wdst, i, wi, tag):
            b = i % 2
            icol = scl[:, 2 * wi + 1:2 * wi + 2]
            scol = scl[:, 2 * wi:2 * wi + 1]
            E(("gd", "ad")[i % 2],
              lambda e, i=i, W=W, b=b: e.dma_start(
                  out=qwi[b][:], in_=W[128 * i:128 * (i + 1), :]),
              reads=[f"qwi_{b}"], writes=[f"qwi_{b}"])
            E("act", lambda s_, b=b, icol=icol: s_.activation(
                qw_y[:], qwi[b][:], AF.Identity, scale=icol,
                bias=epsc[:, 2:3]),
              reads=[f"qwi_{b}", "scl", "epsc"], writes=["qw_y"])
            E("dve", lambda v: v.tensor_copy(qw_i[:], qw_y[:]),
              reads=["qw_y"], writes=["qw_i"])
            E("dve", lambda v: v.scalar_tensor_tensor(
                qw_g[:], qw_i[:], 1.0, qw_y[:], ALU.bypass, ALU.is_gt),
              reads=["qw_i", "qw_y"], writes=["qw_g"])
            E("dve", lambda v: v.scalar_tensor_tensor(
                qw_g[:], qw_g[:], -1.0, qw_i[:], ALU.mult, ALU.add),
              reads=["qw_g", "qw_i"], writes=["qw_g"])
            E("dve", lambda v, scol=scol: v.tensor_scalar(
                qw_o[:], qw_g[:], scol, None, ALU.mult),
              reads=["qw_g", "scl"], writes=["qw_o"])
            E(("ad", "gd")[i % 2],
              lambda e, i=i, Wdst=Wdst: e.dma_start(
                  out=Wdst[128 * i:128 * (i + 1), :], in_=qw_o[:]),
              reads=["qw_o"], writes=[(tag, i)])

        for i in range(32):
            q2_q.append(lambda i=i: qp2_tile(W_fc, wqfcd, i, 0, "wqfc"))
        for i in range(32):
            q2_q.append(lambda i=i: qp2_tile(W_pj_t, wqpjd_t, i, 1, "wqpj"))

        def drain(q, n):
            for _ in range(min(n, len(q))):
                q.pop(0)()

        def rsqrt_cols(src_col, dst_col, n_mean, eps_col, wkey):
            E("act", lambda s: s.activation(lncol[:, 12:13], src_col, AF.Ln,
                                            scale=1.0 / n_mean, bias=eps_col),
              reads=["lncol", "epsc"], writes=["lncol"])
            E("act", lambda s: s.activation(dst_col, lncol[:, 12:13], AF.Exp,
                                            scale=-0.5),
              reads=["lncol"], writes=[wkey])

        def ln_chunk(s, c):
            i = c % 2
            E("sd", lambda e: e.dma_start(
                out=xb[i][:], in_=x[s, 128 * c:128 * (c + 1), :]),
              reads=[f"xb_{i}"], writes=[f"xb_{i}"])
            E("dve", lambda v: v.tensor_reduce(lncol[:, 0:1], xb[i][:], AXX,
                                               ALU.add),
              reads=[f"xb_{i}"], writes=["lncol"])
            E("dve", lambda v: v.tensor_scalar(lncol[:, 1:2], lncol[:, 0:1],
                                               1.0 / D, None, ALU.mult),
              reads=["lncol"], writes=["lncol"])
            E("dve", lambda v: v.tensor_scalar(cent[:], xb[i][:], lncol[:, 1:2],
                                               None, ALU.subtract),
              reads=[f"xb_{i}", "lncol"], writes=["cent"])
            E("dve", lambda v: v.scalar_tensor_tensor(
                sqj[:], cent[:], 1.0, cent[:], ALU.bypass, ALU.mult,
                accum_out=lncol[:, 2:3]),
              reads=["cent"], writes=["sqj", "lncol"])
            rsqrt_cols(lncol[:, 2:3], lncol[:, 4:5], D, epsc[:, 0:1], "lncol")
            xnb = s1b[i]
            E("dve", lambda v, xnb=xnb: v.tensor_scalar(
                xnb[:], cent[:], lncol[:, 4:5], None, ALU.mult),
              reads=["cent", "lncol"], writes=[f"s1b_{i}"])
            E("dve", lambda v, xnb=xnb: v.scalar_tensor_tensor(
                sqj[:], xnb[:], 1.0, xnb[:], ALU.bypass, ALU.mult,
                accum_out=lncol[:, 5:6]),
              reads=[f"s1b_{i}"], writes=["sqj", "lncol"])
            rsqrt_cols(lncol[:, 5:6], r1b[:, 16 * s + c:16 * s + c + 1], D,
                       epsc[:, 1:2], "r1b")
            E("ad", lambda e, xnb=xnb, s=s, c=c: e.dma_start(
                out=xnd[2048 * s + 128 * c:2048 * s + 128 * (c + 1), :],
                in_=xnb[:]),
              reads=[f"s1b_{i}"], writes=[("xnd", s, c)])

        def rope_qknorm(src, n_cols, cname, sname, rk, dst, wk, cc,
                        eng="dve", rbuf=None, rbk="rtmp"):
            nh = n_cols // 64
            lc, lck = (lnc_q, "lnc_q") if n_cols == 1024 else (lnc_k, "lnc_k")
            E("dve", lambda v: v.tensor_tensor(sqj[:, 0:n_cols], src, src,
                                               ALU.mult),
              reads=[rk], writes=["sqj"])
            E("dve", lambda v: v.tensor_reduce(
                lc[:, 0:nh],
                sqj[:, 0:n_cols].rearrange("p (h i) -> p h i", i=64),
                AXX, ALU.add),
              reads=["sqj"], writes=[lck])
            E("act", lambda s: s.activation(lc[:, 0:nh], lc[:, 0:nh], AF.Ln,
                                            scale=1.0 / 64.0, bias=epsc[:, 1:2]),
              reads=[lck, "epsc"], writes=[lck])
            E("act", lambda s: s.activation(lc[:, 0:nh], lc[:, 0:nh],
                                            AF.Exp, scale=-0.5),
              reads=[lck], writes=[lck])
            nbuf = qn_ if n_cols == 1024 else kn_
            nk = "qn" if n_cols == 1024 else "kn"
            E("dve", lambda v, nbuf=nbuf: v.tensor_tensor(
                nbuf[:, 0:n_cols].rearrange("p (h i) -> p h i", i=64),
                src.rearrange("p (h i) -> p h i", i=64),
                lc[:, 0:nh].unsqueeze(2).broadcast_to([128, nh, 64]),
                ALU.mult),
              reads=[rk, lck], writes=[nk])
            nb = nbuf[:, 0:n_cols].rearrange("p (h two i) -> p h two i",
                                             two=2, i=32)
            db = dst.rearrange("p (h two i) -> p h two i", two=2, i=32)
            e_, o_ = nb[:, :, 0, :], nb[:, :, 1, :]
            de, do = db[:, :, 0, :], db[:, :, 1, :]
            cosb = tab[cname][:, cc, :].unsqueeze(1).broadcast_to([128, nh, 32])
            sinb = tab[sname][:, cc, :].unsqueeze(1).broadcast_to([128, nh, 32])
            if rbuf is None:
                rbuf = rtmp
            half = rbuf[:, 0:n_cols // 2].rearrange("p (h i) -> p h i", i=32)
            E(eng, lambda v: v.tensor_tensor(half, o_, sinb, ALU.mult),
              reads=[nk, sname], writes=[rbk])
            E(eng, lambda v: v.tensor_tensor(de, e_, cosb, ALU.mult),
              reads=[nk, cname], writes=[wk])
            E(eng, lambda v: v.tensor_tensor(de, de, half, ALU.subtract),
              reads=[rbk, wk], writes=[wk])
            E(eng, lambda v: v.tensor_tensor(half, e_, sinb, ALU.mult),
              reads=[nk, sname], writes=[rbk])
            E(eng, lambda v: v.tensor_tensor(do, o_, cosb, ALU.mult),
              reads=[nk, cname], writes=[wk])
            E(eng, lambda v: v.tensor_tensor(do, do, half, ALU.add),
              reads=[rbk, wk], writes=[wk])

        def attention(s, aidx, hook_q=None):
            src_d = xnd if aidx == 0 else s1d
            srctag = "xnd" if aidx == 0 else "s1d"
            rbuf, rtag = (r1b, "r1b") if aidx == 0 else (r2b, "r2b")
            cname = "cos_g" if aidx == 0 else "cos_l"
            sname = "sin_g" if aidx == 0 else "sin_l"
            qw, qtag = (wq1, "wq1") if aidx == 0 else (wq2, "wq2")
            wo_w, wo_tag = (wo1, "wo1") if aidx == 0 else (wo2, "wo2")
            for sp in range(NSP):
                row0 = 2048 * s + 512 * sp
                span_keys = [(srctag, s, 4 * sp + j) for j in range(4)]
                for dk in range(8):
                    E("sd",
                      lambda e, dk=dk: e.dma_start(
                          out=xnf[:, dk, :],
                          in_=src_d[row0:row0 + 512, 128 * dk:128 * (dk + 1)],
                          transpose=True),
                      reads=span_keys + ["xnf"], writes=["xnf"])
                if aidx == 0 and SUBPH >= 2:
                    for lc in range(2):
                        for k in range(8):
                            E("pe", lambda t, k=k, lc=lc: t.matmul(
                                bank(lc), kvd[k][:, 128 * lc:128 * (lc + 1)],
                                xnf[:, k, :],
                                start=(k == 0), stop=(k == 7)),
                              reads=["xnf", f"kvd_{k}"], inc=(k == 7),
                              writes=[PB(lc)])
                        E("act", lambda s_, lc=lc: s_.activation(
                            latf[:, lc, :], bank(lc), AF.Copy),
                          reads=[PB(lc)], writes=["latf"])
                def prod_mm(tci):
                    c = 4 * sp + tci
                    qtb, qtk = (qtm, "qtm") if tci % 2 == 0 else (qtm2, "qtm2")
                    ktb, ktk = (ktm, "ktm") if tci % 2 == 0 else (ktm2, "ktm2")
                    rcol = rbuf[:, 16 * s + c:16 * s + c + 1]
                    for half in range(2):
                        for k in range(8):
                            E("pe", lambda t, k=k, half=half, tci=tci: t.matmul(
                                bank(2 + half),
                                xnf[:, k, 128 * tci:128 * (tci + 1)],
                                qw[k][:, 512 * half:512 * (half + 1)],
                                start=(k == 0), stop=(k == 7)),
                              reads=["xnf", f"{qtag}_{k}"], inc=(k == 7),
                              writes=[PB(2 + half)])
                    E("act", lambda s_, rcol=rcol, qtb=qtb: s_.activation(
                        qtb[:], ps[0:128, 1024:2048], AF.Copy, scale=rcol),
                      reads=[PB(2), PB(3), rtag], writes=[qtk])
                    if aidx == 0:
                        for lc in range(2):
                            E("pe", lambda t, lc=lc, tci=tci: t.matmul(
                                bank(0)[:, 0:256],
                                latf[:, lc, 128 * tci:128 * (tci + 1)],
                                kup[lc][:], start=(lc == 0), stop=(lc == 1)),
                              reads=["latf", f"kup_{lc}"], inc=(lc == 1),
                              writes=[PB(0)])
                    else:
                        for k in range(8):
                            E("pe", lambda t, k=k, tci=tci: t.matmul(
                                bank(0)[:, 0:256],
                                xnf[:, k, 128 * tci:128 * (tci + 1)],
                                wk2[k][:], start=(k == 0), stop=(k == 7)),
                              reads=["xnf", f"wk2_{k}"], inc=(k == 7),
                              writes=[PB(0)])
                    E("act", lambda s_, rcol=rcol, ktb=ktb: s_.activation(
                        ktb[:], bank(0)[:, 0:256], AF.Copy, scale=rcol),
                      reads=[PB(0), rtag], writes=[ktk])
                    if aidx == 0:
                        for lc in range(2):
                            E("pe", lambda t, lc=lc, tci=tci: t.matmul(
                                bank(1)[:, 0:256],
                                latf[:, lc, 128 * tci:128 * (tci + 1)],
                                vup[lc][:], start=(lc == 0), stop=(lc == 1)),
                              reads=["latf", f"vup_{lc}"], inc=(lc == 1),
                              writes=[PB(1)])
                    else:
                        for k in range(8):
                            E("pe", lambda t, k=k, tci=tci: t.matmul(
                                bank(1)[:, 0:256],
                                xnf[:, k, 128 * tci:128 * (tci + 1)],
                                wv2[k][:], start=(k == 0), stop=(k == 7)),
                              reads=["xnf", f"wv2_{k}"], inc=(k == 7),
                              writes=[PB(1)])
                    E("act", lambda s_, rcol=rcol, c=c: s_.activation(
                        vb.rearrange("p (g y) -> p g y", g=4)[:, :,
                                                              65 * c:65 * c + 64],
                        bank(1)[:, 0:256].rearrange("p (g y) -> p g y", g=4),
                        AF.Copy, scale=rcol),
                      reads=[PB(1), rtag], writes=["vb"])

                def prod_rope(tci):
                    c = 4 * sp + tci
                    qb_, qk_ = (qrn, "qrn") if tci % 2 == 0 else (qrn2, "qrn2")
                    kb_, kk_ = (krn, "krn") if tci % 2 == 0 else (krn2, "krn2")
                    qtb, qtk = (qtm, "qtm") if tci % 2 == 0 else (qtm2, "qtm2")
                    ktb, ktk = (ktm, "ktm") if tci % 2 == 0 else (ktm2, "ktm2")
                    rope_qknorm(qtb[:], 1024, cname, sname, qtk, qb_[:],
                                qk_, c)
                    rope_qknorm(ktb[:], 256, cname, sname, ktk, kb_[:],
                                kk_, c, eng="pool", rbuf=rtmpk, rbk="rtmpk")

                def prod_tr(tci):
                    c = 4 * sp + tci
                    qb_, qk_ = (qrn, "qrn") if tci % 2 == 0 else (qrn2, "qrn2")
                    kb_, kk_ = (krn, "krn") if tci % 2 == 0 else (krn2, "krn2")
                    for mc in range(8):
                        E("pe", lambda t, mc=mc, qb_=qb_: t.transpose(
                            ps16[0:128, 6144 + 128 * mc:6144 + 128 * (mc + 1)],
                            qb_[:, 128 * mc:128 * (mc + 1)], id_f16),
                          reads=[qk_, "id_f16"], inc=(mc == 7),
                          writes=[PB(6), PB(7)])
                    E("dve", lambda v, tci=tci: v.tensor_copy(
                        q1f[:, :, 128 * tci:128 * (tci + 1)],
                        ps16[0:128, 6144:7168].rearrange("p (a b) -> p a b",
                                                         a=8)),
                      reads=[PB(6), PB(7)], writes=["q1f"])
                    for mc in range(2):
                        E("pe", lambda t, mc=mc, kb_=kb_: t.transpose(
                            ps16[0:128, 6144 + 128 * mc:6144 + 128 * (mc + 1)],
                            kb_[:, 128 * mc:128 * (mc + 1)], id_f16),
                          reads=[kk_, "id_f16"], inc=(mc == 1),
                          writes=[PB(6)])
                    E("dve", lambda v, c=c: v.tensor_copy(
                        k1f[:, :, 128 * c:128 * (c + 1)],
                        ps16[0:128, 6144:6400].rearrange("p (a b) -> p a b",
                                                         a=2)),
                      reads=[PB(6)], writes=["k1f"])

                for tci in range(4):
                    prod_mm(tci)
                    if tci > 1:
                        prod_tr(tci - 2)
                    prod_rope(tci)
                prod_tr(2)
                prod_tr(3)
                for g in range(4):
                    kb = 64 * (g % 2)
                    mck = g // 2
                    if aidx == 0:
                        krange = list(range(0, 4 * sp + 4))
                    else:
                        krange = list(range(max(4 * sp - 2, 0), 4 * sp + 4))
                    nk_ = len(krange)
                    heads = []
                    for hh in range(4):
                        h = 4 * g + hh
                        qb = 64 * ((h // 4) % 2)
                        qmc = (h % 4) + 4 * (h // 8)
                        assert qb == kb
                        heads.append((hh, qmc))

                    def mask_for(kc):
                        if aidx == 0:
                            rr = kc - 4 * sp
                            if rr < 0:
                                return None, None
                            return mA[rr], f"mA_{rr}"
                        rr = kc - 4 * sp + 2
                        return mB[rr], f"mB_{rr}"

                    def pbuf(ki):
                        return (ptbp, ("ptbA", "ptbB")) if ki % 2 == 0 \
                            else (ptbp2, ("ptbA2", "ptbB2"))

                    def emit_qk(kc, pr, kb=kb, mck=mck, heads=heads):
                        for hh, qmc in heads[2 * pr:2 * pr + 2]:
                            xtra = [PB(2), PB(3)] if (pr, hh % 2) == (0, 0) \
                                else []
                            E("pe", lambda t, kc=kc, hh=hh, qmc=qmc: t.matmul(
                                bank(hh),
                                k1f[kb:kb + 64, mck, 128 * kc:128 * (kc + 1)],
                                q1f[kb:kb + 64, qmc, :],
                                start=True, stop=True),
                              reads=["k1f", "q1f"] + xtra, writes=[PB(hh)])

                    def emit_evac(ki, kc, pr, mask_for=mask_for):
                        mask, mkey = mask_for(kc)
                        buf, keys = pbuf(ki)
                        E("act", lambda s_, pr=pr, buf=buf: s_.activation(
                            buf[pr][:].rearrange("p a b -> p (a b)"),
                            ps[0:128, 1024 * pr:1024 * (pr + 1)],
                            AF.Exp, scale=0.125),
                          reads=[PB(2 * pr), PB(2 * pr + 1)],
                          writes=[keys[pr]])
                        if mask is not None:
                            E("dve", lambda v, pr=pr, mask=mask, buf=buf:
                              v.tensor_tensor(
                                  buf[pr][:], buf[pr][:],
                                  mask[:].unsqueeze(1).broadcast_to(
                                      [128, 2, 512]), ALU.mult),
                              reads=[keys[pr], mkey], writes=[keys[pr]])

                    def emit_pv(ki, kc, g=g, heads=heads, nk_=nk_):
                        buf, keys = pbuf(ki)
                        for hh, qmc in heads:
                            xtra = [keys[1]] if hh == 0 else []
                            E("pe", lambda t, kc=kc, hh=hh, ki=ki, buf=buf:
                              t.matmul(
                                  ps[0:65, 2048 + 512 * hh:2048 + 512 * (hh + 1)],
                                  vb[:, 1040 * g + 65 * kc:
                                     1040 * g + 65 * (kc + 1)],
                                  buf[hh // 2][:, hh % 2, :],
                                  start=(ki == 0), stop=(ki == nk_ - 1)),
                              reads=[keys[hh // 2], "vb"] + xtra,
                              inc=(hh == 3), writes=[PB(4 + hh)])

                    for ki, kc in enumerate(krange):
                        emit_qk(kc, 0)
                        emit_evac(ki, kc, 0)
                        emit_qk(kc, 1)
                        emit_evac(ki, kc, 1)
                        if ki > 0:
                            emit_pv(ki - 1, krange[ki - 1])
                    emit_pv(nk_ - 1, krange[nk_ - 1])

                    gp = g % 2
                    E("act", lambda s_: s_.activation(
                        lnrow[:], ps[64:65, 2048:4096], AF.Ln),
                      reads=[PB(4), PB(5), PB(6), PB(7)],
                      writes=["lnrow"])
                    E("act", lambda s_: s_.activation(
                        rec_sb[0:1, 0:2048], lnrow[:], AF.Exp,
                        scale=-1.0),
                      reads=["lnrow", PB(4), PB(5), PB(6), PB(7)],
                      writes=["recb"])
                    E("ad", lambda e, gp=gp: e.dma_start(
                        out=recd[gp:gp + 1, :], in_=rec_sb[0:1, 0:2048]),
                      reads=["recb"], writes=[("recd", gp)])
                    E("gd", lambda e, gp=gp: e.dma_start(
                        out=recb[0:64, 0:2048],
                        in_=recd[gp][None, :].broadcast_to([64, 2048])),
                      reads=[("recd", gp), "recb"], writes=["recb"])
                    for par in range(2):
                        E("dve", lambda v, par=par, g=g: v.tensor_tensor(
                            attn_m[64 * par:64 * (par + 1),
                                   2 * g:2 * g + 2, :],
                            ps[0:64, 2048:4096].rearrange(
                                "p (a c b) -> p a c b", a=2, c=2)[:, :, par, :],
                            recb[0:64, 0:2048].rearrange(
                                "p (a c b) -> p a c b", a=2, c=2)[:, :, par, :],
                            ALU.mult),
                          reads=[PB(4 + par), PB(6 + par), "recb"],
                          writes=["attn_m"])
                for tci in range(4 if SUBPH >= 5 else 0):
                    c = 4 * sp + tci
                    i = tci % 2
                    ldb = ld_a if i == 0 else ld_b
                    ldk = "ld_a" if i == 0 else "ld_b"
                    E("sd", lambda e, c=c, ldb=ldb, s=s: e.dma_start(
                        out=ldb[:],
                        in_=src_d[2048 * s + 128 * c:2048 * s + 128 * (c + 1), :]),
                      reads=[ldk, (srctag, s, c)], writes=[ldk])
                    for half in range(2):
                        for mc in range(8):
                            E("pe", lambda t, mc=mc, half=half, tci=tci: t.matmul(
                                bank(2 + half),
                                attn_m[:, mc, 128 * tci:128 * (tci + 1)],
                                wo_w[mc][:, 512 * half:512 * (half + 1)],
                                start=(mc == 0), stop=(mc == 7)),
                              reads=["attn_m", f"{wo_tag}_{mc}"], inc=(mc == 7),
                              writes=[PB(2 + half)])
                    sb_o = s1b[i]
                    for half in range(2):
                        E("dve", lambda v, half=half, sb_o=sb_o, ldb=ldb:
                          v.scalar_tensor_tensor(
                              sb_o[:, 512 * half:512 * (half + 1)],
                              bank(2 + half), 1.0,
                              ldb[:, 512 * half:512 * (half + 1)],
                              ALU.bypass, ALU.add),
                          reads=[PB(2 + half), ldk], writes=[f"s1b_{i}"])
                    if aidx == 0:
                        E("dve", lambda v, sb_o=sb_o: v.scalar_tensor_tensor(
                            sqj[:], sb_o[:], 1.0, sb_o[:],
                            ALU.bypass, ALU.mult, accum_out=lncol[:, 6:7]),
                          reads=[f"s1b_{i}"], writes=["sqj", "lncol"])
                        rsqrt_cols(lncol[:, 6:7],
                                   r2b[:, 16 * s + c:16 * s + c + 1], D,
                                   epsc[:, 1:2], "r2b")
                        dst, dk_ = s1d, ("s1d", s, c)
                    else:
                        dst, dk_ = sdd, ("sdd", s, c)
                    E("gd", lambda e, sb_o=sb_o, s=s, c=c, dst=dst: e.dma_start(
                        out=dst[2048 * s + 128 * c:2048 * s + 128 * (c + 1), :],
                        in_=sb_o[:]),
                      reads=[f"s1b_{i}"], writes=[dk_])
                    if hook_q is not None:
                        rem = (NSP - sp) * 4 - tci
                        drain(hook_q, (len(hook_q) + rem - 1) // rem)

        def ffn_pass1():
            act_set_load(ACT_GELU_SET)
            for k in range(8):
                E("sd",
                  lambda e, k=k: e.dma_start(
                      out=wfcT[k][:], in_=wqfcd[:, 128 * k:128 * (k + 1)],
                      transpose=True),
                  reads=[("wqfc", i) for i in range(32)] + [f"wfcT_{k}"],
                  writes=[f"wfcT_{k}"])
            def xnf_load(gsp):
                row0 = 512 * gsp
                xb_, xk_ = (xnf, "xnf") if gsp % 2 == 0 else (xnf2, "xnf2")
                span_keys = [("xnd", gsp // 4, 4 * (gsp % 4) + j)
                             for j in range(4)]
                for dk in range(8):
                    E("sd",
                      lambda e, dk=dk, row0=row0, xb_=xb_: e.dma_start(
                          out=xb_[:, dk, :],
                          in_=xnd[row0:row0 + 512, 128 * dk:128 * (dk + 1)],
                          transpose=True),
                      reads=span_keys + [xk_], writes=[xk_])

            xnf_load(0)
            for gsp in range(8):
                if gsp + 1 < 8:
                    xnf_load(gsp + 1)
                row0 = 512 * gsp
                xb_, xk_ = (xnf, "xnf") if gsp % 2 == 0 else (xnf2, "xnf2")
                for k in range(8):
                    E("pe", lambda t, k=k, xb_=xb_: t.matmul(
                        ps[0:16, 3584:4096], afcT[:, 16 * k:16 * (k + 1)],
                        xb_[:, k, :], start=(k == 0), stop=(k == 7)),
                      reads=[xk_, "afcT"], inc=(k == 7),
                      writes=[PB(7)])
                E("dve", lambda v: v.memset(up1[:], 1.0), writes=["up1"])
                E("act", lambda s_: s_.activation(
                    up1[0:16, :], ps[0:16, 3584:4096], AF.Copy, scale=2.0),
                  reads=[PB(7), "up1"], writes=["up1"])
                for fk in range(32):
                    bk = fk % 4
                    for k in range(8):
                        E("pe", lambda t, k=k, fk=fk, bk=bk, xb_=xb_: t.matmul(
                            bank(bk), wfcT[k][:, 128 * fk:128 * (fk + 1)],
                            xb_[:, k, :], start=(k == 0), stop=False),
                          reads=[xk_, f"wfcT_{k}"], inc=False,
                          writes=[("pb", bk)])
                    E("pe", lambda t, fk=fk, bk=bk: t.matmul(
                        bank(bk), bfcT[0:17, 128 * fk:128 * (fk + 1)], up1[:],
                        start=False, stop=True),
                      reads=["bfcT", "up1"], writes=[PB(bk)])
                    if fk % 2 == 0:
                        continue
                    wi = (fk // 2) % 2
                    wkey = f"h2wp_{wi}"
                    bk0 = bk - 1
                    E("act", lambda s_, bk0=bk0, wi=wi: s_.activation(
                        h2wp[wi][:], ps[0:128, 512 * bk0:512 * bk0 + 1024],
                        AF.Gelu),
                      reads=[PB(bk0), PB(bk0 + 1)], writes=[wkey])
                    E("gd", lambda e, fk=fk, wi=wi, row0=row0, gsp=gsp:
                      e.dma_start(
                          out=h2d[128 * (fk - 1):128 * (fk + 1),
                                  row0:row0 + 512].rearrange(
                              "(a p) t -> p a t", a=2),
                          in_=h2wp[wi][:].rearrange("p (a t) -> p a t", a=2)),
                      reads=[wkey],
                      writes=[("h2d", gsp, fk - 1), ("h2d", gsp, fk)])
                    for j in range(2):
                        fj = fk - 1 + j
                        E("pe", lambda t, fj=fj, wi=wi, j=j: t.matmul(
                            ps[0:16, 3072:3584],
                            apjT[:, 16 * fj:16 * (fj + 1)],
                            h2wp[wi][:, 512 * j:512 * (j + 1)],
                            start=(fj == 0), stop=(fj == 31)),
                          reads=[wkey, "apjT"], inc=(fj == 31),
                          writes=[PB(6)])
                E("dve", lambda v, gsp=gsp: v.memset(u2all[:, gsp, :], 1.0),
                  writes=["u2all"])
                E("act", lambda s_, gsp=gsp: s_.activation(
                    u2all[0:16, gsp, :], ps[0:16, 3072:3584], AF.Copy,
                    scale=2.0),
                  reads=[PB(6), "u2all"], writes=["u2all"])

        def ffn_pass2():
            act_set_load(ACT_SILU_SET)
            for fk in range(32):
                E("sd",
                  lambda e, fk=fk: e.dma_start(
                      out=wpjT[fk][:], in_=wqpjd[:, 128 * fk:128 * (fk + 1)],
                      transpose=True),
                  reads=[("wqpj", i) for i in range(32)] + [f"wpjT_{fk}"],
                  writes=[f"wpjT_{fk}"])
            for cg in range(32):
                s, c = cg // 16, cg % 16
                i = cg % 2
                hr, hk = h2r[i], f"h2r_{i}"
                E("sd",
                  lambda e, cg=cg, hr=hr: e.dma_start(
                      out=hr[:],
                      in_=h2dv[:, :, 128 * cg:128 * (cg + 1)]),
                  reads=[("h2d", cg // 4, fk) for fk in range(32)] + [hk],
                  writes=[hk])
                pb0 = 2 * i
                for half in range(2):
                    for fk in range(32):
                        E("pe", lambda t, fk=fk, half=half, hr=hr, pb0=pb0:
                          t.matmul(
                              bank(pb0 + half), hr[:, fk, :],
                              wpjT[fk][:, 512 * half:512 * (half + 1)],
                              start=(fk == 0), stop=False),
                          reads=[hk, f"wpjT_{fk}"], inc=False,
                          writes=[("pb", pb0 + half)])
                    E("pe", lambda t, half=half, cg=cg, pb0=pb0: t.matmul(
                        bank(pb0 + half),
                        u2all[:, cg // 4, 128 * (cg % 4):128 * (cg % 4) + 128],
                        bpjT[0:17, 512 * half:512 * (half + 1)],
                        start=False, stop=True),
                      reads=["u2all", "bpjT"], writes=[PB(pb0 + half)])
                    E("act", lambda s_, half=half, pb0=pb0: s_.activation(
                        gbuf[:, 512 * half:512 * (half + 1)],
                        bank(pb0 + half), AF.Silu),
                      reads=[PB(pb0 + half)], writes=["gbuf"])
                ldb = ld_a if i == 0 else ld_b
                ldk = "ld_a" if i == 0 else "ld_b"
                E("ad", lambda e, s=s, c=c, ldb=ldb: e.dma_start(
                    out=ldb[:],
                    in_=sdd[2048 * s + 128 * c:2048 * s + 128 * (c + 1), :]),
                  reads=[("sdd", s, c), ldk], writes=[ldk])
                E("sd", lambda e, s=s, c=c, i=i: e.dma_start(
                    out=xb[i][:], in_=x[s, 128 * c:128 * (c + 1), :]),
                  reads=[f"xb_{i}"], writes=[f"xb_{i}"])
                E("dve", lambda v, ldb=ldb: v.tensor_tensor(
                    gbuf[:], gbuf[:], ldb[:], ALU.mult),
                  reads=["gbuf", ldk], writes=["gbuf"])
                E("dve", lambda v, i=i: v.tensor_tensor(
                    gbuf[:], gbuf[:], xb[i][:], ALU.add),
                  reads=["gbuf", f"xb_{i}"], writes=["gbuf"])
                E("gd", lambda e, s=s, c=c: e.dma_start(
                    out=out[s, 128 * c:128 * (c + 1), :], in_=gbuf[:]),
                  reads=["gbuf"], writes=[("out", s, c)])

        for k in range(8):
            E("gd", lambda e, k=k: e.dma_start(
                out=wo1[k][:], in_=g_wo[128 * k:128 * (k + 1), :]),
              writes=[f"wo1_{k}"])
        for k in range(2):
            E("gd", lambda e, k=k: e.dma_start(
                out=vup[k][:], in_=g_vup[128 * k:128 * (k + 1), :]),
              writes=[f"vup_{k}"])
        if PHASES >= 1:
            for c in range(NCH):
                ln_chunk(0, c)
                drain(w1_q, 2)
            drain(w1_q, 99)
            for k in range(8):
                E("gd", lambda e, k=k: e.dma_start(
                    out=wo2[k][:], in_=l_wo[128 * k:128 * (k + 1), :]),
                  writes=[f"wo2_{k}"])
        if PHASES >= 2:
            attention(0, 0, hook_q=w2_q + q1_q)
        if PHASES >= 3:
            for c in range(NCH):
                ln_chunk(1, c)
            attention(1, 0, hook_q=lora_q)
        if PHASES >= 4:
            q2_first = q2_q[:len(q2_q) // 2]
            q2_second = q2_q[len(q2_q) // 2:]
            attention(0, 1, hook_q=q2_first)
            attention(1, 1, hook_q=q2_second)
            drain(q2_first, 999)
            drain(q2_second, 999)
        if PHASES >= 5:
            ffn_pass1()
        if DEBUG_TAPS and PHASES >= 5:
            E("sd", lambda e: e.dma_start(out=dbg1, in_=wfcT[0][:]),
              reads=["wfcT_0"], writes=[("dbg1",)])
            E("sd", lambda e: e.dma_start(out=dbg2, in_=bfcT[0:17, :]),
              reads=["bfcT"], writes=[("dbg2",)])
        if PHASES >= 6:
            ffn_pass2()
        p.wait_all("sd")

    return nc


_INPUT_MAP = [
    ("norm1_w", "L0_attn_norm1_w"), ("norm2_w", "L0_attn_norm2_w"),
    ("g_wq", "L0_attn_g_wq"), ("g_kv_down", "L0_attn_g_kv_down"),
    ("g_k_up", "L0_attn_g_k_up"), ("g_v_up", "L0_attn_g_v_up"),
    ("g_wo", "L0_attn_g_wo"), ("l_wq", "L0_attn_l_wq"),
    ("l_wk", "L0_attn_l_wk"), ("l_wv", "L0_attn_l_wv"),
    ("l_wo", "L0_attn_l_wo"), ("W_fc", "L0_ffn_W_fc"),
    ("b_fc", "L0_ffn_b_fc"), ("A_fc", "L0_ffn_A_fc"),
    ("B_fc", "L0_ffn_B_fc"), ("W_proj", "L0_ffn_W_proj"),
    ("b_proj", "L0_ffn_b_proj"), ("A_proj", "L0_ffn_A_proj"),
    ("B_proj", "L0_ffn_B_proj"),
]


def kernel(**inputs):
    nc = build_program()
    xfull = np.ascontiguousarray(np.asarray(inputs["L0_attn_hidden"], np.float32))
    w = {k: np.ascontiguousarray(np.asarray(inputs[v], np.float32))
         for k, v in _INPUT_MAP}
    in_maps = []
    for core in range(8):
        m = {"x": xfull[2 * core:2 * core + 2]}
        m.update(w)
        in_maps.append(m)
    res = run_bass_kernel_spmd(nc, in_maps, core_ids=list(range(8)))
    return np.concatenate([np.asarray(r["out"], np.float32)
                           for r in res.results], axis=0)


if __name__ == "__main__":
    nc = build_program()
    print("build ok")


# version 37 (final)
# speedup vs baseline: 1.0509x; 1.0002x over previous
import math
import numpy as np

import concourse.bass as bass
import concourse.mybir as mybir
from concourse.bass_utils import run_bass_kernel_spmd

F32 = mybir.dt.float32
F16 = mybir.dt.float16
BF16 = mybir.dt.bfloat16
I32 = mybir.dt.int32
AF = mybir.ActivationFunctionType
ALU = mybir.AluOpType
AXX = mybir.AxisListType.X

D = 1024
LAT = 256
T = 2048
NSEQ = 2
F = 4096
NCH = 16
NSP = 4
PI = math.pi
TWO_PI = 2.0 * math.pi
EPS_LN = 1e-5
EPS_RMS = 1e-6
BASE_G = 1000000.0
BASE_L = 10000.0


def _act_set_ids():
    fallback = {"natural_log_exp_and_others": 6, "trig_and_small": 9,
                "gelu_and_others": 10, "silu_and_others": 18}
    try:
        from concourse.hw_specs import get_activation_tables
        names = list(get_activation_tables("TRN2"))
        return {k: names.index(k) for k in fallback}
    except Exception:
        return fallback


_ACT_SETS = _act_set_ids()
ACT_COMBO_SET = _ACT_SETS["natural_log_exp_and_others"]
ACT_TRIG_SET = _ACT_SETS["trig_and_small"]
ACT_GELU_SET = _ACT_SETS["gelu_and_others"]
ACT_SILU_SET = _ACT_SETS["silu_and_others"]

DEBUG_TAPS = False
PHASES = 99
SUBPH = 9


class Prog:

    def __init__(self, nc, blk):
        self.nc, self.blk = nc, blk
        self.eng = {}
        self.reg = {}
        self.overlap = {}
        self.opvc = {}

    WAVE = 12

    def engine(self, name, sec, step):
        sem = self.nc.alloc_semaphore(name=f"sem_{name}")
        self.eng[name] = dict(sem=sem, cnt=0, step=step, sec=sec, pend=[],
                              barriers=[0], bar_cnt=0, vc={})

    def _learn(self, know, pe_, pc_):
        if pc_ > know.get(pe_, -1):
            know[pe_] = pc_
        src = self.opvc.get((pe_, pc_))
        if src:
            for k2, v2 in src.items():
                if v2 > know.get(k2, -1):
                    know[k2] = v2

    def close_wave(self, en):
        e = self.eng[en]
        if e["cnt"] == e["barriers"][-1]:
            return
        val = e["cnt"]

        def body(obj):
            obj.wait_ge(e["sem"], val)

        e["sec"](body)
        e["barriers"].append(val)
        e["bar_cnt"] = 0
        self._learn(e["vc"], en, val)

    def _expand(self, keys):
        out = []
        for k in keys:
            if isinstance(k, str) and k in self.overlap:
                out.extend(self.overlap[k])
            else:
                out.append(k)
        return out

    def _need(self, waits, en, pe, pc):
        if pe == en == "pe":
            return
        if pc > waits.get(pe, -1):
            waits[pe] = pc

    def emit(self, en, f, reads=(), writes=(), inc=True):
        e = self.eng[en]
        reads = self._expand(reads)
        writes = self._expand(writes)
        waits = {}
        for r in reads:
            rec = self.reg.get(r)
            if rec:
                for pe_, pc_ in rec[0].items():
                    self._need(waits, en, pe_, pc_)
        for wk in writes:
            rec = self.reg.get(wk)
            if rec:
                for pe_, pc_ in rec[0].items():
                    self._need(waits, en, pe_, pc_)
                for pe_, pc_ in rec[1].items():
                    self._need(waits, en, pe_, pc_)
        know = e["vc"]
        real = []
        for pe_, pc_ in sorted(waits.items(), key=str):
            if self.eng[pe_]["step"] == 16 and pe_ != en:
                bars = self.eng[pe_]["barriers"]
                if pc_ > bars[-1]:
                    self.close_wave(pe_)
                    bars = self.eng[pe_]["barriers"]
                import bisect as _bi
                pc_ = bars[_bi.bisect_left(bars, pc_)]
            if pc_ > know.get(pe_, -1):
                real.append((pe_, pc_))
                self._learn(know, pe_, pc_)

        def body(obj):
            for pe_, pc_ in real:
                obj.wait_ge(self.eng[pe_]["sem"], pc_)
            ins = f(obj)
            if inc:
                if isinstance(ins, (list, tuple)):
                    ins = ins[-1]
                ins.then_inc(e["sem"], e["step"])

        e["sec"](body)
        if inc and e["step"] == 16:
            e["bar_cnt"] += 1
        if inc:
            e["cnt"] += e["step"]
            c = e["cnt"]
            self.opvc[(en, c)] = dict(know, **{en: c})
            if e["step"] == 16 and e["bar_cnt"] >= self.WAVE:
                self.close_wave(en)
            for wk in e["pend"]:
                self.reg[wk] = [{en: c}, {}]
            e["pend"] = []
            for wk in writes:
                self.reg[wk] = [{en: c}, {}]
            for r in reads:
                rec = self.reg.setdefault(r, [{}, {}])
                rec[1][en] = c
        else:
            nxt = e["cnt"] + e["step"]
            for wk in writes:
                self.reg[wk] = [{en: nxt}, {}]
                e["pend"].append(wk)
            for r in reads:
                rec = self.reg.setdefault(r, [{}, {}])
                rec[1][en] = nxt

    def wait_all(self, en):
        e = self.eng[en]
        for pe_, pe in self.eng.items():
            if pe["step"] == 16:
                self.close_wave(pe_)
        know = e["vc"]
        real = []
        for pe_, pe in self.eng.items():
            if pe_ == en:
                continue
            if pe["cnt"] > know.get(pe_, -1) and pe["cnt"] > 0:
                real.append((pe_, pe["cnt"]))
                self._learn(know, pe_, pe["cnt"])

        def body(obj):
            for pe_, pc_ in real:
                obj.wait_ge(self.eng[pe_]["sem"], pc_)

        e["sec"](body)


class Arena:
    def __init__(self, nc, name, ncols):
        self.t = nc.alloc_sbuf_tensor(name, [128, ncols], BF16)
        self.ncols = ncols
        self.regions = {}

    def view(self, name, off, ncols, dtype, shape):
        assert name not in self.regions
        assert off + ncols <= self.ncols, (name, off, ncols)
        self.regions[name] = (off, ncols)
        ap = self.t[:, off:off + ncols]
        if dtype != BF16:
            ap = ap.bitcast(dtype)
        free = shape[1:]
        n = 1
        for xx in free:
            n *= xx
        assert n == ap.shape[1], (name, n, ap.shape)
        if len(free) > 1:
            subs = " ".join(f"f{i}" for i in range(len(free)))
            kw = {f"f{i}": free[i] for i in range(len(free))}
            ap = ap.rearrange(f"p ({subs}) -> p {subs}", **kw)
        if shape[0] != 128:
            ap = ap[0:shape[0]]
        return ap

    def overlaps(self):
        names = list(self.regions)
        out = {}
        for a in names:
            oa, na = self.regions[a]
            out[a] = tuple(
                b for b in names
                if not (self.regions[b][0] + self.regions[b][1] <= oa
                        or oa + na <= self.regions[b][0]))
        return out


def build_program():
    nc = bass.Bass("TRN2", target_bir_lowering=False, debug=False)

    x = nc.dram_tensor("x", [NSEQ, T, D], F32, kind="ExternalInput").ap()
    n1w = nc.dram_tensor("norm1_w", [D], F32, kind="ExternalInput").ap()
    n2w = nc.dram_tensor("norm2_w", [D], F32, kind="ExternalInput").ap()
    g_wq = nc.dram_tensor("g_wq", [D, D], F32, kind="ExternalInput").ap()
    g_kvd = nc.dram_tensor("g_kv_down", [D, LAT], F32, kind="ExternalInput").ap()
    g_kup = nc.dram_tensor("g_k_up", [LAT, LAT], F32, kind="ExternalInput").ap()
    g_vup = nc.dram_tensor("g_v_up", [LAT, LAT], F32, kind="ExternalInput").ap()
    g_wo = nc.dram_tensor("g_wo", [D, D], F32, kind="ExternalInput").ap()
    l_wq = nc.dram_tensor("l_wq", [D, D], F32, kind="ExternalInput").ap()
    l_wk = nc.dram_tensor("l_wk", [D, LAT], F32, kind="ExternalInput").ap()
    l_wv = nc.dram_tensor("l_wv", [D, LAT], F32, kind="ExternalInput").ap()
    l_wo = nc.dram_tensor("l_wo", [D, D], F32, kind="ExternalInput").ap()
    W_fc = nc.dram_tensor("W_fc", [F, D], F32, kind="ExternalInput").ap()
    b_fc = nc.dram_tensor("b_fc", [F], F32, kind="ExternalInput").ap()
    A_fc = nc.dram_tensor("A_fc", [16, D], F32, kind="ExternalInput").ap()
    B_fc = nc.dram_tensor("B_fc", [F, 16], F32, kind="ExternalInput").ap()
    W_pj = nc.dram_tensor("W_proj", [D, F], F32, kind="ExternalInput").ap()
    b_pj = nc.dram_tensor("b_proj", [D], F32, kind="ExternalInput").ap()
    A_pj = nc.dram_tensor("A_proj", [16, F], F32, kind="ExternalInput").ap()
    B_pj = nc.dram_tensor("B_proj", [D, 16], F32, kind="ExternalInput").ap()
    out = nc.dram_tensor("out", [NSEQ, T, D], F32, kind="ExternalOutput").ap()

    sk = "ExternalOutput" if DEBUG_TAPS else "Internal"
    xnd = nc.dram_tensor("xnd", [NSEQ * T, D], BF16, kind=sk).ap()
    s1d = nc.dram_tensor("s1d", [NSEQ * T, D], BF16, kind=sk).ap()
    sdd = nc.dram_tensor("sdd", [NSEQ * T, D], BF16, kind=sk).ap()
    h2d = nc.dram_tensor("h2d", [F, NSEQ * T], BF16, kind=sk).ap()
    h2dv = h2d.rearrange("(fk p) tok -> p fk tok", fk=32, p=128)
    wqfcd = nc.dram_tensor("wqfcd", [F, D], BF16, kind=sk).ap()
    wqpjd = nc.dram_tensor("wqpjd", [D, F], BF16, kind=sk).ap()
    recd = nc.dram_tensor("recd", [2, 2048], BF16, kind="Internal").ap()
    dbg1 = nc.dram_tensor("dbg1", [128, F], BF16, kind=sk).ap()
    dbg2 = nc.dram_tensor("dbg2", [17, F], BF16, kind=sk).ap()

    AR = Arena(nc, "arena", 98304)
    V = AR.view
    wq1 = [V(f"wq1_{k}", 1024 * k, 1024, BF16, [128, 1024]) for k in range(8)]
    kvd = [V(f"kvd_{k}", 8192 + 256 * k, 256, BF16, [128, 256]) for k in range(8)]
    kup = [V(f"kup_{k}", 10240 + 256 * k, 256, BF16, [128, 256]) for k in range(2)]
    vup = [V(f"vup_{k}", 10752 + 256 * k, 256, BF16, [128, 256]) for k in range(2)]
    wo1 = [V(f"wo1_{k}", 11264 + 1024 * k, 1024, BF16, [128, 1024]) for k in range(8)]
    wq2 = [V(f"wq2_{k}", 19456 + 1024 * k, 1024, BF16, [128, 1024]) for k in range(8)]
    wk2 = [V(f"wk2_{k}", 27648 + 256 * k, 256, BF16, [128, 256]) for k in range(8)]
    wv2 = [V(f"wv2_{k}", 29696 + 256 * k, 256, BF16, [128, 256]) for k in range(8)]
    wo2 = [V(f"wo2_{k}", 31744 + 1024 * k, 1024, BF16, [128, 1024]) for k in range(8)]
    qwi = [V(f"qwi_{i}", 2048 * i, 2048, F32, [128, 1024]) for i in range(2)]
    qw_y = V("qw_y", 4096, 2048, F32, [128, 1024])
    qw_i = V("qw_i", 6144, 2048, I32, [128, 1024])
    qw_g = V("qw_g", 8192, 2048, F32, [128, 1024])
    qw_o = V("qw_o", 10240, 1024, BF16, [128, 1024])
    wfcT = [V(f"wfcT_{k}", 4096 * k, 4096, BF16, [128, 4096]) for k in range(8)]
    wpjT = [V(f"wpjT_{k}", 1024 * k, 1024, BF16, [128, 1024]) for k in range(32)]

    C0 = 39936
    q1f = V("q1f", C0, 4096, BF16, [128, 8, 512])
    k1f = V("k1f", C0 + 4096, 4096, BF16, [128, 2, 2048])
    vb = V("vb", C0 + 8192, 4160, BF16, [128, 4160])
    attn_m = V("attn_m", C0 + 12352, 4096, BF16, [128, 8, 512])
    latf = V("latf", C0 + 16448, 1024, BF16, [128, 2, 512])
    xnf = V("xnf", C0 + 17472, 4096, BF16, [128, 8, 512])
    ptb = [V(f"ptb_{i}", C0 + 21568 + 512 * i, 512, BF16, [128, 512])
           for i in range(2)]
    qtm = V("qtm", C0 + 22592, 1024, F16, [128, 1024])
    qn_ = V("qn", C0 + 23616, 1024, F16, [128, 1024])
    qrn = V("qrn", 0, 0, F16, [128, 0]) if False else None
    rtmp = V("rtmp", C0 + 24640, 512, F16, [128, 512])
    ktm = V("ktm", C0 + 25152, 256, F16, [128, 256])
    kn_ = V("kn", C0 + 25408, 256, F16, [128, 256])
    krn = None
    recb = V("recb", C0 + 25664, 2048, BF16, [64, 2048])
    rec_sb = V("rec_sb", C0 + 25664, 2048, BF16, [1, 2048])
    sqj = V("sqj", C0 + 27712, 1024, BF16, [128, 1024])
    ld_a = V("ld_a", C0 + 28736, 1024, BF16, [128, 1024])
    ld_b = V("ld_b", C0 + 29760, 1024, BF16, [128, 1024])
    s1b = [V(f"s1b_{i}", C0 + 30784 + 1024 * i, 1024, BF16, [128, 1024])
           for i in range(2)]
    u2all = V("u2all", C0 + 22592, 4096, BF16, [17, 8, 512])
    xnf2 = V("xnf2", C0 + 8192, 4096, BF16, [128, 8, 512])
    h2r = [V(f"h2r_{i}", C0 + 4096 * i, 4096, BF16, [128, 32, 128])
           for i in range(2)]
    h2w = [V(f"h2w_{i}", C0 + 21568 + 512 * i, 512, BF16, [128, 512])
           for i in range(2)]
    h2wp = [V("h2wp_0", C0 + 21568, 1024, BF16, [128, 1024]),
            V("h2wp_1", C0 + 27712, 1024, BF16, [128, 1024])]
    CEND = C0 + 32832

    D0 = CEND
    mA = [V(f"mA_{r}", D0 + 512 * r, 512, BF16, [128, 512]) for r in range(4)]
    mB = [V(f"mB_{r}", D0 + 2048 + 512 * r, 512, BF16, [128, 512])
          for r in range(6)]
    tab = {}
    for i, nm in enumerate(("cos_g", "sin_g", "cos_l", "sin_l")):
        tab[nm] = V(nm, D0 + 5120 + 512 * i, 512, F16, [128, 16, 32])
    id_bf = V("id_bf", D0 + 7168, 128, BF16, [128, 128])
    id_f16 = V("id_f16", D0 + 7296, 128, F16, [128, 128])
    id_f32 = V("id_f32", D0 + 7424, 256, F32, [128, 128])
    ones_bf = V("ones_bf", D0 + 7680, 64, BF16, [1, 64])
    ones_f32 = V("ones_f32", D0 + 7744, 256, F32, [1, 128])
    n12 = V("n12", D0 + 8000, 32, F32, [128, 16])
    r1b = V("r1b", D0 + 8032, 64, F32, [128, 32])
    r2b = V("r2b", D0 + 8096, 64, F32, [128, 32])
    lncol = V("lncol", D0 + 8160, 64, F32, [128, 32])
    scl = V("scl", D0 + 8224, 16, F32, [128, 8])
    bfcT = V("bfcT", D0 + 8240, 4096, BF16, [17, 4096])
    q1wi = [V(f"q1wi_{i}", D0 + 8240 + 2048 * i, 2048, F32, [128, 1024])
            for i in range(2)]
    bpjT = V("bpjT", D0 + 12336, 1024, BF16, [17, 1024])
    afcT = V("afcT", D0 + 13360, 128, BF16, [128, 128])
    apjT = V("apjT", D0 + 13488, 512, BF16, [128, 512])
    up1 = V("up1", D0 + 14000, 512, BF16, [17, 512])
    cent = V("cent", D0 + 14512, 2048, F32, [128, 1024])
    qtm2 = V("qtm2", D0 + 14512, 1024, F16, [128, 1024])
    ktm2 = V("ktm2", D0 + 15536, 256, F16, [128, 256])
    gbuf = V("gbuf", D0 + 14512, 2048, F32, [128, 1024])
    apstg = V("apstg", D0 + 14512, 1024, BF16, [16, 1024])
    xb = [V(f"xb_{i}", D0 + 16560 + 2048 * i, 2048, F32, [128, 1024])
          for i in range(2)]
    lnrow = V("lnrow", D0 + 16560, 4096, F32, [1, 2048])
    wtmp = V("wtmp", D0 + 20656, 2048, F32, [128, 1024])
    tb = [V(f"tb_{i}", D0 + 16560 + 1024 * i, 1024, F32, [128, 512])
          for i in range(4)]
    tbi = V("tbi", D0 + 20656, 1024, I32, [128, 512])
    invb = V("invb", D0 + 22704, 64, F32, [128, 32])
    epsc = V("epsc", D0 + 22768, 16, F32, [128, 8])
    D0X = D0 + 22784
    ptb += [V(f"ptb_{i + 2}", D0X + 512 * i, 512, BF16, [128, 512])
            for i in range(2)]
    ptbp = [V("ptbA", C0 + 21568, 1024, BF16, [128, 2, 512]),
            V("ptbB", D0X, 1024, BF16, [128, 2, 512])]
    ptbp2 = [V("ptbA2", C0 + 22592, 1024, BF16, [128, 2, 512]),
             V("ptbB2", C0 + 23616, 1024, BF16, [128, 2, 512])]
    lnc_q = V("lnc_q", D0X + 1024, 40, F32, [128, 20])
    lnc_k = V("lnc_k", D0X + 1064, 16, F32, [128, 8])
    qrn = V("qrn", D0X + 1080, 1024, F16, [128, 1024])
    krn = V("krn", D0X + 2104, 256, F16, [128, 256])
    TOT = D0X + 2360
    assert TOT <= 98304, TOT
    qrn2 = V("qrn2", C0 + 12352, 1024, F16, [128, 1024])
    krn2 = V("krn2", C0 + 13376, 256, F16, [128, 256])
    rtmpk = V("rtmpk", TOT, 128, F16, [128, 128])
    assert TOT + 128 <= 98304

    ps = nc.alloc_psum_tensor("ps", [128, 4096], F32)
    ps16 = ps[:].bitcast(F16)
    psbf = ps[:].bitcast(BF16)

    def bank(i, rows=128):
        return ps[0:rows, 512 * i:512 * (i + 1)]

    with nc.Block() as blk:
        p = Prog(nc, blk)
        p.overlap = AR.overlaps()
        for name, sec, step in [("pe", blk.tensor, 1), ("act", blk.scalar, 1),
                                ("dve", blk.vector, 1), ("pool", blk.gpsimd, 1),
                                ("sd", blk.sync, 16), ("gd", blk.gpsimd, 16),
                                ("ad", blk.scalar, 16)]:
            p.engine(name, sec, step)
        E = p.emit

        def PB(i):
            return ("pb", i)

        def act_set_load(set_id):
            def body(s):
                s.add_instruction(mybir.InstLoadActFuncSet(
                    name=nc.get_next_instruction_name(),
                    act_func_set_id=set_id, ins=[], outs=[]))
            p.eng["act"]["sec"](body)

        for idv, dtv in ((id_bf, "id_bf"), (id_f16, "id_f16"), (id_f32, "id_f32")):
            E("dve", lambda v, idv=idv: v.memset(idv, 1.0), writes=[dtv])
            E("pool", lambda g, idv=idv: g.affine_select(
                out=idv, in_=idv, pattern=[[1, 128]], compare_op=ALU.is_equal,
                fill=0.0, base=0, channel_multiplier=-1),
                reads=[dtv], writes=[dtv])
        E("dve", lambda v: v.memset(ones_bf, 1.0), writes=["ones_bf"])
        E("dve", lambda v: v.memset(epsc[:, 0:1], EPS_LN), writes=["epsc"])
        E("dve", lambda v: v.memset(epsc[:, 1:2], EPS_RMS), reads=["epsc"], writes=["epsc"])
        E("dve", lambda v: v.memset(epsc[:, 2:3], 0.5), reads=["epsc"], writes=["epsc"])
        E("dve", lambda v: v.memset(ones_f32, 1.0), writes=["ones_f32"])
        E("dve", lambda v: v.memset(vb[:, 64::65], 1.0), writes=["vb"])
        for r in range(4):
            E("dve", lambda v, r=r: v.memset(mA[r], 1.0), writes=[f"mA_{r}"])
            E("pool", lambda g, r=r: g.affine_select(
                out=mA[r], in_=mA[r], pattern=[[1, 512]], compare_op=ALU.is_ge,
                fill=0.0, base=-128 * r, channel_multiplier=-1),
                reads=[f"mA_{r}"], writes=[f"mA_{r}"])
        for ri in range(6):
            r = ri - 2
            E("dve", lambda v, ri=ri: v.memset(mB[ri], 1.0), writes=[f"mB_{ri}"])
            E("pool", lambda g, ri=ri, r=r: g.affine_select(
                out=mB[ri], in_=mB[ri], pattern=[[1, 512]], compare_op=ALU.is_ge,
                fill=0.0, base=-128 * r, channel_multiplier=-1),
                reads=[f"mB_{ri}"], writes=[f"mB_{ri}"])
            E("pool", lambda g, ri=ri, r=r: g.affine_select(
                out=mB[ri], in_=mB[ri], pattern=[[-1, 512]], compare_op=ALU.is_ge,
                fill=0.0, base=128 * r + 255, channel_multiplier=1),
                reads=[f"mB_{ri}"], writes=[f"mB_{ri}"])
        E("sd", lambda e: e.dma_start(
            out=n12[:, 0:8], in_=n1w.rearrange("(c p) -> p c", p=128),
            allow_slow_non_contiguous=True), writes=["n12"])
        E("sd", lambda e: e.dma_start(
            out=n12[:, 8:16], in_=n2w.rearrange("(c p) -> p c", p=128),
            allow_slow_non_contiguous=True), reads=["n12"], writes=["n12"])

        for base, cname, sname in ((BASE_G, "cos_g", "sin_g"),
                                   (BASE_L, "cos_l", "sin_l")):
            E("pool", lambda g: g.iota(tb[3][0:1, 0:32], pattern=[[1, 32]], base=0,
                                       channel_multiplier=0,
                                       allow_small_or_imprecise_dtypes=True),
              writes=["tb_3"])
            act_set_load(ACT_COMBO_SET)
            E("act", lambda s, base=base: s.activation(
                tb[3][0:1, 32:64], tb[3][0:1, 0:32], AF.Exp,
                scale=-math.log(base) / 32.0), reads=["tb_3"], writes=["tb_3"])
            E("pe", lambda t: t.matmul(ps[0:128, 0:32], ones_f32, tb[3][0:1, 32:64],
                                       start=True, stop=True),
              reads=["ones_f32", "tb_3"], writes=[PB(0)])
            E("dve", lambda v: v.tensor_copy(invb[:, 0:32], ps[0:128, 0:32]),
              reads=[PB(0)], writes=["invb"])
            E("pool", lambda g: g.iota(tb[0][:], pattern=[[128, 16], [0, 32]],
                                       base=0, channel_multiplier=1,
                                       allow_small_or_imprecise_dtypes=True),
              writes=["tb_0"])
            E("dve", lambda v: v.tensor_tensor(
                tb[1][:].rearrange("p (a b) -> p a b", a=16),
                tb[0][:].rearrange("p (a b) -> p a b", a=16),
                invb[:, 0:32].unsqueeze(1).broadcast_to([128, 16, 32]),
                ALU.mult), reads=["tb_0", "invb"], writes=["tb_1"])
            act_set_load(ACT_TRIG_SET)
            for trig, shift in ((sname, 0.0), (cname, PI / 2.0)):
                E("dve", lambda v, shift=shift: v.tensor_scalar(
                    tb[0][:], tb[1][:], shift, 1.0 / TWO_PI, ALU.add, ALU.mult),
                  reads=["tb_1"], writes=["tb_0"])
                E("dve", lambda v: v.tensor_copy(tbi[:], tb[0][:]),
                  reads=["tb_0"], writes=["tbi"])
                E("dve", lambda v, shift=shift: v.tensor_scalar(
                    tb[2][:], tb[1][:], shift, None, ALU.add),
                  reads=["tb_1"], writes=["tb_2"])
                E("dve", lambda v: v.scalar_tensor_tensor(
                    tb[2][:], tbi[:], -TWO_PI, tb[2][:], ALU.mult, ALU.add),
                  reads=["tbi", "tb_2"], writes=["tb_2"])
                for cmp_op, thr, fix in ((ALU.is_gt, PI, -TWO_PI),
                                         (ALU.is_lt, -PI, TWO_PI)):
                    E("dve",
                      lambda v, cmp_op=cmp_op, thr=thr: v.tensor_single_scalar(
                          tb[0][:], tb[2][:], thr, cmp_op),
                      reads=["tb_2"], writes=["tb_0"])
                    E("dve", lambda v, fix=fix: v.scalar_tensor_tensor(
                        tb[2][:], tb[0][:], fix, tb[2][:], ALU.mult, ALU.add),
                      reads=["tb_0", "tb_2"], writes=["tb_2"])
                E("dve", lambda v: v.tensor_scalar(
                    tb[2][:], tb[2][:], PI, -PI, ALU.min, ALU.max),
                  reads=["tb_2"], writes=["tb_2"])
                E("act", lambda s, trig=trig: s.activation(
                    tab[trig].rearrange("p a b -> p (a b)"), tb[2][:], AF.Sin),
                  reads=["tb_2"], writes=[trig])

        act_set_load(ACT_COMBO_SET)

        def perm_in(ap2d):
            return ap2d.rearrange("p (h i two) -> p h two i", two=2, i=32)

        def perm_q_pairs(src2d, dst2d):
            src6 = src2d.rearrange("p (a b c d e) -> p a b c d e",
                                   a=2, b=2, c=4, d=32, e=2)
            dst6 = dst2d.rearrange("p (a c b e d) -> p a c b e d",
                                   a=2, c=4, b=2, e=2, d=32)
            pairs = []
            for a in range(2):
                for b in range(2):
                    inp = src6[:, a, b].transpose([0, 1, 3, 2])
                    outp = dst6[:, a, :, b]
                    pairs.append((inp, outp))
            return pairs

        def prep_tile(dst, src, k, n_out, fold_col, permute, tag):
            E("ad", lambda e, k=k: e.dma_start(
                out=wtmp[0:128, 0:n_out], in_=src[128 * k:128 * (k + 1), :]),
              reads=["wtmp"], writes=["wtmp"])
            inv = wtmp[0:128, 0:n_out]
            col = (n12[:, fold_col + k:fold_col + k + 1]
                   if fold_col is not None else None)
            if permute == "q":
                for inp, outp in perm_q_pairs(inv, dst[k][:]):
                    E("dve", lambda v, inp=inp, outp=outp, col=col:
                      v.tensor_scalar(outp, inp, col, None, ALU.mult),
                      reads=["wtmp", "n12"], writes=[f"{tag}_{k}"])
                return
            if permute:
                inv = perm_in(inv)
            if col is not None:
                E("dve", lambda v, inv=inv, col=col: v.tensor_scalar(
                    dst[k][:], inv, col, None, ALU.mult),
                  reads=["wtmp", "n12"], writes=[f"{tag}_{k}"])
            else:
                E("dve", lambda v, inv=inv: v.tensor_copy(dst[k][:], inv),
                  reads=["wtmp"], writes=[f"{tag}_{k}"])

        w1_q = []
        for k in range(8):
            w1_q.append(lambda k=k: prep_tile(wq1, g_wq, k, 1024, 0, "q", "wq1"))
        for k in range(8):
            w1_q.append(lambda k=k: prep_tile(kvd, g_kvd, k, 256, 0, False, "kvd"))
        for k in range(2):
            def kup_tile(k=k):
                E("ad", lambda e: e.dma_start(
                    out=wtmp[0:128, 0:256], in_=g_kup[128 * k:128 * (k + 1), :]),
                  reads=["wtmp"], writes=["wtmp"])
                E("dve", lambda v: v.tensor_copy(
                    kup[k][:], perm_in(wtmp[0:128, 0:256])),
                  reads=["wtmp"], writes=[f"kup_{k}"])
            w1_q.append(kup_tile)

        w2_q = []
        for k in range(8):
            w2_q.append(lambda k=k: prep_tile(wq2, l_wq, k, 1024, 8, "q", "wq2"))
        for k in range(8):
            w2_q.append(lambda k=k: prep_tile(wk2, l_wk, k, 256, 8, True, "wk2"))
        for k in range(8):
            w2_q.append(lambda k=k: prep_tile(wv2, l_wv, k, 256, 8, False, "wv2"))

        lora_q = []

        def lp_afc():
            E("ad", lambda e: e.dma_start(out=wtmp[0:16, 0:1024], in_=A_fc),
              reads=["wtmp"], writes=["wtmp"])
            E("dve", lambda v: v.tensor_copy(sqj[0:16, :], wtmp[0:16, 0:1024]),
              reads=["wtmp"], writes=["sqj"])
            for k in range(8):
                E("pe", lambda t, k=k: t.transpose(
                    psbf[0:128, 7168 + 16 * k:7168 + 16 * (k + 1)],
                    sqj[0:16, 128 * k:128 * (k + 1)], id_bf[0:16, 0:16]),
                  reads=["sqj", "id_bf"], inc=(k == 7),
                  writes=[PB(7)])
            E("dve", lambda v: v.tensor_copy(afcT[:], psbf[0:128, 7168:7296]),
              reads=[PB(7)], writes=["afcT"])
        lora_q.append(lp_afc)

        for q_ in range(4):
            def lp_apj(q_=q_):
                E("ad", lambda e: e.dma_start(
                    out=wtmp[0:16, 0:1024], in_=A_pj[:, 1024 * q_:1024 * (q_ + 1)]),
                  reads=["wtmp"], writes=["wtmp"])
                E("dve", lambda v: v.tensor_copy(apstg[0:16, :],
                                                 wtmp[0:16, 0:1024]),
                  reads=["wtmp"], writes=["apstg"])
                for j in range(8):
                    fk = 8 * q_ + j
                    E("pe", lambda t, j=j, fk=fk: t.transpose(
                        psbf[0:128, 7168 + 16 * j:7168 + 16 * (j + 1)],
                        apstg[0:16, 128 * j:128 * (j + 1)], id_bf[0:16, 0:16]),
                      reads=["apstg", "id_bf"], inc=(j == 7),
                      writes=[PB(7)])
                E("dve", lambda v, q_=q_: v.tensor_copy(
                    apjT[:, 128 * q_:128 * (q_ + 1)], psbf[0:128, 7168:7296]),
                  reads=[PB(7)], writes=["apjT"])
            lora_q.append(lp_apj)

        for k in range(32):
            def lp_bfc(k=k):
                E("ad", lambda e: e.dma_start(
                    out=wtmp[0:128, 0:16], in_=B_fc[128 * k:128 * (k + 1), :]),
                  reads=["wtmp"], writes=["wtmp"])
                E("dve", lambda v: v.tensor_copy(sqj[0:128, 0:16],
                                                 wtmp[0:128, 0:16]),
                  reads=["wtmp"], writes=["sqj"])
                E("pe", lambda t: t.transpose(
                    psbf[0:16, 7168:7296], sqj[0:128, 0:16], id_bf),
                  reads=["sqj", "id_bf"], writes=[PB(7)])
                E("dve", lambda v: v.tensor_copy(
                    bfcT[0:16, 128 * k:128 * (k + 1)], psbf[0:16, 7168:7296]),
                  reads=[PB(7)], writes=["bfcT"])
            lora_q.append(lp_bfc)
        for k in range(8):
            def lp_bpj(k=k):
                E("ad", lambda e: e.dma_start(
                    out=wtmp[0:128, 0:16], in_=B_pj[128 * k:128 * (k + 1), :]),
                  reads=["wtmp"], writes=["wtmp"])
                E("dve", lambda v: v.tensor_copy(sqj[0:128, 0:16],
                                                 wtmp[0:128, 0:16]),
                  reads=["wtmp"], writes=["sqj"])
                E("pe", lambda t: t.transpose(
                    psbf[0:16, 7168:7296], sqj[0:128, 0:16], id_bf),
                  reads=["sqj", "id_bf"], writes=[PB(7)])
                E("dve", lambda v: v.tensor_copy(
                    bpjT[0:16, 128 * k:128 * (k + 1)], psbf[0:16, 7168:7296]),
                  reads=[PB(7)], writes=["bpjT"])
            lora_q.append(lp_bpj)

        def lp_bias():
            E("gd", lambda e: e.dma_start(out=bfcT[16:17, :], in_=b_fc[None, :]),
              writes=["bfcT"])
            E("gd", lambda e: e.dma_start(out=bpjT[16:17, :], in_=b_pj[None, :]),
              writes=["bpjT"])
        lora_q.append(lp_bias)

        q1_q = []
        W_pj_t = W_pj.rearrange("d (a b) -> (d a) b", a=4)

        def qp1_tile(W, i, maxc, tag, first):
            b = i % 2
            E(("gd", "ad")[i % 2],
              lambda e, i=i, W=W, b=b: e.dma_start(
                  out=q1wi[b][:], in_=W[128 * i:128 * (i + 1), :]),
              reads=[f"q1wi_{b}"], writes=[f"q1wi_{b}"])
            E("dve", lambda v, b=b: v.tensor_reduce(
                lncol[:, 20:21], q1wi[b][:], AXX, ALU.max,
                apply_absolute_value=True),
              reads=[f"q1wi_{b}"], writes=["lncol"])
            if first:
                E("dve", lambda v, maxc=maxc: v.tensor_copy(maxc, lncol[:, 20:21]),
                  reads=["lncol"], writes=[tag])
            else:
                E("dve", lambda v, maxc=maxc: v.tensor_tensor(
                    maxc, maxc, lncol[:, 20:21], ALU.max),
                  reads=["lncol", tag], writes=[tag])

        for i in range(32):
            q1_q.append(lambda i=i: qp1_tile(W_fc, i, lncol[:, 16:17], "mx_fc",
                                             i == 0))
        for i in range(32):
            q1_q.append(lambda i=i: qp1_tile(W_pj_t, i, lncol[:, 17:18], "mx_pj",
                                             i == 0))

        def qscale(maxc, tag, wi):
            E("pe", lambda t, maxc=maxc: t.transpose(ps[0:1, 3584:3712], maxc,
                                                     id_f32),
              reads=[tag, "id_f32"], writes=[PB(7)])
            E("dve", lambda v: v.tensor_reduce(
                lncol[0:1, 24:25], ps[0:1, 3584:3712], AXX, ALU.max),
              reads=[PB(7)], writes=["lncol"])
            E("dve", lambda v: v.reciprocal(lncol[0:1, 25:26], lncol[0:1, 24:25]),
              reads=["lncol"], writes=["lncol"])
            E("dve", lambda v: v.tensor_scalar(
                lncol[0:1, 26:27], lncol[0:1, 24:25], 1.0 / 127.0, None, ALU.mult),
              reads=["lncol"], writes=["lncol"])
            E("dve", lambda v: v.tensor_scalar(
                lncol[0:1, 27:28], lncol[0:1, 25:26], 127.0, None, ALU.mult),
              reads=["lncol"], writes=["lncol"])
            E("pe", lambda t: t.matmul(ps[0:128, 3584:3586], ones_f32,
                                       lncol[0:1, 26:28], start=True, stop=True),
              reads=["lncol", "ones_f32"], writes=[PB(7)])
            E("dve", lambda v, wi=wi: v.tensor_copy(
                scl[:, 2 * wi:2 * wi + 2], ps[0:128, 3584:3586]),
              reads=[PB(7)], writes=["scl"])

        q1_q.append(lambda: qscale(lncol[:, 16:17], "mx_fc", 0))
        q1_q.append(lambda: qscale(lncol[:, 17:18], "mx_pj", 1))

        q2_q = []
        wqpjd_t = wqpjd.rearrange("d (a b) -> (d a) b", a=4)

        def qp2_tile(W, Wdst, i, wi, tag):
            b = i % 2
            icol = scl[:, 2 * wi + 1:2 * wi + 2]
            scol = scl[:, 2 * wi:2 * wi + 1]
            E(("gd", "ad")[i % 2],
              lambda e, i=i, W=W, b=b: e.dma_start(
                  out=qwi[b][:], in_=W[128 * i:128 * (i + 1), :]),
              reads=[f"qwi_{b}"], writes=[f"qwi_{b}"])
            E("act", lambda s_, b=b, icol=icol: s_.activation(
                qw_y[:], qwi[b][:], AF.Identity, scale=icol,
                bias=epsc[:, 2:3]),
              reads=[f"qwi_{b}", "scl", "epsc"], writes=["qw_y"])
            E("dve", lambda v: v.tensor_copy(qw_i[:], qw_y[:]),
              reads=["qw_y"], writes=["qw_i"])
            E("dve", lambda v: v.scalar_tensor_tensor(
                qw_g[:], qw_i[:], 1.0, qw_y[:], ALU.bypass, ALU.is_gt),
              reads=["qw_i", "qw_y"], writes=["qw_g"])
            E("dve", lambda v: v.scalar_tensor_tensor(
                qw_g[:], qw_g[:], -1.0, qw_i[:], ALU.mult, ALU.add),
              reads=["qw_g", "qw_i"], writes=["qw_g"])
            E("dve", lambda v, scol=scol: v.tensor_scalar(
                qw_o[:], qw_g[:], scol, None, ALU.mult),
              reads=["qw_g", "scl"], writes=["qw_o"])
            E(("ad", "gd")[i % 2],
              lambda e, i=i, Wdst=Wdst: e.dma_start(
                  out=Wdst[128 * i:128 * (i + 1), :], in_=qw_o[:]),
              reads=["qw_o"], writes=[(tag, i)])

        for i in range(32):
            q2_q.append(lambda i=i: qp2_tile(W_fc, wqfcd, i, 0, "wqfc"))
        for i in range(32):
            q2_q.append(lambda i=i: qp2_tile(W_pj_t, wqpjd_t, i, 1, "wqpj"))

        def drain(q, n):
            for _ in range(min(n, len(q))):
                q.pop(0)()

        def rsqrt_cols(src_col, dst_col, n_mean, eps_col, wkey):
            E("act", lambda s: s.activation(lncol[:, 12:13], src_col, AF.Ln,
                                            scale=1.0 / n_mean, bias=eps_col),
              reads=["lncol", "epsc"], writes=["lncol"])
            E("act", lambda s: s.activation(dst_col, lncol[:, 12:13], AF.Exp,
                                            scale=-0.5),
              reads=["lncol"], writes=[wkey])

        def ln_chunk(s, c):
            i = c % 2
            E(("ad", "gd")[c % 2], lambda e: e.dma_start(
                out=xb[i][:], in_=x[s, 128 * c:128 * (c + 1), :]),
              reads=[f"xb_{i}"], writes=[f"xb_{i}"])
            E("dve", lambda v: v.tensor_reduce(lncol[:, 0:1], xb[i][:], AXX,
                                               ALU.add),
              reads=[f"xb_{i}"], writes=["lncol"])
            E("dve", lambda v: v.tensor_scalar(lncol[:, 1:2], lncol[:, 0:1],
                                               1.0 / D, None, ALU.mult),
              reads=["lncol"], writes=["lncol"])
            E("dve", lambda v: v.tensor_scalar(cent[:], xb[i][:], lncol[:, 1:2],
                                               None, ALU.subtract),
              reads=[f"xb_{i}", "lncol"], writes=["cent"])
            E("dve", lambda v: v.scalar_tensor_tensor(
                sqj[:], cent[:], 1.0, cent[:], ALU.bypass, ALU.mult,
                accum_out=lncol[:, 2:3]),
              reads=["cent"], writes=["sqj", "lncol"])
            rsqrt_cols(lncol[:, 2:3], lncol[:, 4:5], D, epsc[:, 0:1], "lncol")
            xnb = s1b[i]
            E("dve", lambda v, xnb=xnb: v.tensor_scalar(
                xnb[:], cent[:], lncol[:, 4:5], None, ALU.mult),
              reads=["cent", "lncol"], writes=[f"s1b_{i}"])
            E("dve", lambda v, xnb=xnb: v.scalar_tensor_tensor(
                sqj[:], xnb[:], 1.0, xnb[:], ALU.bypass, ALU.mult,
                accum_out=lncol[:, 5:6]),
              reads=[f"s1b_{i}"], writes=["sqj", "lncol"])
            rsqrt_cols(lncol[:, 5:6], r1b[:, 16 * s + c:16 * s + c + 1], D,
                       epsc[:, 1:2], "r1b")
            E("ad", lambda e, xnb=xnb, s=s, c=c: e.dma_start(
                out=xnd[2048 * s + 128 * c:2048 * s + 128 * (c + 1), :],
                in_=xnb[:]),
              reads=[f"s1b_{i}"], writes=[("xnd", s, c)])

        def rope_qknorm(src, n_cols, cname, sname, rk, dst, wk, cc,
                        eng="dve", rbuf=None, rbk="rtmp"):
            nh = n_cols // 64
            lc, lck = (lnc_q, "lnc_q") if n_cols == 1024 else (lnc_k, "lnc_k")
            E("dve", lambda v: v.tensor_tensor(sqj[:, 0:n_cols], src, src,
                                               ALU.mult),
              reads=[rk], writes=["sqj"])
            E("dve", lambda v: v.tensor_reduce(
                lc[:, 0:nh],
                sqj[:, 0:n_cols].rearrange("p (h i) -> p h i", i=64),
                AXX, ALU.add),
              reads=["sqj"], writes=[lck])
            E("act", lambda s: s.activation(lc[:, 0:nh], lc[:, 0:nh], AF.Ln,
                                            scale=1.0 / 64.0, bias=epsc[:, 1:2]),
              reads=[lck, "epsc"], writes=[lck])
            E("act", lambda s: s.activation(lc[:, 0:nh], lc[:, 0:nh],
                                            AF.Exp, scale=-0.5),
              reads=[lck], writes=[lck])
            nbuf = qn_ if n_cols == 1024 else kn_
            nk = "qn" if n_cols == 1024 else "kn"
            E("dve", lambda v, nbuf=nbuf: v.tensor_tensor(
                nbuf[:, 0:n_cols].rearrange("p (h i) -> p h i", i=64),
                src.rearrange("p (h i) -> p h i", i=64),
                lc[:, 0:nh].unsqueeze(2).broadcast_to([128, nh, 64]),
                ALU.mult),
              reads=[rk, lck], writes=[nk])
            nb = nbuf[:, 0:n_cols].rearrange("p (h two i) -> p h two i",
                                             two=2, i=32)
            db = dst.rearrange("p (h two i) -> p h two i", two=2, i=32)
            e_, o_ = nb[:, :, 0, :], nb[:, :, 1, :]
            de, do = db[:, :, 0, :], db[:, :, 1, :]
            cosb = tab[cname][:, cc, :].unsqueeze(1).broadcast_to([128, nh, 32])
            sinb = tab[sname][:, cc, :].unsqueeze(1).broadcast_to([128, nh, 32])
            if rbuf is None:
                rbuf = rtmp
            half = rbuf[:, 0:n_cols // 2].rearrange("p (h i) -> p h i", i=32)
            E(eng, lambda v: v.tensor_tensor(half, o_, sinb, ALU.mult),
              reads=[nk, sname], writes=[rbk])
            E(eng, lambda v: v.tensor_tensor(de, e_, cosb, ALU.mult),
              reads=[nk, cname], writes=[wk])
            E(eng, lambda v: v.tensor_tensor(de, de, half, ALU.subtract),
              reads=[rbk, wk], writes=[wk])
            E(eng, lambda v: v.tensor_tensor(half, e_, sinb, ALU.mult),
              reads=[nk, sname], writes=[rbk])
            E(eng, lambda v: v.tensor_tensor(do, o_, cosb, ALU.mult),
              reads=[nk, cname], writes=[wk])
            E(eng, lambda v: v.tensor_tensor(do, do, half, ALU.add),
              reads=[rbk, wk], writes=[wk])

        def attention(s, aidx, hook_q=None):
            src_d = xnd if aidx == 0 else s1d
            srctag = "xnd" if aidx == 0 else "s1d"
            rbuf, rtag = (r1b, "r1b") if aidx == 0 else (r2b, "r2b")
            cname = "cos_g" if aidx == 0 else "cos_l"
            sname = "sin_g" if aidx == 0 else "sin_l"
            qw, qtag = (wq1, "wq1") if aidx == 0 else (wq2, "wq2")
            wo_w, wo_tag = (wo1, "wo1") if aidx == 0 else (wo2, "wo2")
            for sp in range(NSP):
                row0 = 2048 * s + 512 * sp
                span_keys = [(srctag, s, 4 * sp + j) for j in range(4)]
                for dk in range(8):
                    E("sd",
                      lambda e, dk=dk: e.dma_start(
                          out=xnf[:, dk, :],
                          in_=src_d[row0:row0 + 512, 128 * dk:128 * (dk + 1)],
                          transpose=True),
                      reads=span_keys + ["xnf"], writes=["xnf"])
                if aidx == 0 and SUBPH >= 2:
                    for lc in range(2):
                        for k in range(8):
                            E("pe", lambda t, k=k, lc=lc: t.matmul(
                                bank(lc), kvd[k][:, 128 * lc:128 * (lc + 1)],
                                xnf[:, k, :],
                                start=(k == 0), stop=(k == 7)),
                              reads=["xnf", f"kvd_{k}"], inc=(k == 7),
                              writes=[PB(lc)])
                        E("act", lambda s_, lc=lc: s_.activation(
                            latf[:, lc, :], bank(lc), AF.Copy),
                          reads=[PB(lc)], writes=["latf"])
                def prod_mm(tci):
                    c = 4 * sp + tci
                    qtb, qtk = (qtm, "qtm") if tci % 2 == 0 else (qtm2, "qtm2")
                    ktb, ktk = (ktm, "ktm") if tci % 2 == 0 else (ktm2, "ktm2")
                    rcol = rbuf[:, 16 * s + c:16 * s + c + 1]
                    for half in range(2):
                        for k in range(8):
                            E("pe", lambda t, k=k, half=half, tci=tci: t.matmul(
                                bank(2 + half),
                                xnf[:, k, 128 * tci:128 * (tci + 1)],
                                qw[k][:, 512 * half:512 * (half + 1)],
                                start=(k == 0), stop=(k == 7)),
                              reads=["xnf", f"{qtag}_{k}"], inc=(k == 7),
                              writes=[PB(2 + half)])
                    E("act", lambda s_, rcol=rcol, qtb=qtb: s_.activation(
                        qtb[:], ps[0:128, 1024:2048], AF.Copy, scale=rcol),
                      reads=[PB(2), PB(3), rtag], writes=[qtk])
                    if aidx == 0:
                        for lc in range(2):
                            E("pe", lambda t, lc=lc, tci=tci: t.matmul(
                                bank(0)[:, 0:256],
                                latf[:, lc, 128 * tci:128 * (tci + 1)],
                                kup[lc][:], start=(lc == 0), stop=(lc == 1)),
                              reads=["latf", f"kup_{lc}"], inc=(lc == 1),
                              writes=[PB(0)])
                    else:
                        for k in range(8):
                            E("pe", lambda t, k=k, tci=tci: t.matmul(
                                bank(0)[:, 0:256],
                                xnf[:, k, 128 * tci:128 * (tci + 1)],
                                wk2[k][:], start=(k == 0), stop=(k == 7)),
                              reads=["xnf", f"wk2_{k}"], inc=(k == 7),
                              writes=[PB(0)])
                    E("act", lambda s_, rcol=rcol, ktb=ktb: s_.activation(
                        ktb[:], bank(0)[:, 0:256], AF.Copy, scale=rcol),
                      reads=[PB(0), rtag], writes=[ktk])
                    if aidx == 0:
                        for lc in range(2):
                            E("pe", lambda t, lc=lc, tci=tci: t.matmul(
                                bank(1)[:, 0:256],
                                latf[:, lc, 128 * tci:128 * (tci + 1)],
                                vup[lc][:], start=(lc == 0), stop=(lc == 1)),
                              reads=["latf", f"vup_{lc}"], inc=(lc == 1),
                              writes=[PB(1)])
                    else:
                        for k in range(8):
                            E("pe", lambda t, k=k, tci=tci: t.matmul(
                                bank(1)[:, 0:256],
                                xnf[:, k, 128 * tci:128 * (tci + 1)],
                                wv2[k][:], start=(k == 0), stop=(k == 7)),
                              reads=["xnf", f"wv2_{k}"], inc=(k == 7),
                              writes=[PB(1)])
                    E("act", lambda s_, rcol=rcol, c=c: s_.activation(
                        vb.rearrange("p (g y) -> p g y", g=4)[:, :,
                                                              65 * c:65 * c + 64],
                        bank(1)[:, 0:256].rearrange("p (g y) -> p g y", g=4),
                        AF.Copy, scale=rcol),
                      reads=[PB(1), rtag], writes=["vb"])

                def prod_rope(tci):
                    c = 4 * sp + tci
                    qb_, qk_ = (qrn, "qrn") if tci % 2 == 0 else (qrn2, "qrn2")
                    kb_, kk_ = (krn, "krn") if tci % 2 == 0 else (krn2, "krn2")
                    qtb, qtk = (qtm, "qtm") if tci % 2 == 0 else (qtm2, "qtm2")
                    ktb, ktk = (ktm, "ktm") if tci % 2 == 0 else (ktm2, "ktm2")
                    rope_qknorm(qtb[:], 1024, cname, sname, qtk, qb_[:],
                                qk_, c)
                    rope_qknorm(ktb[:], 256, cname, sname, ktk, kb_[:],
                                kk_, c, eng="pool", rbuf=rtmpk, rbk="rtmpk")

                def prod_tr(tci):
                    c = 4 * sp + tci
                    qb_, qk_ = (qrn, "qrn") if tci % 2 == 0 else (qrn2, "qrn2")
                    kb_, kk_ = (krn, "krn") if tci % 2 == 0 else (krn2, "krn2")
                    for mc in range(8):
                        E("pe", lambda t, mc=mc, qb_=qb_: t.transpose(
                            ps16[0:128, 6144 + 128 * mc:6144 + 128 * (mc + 1)],
                            qb_[:, 128 * mc:128 * (mc + 1)], id_f16),
                          reads=[qk_, "id_f16"], inc=(mc == 7),
                          writes=[PB(6), PB(7)])
                    E("dve", lambda v, tci=tci: v.tensor_copy(
                        q1f[:, :, 128 * tci:128 * (tci + 1)],
                        ps16[0:128, 6144:7168].rearrange("p (a b) -> p a b",
                                                         a=8)),
                      reads=[PB(6), PB(7)], writes=["q1f"])
                    for mc in range(2):
                        E("pe", lambda t, mc=mc, kb_=kb_: t.transpose(
                            ps16[0:128, 6144 + 128 * mc:6144 + 128 * (mc + 1)],
                            kb_[:, 128 * mc:128 * (mc + 1)], id_f16),
                          reads=[kk_, "id_f16"], inc=(mc == 1),
                          writes=[PB(6)])
                    E("dve", lambda v, c=c: v.tensor_copy(
                        k1f[:, :, 128 * c:128 * (c + 1)],
                        ps16[0:128, 6144:6400].rearrange("p (a b) -> p a b",
                                                         a=2)),
                      reads=[PB(6)], writes=["k1f"])

                for tci in range(4):
                    prod_mm(tci)
                    if tci > 1:
                        prod_tr(tci - 2)
                    prod_rope(tci)
                prod_tr(2)
                prod_tr(3)
                for g in range(4):
                    kb = 64 * (g % 2)
                    mck = g // 2
                    if aidx == 0:
                        krange = list(range(0, 4 * sp + 4))
                    else:
                        krange = list(range(max(4 * sp - 2, 0), 4 * sp + 4))
                    nk_ = len(krange)
                    heads = []
                    for hh in range(4):
                        h = 4 * g + hh
                        qb = 64 * ((h // 4) % 2)
                        qmc = (h % 4) + 4 * (h // 8)
                        assert qb == kb
                        heads.append((hh, qmc))

                    def mask_for(kc):
                        if aidx == 0:
                            rr = kc - 4 * sp
                            if rr < 0:
                                return None, None
                            return mA[rr], f"mA_{rr}"
                        rr = kc - 4 * sp + 2
                        return mB[rr], f"mB_{rr}"

                    def pbuf(ki):
                        return (ptbp, ("ptbA", "ptbB")) if ki % 2 == 0 \
                            else (ptbp2, ("ptbA2", "ptbB2"))

                    def emit_qk(kc, pr, kb=kb, mck=mck, heads=heads):
                        for hh, qmc in heads[2 * pr:2 * pr + 2]:
                            xtra = [PB(2), PB(3)] if (pr, hh % 2) == (0, 0) \
                                else []
                            E("pe", lambda t, kc=kc, hh=hh, qmc=qmc: t.matmul(
                                bank(hh),
                                k1f[kb:kb + 64, mck, 128 * kc:128 * (kc + 1)],
                                q1f[kb:kb + 64, qmc, :],
                                start=True, stop=True),
                              reads=["k1f", "q1f"] + xtra, writes=[PB(hh)])

                    def emit_evac(ki, kc, pr, mask_for=mask_for):
                        mask, mkey = mask_for(kc)
                        buf, keys = pbuf(ki)
                        E("act", lambda s_, pr=pr, buf=buf: s_.activation(
                            buf[pr][:].rearrange("p a b -> p (a b)"),
                            ps[0:128, 1024 * pr:1024 * (pr + 1)],
                            AF.Exp, scale=0.125),
                          reads=[PB(2 * pr), PB(2 * pr + 1)],
                          writes=[keys[pr]])
                        if mask is not None:
                            E("dve", lambda v, pr=pr, mask=mask, buf=buf:
                              v.tensor_tensor(
                                  buf[pr][:], buf[pr][:],
                                  mask[:].unsqueeze(1).broadcast_to(
                                      [128, 2, 512]), ALU.mult),
                              reads=[keys[pr], mkey], writes=[keys[pr]])

                    def emit_pv(ki, kc, g=g, heads=heads, nk_=nk_):
                        buf, keys = pbuf(ki)
                        for hh, qmc in heads:
                            xtra = [keys[1]] if hh == 0 else []
                            E("pe", lambda t, kc=kc, hh=hh, ki=ki, buf=buf:
                              t.matmul(
                                  ps[0:65, 2048 + 512 * hh:2048 + 512 * (hh + 1)],
                                  vb[:, 1040 * g + 65 * kc:
                                     1040 * g + 65 * (kc + 1)],
                                  buf[hh // 2][:, hh % 2, :],
                                  start=(ki == 0), stop=(ki == nk_ - 1)),
                              reads=[keys[hh // 2], "vb"] + xtra,
                              inc=(hh == 3), writes=[PB(4 + hh)])

                    for ki, kc in enumerate(krange):
                        emit_qk(kc, 0)
                        emit_evac(ki, kc, 0)
                        emit_qk(kc, 1)
                        emit_evac(ki, kc, 1)
                        if ki > 0:
                            emit_pv(ki - 1, krange[ki - 1])
                    emit_pv(nk_ - 1, krange[nk_ - 1])

                    gp = g % 2
                    E("act", lambda s_: s_.activation(
                        lnrow[:], ps[64:65, 2048:4096], AF.Ln),
                      reads=[PB(4), PB(5), PB(6), PB(7)],
                      writes=["lnrow"])
                    E("act", lambda s_: s_.activation(
                        rec_sb[0:1, 0:2048], lnrow[:], AF.Exp,
                        scale=-1.0),
                      reads=["lnrow", PB(4), PB(5), PB(6), PB(7)],
                      writes=["recb"])
                    E("ad", lambda e, gp=gp: e.dma_start(
                        out=recd[gp:gp + 1, :], in_=rec_sb[0:1, 0:2048]),
                      reads=["recb"], writes=[("recd", gp)])
                    E("gd", lambda e, gp=gp: e.dma_start(
                        out=recb[0:64, 0:2048],
                        in_=recd[gp][None, :].broadcast_to([64, 2048])),
                      reads=[("recd", gp), "recb"], writes=["recb"])
                    for par in range(2):
                        E("dve", lambda v, par=par, g=g: v.tensor_tensor(
                            attn_m[64 * par:64 * (par + 1),
                                   2 * g:2 * g + 2, :],
                            ps[0:64, 2048:4096].rearrange(
                                "p (a c b) -> p a c b", a=2, c=2)[:, :, par, :],
                            recb[0:64, 0:2048].rearrange(
                                "p (a c b) -> p a c b", a=2, c=2)[:, :, par, :],
                            ALU.mult),
                          reads=[PB(4 + par), PB(6 + par), "recb"],
                          writes=["attn_m"])
                for tci in range(4 if SUBPH >= 5 else 0):
                    c = 4 * sp + tci
                    i = tci % 2
                    ldb = ld_a if i == 0 else ld_b
                    ldk = "ld_a" if i == 0 else "ld_b"
                    E("sd", lambda e, c=c, ldb=ldb, s=s: e.dma_start(
                        out=ldb[:],
                        in_=src_d[2048 * s + 128 * c:2048 * s + 128 * (c + 1), :]),
                      reads=[ldk, (srctag, s, c)], writes=[ldk])
                    for half in range(2):
                        for mc in range(8):
                            E("pe", lambda t, mc=mc, half=half, tci=tci: t.matmul(
                                bank(2 + half),
                                attn_m[:, mc, 128 * tci:128 * (tci + 1)],
                                wo_w[mc][:, 512 * half:512 * (half + 1)],
                                start=(mc == 0), stop=(mc == 7)),
                              reads=["attn_m", f"{wo_tag}_{mc}"], inc=(mc == 7),
                              writes=[PB(2 + half)])
                    sb_o = s1b[i]
                    for half in range(2):
                        E("dve", lambda v, half=half, sb_o=sb_o, ldb=ldb:
                          v.scalar_tensor_tensor(
                              sb_o[:, 512 * half:512 * (half + 1)],
                              bank(2 + half), 1.0,
                              ldb[:, 512 * half:512 * (half + 1)],
                              ALU.bypass, ALU.add),
                          reads=[PB(2 + half), ldk], writes=[f"s1b_{i}"])
                    if aidx == 0:
                        E("dve", lambda v, sb_o=sb_o: v.scalar_tensor_tensor(
                            sqj[:], sb_o[:], 1.0, sb_o[:],
                            ALU.bypass, ALU.mult, accum_out=lncol[:, 6:7]),
                          reads=[f"s1b_{i}"], writes=["sqj", "lncol"])
                        rsqrt_cols(lncol[:, 6:7],
                                   r2b[:, 16 * s + c:16 * s + c + 1], D,
                                   epsc[:, 1:2], "r2b")
                        dst, dk_ = s1d, ("s1d", s, c)
                    else:
                        dst, dk_ = sdd, ("sdd", s, c)
                    E("gd", lambda e, sb_o=sb_o, s=s, c=c, dst=dst: e.dma_start(
                        out=dst[2048 * s + 128 * c:2048 * s + 128 * (c + 1), :],
                        in_=sb_o[:]),
                      reads=[f"s1b_{i}"], writes=[dk_])
                    if hook_q is not None:
                        rem = (NSP - sp) * 4 - tci
                        drain(hook_q, (len(hook_q) + rem - 1) // rem)

        def ffn_pass1():
            act_set_load(ACT_GELU_SET)
            for k in range(8):
                E("sd",
                  lambda e, k=k: e.dma_start(
                      out=wfcT[k][:], in_=wqfcd[:, 128 * k:128 * (k + 1)],
                      transpose=True),
                  reads=[("wqfc", i) for i in range(32)] + [f"wfcT_{k}"],
                  writes=[f"wfcT_{k}"])
            def xnf_load(gsp):
                row0 = 512 * gsp
                xb_, xk_ = (xnf, "xnf") if gsp % 2 == 0 else (xnf2, "xnf2")
                span_keys = [("xnd", gsp // 4, 4 * (gsp % 4) + j)
                             for j in range(4)]
                for dk in range(8):
                    E("sd",
                      lambda e, dk=dk, row0=row0, xb_=xb_: e.dma_start(
                          out=xb_[:, dk, :],
                          in_=xnd[row0:row0 + 512, 128 * dk:128 * (dk + 1)],
                          transpose=True),
                      reads=span_keys + [xk_], writes=[xk_])

            xnf_load(0)
            for gsp in range(8):
                if gsp + 1 < 8:
                    xnf_load(gsp + 1)
                row0 = 512 * gsp
                xb_, xk_ = (xnf, "xnf") if gsp % 2 == 0 else (xnf2, "xnf2")
                for k in range(8):
                    E("pe", lambda t, k=k, xb_=xb_: t.matmul(
                        ps[0:16, 3584:4096], afcT[:, 16 * k:16 * (k + 1)],
                        xb_[:, k, :], start=(k == 0), stop=(k == 7)),
                      reads=[xk_, "afcT"], inc=(k == 7),
                      writes=[PB(7)])
                E("dve", lambda v: v.memset(up1[:], 1.0), writes=["up1"])
                E("act", lambda s_: s_.activation(
                    up1[0:16, :], ps[0:16, 3584:4096], AF.Copy, scale=2.0),
                  reads=[PB(7), "up1"], writes=["up1"])
                for fk in range(32):
                    bk = fk % 4
                    for k in range(8):
                        E("pe", lambda t, k=k, fk=fk, bk=bk, xb_=xb_: t.matmul(
                            bank(bk), wfcT[k][:, 128 * fk:128 * (fk + 1)],
                            xb_[:, k, :], start=(k == 0), stop=False),
                          reads=[xk_, f"wfcT_{k}"], inc=False,
                          writes=[("pb", bk)])
                    E("pe", lambda t, fk=fk, bk=bk: t.matmul(
                        bank(bk), bfcT[0:17, 128 * fk:128 * (fk + 1)], up1[:],
                        start=False, stop=True),
                      reads=["bfcT", "up1"], writes=[PB(bk)])
                    if fk % 2 == 0:
                        continue
                    wi = (fk // 2) % 2
                    wkey = f"h2wp_{wi}"
                    bk0 = bk - 1
                    E("act", lambda s_, bk0=bk0, wi=wi: s_.activation(
                        h2wp[wi][:], ps[0:128, 512 * bk0:512 * bk0 + 1024],
                        AF.Gelu),
                      reads=[PB(bk0), PB(bk0 + 1)], writes=[wkey])
                    E("gd", lambda e, fk=fk, wi=wi, row0=row0, gsp=gsp:
                      e.dma_start(
                          out=h2d[128 * (fk - 1):128 * (fk + 1),
                                  row0:row0 + 512].rearrange(
                              "(a p) t -> p a t", a=2),
                          in_=h2wp[wi][:].rearrange("p (a t) -> p a t", a=2)),
                      reads=[wkey],
                      writes=[("h2d", gsp, fk - 1), ("h2d", gsp, fk)])
                    for j in range(2):
                        fj = fk - 1 + j
                        E("pe", lambda t, fj=fj, wi=wi, j=j: t.matmul(
                            ps[0:16, 3072:3584],
                            apjT[:, 16 * fj:16 * (fj + 1)],
                            h2wp[wi][:, 512 * j:512 * (j + 1)],
                            start=(fj == 0), stop=(fj == 31)),
                          reads=[wkey, "apjT"], inc=(fj == 31),
                          writes=[PB(6)])
                E("dve", lambda v, gsp=gsp: v.memset(u2all[:, gsp, :], 1.0),
                  writes=["u2all"])
                E("act", lambda s_, gsp=gsp: s_.activation(
                    u2all[0:16, gsp, :], ps[0:16, 3072:3584], AF.Copy,
                    scale=2.0),
                  reads=[PB(6), "u2all"], writes=["u2all"])

        def ffn_pass2():
            act_set_load(ACT_SILU_SET)
            for fk in range(32):
                E("sd",
                  lambda e, fk=fk: e.dma_start(
                      out=wpjT[fk][:], in_=wqpjd[:, 128 * fk:128 * (fk + 1)],
                      transpose=True),
                  reads=[("wqpj", i) for i in range(32)] + [f"wpjT_{fk}"],
                  writes=[f"wpjT_{fk}"])
            for cg in range(32):
                s, c = cg // 16, cg % 16
                i = cg % 2
                hr, hk = h2r[i], f"h2r_{i}"
                E("sd",
                  lambda e, cg=cg, hr=hr: e.dma_start(
                      out=hr[:],
                      in_=h2dv[:, :, 128 * cg:128 * (cg + 1)]),
                  reads=[("h2d", cg // 4, fk) for fk in range(32)] + [hk],
                  writes=[hk])
                pb0 = 2 * i
                for half in range(2):
                    for fk in range(32):
                        E("pe", lambda t, fk=fk, half=half, hr=hr, pb0=pb0:
                          t.matmul(
                              bank(pb0 + half), hr[:, fk, :],
                              wpjT[fk][:, 512 * half:512 * (half + 1)],
                              start=(fk == 0), stop=False),
                          reads=[hk, f"wpjT_{fk}"], inc=False,
                          writes=[("pb", pb0 + half)])
                    E("pe", lambda t, half=half, cg=cg, pb0=pb0: t.matmul(
                        bank(pb0 + half),
                        u2all[:, cg // 4, 128 * (cg % 4):128 * (cg % 4) + 128],
                        bpjT[0:17, 512 * half:512 * (half + 1)],
                        start=False, stop=True),
                      reads=["u2all", "bpjT"], writes=[PB(pb0 + half)])
                    E("act", lambda s_, half=half, pb0=pb0: s_.activation(
                        gbuf[:, 512 * half:512 * (half + 1)],
                        bank(pb0 + half), AF.Silu),
                      reads=[PB(pb0 + half)], writes=["gbuf"])
                ldb = ld_a if i == 0 else ld_b
                ldk = "ld_a" if i == 0 else "ld_b"
                E("ad", lambda e, s=s, c=c, ldb=ldb: e.dma_start(
                    out=ldb[:],
                    in_=sdd[2048 * s + 128 * c:2048 * s + 128 * (c + 1), :]),
                  reads=[("sdd", s, c), ldk], writes=[ldk])
                E("sd", lambda e, s=s, c=c, i=i: e.dma_start(
                    out=xb[i][:], in_=x[s, 128 * c:128 * (c + 1), :]),
                  reads=[f"xb_{i}"], writes=[f"xb_{i}"])
                E("dve", lambda v, ldb=ldb: v.tensor_tensor(
                    gbuf[:], gbuf[:], ldb[:], ALU.mult),
                  reads=["gbuf", ldk], writes=["gbuf"])
                E("dve", lambda v, i=i: v.tensor_tensor(
                    gbuf[:], gbuf[:], xb[i][:], ALU.add),
                  reads=["gbuf", f"xb_{i}"], writes=["gbuf"])
                E("gd", lambda e, s=s, c=c: e.dma_start(
                    out=out[s, 128 * c:128 * (c + 1), :], in_=gbuf[:]),
                  reads=["gbuf"], writes=[("out", s, c)])

        for k in range(8):
            E("gd", lambda e, k=k: e.dma_start(
                out=wo1[k][:], in_=g_wo[128 * k:128 * (k + 1), :]),
              writes=[f"wo1_{k}"])
        for k in range(2):
            E("gd", lambda e, k=k: e.dma_start(
                out=vup[k][:], in_=g_vup[128 * k:128 * (k + 1), :]),
              writes=[f"vup_{k}"])
        if PHASES >= 1:
            for c in range(NCH):
                ln_chunk(0, c)
                drain(w1_q, 2)
            drain(w1_q, 99)
            for k in range(8):
                E("gd", lambda e, k=k: e.dma_start(
                    out=wo2[k][:], in_=l_wo[128 * k:128 * (k + 1), :]),
                  writes=[f"wo2_{k}"])
        if PHASES >= 2:
            attention(0, 0, hook_q=w2_q + q1_q)
        if PHASES >= 3:
            for c in range(NCH):
                ln_chunk(1, c)
            attention(1, 0, hook_q=lora_q)
        if PHASES >= 4:
            q2_first = q2_q[:len(q2_q) // 2]
            q2_second = q2_q[len(q2_q) // 2:]
            attention(0, 1, hook_q=q2_first)
            attention(1, 1, hook_q=q2_second)
            drain(q2_first, 999)
            drain(q2_second, 999)
        if PHASES >= 5:
            ffn_pass1()
        if DEBUG_TAPS and PHASES >= 5:
            E("sd", lambda e: e.dma_start(out=dbg1, in_=wfcT[0][:]),
              reads=["wfcT_0"], writes=[("dbg1",)])
            E("sd", lambda e: e.dma_start(out=dbg2, in_=bfcT[0:17, :]),
              reads=["bfcT"], writes=[("dbg2",)])
        if PHASES >= 6:
            ffn_pass2()
        p.wait_all("sd")

    return nc


_INPUT_MAP = [
    ("norm1_w", "L0_attn_norm1_w"), ("norm2_w", "L0_attn_norm2_w"),
    ("g_wq", "L0_attn_g_wq"), ("g_kv_down", "L0_attn_g_kv_down"),
    ("g_k_up", "L0_attn_g_k_up"), ("g_v_up", "L0_attn_g_v_up"),
    ("g_wo", "L0_attn_g_wo"), ("l_wq", "L0_attn_l_wq"),
    ("l_wk", "L0_attn_l_wk"), ("l_wv", "L0_attn_l_wv"),
    ("l_wo", "L0_attn_l_wo"), ("W_fc", "L0_ffn_W_fc"),
    ("b_fc", "L0_ffn_b_fc"), ("A_fc", "L0_ffn_A_fc"),
    ("B_fc", "L0_ffn_B_fc"), ("W_proj", "L0_ffn_W_proj"),
    ("b_proj", "L0_ffn_b_proj"), ("A_proj", "L0_ffn_A_proj"),
    ("B_proj", "L0_ffn_B_proj"),
]


def kernel(**inputs):
    nc = build_program()
    xfull = np.ascontiguousarray(np.asarray(inputs["L0_attn_hidden"], np.float32))
    w = {k: np.ascontiguousarray(np.asarray(inputs[v], np.float32))
         for k, v in _INPUT_MAP}
    in_maps = []
    for core in range(8):
        m = {"x": xfull[2 * core:2 * core + 2]}
        m.update(w)
        in_maps.append(m)
    res = run_bass_kernel_spmd(nc, in_maps, core_ids=list(range(8)))
    return np.concatenate([np.asarray(r["out"], np.float32)
                           for r in res.results], axis=0)


if __name__ == "__main__":
    nc = build_program()
    print("build ok")


# version 38
# speedup vs baseline: 1.0551x; 1.0040x over previous
import math
import numpy as np

import concourse.bass as bass
import concourse.mybir as mybir
from concourse.bass_utils import run_bass_kernel_spmd

F32 = mybir.dt.float32
F16 = mybir.dt.float16
BF16 = mybir.dt.bfloat16
I32 = mybir.dt.int32
AF = mybir.ActivationFunctionType
ALU = mybir.AluOpType
AXX = mybir.AxisListType.X

D = 1024
LAT = 256
T = 2048
NSEQ = 2
F = 4096
NCH = 16
NSP = 4
PI = math.pi
TWO_PI = 2.0 * math.pi
EPS_LN = 1e-5
EPS_RMS = 1e-6
BASE_G = 1000000.0
BASE_L = 10000.0


def _act_set_ids():
    fallback = {"natural_log_exp_and_others": 6, "trig_and_small": 9,
                "gelu_and_others": 10, "silu_and_others": 18}
    try:
        from concourse.hw_specs import get_activation_tables
        names = list(get_activation_tables("TRN2"))
        return {k: names.index(k) for k in fallback}
    except Exception:
        return fallback


_ACT_SETS = _act_set_ids()
ACT_COMBO_SET = _ACT_SETS["natural_log_exp_and_others"]
ACT_TRIG_SET = _ACT_SETS["trig_and_small"]
ACT_GELU_SET = _ACT_SETS["gelu_and_others"]
ACT_SILU_SET = _ACT_SETS["silu_and_others"]

DEBUG_TAPS = False
PHASES = 99
SUBPH = 9


class Prog:

    def __init__(self, nc, blk):
        self.nc, self.blk = nc, blk
        self.eng = {}
        self.reg = {}
        self.overlap = {}
        self.opvc = {}

    WAVE = 24

    def engine(self, name, sec, step):
        sem = self.nc.alloc_semaphore(name=f"sem_{name}")
        self.eng[name] = dict(sem=sem, cnt=0, step=step, sec=sec, pend=[],
                              barriers=[0], bar_cnt=0, vc={})

    def _learn(self, know, pe_, pc_):
        if pc_ > know.get(pe_, -1):
            know[pe_] = pc_
        src = self.opvc.get((pe_, pc_))
        if src:
            for k2, v2 in src.items():
                if v2 > know.get(k2, -1):
                    know[k2] = v2

    def close_wave(self, en):
        e = self.eng[en]
        if e["cnt"] == e["barriers"][-1]:
            return
        val = e["cnt"]

        def body(obj):
            obj.wait_ge(e["sem"], val)

        e["sec"](body)
        e["barriers"].append(val)
        e["bar_cnt"] = 0
        self._learn(e["vc"], en, val)

    def _expand(self, keys):
        out = []
        for k in keys:
            if isinstance(k, str) and k in self.overlap:
                out.extend(self.overlap[k])
            else:
                out.append(k)
        return out

    def _need(self, waits, en, pe, pc):
        if pe == en == "pe":
            return
        if pc > waits.get(pe, -1):
            waits[pe] = pc

    def emit(self, en, f, reads=(), writes=(), inc=True):
        e = self.eng[en]
        reads = self._expand(reads)
        writes = self._expand(writes)
        waits = {}
        for r in reads:
            rec = self.reg.get(r)
            if rec:
                for pe_, pc_ in rec[0].items():
                    self._need(waits, en, pe_, pc_)
        for wk in writes:
            rec = self.reg.get(wk)
            if rec:
                for pe_, pc_ in rec[0].items():
                    self._need(waits, en, pe_, pc_)
                for pe_, pc_ in rec[1].items():
                    self._need(waits, en, pe_, pc_)
        know = e["vc"]
        real = []
        for pe_, pc_ in sorted(waits.items(), key=str):
            if self.eng[pe_]["step"] == 16 and pe_ != en:
                bars = self.eng[pe_]["barriers"]
                if pc_ > bars[-1]:
                    self.close_wave(pe_)
                    bars = self.eng[pe_]["barriers"]
                import bisect as _bi
                pc_ = bars[_bi.bisect_left(bars, pc_)]
            if pc_ > know.get(pe_, -1):
                real.append((pe_, pc_))
                self._learn(know, pe_, pc_)

        def body(obj):
            for pe_, pc_ in real:
                obj.wait_ge(self.eng[pe_]["sem"], pc_)
            ins = f(obj)
            if inc:
                if isinstance(ins, (list, tuple)):
                    ins = ins[-1]
                ins.then_inc(e["sem"], e["step"])

        e["sec"](body)
        if inc and e["step"] == 16:
            e["bar_cnt"] += 1
        if inc:
            e["cnt"] += e["step"]
            c = e["cnt"]
            self.opvc[(en, c)] = dict(know, **{en: c})
            if e["step"] == 16 and e["bar_cnt"] >= self.WAVE:
                self.close_wave(en)
            for wk in e["pend"]:
                self.reg[wk] = [{en: c}, {}]
            e["pend"] = []
            for wk in writes:
                self.reg[wk] = [{en: c}, {}]
            for r in reads:
                rec = self.reg.setdefault(r, [{}, {}])
                rec[1][en] = c
        else:
            nxt = e["cnt"] + e["step"]
            for wk in writes:
                self.reg[wk] = [{en: nxt}, {}]
                e["pend"].append(wk)
            for r in reads:
                rec = self.reg.setdefault(r, [{}, {}])
                rec[1][en] = nxt

    def wait_all(self, en):
        e = self.eng[en]
        for pe_, pe in self.eng.items():
            if pe["step"] == 16:
                self.close_wave(pe_)
        know = e["vc"]
        real = []
        for pe_, pe in self.eng.items():
            if pe_ == en:
                continue
            if pe["cnt"] > know.get(pe_, -1) and pe["cnt"] > 0:
                real.append((pe_, pe["cnt"]))
                self._learn(know, pe_, pe["cnt"])

        def body(obj):
            for pe_, pc_ in real:
                obj.wait_ge(self.eng[pe_]["sem"], pc_)

        e["sec"](body)


class Arena:
    def __init__(self, nc, name, ncols):
        self.t = nc.alloc_sbuf_tensor(name, [128, ncols], BF16)
        self.ncols = ncols
        self.regions = {}

    def view(self, name, off, ncols, dtype, shape):
        assert name not in self.regions
        assert off + ncols <= self.ncols, (name, off, ncols)
        self.regions[name] = (off, ncols)
        ap = self.t[:, off:off + ncols]
        if dtype != BF16:
            ap = ap.bitcast(dtype)
        free = shape[1:]
        n = 1
        for xx in free:
            n *= xx
        assert n == ap.shape[1], (name, n, ap.shape)
        if len(free) > 1:
            subs = " ".join(f"f{i}" for i in range(len(free)))
            kw = {f"f{i}": free[i] for i in range(len(free))}
            ap = ap.rearrange(f"p ({subs}) -> p {subs}", **kw)
        if shape[0] != 128:
            ap = ap[0:shape[0]]
        return ap

    def overlaps(self):
        names = list(self.regions)
        out = {}
        for a in names:
            oa, na = self.regions[a]
            out[a] = tuple(
                b for b in names
                if not (self.regions[b][0] + self.regions[b][1] <= oa
                        or oa + na <= self.regions[b][0]))
        return out


def build_program():
    nc = bass.Bass("TRN2", target_bir_lowering=False, debug=False)

    x = nc.dram_tensor("x", [NSEQ, T, D], F32, kind="ExternalInput").ap()
    n1w = nc.dram_tensor("norm1_w", [D], F32, kind="ExternalInput").ap()
    n2w = nc.dram_tensor("norm2_w", [D], F32, kind="ExternalInput").ap()
    g_wq = nc.dram_tensor("g_wq", [D, D], F32, kind="ExternalInput").ap()
    g_kvd = nc.dram_tensor("g_kv_down", [D, LAT], F32, kind="ExternalInput").ap()
    g_kup = nc.dram_tensor("g_k_up", [LAT, LAT], F32, kind="ExternalInput").ap()
    g_vup = nc.dram_tensor("g_v_up", [LAT, LAT], F32, kind="ExternalInput").ap()
    g_wo = nc.dram_tensor("g_wo", [D, D], F32, kind="ExternalInput").ap()
    l_wq = nc.dram_tensor("l_wq", [D, D], F32, kind="ExternalInput").ap()
    l_wk = nc.dram_tensor("l_wk", [D, LAT], F32, kind="ExternalInput").ap()
    l_wv = nc.dram_tensor("l_wv", [D, LAT], F32, kind="ExternalInput").ap()
    l_wo = nc.dram_tensor("l_wo", [D, D], F32, kind="ExternalInput").ap()
    W_fc = nc.dram_tensor("W_fc", [F, D], F32, kind="ExternalInput").ap()
    b_fc = nc.dram_tensor("b_fc", [F], F32, kind="ExternalInput").ap()
    A_fc = nc.dram_tensor("A_fc", [16, D], F32, kind="ExternalInput").ap()
    B_fc = nc.dram_tensor("B_fc", [F, 16], F32, kind="ExternalInput").ap()
    W_pj = nc.dram_tensor("W_proj", [D, F], F32, kind="ExternalInput").ap()
    b_pj = nc.dram_tensor("b_proj", [D], F32, kind="ExternalInput").ap()
    A_pj = nc.dram_tensor("A_proj", [16, F], F32, kind="ExternalInput").ap()
    B_pj = nc.dram_tensor("B_proj", [D, 16], F32, kind="ExternalInput").ap()
    out = nc.dram_tensor("out", [NSEQ, T, D], F32, kind="ExternalOutput").ap()

    sk = "ExternalOutput" if DEBUG_TAPS else "Internal"
    xnd = nc.dram_tensor("xnd", [NSEQ * T, D], BF16, kind=sk).ap()
    s1d = nc.dram_tensor("s1d", [NSEQ * T, D], BF16, kind=sk).ap()
    sdd = nc.dram_tensor("sdd", [NSEQ * T, D], BF16, kind=sk).ap()
    h2d = nc.dram_tensor("h2d", [F, NSEQ * T], BF16, kind=sk).ap()
    h2dv = h2d.rearrange("(fk p) tok -> p fk tok", fk=32, p=128)
    wqfcd = nc.dram_tensor("wqfcd", [F, D], BF16, kind=sk).ap()
    wqpjd = nc.dram_tensor("wqpjd", [D, F], BF16, kind=sk).ap()
    recd = nc.dram_tensor("recd", [2, 2048], BF16, kind="Internal").ap()
    dbg1 = nc.dram_tensor("dbg1", [128, F], BF16, kind=sk).ap()
    dbg2 = nc.dram_tensor("dbg2", [17, F], BF16, kind=sk).ap()

    AR = Arena(nc, "arena", 98304)
    V = AR.view
    wq1 = [V(f"wq1_{k}", 1024 * k, 1024, BF16, [128, 1024]) for k in range(8)]
    kvd = [V(f"kvd_{k}", 8192 + 256 * k, 256, BF16, [128, 256]) for k in range(8)]
    kup = [V(f"kup_{k}", 10240 + 256 * k, 256, BF16, [128, 256]) for k in range(2)]
    vup = [V(f"vup_{k}", 10752 + 256 * k, 256, BF16, [128, 256]) for k in range(2)]
    wo1 = [V(f"wo1_{k}", 11264 + 1024 * k, 1024, BF16, [128, 1024]) for k in range(8)]
    wq2 = [V(f"wq2_{k}", 19456 + 1024 * k, 1024, BF16, [128, 1024]) for k in range(8)]
    wk2 = [V(f"wk2_{k}", 27648 + 256 * k, 256, BF16, [128, 256]) for k in range(8)]
    wv2 = [V(f"wv2_{k}", 29696 + 256 * k, 256, BF16, [128, 256]) for k in range(8)]
    wo2 = [V(f"wo2_{k}", 31744 + 1024 * k, 1024, BF16, [128, 1024]) for k in range(8)]
    qwi = [V(f"qwi_{i}", 2048 * i, 2048, F32, [128, 1024]) for i in range(2)]
    qw_y = V("qw_y", 4096, 2048, F32, [128, 1024])
    qw_i = V("qw_i", 6144, 2048, I32, [128, 1024])
    qw_g = V("qw_g", 8192, 2048, F32, [128, 1024])
    qw_o = V("qw_o", 10240, 1024, BF16, [128, 1024])
    wfcT = [V(f"wfcT_{k}", 4096 * k, 4096, BF16, [128, 4096]) for k in range(8)]
    wpjT = [V(f"wpjT_{k}", 1024 * k, 1024, BF16, [128, 1024]) for k in range(32)]

    C0 = 39936
    q1f = V("q1f", C0, 4096, BF16, [128, 8, 512])
    k1f = V("k1f", C0 + 4096, 4096, BF16, [128, 2, 2048])
    vb = V("vb", C0 + 8192, 4160, BF16, [128, 4160])
    attn_m = V("attn_m", C0 + 12352, 4096, BF16, [128, 8, 512])
    latf = V("latf", C0 + 16448, 1024, BF16, [128, 2, 512])
    xnf = V("xnf", C0 + 17472, 4096, BF16, [128, 8, 512])
    ptb = [V(f"ptb_{i}", C0 + 21568 + 512 * i, 512, BF16, [128, 512])
           for i in range(2)]
    qtm = V("qtm", C0 + 22592, 1024, F16, [128, 1024])
    qn_ = V("qn", C0 + 23616, 1024, F16, [128, 1024])
    qrn = V("qrn", 0, 0, F16, [128, 0]) if False else None
    rtmp = V("rtmp", C0 + 24640, 512, F16, [128, 512])
    ktm = V("ktm", C0 + 25152, 256, F16, [128, 256])
    kn_ = V("kn", C0 + 25408, 256, F16, [128, 256])
    krn = None
    recb = V("recb", C0 + 25664, 2048, BF16, [64, 2048])
    rec_sb = V("rec_sb", C0 + 25664, 2048, BF16, [1, 2048])
    sqj = V("sqj", C0 + 27712, 1024, BF16, [128, 1024])
    ld_a = V("ld_a", C0 + 28736, 1024, BF16, [128, 1024])
    ld_b = V("ld_b", C0 + 29760, 1024, BF16, [128, 1024])
    s1b = [V(f"s1b_{i}", C0 + 30784 + 1024 * i, 1024, BF16, [128, 1024])
           for i in range(2)]
    u2all = V("u2all", C0 + 22592, 4096, BF16, [17, 8, 512])
    xnf2 = V("xnf2", C0 + 8192, 4096, BF16, [128, 8, 512])
    h2r = [V(f"h2r_{i}", C0 + 4096 * i, 4096, BF16, [128, 32, 128])
           for i in range(2)]
    h2w = [V(f"h2w_{i}", C0 + 21568 + 512 * i, 512, BF16, [128, 512])
           for i in range(2)]
    h2wp = [V("h2wp_0", C0 + 21568, 1024, BF16, [128, 1024]),
            V("h2wp_1", C0 + 27712, 1024, BF16, [128, 1024])]
    CEND = C0 + 32832

    D0 = CEND
    mA = [V(f"mA_{r}", D0 + 512 * r, 512, BF16, [128, 512]) for r in range(4)]
    mB = [V(f"mB_{r}", D0 + 2048 + 512 * r, 512, BF16, [128, 512])
          for r in range(6)]
    tab = {}
    for i, nm in enumerate(("cos_g", "sin_g", "cos_l", "sin_l")):
        tab[nm] = V(nm, D0 + 5120 + 512 * i, 512, F16, [128, 16, 32])
    id_bf = V("id_bf", D0 + 7168, 128, BF16, [128, 128])
    id_f16 = V("id_f16", D0 + 7296, 128, F16, [128, 128])
    id_f32 = V("id_f32", D0 + 7424, 256, F32, [128, 128])
    ones_bf = V("ones_bf", D0 + 7680, 64, BF16, [1, 64])
    ones_f32 = V("ones_f32", D0 + 7744, 256, F32, [1, 128])
    n12 = V("n12", D0 + 8000, 32, F32, [128, 16])
    r1b = V("r1b", D0 + 8032, 64, F32, [128, 32])
    r2b = V("r2b", D0 + 8096, 64, F32, [128, 32])
    lncol = V("lncol", D0 + 8160, 64, F32, [128, 32])
    scl = V("scl", D0 + 8224, 16, F32, [128, 8])
    bfcT = V("bfcT", D0 + 8240, 4096, BF16, [17, 4096])
    q1wi = [V(f"q1wi_{i}", D0 + 8240 + 2048 * i, 2048, F32, [128, 1024])
            for i in range(2)]
    bpjT = V("bpjT", D0 + 12336, 1024, BF16, [17, 1024])
    afcT = V("afcT", D0 + 13360, 128, BF16, [128, 128])
    apjT = V("apjT", D0 + 13488, 512, BF16, [128, 512])
    up1 = V("up1", D0 + 14000, 512, BF16, [17, 512])
    cent = V("cent", D0 + 14512, 2048, F32, [128, 1024])
    qtm2 = V("qtm2", D0 + 14512, 1024, F16, [128, 1024])
    ktm2 = V("ktm2", D0 + 15536, 256, F16, [128, 256])
    gbuf = V("gbuf", D0 + 14512, 2048, F32, [128, 1024])
    apstg = V("apstg", D0 + 14512, 1024, BF16, [16, 1024])
    xb = [V(f"xb_{i}", D0 + 16560 + 2048 * i, 2048, F32, [128, 1024])
          for i in range(2)]
    lnrow = V("lnrow", D0 + 16560, 4096, F32, [1, 2048])
    wtmp = V("wtmp", D0 + 20656, 2048, F32, [128, 1024])
    tb = [V(f"tb_{i}", D0 + 16560 + 1024 * i, 1024, F32, [128, 512])
          for i in range(4)]
    tbi = V("tbi", D0 + 20656, 1024, I32, [128, 512])
    invb = V("invb", D0 + 22704, 64, F32, [128, 32])
    epsc = V("epsc", D0 + 22768, 16, F32, [128, 8])
    D0X = D0 + 22784
    ptb += [V(f"ptb_{i + 2}", D0X + 512 * i, 512, BF16, [128, 512])
            for i in range(2)]
    ptbp = [V("ptbA", C0 + 21568, 1024, BF16, [128, 2, 512]),
            V("ptbB", D0X, 1024, BF16, [128, 2, 512])]
    ptbp2 = [V("ptbA2", C0 + 22592, 1024, BF16, [128, 2, 512]),
             V("ptbB2", C0 + 23616, 1024, BF16, [128, 2, 512])]
    lnc_q = V("lnc_q", D0X + 1024, 40, F32, [128, 20])
    lnc_k = V("lnc_k", D0X + 1064, 16, F32, [128, 8])
    qrn = V("qrn", D0X + 1080, 1024, F16, [128, 1024])
    krn = V("krn", D0X + 2104, 256, F16, [128, 256])
    TOT = D0X + 2360
    assert TOT <= 98304, TOT
    qrn2 = V("qrn2", C0 + 12352, 1024, F16, [128, 1024])
    krn2 = V("krn2", C0 + 13376, 256, F16, [128, 256])
    rtmpk = V("rtmpk", TOT, 128, F16, [128, 128])
    assert TOT + 128 <= 98304

    ps = nc.alloc_psum_tensor("ps", [128, 4096], F32)
    ps16 = ps[:].bitcast(F16)
    psbf = ps[:].bitcast(BF16)

    def bank(i, rows=128):
        return ps[0:rows, 512 * i:512 * (i + 1)]

    with nc.Block() as blk:
        p = Prog(nc, blk)
        p.overlap = AR.overlaps()
        for name, sec, step in [("pe", blk.tensor, 1), ("act", blk.scalar, 1),
                                ("dve", blk.vector, 1), ("pool", blk.gpsimd, 1),
                                ("sd", blk.sync, 16), ("gd", blk.gpsimd, 16),
                                ("ad", blk.scalar, 16)]:
            p.engine(name, sec, step)
        E = p.emit

        def PB(i):
            return ("pb", i)

        def act_set_load(set_id):
            def body(s):
                s.add_instruction(mybir.InstLoadActFuncSet(
                    name=nc.get_next_instruction_name(),
                    act_func_set_id=set_id, ins=[], outs=[]))
            p.eng["act"]["sec"](body)

        for idv, dtv in ((id_bf, "id_bf"), (id_f16, "id_f16"), (id_f32, "id_f32")):
            E("dve", lambda v, idv=idv: v.memset(idv, 1.0), writes=[dtv])
            E("pool", lambda g, idv=idv: g.affine_select(
                out=idv, in_=idv, pattern=[[1, 128]], compare_op=ALU.is_equal,
                fill=0.0, base=0, channel_multiplier=-1),
                reads=[dtv], writes=[dtv])
        E("dve", lambda v: v.memset(ones_bf, 1.0), writes=["ones_bf"])
        E("dve", lambda v: v.memset(epsc[:, 0:1], EPS_LN), writes=["epsc"])
        E("dve", lambda v: v.memset(epsc[:, 1:2], EPS_RMS), reads=["epsc"], writes=["epsc"])
        E("dve", lambda v: v.memset(epsc[:, 2:3], 0.5), reads=["epsc"], writes=["epsc"])
        E("dve", lambda v: v.memset(ones_f32, 1.0), writes=["ones_f32"])
        E("dve", lambda v: v.memset(vb[:, 64::65], 1.0), writes=["vb"])
        for r in range(4):
            E("dve", lambda v, r=r: v.memset(mA[r], 1.0), writes=[f"mA_{r}"])
            E("pool", lambda g, r=r: g.affine_select(
                out=mA[r], in_=mA[r], pattern=[[1, 512]], compare_op=ALU.is_ge,
                fill=0.0, base=-128 * r, channel_multiplier=-1),
                reads=[f"mA_{r}"], writes=[f"mA_{r}"])
        for ri in range(6):
            r = ri - 2
            E("dve", lambda v, ri=ri: v.memset(mB[ri], 1.0), writes=[f"mB_{ri}"])
            E("pool", lambda g, ri=ri, r=r: g.affine_select(
                out=mB[ri], in_=mB[ri], pattern=[[1, 512]], compare_op=ALU.is_ge,
                fill=0.0, base=-128 * r, channel_multiplier=-1),
                reads=[f"mB_{ri}"], writes=[f"mB_{ri}"])
            E("pool", lambda g, ri=ri, r=r: g.affine_select(
                out=mB[ri], in_=mB[ri], pattern=[[-1, 512]], compare_op=ALU.is_ge,
                fill=0.0, base=128 * r + 255, channel_multiplier=1),
                reads=[f"mB_{ri}"], writes=[f"mB_{ri}"])
        E("sd", lambda e: e.dma_start(
            out=n12[:, 0:8], in_=n1w.rearrange("(c p) -> p c", p=128),
            allow_slow_non_contiguous=True), writes=["n12"])
        E("sd", lambda e: e.dma_start(
            out=n12[:, 8:16], in_=n2w.rearrange("(c p) -> p c", p=128),
            allow_slow_non_contiguous=True), reads=["n12"], writes=["n12"])

        for base, cname, sname in ((BASE_G, "cos_g", "sin_g"),
                                   (BASE_L, "cos_l", "sin_l")):
            E("pool", lambda g: g.iota(tb[3][0:1, 0:32], pattern=[[1, 32]], base=0,
                                       channel_multiplier=0,
                                       allow_small_or_imprecise_dtypes=True),
              writes=["tb_3"])
            act_set_load(ACT_COMBO_SET)
            E("act", lambda s, base=base: s.activation(
                tb[3][0:1, 32:64], tb[3][0:1, 0:32], AF.Exp,
                scale=-math.log(base) / 32.0), reads=["tb_3"], writes=["tb_3"])
            E("pe", lambda t: t.matmul(ps[0:128, 0:32], ones_f32, tb[3][0:1, 32:64],
                                       start=True, stop=True),
              reads=["ones_f32", "tb_3"], writes=[PB(0)])
            E("dve", lambda v: v.tensor_copy(invb[:, 0:32], ps[0:128, 0:32]),
              reads=[PB(0)], writes=["invb"])
            E("pool", lambda g: g.iota(tb[0][:], pattern=[[128, 16], [0, 32]],
                                       base=0, channel_multiplier=1,
                                       allow_small_or_imprecise_dtypes=True),
              writes=["tb_0"])
            E("dve", lambda v: v.tensor_tensor(
                tb[1][:].rearrange("p (a b) -> p a b", a=16),
                tb[0][:].rearrange("p (a b) -> p a b", a=16),
                invb[:, 0:32].unsqueeze(1).broadcast_to([128, 16, 32]),
                ALU.mult), reads=["tb_0", "invb"], writes=["tb_1"])
            act_set_load(ACT_TRIG_SET)
            for trig, shift in ((sname, 0.0), (cname, PI / 2.0)):
                E("dve", lambda v, shift=shift: v.tensor_scalar(
                    tb[0][:], tb[1][:], shift, 1.0 / TWO_PI, ALU.add, ALU.mult),
                  reads=["tb_1"], writes=["tb_0"])
                E("dve", lambda v: v.tensor_copy(tbi[:], tb[0][:]),
                  reads=["tb_0"], writes=["tbi"])
                E("dve", lambda v, shift=shift: v.tensor_scalar(
                    tb[2][:], tb[1][:], shift, None, ALU.add),
                  reads=["tb_1"], writes=["tb_2"])
                E("dve", lambda v: v.scalar_tensor_tensor(
                    tb[2][:], tbi[:], -TWO_PI, tb[2][:], ALU.mult, ALU.add),
                  reads=["tbi", "tb_2"], writes=["tb_2"])
                for cmp_op, thr, fix in ((ALU.is_gt, PI, -TWO_PI),
                                         (ALU.is_lt, -PI, TWO_PI)):
                    E("dve",
                      lambda v, cmp_op=cmp_op, thr=thr: v.tensor_single_scalar(
                          tb[0][:], tb[2][:], thr, cmp_op),
                      reads=["tb_2"], writes=["tb_0"])
                    E("dve", lambda v, fix=fix: v.scalar_tensor_tensor(
                        tb[2][:], tb[0][:], fix, tb[2][:], ALU.mult, ALU.add),
                      reads=["tb_0", "tb_2"], writes=["tb_2"])
                E("dve", lambda v: v.tensor_scalar(
                    tb[2][:], tb[2][:], PI, -PI, ALU.min, ALU.max),
                  reads=["tb_2"], writes=["tb_2"])
                E("act", lambda s, trig=trig: s.activation(
                    tab[trig].rearrange("p a b -> p (a b)"), tb[2][:], AF.Sin),
                  reads=["tb_2"], writes=[trig])

        act_set_load(ACT_COMBO_SET)

        def perm_in(ap2d):
            return ap2d.rearrange("p (h i two) -> p h two i", two=2, i=32)

        def perm_q_pairs(src2d, dst2d):
            src6 = src2d.rearrange("p (a b c d e) -> p a b c d e",
                                   a=2, b=2, c=4, d=32, e=2)
            dst6 = dst2d.rearrange("p (a c b e d) -> p a c b e d",
                                   a=2, c=4, b=2, e=2, d=32)
            pairs = []
            for a in range(2):
                for b in range(2):
                    inp = src6[:, a, b].transpose([0, 1, 3, 2])
                    outp = dst6[:, a, :, b]
                    pairs.append((inp, outp))
            return pairs

        def prep_tile(dst, src, k, n_out, fold_col, permute, tag):
            E("ad", lambda e, k=k: e.dma_start(
                out=wtmp[0:128, 0:n_out], in_=src[128 * k:128 * (k + 1), :]),
              reads=["wtmp"], writes=["wtmp"])
            inv = wtmp[0:128, 0:n_out]
            col = (n12[:, fold_col + k:fold_col + k + 1]
                   if fold_col is not None else None)
            if permute == "q":
                for inp, outp in perm_q_pairs(inv, dst[k][:]):
                    E("dve", lambda v, inp=inp, outp=outp, col=col:
                      v.tensor_scalar(outp, inp, col, None, ALU.mult),
                      reads=["wtmp", "n12"], writes=[f"{tag}_{k}"])
                return
            if permute:
                inv = perm_in(inv)
            if col is not None:
                E("dve", lambda v, inv=inv, col=col: v.tensor_scalar(
                    dst[k][:], inv, col, None, ALU.mult),
                  reads=["wtmp", "n12"], writes=[f"{tag}_{k}"])
            else:
                E("dve", lambda v, inv=inv: v.tensor_copy(dst[k][:], inv),
                  reads=["wtmp"], writes=[f"{tag}_{k}"])

        w1_q = []
        for k in range(8):
            w1_q.append(lambda k=k: prep_tile(wq1, g_wq, k, 1024, 0, "q", "wq1"))
        for k in range(8):
            w1_q.append(lambda k=k: prep_tile(kvd, g_kvd, k, 256, 0, False, "kvd"))
        for k in range(2):
            def kup_tile(k=k):
                E("ad", lambda e: e.dma_start(
                    out=wtmp[0:128, 0:256], in_=g_kup[128 * k:128 * (k + 1), :]),
                  reads=["wtmp"], writes=["wtmp"])
                E("dve", lambda v: v.tensor_copy(
                    kup[k][:], perm_in(wtmp[0:128, 0:256])),
                  reads=["wtmp"], writes=[f"kup_{k}"])
            w1_q.append(kup_tile)

        w2_q = []
        for k in range(8):
            w2_q.append(lambda k=k: prep_tile(wq2, l_wq, k, 1024, 8, "q", "wq2"))
        for k in range(8):
            w2_q.append(lambda k=k: prep_tile(wk2, l_wk, k, 256, 8, True, "wk2"))
        for k in range(8):
            w2_q.append(lambda k=k: prep_tile(wv2, l_wv, k, 256, 8, False, "wv2"))

        lora_q = []

        def lp_afc():
            E("ad", lambda e: e.dma_start(out=wtmp[0:16, 0:1024], in_=A_fc),
              reads=["wtmp"], writes=["wtmp"])
            E("dve", lambda v: v.tensor_copy(sqj[0:16, :], wtmp[0:16, 0:1024]),
              reads=["wtmp"], writes=["sqj"])
            for k in range(8):
                E("pe", lambda t, k=k: t.transpose(
                    psbf[0:128, 7168 + 16 * k:7168 + 16 * (k + 1)],
                    sqj[0:16, 128 * k:128 * (k + 1)], id_bf[0:16, 0:16]),
                  reads=["sqj", "id_bf"], inc=(k == 7),
                  writes=[PB(7)])
            E("dve", lambda v: v.tensor_copy(afcT[:], psbf[0:128, 7168:7296]),
              reads=[PB(7)], writes=["afcT"])
        lora_q.append(lp_afc)

        for q_ in range(4):
            def lp_apj(q_=q_):
                E("ad", lambda e: e.dma_start(
                    out=wtmp[0:16, 0:1024], in_=A_pj[:, 1024 * q_:1024 * (q_ + 1)]),
                  reads=["wtmp"], writes=["wtmp"])
                E("dve", lambda v: v.tensor_copy(apstg[0:16, :],
                                                 wtmp[0:16, 0:1024]),
                  reads=["wtmp"], writes=["apstg"])
                for j in range(8):
                    fk = 8 * q_ + j
                    E("pe", lambda t, j=j, fk=fk: t.transpose(
                        psbf[0:128, 7168 + 16 * j:7168 + 16 * (j + 1)],
                        apstg[0:16, 128 * j:128 * (j + 1)], id_bf[0:16, 0:16]),
                      reads=["apstg", "id_bf"], inc=(j == 7),
                      writes=[PB(7)])
                E("dve", lambda v, q_=q_: v.tensor_copy(
                    apjT[:, 128 * q_:128 * (q_ + 1)], psbf[0:128, 7168:7296]),
                  reads=[PB(7)], writes=["apjT"])
            lora_q.append(lp_apj)

        for k in range(32):
            def lp_bfc(k=k):
                E("ad", lambda e: e.dma_start(
                    out=wtmp[0:128, 0:16], in_=B_fc[128 * k:128 * (k + 1), :]),
                  reads=["wtmp"], writes=["wtmp"])
                E("dve", lambda v: v.tensor_copy(sqj[0:128, 0:16],
                                                 wtmp[0:128, 0:16]),
                  reads=["wtmp"], writes=["sqj"])
                E("pe", lambda t: t.transpose(
                    psbf[0:16, 7168:7296], sqj[0:128, 0:16], id_bf),
                  reads=["sqj", "id_bf"], writes=[PB(7)])
                E("dve", lambda v: v.tensor_copy(
                    bfcT[0:16, 128 * k:128 * (k + 1)], psbf[0:16, 7168:7296]),
                  reads=[PB(7)], writes=["bfcT"])
            lora_q.append(lp_bfc)
        for k in range(8):
            def lp_bpj(k=k):
                E("ad", lambda e: e.dma_start(
                    out=wtmp[0:128, 0:16], in_=B_pj[128 * k:128 * (k + 1), :]),
                  reads=["wtmp"], writes=["wtmp"])
                E("dve", lambda v: v.tensor_copy(sqj[0:128, 0:16],
                                                 wtmp[0:128, 0:16]),
                  reads=["wtmp"], writes=["sqj"])
                E("pe", lambda t: t.transpose(
                    psbf[0:16, 7168:7296], sqj[0:128, 0:16], id_bf),
                  reads=["sqj", "id_bf"], writes=[PB(7)])
                E("dve", lambda v: v.tensor_copy(
                    bpjT[0:16, 128 * k:128 * (k + 1)], psbf[0:16, 7168:7296]),
                  reads=[PB(7)], writes=["bpjT"])
            lora_q.append(lp_bpj)

        def lp_bias():
            E("gd", lambda e: e.dma_start(out=bfcT[16:17, :], in_=b_fc[None, :]),
              writes=["bfcT"])
            E("gd", lambda e: e.dma_start(out=bpjT[16:17, :], in_=b_pj[None, :]),
              writes=["bpjT"])
        lora_q.append(lp_bias)

        q1_q = []
        W_pj_t = W_pj.rearrange("d (a b) -> (d a) b", a=4)

        def qp1_tile(W, i, maxc, tag, first):
            b = i % 2
            E(("gd", "ad")[i % 2],
              lambda e, i=i, W=W, b=b: e.dma_start(
                  out=q1wi[b][:], in_=W[128 * i:128 * (i + 1), :]),
              reads=[f"q1wi_{b}"], writes=[f"q1wi_{b}"])
            E("dve", lambda v, b=b: v.tensor_reduce(
                lncol[:, 20:21], q1wi[b][:], AXX, ALU.max,
                apply_absolute_value=True),
              reads=[f"q1wi_{b}"], writes=["lncol"])
            if first:
                E("dve", lambda v, maxc=maxc: v.tensor_copy(maxc, lncol[:, 20:21]),
                  reads=["lncol"], writes=[tag])
            else:
                E("dve", lambda v, maxc=maxc: v.tensor_tensor(
                    maxc, maxc, lncol[:, 20:21], ALU.max),
                  reads=["lncol", tag], writes=[tag])

        for i in range(32):
            q1_q.append(lambda i=i: qp1_tile(W_fc, i, lncol[:, 16:17], "mx_fc",
                                             i == 0))
        for i in range(32):
            q1_q.append(lambda i=i: qp1_tile(W_pj_t, i, lncol[:, 17:18], "mx_pj",
                                             i == 0))

        def qscale(maxc, tag, wi):
            E("pe", lambda t, maxc=maxc: t.transpose(ps[0:1, 3584:3712], maxc,
                                                     id_f32),
              reads=[tag, "id_f32"], writes=[PB(7)])
            E("dve", lambda v: v.tensor_reduce(
                lncol[0:1, 24:25], ps[0:1, 3584:3712], AXX, ALU.max),
              reads=[PB(7)], writes=["lncol"])
            E("dve", lambda v: v.reciprocal(lncol[0:1, 25:26], lncol[0:1, 24:25]),
              reads=["lncol"], writes=["lncol"])
            E("dve", lambda v: v.tensor_scalar(
                lncol[0:1, 26:27], lncol[0:1, 24:25], 1.0 / 127.0, None, ALU.mult),
              reads=["lncol"], writes=["lncol"])
            E("dve", lambda v: v.tensor_scalar(
                lncol[0:1, 27:28], lncol[0:1, 25:26], 127.0, None, ALU.mult),
              reads=["lncol"], writes=["lncol"])
            E("pe", lambda t: t.matmul(ps[0:128, 3584:3586], ones_f32,
                                       lncol[0:1, 26:28], start=True, stop=True),
              reads=["lncol", "ones_f32"], writes=[PB(7)])
            E("dve", lambda v, wi=wi: v.tensor_copy(
                scl[:, 2 * wi:2 * wi + 2], ps[0:128, 3584:3586]),
              reads=[PB(7)], writes=["scl"])

        q1_q.append(lambda: qscale(lncol[:, 16:17], "mx_fc", 0))
        q1_q.append(lambda: qscale(lncol[:, 17:18], "mx_pj", 1))

        q2_q = []
        wqpjd_t = wqpjd.rearrange("d (a b) -> (d a) b", a=4)

        def qp2_tile(W, Wdst, i, wi, tag):
            b = i % 2
            icol = scl[:, 2 * wi + 1:2 * wi + 2]
            scol = scl[:, 2 * wi:2 * wi + 1]
            E(("gd", "ad")[i % 2],
              lambda e, i=i, W=W, b=b: e.dma_start(
                  out=qwi[b][:], in_=W[128 * i:128 * (i + 1), :]),
              reads=[f"qwi_{b}"], writes=[f"qwi_{b}"])
            E("act", lambda s_, b=b, icol=icol: s_.activation(
                qw_y[:], qwi[b][:], AF.Identity, scale=icol,
                bias=epsc[:, 2:3]),
              reads=[f"qwi_{b}", "scl", "epsc"], writes=["qw_y"])
            E("dve", lambda v: v.tensor_copy(qw_i[:], qw_y[:]),
              reads=["qw_y"], writes=["qw_i"])
            E("dve", lambda v: v.scalar_tensor_tensor(
                qw_g[:], qw_i[:], 1.0, qw_y[:], ALU.bypass, ALU.is_gt),
              reads=["qw_i", "qw_y"], writes=["qw_g"])
            E("dve", lambda v: v.scalar_tensor_tensor(
                qw_g[:], qw_g[:], -1.0, qw_i[:], ALU.mult, ALU.add),
              reads=["qw_g", "qw_i"], writes=["qw_g"])
            E("dve", lambda v, scol=scol: v.tensor_scalar(
                qw_o[:], qw_g[:], scol, None, ALU.mult),
              reads=["qw_g", "scl"], writes=["qw_o"])
            E(("ad", "gd")[i % 2],
              lambda e, i=i, Wdst=Wdst: e.dma_start(
                  out=Wdst[128 * i:128 * (i + 1), :], in_=qw_o[:]),
              reads=["qw_o"], writes=[(tag, i)])

        for i in range(32):
            q2_q.append(lambda i=i: qp2_tile(W_fc, wqfcd, i, 0, "wqfc"))
        for i in range(32):
            q2_q.append(lambda i=i: qp2_tile(W_pj_t, wqpjd_t, i, 1, "wqpj"))

        def drain(q, n):
            for _ in range(min(n, len(q))):
                q.pop(0)()

        def rsqrt_cols(src_col, dst_col, n_mean, eps_col, wkey):
            E("act", lambda s: s.activation(lncol[:, 12:13], src_col, AF.Ln,
                                            scale=1.0 / n_mean, bias=eps_col),
              reads=["lncol", "epsc"], writes=["lncol"])
            E("act", lambda s: s.activation(dst_col, lncol[:, 12:13], AF.Exp,
                                            scale=-0.5),
              reads=["lncol"], writes=[wkey])

        def ln_chunk(s, c):
            i = c % 2
            E(("ad", "gd")[c % 2], lambda e: e.dma_start(
                out=xb[i][:], in_=x[s, 128 * c:128 * (c + 1), :]),
              reads=[f"xb_{i}"], writes=[f"xb_{i}"])
            E("dve", lambda v: v.tensor_reduce(lncol[:, 0:1], xb[i][:], AXX,
                                               ALU.add),
              reads=[f"xb_{i}"], writes=["lncol"])
            E("dve", lambda v: v.tensor_scalar(lncol[:, 1:2], lncol[:, 0:1],
                                               1.0 / D, None, ALU.mult),
              reads=["lncol"], writes=["lncol"])
            E("dve", lambda v: v.tensor_scalar(cent[:], xb[i][:], lncol[:, 1:2],
                                               None, ALU.subtract),
              reads=[f"xb_{i}", "lncol"], writes=["cent"])
            E("dve", lambda v: v.scalar_tensor_tensor(
                sqj[:], cent[:], 1.0, cent[:], ALU.bypass, ALU.mult,
                accum_out=lncol[:, 2:3]),
              reads=["cent"], writes=["sqj", "lncol"])
            rsqrt_cols(lncol[:, 2:3], lncol[:, 4:5], D, epsc[:, 0:1], "lncol")
            xnb = s1b[i]
            E("dve", lambda v, xnb=xnb: v.tensor_scalar(
                xnb[:], cent[:], lncol[:, 4:5], None, ALU.mult),
              reads=["cent", "lncol"], writes=[f"s1b_{i}"])
            E("dve", lambda v, xnb=xnb: v.scalar_tensor_tensor(
                sqj[:], xnb[:], 1.0, xnb[:], ALU.bypass, ALU.mult,
                accum_out=lncol[:, 5:6]),
              reads=[f"s1b_{i}"], writes=["sqj", "lncol"])
            rsqrt_cols(lncol[:, 5:6], r1b[:, 16 * s + c:16 * s + c + 1], D,
                       epsc[:, 1:2], "r1b")
            E("ad", lambda e, xnb=xnb, s=s, c=c: e.dma_start(
                out=xnd[2048 * s + 128 * c:2048 * s + 128 * (c + 1), :],
                in_=xnb[:]),
              reads=[f"s1b_{i}"], writes=[("xnd", s, c)])

        def rope_qknorm(src, n_cols, cname, sname, rk, dst, wk, cc,
                        eng="dve", rbuf=None, rbk="rtmp"):
            nh = n_cols // 64
            lc, lck = (lnc_q, "lnc_q") if n_cols == 1024 else (lnc_k, "lnc_k")
            E("dve", lambda v: v.tensor_tensor(sqj[:, 0:n_cols], src, src,
                                               ALU.mult),
              reads=[rk], writes=["sqj"])
            E("dve", lambda v: v.tensor_reduce(
                lc[:, 0:nh],
                sqj[:, 0:n_cols].rearrange("p (h i) -> p h i", i=64),
                AXX, ALU.add),
              reads=["sqj"], writes=[lck])
            E("act", lambda s: s.activation(lc[:, 0:nh], lc[:, 0:nh], AF.Ln,
                                            scale=1.0 / 64.0, bias=epsc[:, 1:2]),
              reads=[lck, "epsc"], writes=[lck])
            E("act", lambda s: s.activation(lc[:, 0:nh], lc[:, 0:nh],
                                            AF.Exp, scale=-0.5),
              reads=[lck], writes=[lck])
            nbuf = qn_ if n_cols == 1024 else kn_
            nk = "qn" if n_cols == 1024 else "kn"
            E("dve", lambda v, nbuf=nbuf: v.tensor_tensor(
                nbuf[:, 0:n_cols].rearrange("p (h i) -> p h i", i=64),
                src.rearrange("p (h i) -> p h i", i=64),
                lc[:, 0:nh].unsqueeze(2).broadcast_to([128, nh, 64]),
                ALU.mult),
              reads=[rk, lck], writes=[nk])
            nb = nbuf[:, 0:n_cols].rearrange("p (h two i) -> p h two i",
                                             two=2, i=32)
            db = dst.rearrange("p (h two i) -> p h two i", two=2, i=32)
            e_, o_ = nb[:, :, 0, :], nb[:, :, 1, :]
            de, do = db[:, :, 0, :], db[:, :, 1, :]
            cosb = tab[cname][:, cc, :].unsqueeze(1).broadcast_to([128, nh, 32])
            sinb = tab[sname][:, cc, :].unsqueeze(1).broadcast_to([128, nh, 32])
            if rbuf is None:
                rbuf = rtmp
            half = rbuf[:, 0:n_cols // 2].rearrange("p (h i) -> p h i", i=32)
            E(eng, lambda v: v.tensor_tensor(half, o_, sinb, ALU.mult),
              reads=[nk, sname], writes=[rbk])
            E(eng, lambda v: v.tensor_tensor(de, e_, cosb, ALU.mult),
              reads=[nk, cname], writes=[wk])
            E(eng, lambda v: v.tensor_tensor(de, de, half, ALU.subtract),
              reads=[rbk, wk], writes=[wk])
            E(eng, lambda v: v.tensor_tensor(half, e_, sinb, ALU.mult),
              reads=[nk, sname], writes=[rbk])
            E(eng, lambda v: v.tensor_tensor(do, o_, cosb, ALU.mult),
              reads=[nk, cname], writes=[wk])
            E(eng, lambda v: v.tensor_tensor(do, do, half, ALU.add),
              reads=[rbk, wk], writes=[wk])

        def attention(s, aidx, hook_q=None):
            src_d = xnd if aidx == 0 else s1d
            srctag = "xnd" if aidx == 0 else "s1d"
            rbuf, rtag = (r1b, "r1b") if aidx == 0 else (r2b, "r2b")
            cname = "cos_g" if aidx == 0 else "cos_l"
            sname = "sin_g" if aidx == 0 else "sin_l"
            qw, qtag = (wq1, "wq1") if aidx == 0 else (wq2, "wq2")
            wo_w, wo_tag = (wo1, "wo1") if aidx == 0 else (wo2, "wo2")
            for sp in range(NSP):
                row0 = 2048 * s + 512 * sp
                span_keys = [(srctag, s, 4 * sp + j) for j in range(4)]
                for dk in range(8):
                    E("sd",
                      lambda e, dk=dk: e.dma_start(
                          out=xnf[:, dk, :],
                          in_=src_d[row0:row0 + 512, 128 * dk:128 * (dk + 1)],
                          transpose=True),
                      reads=span_keys + ["xnf"], writes=["xnf"])
                if aidx == 0 and SUBPH >= 2:
                    for lc in range(2):
                        for k in range(8):
                            E("pe", lambda t, k=k, lc=lc: t.matmul(
                                bank(lc), kvd[k][:, 128 * lc:128 * (lc + 1)],
                                xnf[:, k, :],
                                start=(k == 0), stop=(k == 7)),
                              reads=["xnf", f"kvd_{k}"], inc=(k == 7),
                              writes=[PB(lc)])
                        E("act", lambda s_, lc=lc: s_.activation(
                            latf[:, lc, :], bank(lc), AF.Copy),
                          reads=[PB(lc)], writes=["latf"])
                def prod_mm(tci):
                    c = 4 * sp + tci
                    qtb, qtk = (qtm, "qtm") if tci % 2 == 0 else (qtm2, "qtm2")
                    ktb, ktk = (ktm, "ktm") if tci % 2 == 0 else (ktm2, "ktm2")
                    rcol = rbuf[:, 16 * s + c:16 * s + c + 1]
                    for half in range(2):
                        for k in range(8):
                            E("pe", lambda t, k=k, half=half, tci=tci: t.matmul(
                                bank(2 + half),
                                xnf[:, k, 128 * tci:128 * (tci + 1)],
                                qw[k][:, 512 * half:512 * (half + 1)],
                                start=(k == 0), stop=(k == 7)),
                              reads=["xnf", f"{qtag}_{k}"], inc=(k == 7),
                              writes=[PB(2 + half)])
                    E("act", lambda s_, rcol=rcol, qtb=qtb: s_.activation(
                        qtb[:], ps[0:128, 1024:2048], AF.Copy, scale=rcol),
                      reads=[PB(2), PB(3), rtag], writes=[qtk])
                    if aidx == 0:
                        for lc in range(2):
                            E("pe", lambda t, lc=lc, tci=tci: t.matmul(
                                bank(0)[:, 0:256],
                                latf[:, lc, 128 * tci:128 * (tci + 1)],
                                kup[lc][:], start=(lc == 0), stop=(lc == 1)),
                              reads=["latf", f"kup_{lc}"], inc=(lc == 1),
                              writes=[PB(0)])
                    else:
                        for k in range(8):
                            E("pe", lambda t, k=k, tci=tci: t.matmul(
                                bank(0)[:, 0:256],
                                xnf[:, k, 128 * tci:128 * (tci + 1)],
                                wk2[k][:], start=(k == 0), stop=(k == 7)),
                              reads=["xnf", f"wk2_{k}"], inc=(k == 7),
                              writes=[PB(0)])
                    E("act", lambda s_, rcol=rcol, ktb=ktb: s_.activation(
                        ktb[:], bank(0)[:, 0:256], AF.Copy, scale=rcol),
                      reads=[PB(0), rtag], writes=[ktk])
                    if aidx == 0:
                        for lc in range(2):
                            E("pe", lambda t, lc=lc, tci=tci: t.matmul(
                                bank(1)[:, 0:256],
                                latf[:, lc, 128 * tci:128 * (tci + 1)],
                                vup[lc][:], start=(lc == 0), stop=(lc == 1)),
                              reads=["latf", f"vup_{lc}"], inc=(lc == 1),
                              writes=[PB(1)])
                    else:
                        for k in range(8):
                            E("pe", lambda t, k=k, tci=tci: t.matmul(
                                bank(1)[:, 0:256],
                                xnf[:, k, 128 * tci:128 * (tci + 1)],
                                wv2[k][:], start=(k == 0), stop=(k == 7)),
                              reads=["xnf", f"wv2_{k}"], inc=(k == 7),
                              writes=[PB(1)])
                    E("act", lambda s_, rcol=rcol, c=c: s_.activation(
                        vb.rearrange("p (g y) -> p g y", g=4)[:, :,
                                                              65 * c:65 * c + 64],
                        bank(1)[:, 0:256].rearrange("p (g y) -> p g y", g=4),
                        AF.Copy, scale=rcol),
                      reads=[PB(1), rtag], writes=["vb"])

                def prod_rope(tci):
                    c = 4 * sp + tci
                    qb_, qk_ = (qrn, "qrn") if tci % 2 == 0 else (qrn2, "qrn2")
                    kb_, kk_ = (krn, "krn") if tci % 2 == 0 else (krn2, "krn2")
                    qtb, qtk = (qtm, "qtm") if tci % 2 == 0 else (qtm2, "qtm2")
                    ktb, ktk = (ktm, "ktm") if tci % 2 == 0 else (ktm2, "ktm2")
                    rope_qknorm(qtb[:], 1024, cname, sname, qtk, qb_[:],
                                qk_, c)
                    rope_qknorm(ktb[:], 256, cname, sname, ktk, kb_[:],
                                kk_, c, eng="pool", rbuf=rtmpk, rbk="rtmpk")

                def prod_tr(tci):
                    c = 4 * sp + tci
                    qb_, qk_ = (qrn, "qrn") if tci % 2 == 0 else (qrn2, "qrn2")
                    kb_, kk_ = (krn, "krn") if tci % 2 == 0 else (krn2, "krn2")
                    for mc in range(8):
                        E("pe", lambda t, mc=mc, qb_=qb_: t.transpose(
                            ps16[0:128, 6144 + 128 * mc:6144 + 128 * (mc + 1)],
                            qb_[:, 128 * mc:128 * (mc + 1)], id_f16),
                          reads=[qk_, "id_f16"], inc=(mc == 7),
                          writes=[PB(6), PB(7)])
                    E("dve", lambda v, tci=tci: v.tensor_copy(
                        q1f[:, :, 128 * tci:128 * (tci + 1)],
                        ps16[0:128, 6144:7168].rearrange("p (a b) -> p a b",
                                                         a=8)),
                      reads=[PB(6), PB(7)], writes=["q1f"])
                    for mc in range(2):
                        E("pe", lambda t, mc=mc, kb_=kb_: t.transpose(
                            ps16[0:128, 6144 + 128 * mc:6144 + 128 * (mc + 1)],
                            kb_[:, 128 * mc:128 * (mc + 1)], id_f16),
                          reads=[kk_, "id_f16"], inc=(mc == 1),
                          writes=[PB(6)])
                    E("dve", lambda v, c=c: v.tensor_copy(
                        k1f[:, :, 128 * c:128 * (c + 1)],
                        ps16[0:128, 6144:6400].rearrange("p (a b) -> p a b",
                                                         a=2)),
                      reads=[PB(6)], writes=["k1f"])

                for tci in range(4):
                    prod_mm(tci)
                    if tci > 1:
                        prod_tr(tci - 2)
                    prod_rope(tci)
                prod_tr(2)
                prod_tr(3)
                for g in range(4):
                    kb = 64 * (g % 2)
                    mck = g // 2
                    if aidx == 0:
                        krange = list(range(0, 4 * sp + 4))
                    else:
                        krange = list(range(max(4 * sp - 2, 0), 4 * sp + 4))
                    nk_ = len(krange)
                    heads = []
                    for hh in range(4):
                        h = 4 * g + hh
                        qb = 64 * ((h // 4) % 2)
                        qmc = (h % 4) + 4 * (h // 8)
                        assert qb == kb
                        heads.append((hh, qmc))

                    def mask_for(kc):
                        if aidx == 0:
                            rr = kc - 4 * sp
                            if rr < 0:
                                return None, None
                            return mA[rr], f"mA_{rr}"
                        rr = kc - 4 * sp + 2
                        return mB[rr], f"mB_{rr}"

                    def pbuf(ki):
                        return (ptbp, ("ptbA", "ptbB")) if ki % 2 == 0 \
                            else (ptbp2, ("ptbA2", "ptbB2"))

                    def emit_qk(kc, pr, kb=kb, mck=mck, heads=heads):
                        for hh, qmc in heads[2 * pr:2 * pr + 2]:
                            xtra = [PB(2), PB(3)] if (pr, hh % 2) == (0, 0) \
                                else []
                            E("pe", lambda t, kc=kc, hh=hh, qmc=qmc: t.matmul(
                                bank(hh),
                                k1f[kb:kb + 64, mck, 128 * kc:128 * (kc + 1)],
                                q1f[kb:kb + 64, qmc, :],
                                start=True, stop=True),
                              reads=["k1f", "q1f"] + xtra, writes=[PB(hh)])

                    def emit_evac(ki, kc, pr, mask_for=mask_for):
                        mask, mkey = mask_for(kc)
                        buf, keys = pbuf(ki)
                        E("act", lambda s_, pr=pr, buf=buf: s_.activation(
                            buf[pr][:].rearrange("p a b -> p (a b)"),
                            ps[0:128, 1024 * pr:1024 * (pr + 1)],
                            AF.Exp, scale=0.125),
                          reads=[PB(2 * pr), PB(2 * pr + 1)],
                          writes=[keys[pr]])
                        if mask is not None:
                            E("dve", lambda v, pr=pr, mask=mask, buf=buf:
                              v.tensor_tensor(
                                  buf[pr][:], buf[pr][:],
                                  mask[:].unsqueeze(1).broadcast_to(
                                      [128, 2, 512]), ALU.mult),
                              reads=[keys[pr], mkey], writes=[keys[pr]])

                    def emit_pv(ki, kc, g=g, heads=heads, nk_=nk_):
                        buf, keys = pbuf(ki)
                        for hh, qmc in heads:
                            xtra = [keys[1]] if hh == 0 else []
                            E("pe", lambda t, kc=kc, hh=hh, ki=ki, buf=buf:
                              t.matmul(
                                  ps[0:65, 2048 + 512 * hh:2048 + 512 * (hh + 1)],
                                  vb[:, 1040 * g + 65 * kc:
                                     1040 * g + 65 * (kc + 1)],
                                  buf[hh // 2][:, hh % 2, :],
                                  start=(ki == 0), stop=(ki == nk_ - 1)),
                              reads=[keys[hh // 2], "vb"] + xtra,
                              inc=(hh == 3), writes=[PB(4 + hh)])

                    for ki, kc in enumerate(krange):
                        emit_qk(kc, 0)
                        emit_evac(ki, kc, 0)
                        emit_qk(kc, 1)
                        emit_evac(ki, kc, 1)
                        if ki > 0:
                            emit_pv(ki - 1, krange[ki - 1])
                    emit_pv(nk_ - 1, krange[nk_ - 1])

                    gp = g % 2
                    E("act", lambda s_: s_.activation(
                        lnrow[:], ps[64:65, 2048:4096], AF.Ln),
                      reads=[PB(4), PB(5), PB(6), PB(7)],
                      writes=["lnrow"])
                    E("act", lambda s_: s_.activation(
                        rec_sb[0:1, 0:2048], lnrow[:], AF.Exp,
                        scale=-1.0),
                      reads=["lnrow", PB(4), PB(5), PB(6), PB(7)],
                      writes=["recb"])
                    E("ad", lambda e, gp=gp: e.dma_start(
                        out=recd[gp:gp + 1, :], in_=rec_sb[0:1, 0:2048]),
                      reads=["recb"], writes=[("recd", gp)])
                    E("gd", lambda e, gp=gp: e.dma_start(
                        out=recb[0:64, 0:2048],
                        in_=recd[gp][None, :].broadcast_to([64, 2048])),
                      reads=[("recd", gp), "recb"], writes=["recb"])
                    for par in range(2):
                        E("dve", lambda v, par=par, g=g: v.tensor_tensor(
                            attn_m[64 * par:64 * (par + 1),
                                   2 * g:2 * g + 2, :],
                            ps[0:64, 2048:4096].rearrange(
                                "p (a c b) -> p a c b", a=2, c=2)[:, :, par, :],
                            recb[0:64, 0:2048].rearrange(
                                "p (a c b) -> p a c b", a=2, c=2)[:, :, par, :],
                            ALU.mult),
                          reads=[PB(4 + par), PB(6 + par), "recb"],
                          writes=["attn_m"])
                for tci in range(4 if SUBPH >= 5 else 0):
                    c = 4 * sp + tci
                    i = tci % 2
                    ldb = ld_a if i == 0 else ld_b
                    ldk = "ld_a" if i == 0 else "ld_b"
                    E("sd", lambda e, c=c, ldb=ldb, s=s: e.dma_start(
                        out=ldb[:],
                        in_=src_d[2048 * s + 128 * c:2048 * s + 128 * (c + 1), :]),
                      reads=[ldk, (srctag, s, c)], writes=[ldk])
                    for half in range(2):
                        for mc in range(8):
                            E("pe", lambda t, mc=mc, half=half, tci=tci: t.matmul(
                                bank(2 + half),
                                attn_m[:, mc, 128 * tci:128 * (tci + 1)],
                                wo_w[mc][:, 512 * half:512 * (half + 1)],
                                start=(mc == 0), stop=(mc == 7)),
                              reads=["attn_m", f"{wo_tag}_{mc}"], inc=(mc == 7),
                              writes=[PB(2 + half)])
                    sb_o = s1b[i]
                    for half in range(2):
                        E("dve", lambda v, half=half, sb_o=sb_o, ldb=ldb:
                          v.scalar_tensor_tensor(
                              sb_o[:, 512 * half:512 * (half + 1)],
                              bank(2 + half), 1.0,
                              ldb[:, 512 * half:512 * (half + 1)],
                              ALU.bypass, ALU.add),
                          reads=[PB(2 + half), ldk], writes=[f"s1b_{i}"])
                    if aidx == 0:
                        E("dve", lambda v, sb_o=sb_o: v.scalar_tensor_tensor(
                            sqj[:], sb_o[:], 1.0, sb_o[:],
                            ALU.bypass, ALU.mult, accum_out=lncol[:, 6:7]),
                          reads=[f"s1b_{i}"], writes=["sqj", "lncol"])
                        rsqrt_cols(lncol[:, 6:7],
                                   r2b[:, 16 * s + c:16 * s + c + 1], D,
                                   epsc[:, 1:2], "r2b")
                        dst, dk_ = s1d, ("s1d", s, c)
                    else:
                        dst, dk_ = sdd, ("sdd", s, c)
                    E("gd", lambda e, sb_o=sb_o, s=s, c=c, dst=dst: e.dma_start(
                        out=dst[2048 * s + 128 * c:2048 * s + 128 * (c + 1), :],
                        in_=sb_o[:]),
                      reads=[f"s1b_{i}"], writes=[dk_])
                    if hook_q is not None:
                        rem = (NSP - sp) * 4 - tci
                        drain(hook_q, (len(hook_q) + rem - 1) // rem)

        def ffn_pass1():
            act_set_load(ACT_GELU_SET)
            for k in range(8):
                E("sd",
                  lambda e, k=k: e.dma_start(
                      out=wfcT[k][:], in_=wqfcd[:, 128 * k:128 * (k + 1)],
                      transpose=True),
                  reads=[("wqfc", i) for i in range(32)] + [f"wfcT_{k}"],
                  writes=[f"wfcT_{k}"])
            def xnf_load(gsp):
                row0 = 512 * gsp
                xb_, xk_ = (xnf, "xnf") if gsp % 2 == 0 else (xnf2, "xnf2")
                span_keys = [("xnd", gsp // 4, 4 * (gsp % 4) + j)
                             for j in range(4)]
                for dk in range(8):
                    E("sd",
                      lambda e, dk=dk, row0=row0, xb_=xb_: e.dma_start(
                          out=xb_[:, dk, :],
                          in_=xnd[row0:row0 + 512, 128 * dk:128 * (dk + 1)],
                          transpose=True),
                      reads=span_keys + [xk_], writes=[xk_])

            xnf_load(0)
            for gsp in range(8):
                if gsp + 1 < 8:
                    xnf_load(gsp + 1)
                row0 = 512 * gsp
                xb_, xk_ = (xnf, "xnf") if gsp % 2 == 0 else (xnf2, "xnf2")
                for k in range(8):
                    E("pe", lambda t, k=k, xb_=xb_: t.matmul(
                        ps[0:16, 3584:4096], afcT[:, 16 * k:16 * (k + 1)],
                        xb_[:, k, :], start=(k == 0), stop=(k == 7)),
                      reads=[xk_, "afcT"], inc=(k == 7),
                      writes=[PB(7)])
                E("dve", lambda v: v.memset(up1[:], 1.0), writes=["up1"])
                E("act", lambda s_: s_.activation(
                    up1[0:16, :], ps[0:16, 3584:4096], AF.Copy, scale=2.0),
                  reads=[PB(7), "up1"], writes=["up1"])
                for fk in range(32):
                    bk = fk % 4
                    for k in range(8):
                        E("pe", lambda t, k=k, fk=fk, bk=bk, xb_=xb_: t.matmul(
                            bank(bk), wfcT[k][:, 128 * fk:128 * (fk + 1)],
                            xb_[:, k, :], start=(k == 0), stop=False),
                          reads=[xk_, f"wfcT_{k}"], inc=False,
                          writes=[("pb", bk)])
                    E("pe", lambda t, fk=fk, bk=bk: t.matmul(
                        bank(bk), bfcT[0:17, 128 * fk:128 * (fk + 1)], up1[:],
                        start=False, stop=True),
                      reads=["bfcT", "up1"], writes=[PB(bk)])
                    if fk % 2 == 0:
                        continue
                    wi = (fk // 2) % 2
                    wkey = f"h2wp_{wi}"
                    bk0 = bk - 1
                    E("act", lambda s_, bk0=bk0, wi=wi: s_.activation(
                        h2wp[wi][:], ps[0:128, 512 * bk0:512 * bk0 + 1024],
                        AF.Gelu),
                      reads=[PB(bk0), PB(bk0 + 1)], writes=[wkey])
                    E("gd", lambda e, fk=fk, wi=wi, row0=row0, gsp=gsp:
                      e.dma_start(
                          out=h2d[128 * (fk - 1):128 * (fk + 1),
                                  row0:row0 + 512].rearrange(
                              "(a p) t -> p a t", a=2),
                          in_=h2wp[wi][:].rearrange("p (a t) -> p a t", a=2)),
                      reads=[wkey],
                      writes=[("h2d", gsp, fk - 1), ("h2d", gsp, fk)])
                    for j in range(2):
                        fj = fk - 1 + j
                        E("pe", lambda t, fj=fj, wi=wi, j=j: t.matmul(
                            ps[0:16, 3072:3584],
                            apjT[:, 16 * fj:16 * (fj + 1)],
                            h2wp[wi][:, 512 * j:512 * (j + 1)],
                            start=(fj == 0), stop=(fj == 31)),
                          reads=[wkey, "apjT"], inc=(fj == 31),
                          writes=[PB(6)])
                E("dve", lambda v, gsp=gsp: v.memset(u2all[:, gsp, :], 1.0),
                  writes=["u2all"])
                E("act", lambda s_, gsp=gsp: s_.activation(
                    u2all[0:16, gsp, :], ps[0:16, 3072:3584], AF.Copy,
                    scale=2.0),
                  reads=[PB(6), "u2all"], writes=["u2all"])

        def ffn_pass2():
            act_set_load(ACT_SILU_SET)
            for fk in range(32):
                E("sd",
                  lambda e, fk=fk: e.dma_start(
                      out=wpjT[fk][:], in_=wqpjd[:, 128 * fk:128 * (fk + 1)],
                      transpose=True),
                  reads=[("wqpj", i) for i in range(32)] + [f"wpjT_{fk}"],
                  writes=[f"wpjT_{fk}"])
            for cg in range(32):
                s, c = cg // 16, cg % 16
                i = cg % 2
                hr, hk = h2r[i], f"h2r_{i}"
                E("sd",
                  lambda e, cg=cg, hr=hr: e.dma_start(
                      out=hr[:],
                      in_=h2dv[:, :, 128 * cg:128 * (cg + 1)]),
                  reads=[("h2d", cg // 4, fk) for fk in range(32)] + [hk],
                  writes=[hk])
                pb0 = 2 * i
                for half in range(2):
                    for fk in range(32):
                        E("pe", lambda t, fk=fk, half=half, hr=hr, pb0=pb0:
                          t.matmul(
                              bank(pb0 + half), hr[:, fk, :],
                              wpjT[fk][:, 512 * half:512 * (half + 1)],
                              start=(fk == 0), stop=False),
                          reads=[hk, f"wpjT_{fk}"], inc=False,
                          writes=[("pb", pb0 + half)])
                    E("pe", lambda t, half=half, cg=cg, pb0=pb0: t.matmul(
                        bank(pb0 + half),
                        u2all[:, cg // 4, 128 * (cg % 4):128 * (cg % 4) + 128],
                        bpjT[0:17, 512 * half:512 * (half + 1)],
                        start=False, stop=True),
                      reads=["u2all", "bpjT"], writes=[PB(pb0 + half)])
                    E("act", lambda s_, half=half, pb0=pb0: s_.activation(
                        gbuf[:, 512 * half:512 * (half + 1)],
                        bank(pb0 + half), AF.Silu),
                      reads=[PB(pb0 + half)], writes=["gbuf"])
                ldb = ld_a if i == 0 else ld_b
                ldk = "ld_a" if i == 0 else "ld_b"
                E("ad", lambda e, s=s, c=c, ldb=ldb: e.dma_start(
                    out=ldb[:],
                    in_=sdd[2048 * s + 128 * c:2048 * s + 128 * (c + 1), :]),
                  reads=[("sdd", s, c), ldk], writes=[ldk])
                E("sd", lambda e, s=s, c=c, i=i: e.dma_start(
                    out=xb[i][:], in_=x[s, 128 * c:128 * (c + 1), :]),
                  reads=[f"xb_{i}"], writes=[f"xb_{i}"])
                E("dve", lambda v, ldb=ldb: v.tensor_tensor(
                    gbuf[:], gbuf[:], ldb[:], ALU.mult),
                  reads=["gbuf", ldk], writes=["gbuf"])
                E("dve", lambda v, i=i: v.tensor_tensor(
                    gbuf[:], gbuf[:], xb[i][:], ALU.add),
                  reads=["gbuf", f"xb_{i}"], writes=["gbuf"])
                E("gd", lambda e, s=s, c=c: e.dma_start(
                    out=out[s, 128 * c:128 * (c + 1), :], in_=gbuf[:]),
                  reads=["gbuf"], writes=[("out", s, c)])

        for k in range(8):
            E("gd", lambda e, k=k: e.dma_start(
                out=wo1[k][:], in_=g_wo[128 * k:128 * (k + 1), :]),
              writes=[f"wo1_{k}"])
        for k in range(2):
            E("gd", lambda e, k=k: e.dma_start(
                out=vup[k][:], in_=g_vup[128 * k:128 * (k + 1), :]),
              writes=[f"vup_{k}"])
        if PHASES >= 1:
            for c in range(NCH):
                ln_chunk(0, c)
                drain(w1_q, 2)
            drain(w1_q, 99)
            for k in range(8):
                E("gd", lambda e, k=k: e.dma_start(
                    out=wo2[k][:], in_=l_wo[128 * k:128 * (k + 1), :]),
                  writes=[f"wo2_{k}"])
        if PHASES >= 2:
            attention(0, 0, hook_q=w2_q + q1_q)
        if PHASES >= 3:
            for c in range(NCH):
                ln_chunk(1, c)
            attention(1, 0, hook_q=lora_q)
        if PHASES >= 4:
            q2_first = q2_q[:len(q2_q) // 2]
            q2_second = q2_q[len(q2_q) // 2:]
            attention(0, 1, hook_q=q2_first)
            attention(1, 1, hook_q=q2_second)
            drain(q2_first, 999)
            drain(q2_second, 999)
        if PHASES >= 5:
            ffn_pass1()
        if DEBUG_TAPS and PHASES >= 5:
            E("sd", lambda e: e.dma_start(out=dbg1, in_=wfcT[0][:]),
              reads=["wfcT_0"], writes=[("dbg1",)])
            E("sd", lambda e: e.dma_start(out=dbg2, in_=bfcT[0:17, :]),
              reads=["bfcT"], writes=[("dbg2",)])
        if PHASES >= 6:
            ffn_pass2()
        p.wait_all("sd")

    return nc


_INPUT_MAP = [
    ("norm1_w", "L0_attn_norm1_w"), ("norm2_w", "L0_attn_norm2_w"),
    ("g_wq", "L0_attn_g_wq"), ("g_kv_down", "L0_attn_g_kv_down"),
    ("g_k_up", "L0_attn_g_k_up"), ("g_v_up", "L0_attn_g_v_up"),
    ("g_wo", "L0_attn_g_wo"), ("l_wq", "L0_attn_l_wq"),
    ("l_wk", "L0_attn_l_wk"), ("l_wv", "L0_attn_l_wv"),
    ("l_wo", "L0_attn_l_wo"), ("W_fc", "L0_ffn_W_fc"),
    ("b_fc", "L0_ffn_b_fc"), ("A_fc", "L0_ffn_A_fc"),
    ("B_fc", "L0_ffn_B_fc"), ("W_proj", "L0_ffn_W_proj"),
    ("b_proj", "L0_ffn_b_proj"), ("A_proj", "L0_ffn_A_proj"),
    ("B_proj", "L0_ffn_B_proj"),
]


def kernel(**inputs):
    nc = build_program()
    xfull = np.ascontiguousarray(np.asarray(inputs["L0_attn_hidden"], np.float32))
    w = {k: np.ascontiguousarray(np.asarray(inputs[v], np.float32))
         for k, v in _INPUT_MAP}
    in_maps = []
    for core in range(8):
        m = {"x": xfull[2 * core:2 * core + 2]}
        m.update(w)
        in_maps.append(m)
    res = run_bass_kernel_spmd(nc, in_maps, core_ids=list(range(8)))
    return np.concatenate([np.asarray(r["out"], np.float32)
                           for r in res.results], axis=0)


if __name__ == "__main__":
    nc = build_program()
    print("build ok")
